# version 02
# baseline (speedup 1.0000x reference)
import numpy as np
import ml_dtypes
import contextlib
import concourse.bass as bass
import concourse.mybir as mybir
from concourse.alu_op_type import AluOpType as OP

F32 = mybir.dt.float32
BF16 = mybir.dt.bfloat16
AF = mybir.ActivationFunctionType
AX = mybir.AxisListType
NPBF = ml_dtypes.bfloat16

TOK = 4096
NB = 2
T = 2048
D = 1024
DC = D // 128

ENGS = ("sp", "pool", "pe", "act", "dve")


class B:

    def __init__(self, nc, es):
        self.nc = nc
        self.es = es
        self.block = es.enter_context(nc.Block())
        self.sems = {}
        self.cnt = {}
        for e in ENGS:
            self._mksem("e_" + e)
        self.ops = {e: [] for e in ENGS}
        self.lastw = {}
        self.consts = {}

    def _mksem(self, name):
        self.sems[name] = self.es.enter_context(self.nc.semaphore(name))
        self.cnt[name] = 0

    def stream(self, name):
        if name not in self.sems:
            self._mksem(name)
        return name

    def raw(self, eng, fn):
        self.ops[eng].append(fn)

    def wait(self, eng, tok):
        if tok is None:
            return
        s, v = tok
        if v <= 0 or self.lastw.get((eng, s), 0) >= v:
            return
        self.lastw[(eng, s)] = v
        sem = self.sems[s]
        self.raw(eng, lambda e: e.wait_ge(sem, v))

    def waits(self, eng, toks):
        for t in toks:
            self.wait(eng, t)

    def dma(self, eng, stream, out, in_, **kw):
        s = self.stream(stream)
        self.cnt[s] += 16
        sem = self.sems[s]
        self.raw(eng, lambda e: e.dma_start(out, in_, **kw).then_inc(sem, 16))
        return (s, self.cnt[s])

    def dma_tr(self, eng, stream, out, in_):
        s = self.stream(stream)
        self.cnt[s] += 16
        sem = self.sems[s]
        self.raw(eng, lambda e: e.dma_start_transpose(out, in_).then_inc(sem, 16))
        return (s, self.cnt[s])

    def tok(self, stream):
        return (stream, self.cnt[stream])

    def op(self, eng, meth, *a, sig=True, **kw):
        if eng in ("dve", "act"):
            self.wait(eng, ("e_" + eng, self.cnt["e_" + eng]))
            sig = True
        if sig:
            s = "e_" + eng
            self.cnt[s] += 1
            v = self.cnt[s]
            sem = self.sems[s]
            self.raw(eng, lambda e: getattr(e, meth)(*a, **kw).then_inc(sem, 1))
            return (s, v)
        self.raw(eng, lambda e: getattr(e, meth)(*a, **kw))
        return None

    def mm(self, out, lhsT, rhs, start, stop, sig=False, tile_position=None,
           skip_group_check=False):
        kw = {}
        if tile_position is not None:
            kw["tile_position"] = tile_position
        if skip_group_check:
            kw["skip_group_check"] = True
        return self.op("pe", "matmul", out, lhsT, rhs, start=start, stop=stop,
                       sig=sig, **kw)

    def flush(self):
        for eng in ENGS:
            lst = self.ops[eng]
            if not lst:
                continue
            meth = {"sp": "sync", "pool": "gpsimd", "pe": "tensor",
                    "act": "scalar", "dve": "vector"}[eng]

            def runner(e, lst=lst):
                for f in lst:
                    f(e)
            getattr(self.block, meth)(runner)
            self.ops[eng] = []

    def scope_guard(self):
        for q in ("sp", "pool"):
            for e in ("e_pe", "e_act", "e_dve"):
                self.wait(q, (e, self.cnt[e]))

    def barrier(self):
        toks = [(s, self.cnt[s]) for s in self.sems if self.cnt[s] > 0]
        for eng in ENGS:
            for t in toks:
                self.wait(eng, t)

    def const(self, name, arr):
        if name not in self.consts:
            self.consts[name] = self.nc.inline_tensor(np.ascontiguousarray(arr), name)
        return self.consts[name]



def host_prep_weights(inp):
    f32 = lambda a: np.ascontiguousarray(np.asarray(a, dtype=np.float32))
    w = {}
    w["w_qkvT"] = f32(np.asarray(inp["L0_attn_w_qkv"]).T)
    w["b_qkv_l"] = f32(np.asarray(inp["L0_attn_b_qkv"]).reshape(24, 128).T)
    w["w_oT"] = f32(np.asarray(inp["L0_attn_w_o"]).T)
    w["b_o_l"] = f32(np.asarray(inp["L0_attn_b_o"]).reshape(8, 128).T)
    w["gate_wT"] = f32(np.asarray(inp["L0_ffn_gate_w"]).T)
    w["moe_w1"] = f32(inp["L0_ffn_w1"])
    w["moe_b1_r"] = f32(np.asarray(inp["L0_ffn_b1"]).reshape(1, 14 * 2048))
    w["moe_w2"] = f32(inp["L0_ffn_w2"])
    w["moe_b2_r"] = f32(np.asarray(inp["L0_ffn_b2"]).reshape(14, 1024))
    w["W_DQ"] = f32(inp["L1_attn_W_DQ"])
    w["W_UQ"] = f32(inp["L1_attn_W_UQ"])
    w["W_DKV"] = f32(inp["L1_attn_W_DKV"])
    w["W_UK"] = f32(inp["L1_attn_W_UK"])
    w["W_UV"] = f32(inp["L1_attn_W_UV"])
    wqr = np.asarray(inp["L1_attn_W_QR"], dtype=np.float32)
    wqr_pad = np.zeros((4096, 1024), dtype=np.float32)
    for h in range(16):
        wqr_pad[:, 64 * h:64 * h + 32] = wqr[:, 32 * h:32 * h + 32]
    w["W_QR_pad"] = np.ascontiguousarray(wqr_pad)
    w["W_KR"] = f32(inp["L1_attn_W_KR"])
    w["W_O1"] = f32(inp["L1_attn_W_O"])
    w["f_wqkv"] = f32(inp["L1_ffn_wqkv"])
    w["f_qs_l"] = f32(np.tile(np.asarray(inp["L1_ffn_q_scale"], np.float32), (1, 2)).T)
    w["f_ks_l"] = f32(np.tile(np.asarray(inp["L1_ffn_k_scale"], np.float32), (1, 2)).T)
    w["f_wo"] = f32(inp["L1_ffn_wo"])
    w["f_lng_l"] = f32(np.asarray(inp["L1_ffn_ln_g"], np.float32).reshape(2, 8, 128).transpose(0, 2, 1))
    w["f_lnb_l"] = f32(np.asarray(inp["L1_ffn_ln_b"], np.float32).reshape(2, 8, 128).transpose(0, 2, 1))
    w["f_w1"] = f32(inp["L1_ffn_w1"])
    w["f_b1_r"] = f32(np.asarray(inp["L1_ffn_b1"], np.float32).reshape(2, 1, 5460))
    w["f_w2"] = f32(inp["L1_ffn_w2"])
    w["f_b2_l"] = f32(np.asarray(inp["L1_ffn_b2"], np.float32).reshape(2, 8, 128).transpose(0, 2, 1))
    return w



def fm_hbm_ap(hbm, ncols=TOK):
    return hbm.rearrange("(c p) t -> p (c t)", p=128)


def emit_xbar_fm(b, dst_fm, hbm_tm, rows, stream="xb", eng="sp", col0=0, ncols=TOK):
    toks = []
    for c in range(rows // 128):
        toks.append(b.dma_tr(eng, stream,
                             dst_fm[:, c * ncols:(c + 1) * ncols],
                             hbm_tm[col0:col0 + ncols, c * 128:(c + 1) * 128]))
    return toks


def emit_prep(b, io, with_lo=True, with_gate=True, pfx="pr"):
    nc = b.nc
    es = contextlib.ExitStack()
    xt = [es.enter_context(nc.sbuf_tensor(f"{pfx}_x{i}", [128, D], F32)) for i in range(2)]
    xnh = es.enter_context(nc.sbuf_tensor(f"{pfx}_xnh", [128, D], BF16))
    xnh32 = es.enter_context(nc.sbuf_tensor(f"{pfx}_xnh32", [128, D], F32))
    xn32 = es.enter_context(nc.sbuf_tensor(f"{pfx}_xn32", [128, D], F32))
    xnl = es.enter_context(nc.sbuf_tensor(f"{pfx}_xnl", [128, D], BF16))
    sq = es.enter_context(nc.sbuf_tensor(f"{pfx}_sq", [128, D], F32))
    st = es.enter_context(nc.sbuf_tensor(f"{pfx}_st", [128, 8], F32))
    ld = b.stream(f"{pfx}_ld")
    stb = b.stream(f"{pfx}_st")

    for i in range(TOK // 128):
        buf = xt[i % 2]
        prev_dve = b.cnt["e_dve"]
        if i >= 2:
            b.wait("sp", ("e_dve", prev_dve))
        t_ld = b.dma("sp", ld, buf[:, :], io["x"][i * 128:(i + 1) * 128, :])
        b.wait("act", t_ld)
        b.wait("act", ("e_dve", prev_dve))
        b.op("act", "activation", sq[:, :], buf[:, :], AF.Square)
        b.wait("dve", ("e_act", b.cnt["e_act"]))
        b.wait("dve", b.tok(f"{pfx}_st") if i > 0 else None)
        b.op("dve", "reduce_sum", st[:, 0:1], buf[:, :], axis=AX.X)
        b.op("dve", "reduce_sum", st[:, 1:2], sq[:, :], axis=AX.X)
        b.op("dve", "tensor_scalar_mul", st[:, 0:1], st[:, 0:1], 1.0 / D)
        b.op("dve", "tensor_scalar_mul", st[:, 1:2], st[:, 1:2], 1.0 / D)
        b.op("dve", "tensor_mul", st[:, 2:3], st[:, 0:1], st[:, 0:1])
        b.op("dve", "tensor_sub", st[:, 2:3], st[:, 1:2], st[:, 2:3])
        t_v = b.op("dve", "tensor_scalar_add", st[:, 3:4], st[:, 2:3], 1e-5)
        b.wait("act", t_v)
        b.op("act", "activation", st[:, 3:4], st[:, 3:4], AF.Ln)
        t_r = b.op("act", "activation", st[:, 3:4], st[:, 3:4], AF.Exp, scale=-0.5)
        b.wait("dve", t_r)
        if with_lo:
            b.op("dve", "tensor_scalar", xn32[:, :], buf[:, :], st[:, 0:1], st[:, 3:4],
                 OP.subtract, OP.mult)
            b.op("dve", "tensor_copy", xnh[:, :], xn32[:, :])
            b.op("dve", "tensor_copy", xnh32[:, :], xnh[:, :])
            b.op("dve", "tensor_sub", xnh32[:, :], xn32[:, :], xnh32[:, :])
            t_l = b.op("dve", "tensor_copy", xnl[:, :], xnh32[:, :])
            b.wait("sp", t_l)
            b.dma("sp", stb, io["xn_tm"][i * 128:(i + 1) * 128, :], xnh[:, :])
            b.dma("sp", stb, io["xnl_tm"][i * 128:(i + 1) * 128, :], xnl[:, :])
        else:
            t_l = b.op("dve", "tensor_scalar", xnh[:, :], buf[:, :], st[:, 0:1],
                       st[:, 3:4], OP.subtract, OP.mult)
            b.wait("sp", t_l)
            b.dma("sp", stb, io["xn_tm"][i * 128:(i + 1) * 128, :], xnh[:, :])
    b.flush()
    b.barrier()
    if not with_gate:
        es.close()
        return

    xn_fm = es.enter_context(nc.sbuf_tensor(f"{pfx}_xnfm", [128, DC * TOK], BF16))
    xnl_fm = es.enter_context(nc.sbuf_tensor(f"{pfx}_xnlfm", [128, DC * TOK], BF16))
    gw32 = es.enter_context(nc.sbuf_tensor(f"{pfx}_gw32", [128, DC * 14], F32))
    gh = es.enter_context(nc.sbuf_tensor(f"{pfx}_gh", [128, DC * 14], BF16))
    gl32 = es.enter_context(nc.sbuf_tensor(f"{pfx}_gl32", [128, DC * 14], F32))
    gl = es.enter_context(nc.sbuf_tensor(f"{pfx}_gl", [128, DC * 14], BF16))
    Wt = es.enter_context(nc.sbuf_tensor(f"{pfx}_W", [128, 16], F32))
    gsc = es.enter_context(nc.sbuf_tensor(f"{pfx}_gsc", [128, 64], F32))
    gps = es.enter_context(nc.psum_tensor(f"{pfx}_gps", [128, 16], F32))

    emit_xbar_fm(b, xn_fm, io["xn_tm"], D, stream=f"{pfx}_xb")
    emit_xbar_fm(b, xnl_fm, io["xnl_tm"], D, stream=f"{pfx}_xb")
    t_g = b.dma("sp", f"{pfx}_gw", gw32[:, :],
                bass.AP(io["gate_wT"], 0, [[14, 128], [14 * 128, DC], [1, 14]]))
    b.wait("dve", t_g)
    b.op("dve", "tensor_copy", gh[:, :], gw32[:, :])
    b.op("dve", "tensor_copy", gl32[:, :], gh[:, :])
    b.op("dve", "tensor_sub", gl32[:, :], gw32[:, :], gl32[:, :])
    t_gl = b.op("dve", "tensor_copy", gl[:, :], gl32[:, :])
    b.wait("pe", t_gl)
    b.wait("pe", b.tok(f"{pfx}_xb"))

    b.stream(f"{pfx}_Wst")
    gps_free_at = 0
    for i in range(TOK // 128):
        b.wait("pe", ("e_dve", gps_free_at))
        first = True
        t_mm = None
        for c in range(DC):
            xh_v = xn_fm[:, c * TOK + i * 128: c * TOK + (i + 1) * 128]
            xl_v = xnl_fm[:, c * TOK + i * 128: c * TOK + (i + 1) * 128]
            b.mm(gps[:, 0:14], xh_v, gh[:, c * 14:(c + 1) * 14], start=first, stop=False)
            b.mm(gps[:, 0:14], xh_v, gl[:, c * 14:(c + 1) * 14], start=False, stop=False)
            t_mm = b.mm(gps[:, 0:14], xl_v, gh[:, c * 14:(c + 1) * 14], start=False,
                        stop=(c == DC - 1), sig=(c == DC - 1))
            first = False
        b.wait("dve", t_mm)
        b.wait("dve", b.tok(f"{pfx}_Wst"))
        c0, c1, c2, c3 = gsc[:, 0:1], gsc[:, 1:2], gsc[:, 2:3], gsc[:, 3:4]
        m1m = gsc[:, 16:30]
        m2m = gsc[:, 32:46]
        l2 = gsc[:, 48:62]
        b.op("dve", "reduce_max", c0, gps[:, 0:14], axis=AX.X)
        b.op("dve", "tensor_scalar", m1m, gps[:, 0:14], c0, None, OP.is_equal)
        b.op("dve", "scalar_tensor_tensor", l2, m1m, -1e30, gps[:, 0:14], OP.mult, OP.add)
        gps_free_at = b.cnt["e_dve"]
        b.op("dve", "reduce_max", c1, l2, axis=AX.X)
        b.op("dve", "tensor_scalar", m2m, l2, c1, None, OP.is_equal)
        b.op("dve", "tensor_sub", c2, c0, c1)
        t_d = b.op("dve", "tensor_copy", c3, c2)
        b.wait("act", t_d)
        t_s = b.op("act", "activation", c2, c2, AF.Sigmoid)
        b.wait("dve", t_s)
        b.op("dve", "tensor_scalar", c3, c2, -1.0, 1.0, OP.mult, OP.add)
        b.op("dve", "tensor_scalar_mul", m1m, m1m, c2)
        b.op("dve", "tensor_scalar_mul", m2m, m2m, c3)
        if i == 0:
            b.op("dve", "memset", Wt[:, 14:16], 0.0, sig=False)
        t_w = b.op("dve", "tensor_add", Wt[:, 0:14], m1m, m2m)
        b.wait("sp", t_w)
        b.dma("sp", f"{pfx}_Wst", io["W_hbm"][i * 128:(i + 1) * 128, :], Wt[:, :])
    b.flush()
    b.barrier()
    es.close()


def emit_l0attn(b, io):
    nc = b.nc
    es = contextlib.ExitStack()
    H, C = 16, 64
    xn_fm = es.enter_context(nc.sbuf_tensor("a0_xnfm", [128, DC * TOK], BF16))
    qk = es.enter_context(nc.sbuf_tensor("a0_qk", [128, 16 * T], BF16))
    wcol = [es.enter_context(nc.sbuf_tensor(f"a0_w{i}", [128, DC * 128], BF16))
            for i in range(2)]
    bq = es.enter_context(nc.sbuf_tensor("a0_bq", [128, 24], F32))
    bo = es.enter_context(nc.sbuf_tensor("a0_bo", [128, 8], F32))
    vstg = [es.enter_context(nc.sbuf_tensor(f"a0_vs{i}", [128, 512], BF16))
            for i in range(2)]
    vpair = [es.enter_context(nc.sbuf_tensor(f"a0_vp{i}", [128, T], BF16))
             for i in range(2)]
    VSL = 68
    vT = es.enter_context(nc.sbuf_tensor("a0_vT", [128, 2 * 16 * VSL], BF16))
    PT = [es.enter_context(nc.sbuf_tensor(f"a0_PT{i}", [128, 1024], BF16))
          for i in range(2)]
    o_fm = es.enter_context(nc.sbuf_tensor("a0_ofm", [128, DC * T], BF16))
    opair = es.enter_context(nc.sbuf_tensor("a0_opair", [128, 128], BF16))
    rcp = es.enter_context(nc.sbuf_tensor("a0_rcp", [128, 2], F32))
    s0b = es.enter_context(nc.sbuf_tensor("a0_s0", [128, 512], BF16))
    ident = es.enter_context(nc.sbuf_tensor("a0_id", [128, 192], BF16))

    ps_s = [es.enter_context(nc.psum_tensor(f"a0_pss{i}", [128, 1024], F32))
            for i in range(2)]
    ps_pv = [es.enter_context(nc.psum_tensor(f"a0_pspv{i}", [128, 260], F32))
             for i in range(2)]
    ps_tr = es.enter_context(nc.psum_tensor("a0_pstr", [128, 1024], BF16))

    for st in ("a0_ld", "a0_xb", "a0_wld", "a0_vst", "a0_vld", "a0_s0st"):
        b.stream(st)
    idp = np.zeros((128, 192), dtype=NPBF)
    idp[:, 0:128] = np.eye(128, dtype=NPBF)
    idp[0:64, 128:192] = np.eye(64, dtype=NPBF)
    idp[64:128, 128:192] = np.eye(64, dtype=NPBF)
    id_hbm = b.const("identp", idp)
    t_id = b.dma("sp", "a0_ld", ident[:, :], id_hbm[:, :])
    b.wait("pe", t_id)

    emit_xbar_fm(b, xn_fm, io["xn_tm"], D, stream="a0_xb")
    b.dma("sp", "a0_ld", bq[:, :], io["b_qkv_l"][:, :])
    b.dma("sp", "a0_ld", bo[:, :], io["b_o_l"][:, :])
    b.wait("pe", b.tok("a0_xb"))

    wfree = [0, 0]
    for bi in range(NB):
        tb = bi * T
        for m in range(24):
            wbuf = wcol[m % 2]
            b.wait("pool", ("e_pe", wfree[m % 2]))
            t_w = b.dma("pool", "a0_wld", wbuf[:, :],
                        bass.AP(io["w_qkvT"], m * 128,
                                [[3072, 128], [128 * 3072, DC], [1, 128]]))
            b.wait("pe", t_w)
            for s in range(4):
                ps = ps_s[0][:, (s % 2) * 512:(s % 2) * 512 + 512]
                b.wait("pe", ("e_act", b.cnt["e_act"]))
                for c in range(DC):
                    b.mm(ps, wbuf[:, c * 128:(c + 1) * 128],
                         xn_fm[:, c * TOK + tb + s * 512: c * TOK + tb + s * 512 + 512],
                         start=(c == 0), stop=(c == DC - 1), sig=(c == DC - 1))
                b.wait("act", ("e_pe", b.cnt["e_pe"]))
                if m < 16:
                    dst = qk[:, m * T + s * 512: m * T + s * 512 + 512]
                    b.op("act", "activation", dst, ps, AF.Identity, bias=bq[:, m:m + 1])
                else:
                    vbuf = vstg[s % 2]
                    b.wait("act", b.tok("a0_vst"))
                    t_e = b.op("act", "activation", vbuf[:, :], ps, AF.Identity,
                               bias=bq[:, m:m + 1])
                    b.wait("sp", t_e)
                    b.dma("sp", "a0_vst",
                          io["v_hbm"][(m - 16) * 128:(m - 15) * 128, tb + s * 512: tb + s * 512 + 512],
                          vbuf[:, :])
            wfree[m % 2] = b.cnt["e_pe"]
        b.flush()

        b.wait("sp", b.tok("a0_vst"))
        vpfree = [0, 0]
        for hp in range(8):
            h0, h1 = 2 * hp, 2 * hp + 1
            vbuf = vpair[hp % 2]
            b.wait("sp", ("e_pe", vpfree[hp % 2]))
            t_v = b.dma("sp", "a0_vld", vbuf[:, :],
                        io["v_hbm"][hp * 128:(hp + 1) * 128, tb: tb + T])
            b.wait("dve", ("e_pe", b.cnt["e_pe"]))
            b.op("dve", "memset", vT[:, :], 1.0)
            b.wait("pe", t_v)
            b.wait("pe", ("e_dve", b.cnt["e_dve"]))
            for hh in range(2):
                b.wait("pe", ("e_dve", b.cnt["e_dve"]))
                for mt in range(16):
                    b.op("pe", "transpose",
                         ps_tr[:, mt * 64:(mt + 1) * 64],
                         vbuf[hh * 64:(hh + 1) * 64, mt * 128:(mt + 1) * 128],
                         ident[hh * 64:(hh + 1) * 64, 128:192], sig=(mt == 15))
                b.wait("dve", ("e_pe", b.cnt["e_pe"]))
                b.op("dve", "tensor_copy",
                     bass.AP(vT, hh * 16 * VSL, [[2 * 16 * VSL, 128], [VSL, 16], [1, 64]]),
                     ps_tr[:, 0:1024])
            vpfree[hp % 2] = b.cnt["e_pe"]
            b.wait("pe", ("e_dve", b.cnt["e_dve"]))

            for lb in range(4):
                lc = tb + lb * 512
                pvfree = b.cnt["e_dve"]
                for rnd in range(8):
                    b.wait("pe", ("e_act", b.cnt["e_act"]))
                    if rnd == 0:
                        b.wait("pe", ("e_dve", pvfree))
                    for sub in range(2):
                        mt = rnd * 2 + sub
                        for hh, h in ((0, h0), (1, h1)):
                            base = 64 * (h % 2)
                            qv = qk[64 * (h % 2):64 * (h % 2) + 64,
                                    (h // 2) * T + lb * 512:(h // 2) * T + lb * 512 + 512]
                            kv = qk[64 * (h % 2):64 * (h % 2) + 64,
                                    (8 + h // 2) * T + mt * 128:(8 + h // 2) * T + mt * 128 + 128]
                            b.mm(ps_s[hh][:, sub * 512:sub * 512 + 512], kv, qv,
                                 start=True, stop=True, sig=(sub == 1 and hh == 1))
                    b.wait("act", ("e_pe", b.cnt["e_pe"]))
                    t_e0 = b.op("act", "activation", PT[0][:, :], ps_s[0][:, :], AF.Exp,
                                scale=1.0 / 32.0)
                    t_e1 = b.op("act", "activation", PT[1][:, :], ps_s[1][:, :], AF.Exp,
                                scale=1.0 / 32.0)
                    b.wait("pe", t_e1)
                    for sub in range(2):
                        mt = rnd * 2 + sub
                        for hh in range(2):
                            for ls in range(4):
                                b.mm(ps_pv[hh][:, ls * 65:ls * 65 + 65],
                                     PT[hh][:, sub * 512 + ls * 128:sub * 512 + ls * 128 + 128],
                                     vT[:, hh * 16 * VSL + mt * VSL: hh * 16 * VSL + mt * VSL + 65],
                                     start=(rnd == 0 and sub == 0 and ls == 0),
                                     stop=(rnd == 7 and sub == 1),
                                     sig=(rnd == 7 and sub == 1 and hh == 1 and ls == 3))
                b.wait("dve", ("e_pe", b.cnt["e_pe"]))
                for ls in range(4):
                    for hh in range(2):
                        b.op("dve", "reciprocal", rcp[:, hh:hh + 1],
                             ps_pv[hh][:, ls * 65 + 64: ls * 65 + 65])
                        t_o = b.op("dve", "tensor_scalar_mul",
                                   opair[:, hh * 64:hh * 64 + 64],
                                   ps_pv[hh][:, ls * 65: ls * 65 + 64],
                                   rcp[:, hh:hh + 1])
                    b.wait("pe", t_o)
                    b.op("pe", "transpose", ps_tr[:, ls * 128:(ls + 1) * 128],
                         opair[:, :], ident[:, 0:128], sig=True)
                    b.wait("dve", ("e_pe", b.cnt["e_pe"]))
                t_c = b.op("dve", "tensor_copy", o_fm[:, hp * T + lb * 512: hp * T + (lb + 1) * 512],
                           ps_tr[:, 0:512])
                b.wait("pe", t_c)
        b.flush()

        for dch in range(DC):
            wbuf = wcol[dch % 2]
            b.wait("pool", ("e_pe", wfree[dch % 2]))
            t_w = b.dma("pool", "a0_wld", wbuf[:, :],
                        bass.AP(io["w_oT"], dch * 128,
                                [[1024, 128], [128 * 1024, DC], [1, 128]]))
            b.wait("pe", t_w)
            for s in range(4):
                ps = ps_s[0][:, (s % 2) * 512:(s % 2) * 512 + 512]
                b.wait("pe", ("e_dve", b.cnt["e_dve"]))
                for c in range(DC):
                    b.mm(ps, wbuf[:, c * 128:(c + 1) * 128],
                         o_fm[:, c * T + s * 512: c * T + s * 512 + 512],
                         start=(c == 0), stop=(c == DC - 1), sig=(c == DC - 1))
                b.wait("dve", ("e_pe", b.cnt["e_pe"]))
                b.wait("dve", b.tok("a0_s0st"))
                t_s = b.op("dve", "scalar_tensor_tensor", s0b[:, :], ps,
                           bo[:, dch:dch + 1],
                           xn_fm[:, dch * TOK + tb + s * 512: dch * TOK + tb + s * 512 + 512],
                           OP.add, OP.add)
                b.wait("sp", t_s)
                b.dma("sp", "a0_s0st",
                      io["s0_fm"][dch * 128:(dch + 1) * 128, tb + s * 512: tb + s * 512 + 512],
                      s0b[:, :])
            wfree[dch % 2] = b.cnt["e_pe"]
        b.flush()
    b.barrier()
    es.close()


def emit_moe(b, io):
    nc = b.nc
    es = contextlib.ExitStack()
    NE = 14
    SLAB = 512
    NSLAB = TOK // SLAB
    xn_fm = es.enter_context(nc.sbuf_tensor("mo_xnfm", [128, DC * TOK], BF16))
    Wtm = es.enter_context(nc.sbuf_tensor("mo_W", [128, 32 * 16], F32))
    Wbf = es.enter_context(nc.sbuf_tensor("mo_Wbf", [128, 16], BF16))
    Wfm = es.enter_context(nc.sbuf_tensor("mo_Wfm", [16, 32 * 128], BF16))
    w1sb = es.enter_context(nc.sbuf_tensor("mo_w1", [128, DC * 2048], BF16))
    w2sb = es.enter_context(nc.sbuf_tensor("mo_w2", [128, 16 * 1024], BF16))
    b1l = es.enter_context(nc.sbuf_tensor("mo_b1", [128, 224], F32))
    b2f = es.enter_context(nc.sbuf_tensor("mo_b2f", [14, 1024], F32))
    b2h = es.enter_context(nc.sbuf_tensor("mo_b2h", [14, 1024], BF16))
    b2l32 = es.enter_context(nc.sbuf_tensor("mo_b2l32", [14, 1024], F32))
    b2lo = es.enter_context(nc.sbuf_tensor("mo_b2l", [14, 1024], BF16))
    Hs = es.enter_context(nc.sbuf_tensor("mo_H", [128, 16 * SLAB], BF16))
    stage = es.enter_context(nc.sbuf_tensor("mo_stg", [128, 4 * 1024], F32))
    ident = es.enter_context(nc.sbuf_tensor("mo_id", [128, 128], BF16))
    psA = [es.enter_context(nc.psum_tensor(f"mo_psA{i}", [128, 512], F32)) for i in range(2)]
    psB = [es.enter_context(nc.psum_tensor(f"mo_psB{i}", [128, 1024], F32)) for i in range(2)]
    psT = es.enter_context(nc.psum_tensor("mo_psT", [16, 128], BF16))

    for st in ("mo_ld", "mo_xb", "mo_w1ld", "mo_w2ld", "mo_acc", "mo_fin", "mo_x1"):
        b.stream(st)
    id_hbm = b.const("ident128b", np.eye(128, dtype=NPBF))
    b.dma("sp", "mo_ld", ident[:, :], id_hbm[:, :])
    emit_xbar_fm(b, xn_fm, io["xn_tm"], D, stream="mo_xb")
    b.dma("sp", "mo_ld", Wtm[:, :],
          bass.AP(io["W_hbm"], 0, [[16, 128], [128 * 16, 32], [1, 16]]))
    b.dma("sp", "mo_ld", b1l[:, :], io["moe_b1_l"][:, :])
    t_b2 = b.dma("sp", "mo_ld", b2f[:, :], io["moe_b2"][:, :])
    b.wait("dve", t_b2)
    b.op("dve", "tensor_copy", b2h[:, :], b2f[:, :])
    b.op("dve", "tensor_copy", b2l32[:, :], b2h[:, :])
    b.op("dve", "tensor_sub", b2l32[:, :], b2f[:, :], b2l32[:, :])
    b.op("dve", "tensor_copy", b2lo[:, :], b2l32[:, :])
    b.wait("dve", b.tok("mo_ld"))
    for i in range(32):
        t_c = b.op("dve", "tensor_copy", Wbf[:, :], Wtm[:, i * 16:(i + 1) * 16])
        b.wait("pe", t_c)
        t_t = b.op("pe", "transpose", psT[:, :], Wbf[:, :], ident[:, :], sig=True)
        b.wait("dve", t_t)
        t_c2 = b.op("dve", "tensor_copy", Wfm[0:16, i * 128:(i + 1) * 128], psT[:, :])
        b.wait("pe", t_c2)
    b.flush()
    b.wait("pe", b.tok("mo_xb"))

    for e in range(NE):
        b.wait("pool", ("e_pe", b.cnt["e_pe"]))
        t_w1 = b.dma("pool", "mo_w1ld", w1sb[:, :],
                     bass.AP(io["moe_w1"], e * 1024 * 2048,
                             [[2048, 128], [128 * 2048, DC], [1, 2048]]))
        t_w2 = b.dma("pool", "mo_w2ld", w2sb[:, :],
                     bass.AP(io["moe_w2"], e * 2048 * 1024,
                             [[1024, 128], [128 * 1024, 16], [1, 1024]]))
        b.wait("pe", t_w1)
        b.wait("pe", t_w2)
        for sl in range(NSLAB):
            tb = sl * SLAB
            for m in range(16):
                ps = psA[m % 2]
                b.wait("pe", ("e_act", b.cnt["e_act"]))
                for c in range(DC):
                    b.mm(ps[:, :], w1sb[:, c * 2048 + m * 128: c * 2048 + m * 128 + 128],
                         xn_fm[:, c * TOK + tb: c * TOK + tb + SLAB],
                         start=(c == 0), stop=(c == DC - 1), sig=(c == DC - 1))
                b.wait("act", ("e_pe", b.cnt["e_pe"]))
                b.wait("act", ("e_pe", b.cnt["e_pe"]))
                b.op("act", "activation", Hs[:, m * SLAB:(m + 1) * SLAB], ps[:, :],
                     AF.Gelu, bias=b1l[:, e * 16 + m: e * 16 + m + 1])
            b.wait("pe", ("e_act", b.cnt["e_act"]))
            for tt in range(4):
                ps = psB[tt % 2]
                b.wait("pe", ("e_dve", b.cnt["e_dve"]))
                for c in range(16):
                    lhs = Hs[:, c * SLAB + tt * 128: c * SLAB + tt * 128 + 128]
                    b.mm(ps[:, 0:512], lhs, w2sb[:, c * 1024: c * 1024 + 512],
                         start=(c == 0), stop=(c == 15))
                    b.mm(ps[:, 512:1024], lhs, w2sb[:, c * 1024 + 512: c * 1024 + 1024],
                         start=(c == 0), stop=(c == 15), sig=(c == 15))
                b.wait("dve", ("e_pe", b.cnt["e_pe"]))
                b.wait("dve", b.tok("mo_acc"))
                b.op("dve", "tensor_scalar_mul", stage[:, tt * 1024:(tt + 1) * 1024],
                     ps[:, :], Wtm[:, (sl * 4 + tt) * 16 + e: (sl * 4 + tt) * 16 + e + 1])
            t_st = ("e_dve", b.cnt["e_dve"])
            b.wait("pool", t_st)
            b.dma("pool", "mo_acc",
                  bass.AP(io["acc_hbm"], tb * D, [[D, 128], [128 * D, 4], [1, D]]),
                  stage[:, :],
                  **({} if e == 0 else {"accum_op": OP.add}))
        b.flush()

    for i in range(32):
        ps = psB[i % 2]
        b.wait("pe", ("e_dve", b.cnt["e_dve"]))
        wv = Wfm[0:14, i * 128:(i + 1) * 128]
        b.mm(ps[:, 0:512], wv, b2h[0:14, 0:512], start=True, stop=False)
        b.mm(ps[:, 0:512], wv, b2lo[0:14, 0:512], start=False, stop=True)
        b.mm(ps[:, 512:1024], wv, b2h[0:14, 512:1024], start=True, stop=False)
        b.mm(ps[:, 512:1024], wv, b2lo[0:14, 512:1024], start=False, stop=True, sig=True)
        b.wait("dve", ("e_pe", b.cnt["e_pe"]))
        b.wait("dve", b.tok("mo_acc"))
        t_c = b.op("dve", "tensor_copy", stage[:, 0:1024], ps[:, :])
        b.wait("pool", t_c)
        b.dma("pool", "mo_acc", io["acc_hbm"][i * 128:(i + 1) * 128, :],
              stage[:, 0:1024], accum_op=OP.add)
    b.flush()

    accT = stage[:, 0:1024]
    xT = stage[:, 1024:2048]
    t1T = stage[:, 2048:3072]
    x1T = stage[:, 3072:4096]
    xhT = es.enter_context(nc.sbuf_tensor("mo_xh", [128, D], BF16))
    xlT = es.enter_context(nc.sbuf_tensor("mo_xl", [128, D], BF16))
    s0T = es.enter_context(nc.sbuf_tensor("mo_s0", [128, D], BF16))
    gT = es.enter_context(nc.sbuf_tensor("mo_g", [128, D], BF16))
    b.wait("sp", b.tok("mo_acc"))
    for i in range(32):
        prev_dve = b.cnt["e_dve"]
        b.wait("sp", ("e_dve", prev_dve))
        b.wait("sp", b.tok("mo_x1"))
        t_a = b.dma("sp", "mo_fin", accT, io["acc_hbm"][i * 128:(i + 1) * 128, :])
        t_h = b.dma("sp", "mo_fin", xhT[:, :], io["xn_tm"][i * 128:(i + 1) * 128, :])
        t_l = b.dma("sp", "mo_fin", xlT[:, :], io["xnl_tm"][i * 128:(i + 1) * 128, :])
        t_x = b.dma("sp", "mo_fin", xT, io["x"][i * 128:(i + 1) * 128, :])
        t_s = b.dma_tr("sp", "mo_fin", s0T[:, :],
                       io["s0_fm"][:, i * 128:(i + 1) * 128])
        b.wait("dve", b.tok("mo_fin"))
        b.op("dve", "tensor_add", t1T, xhT[:, :], xlT[:, :])
        b.op("dve", "tensor_add", t1T, t1T, accT)
        t_cl = b.op("dve", "tensor_scalar", t1T, t1T, -100.0, 100.0,
                    OP.max, OP.min)
        b.wait("act", t_cl)
        t_g = b.op("act", "activation", gT[:, :], t1T, AF.Silu)
        b.wait("dve", t_g)
        b.op("dve", "tensor_mul", gT[:, :], gT[:, :], s0T[:, :])
        t_x1 = b.op("dve", "tensor_add", x1T, xT, gT[:, :])
        b.wait("sp", t_x1)
        b.dma("sp", "mo_x1", io["x1"][i * 128:(i + 1) * 128, :], x1T)
    b.flush()
    b.barrier()
    es.close()


def rope_scale_np(Tn=T, dhr=32):
    t = np.arange(Tn, dtype=np.float64)
    inv = 1.0 / 10000.0 ** (np.arange(0, dhr, 2, dtype=np.float64) / dhr)
    freqs = t[:, None] * inv[None, :]
    emb = np.concatenate([freqs, freqs], -1)
    return (np.cos(emb) + np.sin(emb)).astype(np.float32)


def emit_l1attn(b, io):
    nc = b.nc
    es = contextlib.ExitStack()
    inv96 = 1.0 / np.sqrt(np.float32(96.0))
    rs = rope_scale_np()
    qsc = np.zeros((128, T), np.float32)
    ksc = np.zeros((128, T), np.float32)
    for p in range(128):
        qsc[p] = rs[:, p % 32] * inv96
    for p in list(range(32)) + list(range(64, 96)):
        ksc[p] = rs[:, p % 32]
    qsc_h = b.const("l1_qsc", qsc)
    ksc_h = b.const("l1_ksc", ksc)
    masks = np.zeros((128, 4 * 512), np.float32)
    ii = np.arange(128)[:, None]
    jj = np.arange(512)[None, :]
    for k in range(4):
        masks[:, k * 512:(k + 1) * 512] = np.where(ii + 128 * k > jj, -1e30, 0.0)
    mask_h = b.const("l1_masks", masks.astype(NPBF))
    idp = np.zeros((128, 192), dtype=NPBF)
    idp[:, 0:128] = np.eye(128, dtype=NPBF)
    idp[0:64, 128:192] = np.eye(64, dtype=NPBF)
    idp[64:128, 128:192] = np.eye(64, dtype=NPBF)
    id_hbm = b.const("identp", idp)

    xn_fm = es.enter_context(nc.sbuf_tensor("l1_xnfm", [128, DC * TOK], BF16))
    qsc_sb = es.enter_context(nc.sbuf_tensor("l1_qscs", [128, T], F32))
    ksc_sb = es.enter_context(nc.sbuf_tensor("l1_kscs", [128, T], F32))
    mask_sb = es.enter_context(nc.sbuf_tensor("l1_msk", [128, 4 * 512], BF16))
    ident = es.enter_context(nc.sbuf_tensor("l1_id", [128, 192], BF16))
    kr = es.enter_context(nc.sbuf_tensor("l1_kr", [128, TOK], BF16))

    for st in ("l1_ld", "l1_xb", "l1_wld", "l1_st", "l1_pld", "l1_s1st"):
        b.stream(st)
    b.dma("sp", "l1_ld", qsc_sb[:, :], qsc_h[:, :])
    b.dma("sp", "l1_ld", ksc_sb[:, :], ksc_h[:, :])
    b.dma("sp", "l1_ld", mask_sb[:, :], mask_h[:, :])
    b.dma("sp", "l1_ld", ident[:, :], id_hbm[:, :])
    emit_xbar_fm(b, xn_fm, io["xn_tm"], D, stream="l1_xb")
    b.wait("pe", b.tok("l1_xb"))
    b.wait("pe", b.tok("l1_ld"))

    esA = contextlib.ExitStack()
    wsmall = esA.enter_context(nc.sbuf_tensor("l1_wsm", [128, DC * 256], BF16))
    wuk = esA.enter_context(nc.sbuf_tensor("l1_wuk", [128, 2 * 1024], BF16))
    wuv = esA.enter_context(nc.sbuf_tensor("l1_wuv", [128, 2 * 1024], BF16))
    wkr = esA.enter_context(nc.sbuf_tensor("l1_wkr", [128, DC * 32], BF16))
    wkrd = esA.enter_context(nc.sbuf_tensor("l1_wkrd", [128, DC * 128], BF16))
    ckv = esA.enter_context(nc.sbuf_tensor("l1_ckv", [128, 2 * T], BF16))
    wcol = [esA.enter_context(nc.sbuf_tensor(f"l1_wc{i}", [128, 32 * 128], BF16))
            for i in range(2)]
    cq = esA.enter_context(nc.sbuf_tensor("l1_cq", [128, 32 * 1024], BF16))
    estg = [esA.enter_context(nc.sbuf_tensor(f"l1_es{i}", [128, 512], BF16))
            for i in range(2)]
    psA = [esA.enter_context(nc.psum_tensor(f"l1_psA{i}", [128, 512], F32))
           for i in range(2)]
    ps4 = esA.enter_context(nc.psum_tensor("l1_ps4", [128, 2048], F32))
    pskr = esA.enter_context(nc.psum_tensor("l1_pskr", [128, 512], F32))

    t_w = b.dma("pool", "l1_wld", wsmall[:, :],
                bass.AP(io["W_DKV"], 0, [[256, 128], [128 * 256, DC], [1, 256]]))
    b.dma("pool", "l1_wld", wuk[:, :],
          bass.AP(io["W_UK"], 0, [[1024, 128], [128 * 1024, 2], [1, 1024]]))
    b.dma("pool", "l1_wld", wuv[:, :],
          bass.AP(io["W_UV"], 0, [[1024, 128], [128 * 1024, 2], [1, 1024]]))
    b.dma("pool", "l1_wld", wkr[:, :],
          bass.AP(io["W_KR"], 0, [[32, 128], [128 * 32, DC], [1, 32]]))
    b.wait("dve", b.tok("l1_wld"))
    b.op("dve", "memset", wkrd[:, :], 0.0)
    for c in range(DC):
        b.op("dve", "tensor_copy", wkrd[:, c * 128: c * 128 + 32],
             wkr[:, c * 32:(c + 1) * 32])
        b.op("dve", "tensor_copy", wkrd[:, c * 128 + 64: c * 128 + 96],
             wkr[:, c * 32:(c + 1) * 32])
    b.wait("pe", b.tok("l1_wld"))
    b.wait("pe", ("e_dve", b.cnt["e_dve"]))

    for bi in range(NB):
        tb = bi * T
        for m in range(2):
            for s in range(4):
                ps = psA[s % 2]
                b.wait("pe", ("e_act", b.cnt["e_act"]))
                b.wait("pe", ("e_dve", b.cnt["e_dve"]))
                for c in range(DC):
                    b.mm(ps[:, :], wsmall[:, c * 256 + m * 128: c * 256 + m * 128 + 128],
                         xn_fm[:, c * TOK + tb + s * 512: c * TOK + tb + s * 512 + 512],
                         start=(c == 0), stop=(c == DC - 1), sig=(c == DC - 1))
                b.wait("act", ("e_pe", b.cnt["e_pe"]))
                b.op("act", "activation", ckv[:, m * T + s * 512: m * T + s * 512 + 512],
                     ps[:, :], AF.Identity)
        for (wt, dst) in ((wuk, io["kc_hbm"]), (wuv, io["v_hbm"])):
            for m in range(DC):
                for s in range(4):
                    ps = psA[s % 2]
                    b.wait("pe", ("e_act", b.cnt["e_act"]))
                    b.wait("pe", ("e_dve", b.cnt["e_dve"]))
                    for c in range(2):
                        b.mm(ps[:, :], wt[:, c * 1024 + m * 128: c * 1024 + m * 128 + 128],
                             ckv[:, c * T + s * 512: c * T + s * 512 + 512],
                             start=(c == 0), stop=(c == 1), sig=(c == 1))
                    b.wait("act", ("e_pe", b.cnt["e_pe"]))
                    stg = estg[s % 2]
                    b.wait("act", b.tok("l1_st"))
                    t_e = b.op("act", "activation", stg[:, :], ps[:, :], AF.Identity)
                    b.wait("sp", t_e)
                    b.dma("sp", "l1_st", dst[m * 128:(m + 1) * 128, tb + s * 512: tb + s * 512 + 512],
                          stg[:, :])
        for s in range(4):
            b.wait("pe", ("e_dve", b.cnt["e_dve"]))
            for c in range(DC):
                b.mm(pskr[:, :], wkrd[:, c * 128:(c + 1) * 128],
                     xn_fm[:, c * TOK + tb + s * 512: c * TOK + tb + s * 512 + 512],
                     start=(c == 0), stop=(c == DC - 1), sig=(c == DC - 1))
            b.wait("dve", ("e_pe", b.cnt["e_pe"]))
            b.op("dve", "tensor_mul", kr[:, tb + s * 512: tb + s * 512 + 512],
                 pskr[:, :], ksc_sb[:, s * 512: s * 512 + 512])
        wfree = [0, 0]
        for hs in range(2):
            ts0 = tb + hs * 1024
            for m in range(32):
                wbuf = wcol[m % 2]
                b.wait("pool", ("e_pe", wfree[m % 2]))
                t_w = b.dma("pool", "l1_wld", wbuf[:, 0:DC * 128],
                            bass.AP(io["W_DQ"], m * 128,
                                    [[4096, 128], [128 * 4096, DC], [1, 128]]))
                b.wait("pe", t_w)
                grp = m % 4
                ps = ps4[:, grp * 512:(grp + 1) * 512]
                if grp == 0:
                    b.wait("pe", ("e_act", b.cnt["e_act"]))
                for c in range(DC):
                    b.mm(ps, wbuf[:, c * 128:(c + 1) * 128],
                         xn_fm[:, c * TOK + ts0: c * TOK + ts0 + 512],
                         start=(c == 0), stop=(c == DC - 1),
                         sig=(c == DC - 1))
                wfree[m % 2] = b.cnt["e_pe"]
                if grp == 3:
                    b.wait("act", ("e_pe", b.cnt["e_pe"]))
                    b.op("act", "activation",
                         bass.AP(cq, (m - 3) * 1024, [[32 * 1024, 128], [1024, 4], [1, 512]]),
                         ps4[:, :], AF.Identity)
            for m in range(32):
                wbuf = wcol[m % 2]
                b.wait("pool", ("e_pe", wfree[m % 2]))
                t_w = b.dma("pool", "l1_wld", wbuf[:, 0:DC * 128],
                            bass.AP(io["W_DQ"], m * 128,
                                    [[4096, 128], [128 * 4096, DC], [1, 128]]))
                b.wait("pe", t_w)
                grp = m % 4
                ps = ps4[:, grp * 512:(grp + 1) * 512]
                if grp == 0:
                    b.wait("pe", ("e_act", b.cnt["e_act"]))
                for c in range(DC):
                    b.mm(ps, wbuf[:, c * 128:(c + 1) * 128],
                         xn_fm[:, c * TOK + ts0 + 512: c * TOK + ts0 + 1024],
                         start=(c == 0), stop=(c == DC - 1),
                         sig=(c == DC - 1))
                wfree[m % 2] = b.cnt["e_pe"]
                if grp == 3:
                    b.wait("act", ("e_pe", b.cnt["e_pe"]))
                    b.op("act", "activation",
                         bass.AP(cq, (m - 3) * 1024 + 512,
                                 [[32 * 1024, 128], [1024, 4], [1, 512]]),
                         ps4[:, :], AF.Identity)
            for (wh, dst_h, scale_mode) in ((io["W_UQ"], io["qc_hbm"], "act"),
                                            (io["W_QR_pad"], io["qr_hbm"], "dve")):
                for m in range(DC):
                    wbuf = wcol[m % 2]
                    b.wait("pool", ("e_pe", wfree[m % 2]))
                    t_w = b.dma("pool", "l1_wld", wbuf[:, :],
                                bass.AP(wh, m * 128,
                                        [[1024, 128], [128 * 1024, 32], [1, 128]]))
                    b.wait("pe", t_w)
                    for half in range(2):
                        ps = psA[half]
                        b.wait("pe", ("e_act", b.cnt["e_act"]))
                        b.wait("pe", ("e_dve", b.cnt["e_dve"]))
                        for c in range(32):
                            b.mm(ps[:, :], wbuf[:, c * 128:(c + 1) * 128],
                                 cq[:, c * 1024 + half * 512: c * 1024 + half * 512 + 512],
                                 start=(c == 0), stop=(c == 31), sig=(c == 31))
                        wfree[m % 2] = b.cnt["e_pe"]
                        stg = estg[half]
                        tloc = hs * 1024 + half * 512
                        if scale_mode == "act":
                            b.wait("act", ("e_pe", b.cnt["e_pe"]))
                            b.wait("act", b.tok("l1_st"))
                            t_e = b.op("act", "activation", stg[:, :], ps[:, :],
                                       AF.Identity, scale=float(inv96))
                        else:
                            b.wait("dve", ("e_pe", b.cnt["e_pe"]))
                            b.wait("dve", b.tok("l1_st"))
                            t_e = b.op("dve", "tensor_mul", stg[:, :], ps[:, :],
                                       qsc_sb[:, (tloc % T): (tloc % T) + 512])
                        b.wait("sp", t_e)
                        b.dma("sp", "l1_st",
                              dst_h[m * 128:(m + 1) * 128, tb + tloc: tb + tloc + 512],
                              stg[:, :])
    b.flush()
    esA.close()

    esB = contextlib.ExitStack()
    VSL = 68
    qcp = [esB.enter_context(nc.sbuf_tensor(f"l1_qc{i}", [128, T], BF16)) for i in range(2)]
    qrp = [esB.enter_context(nc.sbuf_tensor(f"l1_qr{i}", [128, T], BF16)) for i in range(2)]
    kcp = [esB.enter_context(nc.sbuf_tensor(f"l1_kc{i}", [128, T], BF16)) for i in range(2)]
    vp = [esB.enter_context(nc.sbuf_tensor(f"l1_vp{i}", [128, T], BF16)) for i in range(2)]
    vT = esB.enter_context(nc.sbuf_tensor("l1_vT", [128, 2 * 16 * VSL], BF16))
    PT = [esB.enter_context(nc.sbuf_tensor(f"l1_PT{i}", [128, 1024], BF16)) for i in range(2)]
    o_fm = esB.enter_context(nc.sbuf_tensor("l1_ofm", [128, DC * T], BF16))
    opair = esB.enter_context(nc.sbuf_tensor("l1_op", [128, 128], BF16))
    rcp = esB.enter_context(nc.sbuf_tensor("l1_rcp", [128, 2], F32))
    s1b = esB.enter_context(nc.sbuf_tensor("l1_s1b", [128, 512], BF16))
    wcolB = [esB.enter_context(nc.sbuf_tensor(f"l1_wB{i}", [128, DC * 128], BF16))
             for i in range(2)]
    ps_s = [esB.enter_context(nc.psum_tensor(f"l1_pss{i}", [128, 1024], F32))
            for i in range(2)]
    ps_pv = [esB.enter_context(nc.psum_tensor(f"l1_pspv{i}", [128, 260], F32))
             for i in range(2)]
    ps_tr = esB.enter_context(nc.psum_tensor("l1_pstr", [128, 1024], BF16))

    b.wait("sp", b.tok("l1_st"))
    b.wait("pe", ("e_dve", b.cnt["e_dve"]))
    b.scope_guard()
    pfree = [0, 0]
    wfreeB = [0, 0]
    for bi in range(NB):
        tb = bi * T
        for hp in range(8):
            h0, h1 = 2 * hp, 2 * hp + 1
            pb = hp % 2
            b.wait("sp", ("e_pe", pfree[pb]))
            t0_ = b.dma("sp", "l1_pld", qcp[pb][:, :], io["qc_hbm"][hp * 128:(hp + 1) * 128, tb: tb + T])
            b.dma("sp", "l1_pld", qrp[pb][:, :], io["qr_hbm"][hp * 128:(hp + 1) * 128, tb: tb + T])
            b.dma("sp", "l1_pld", kcp[pb][:, :], io["kc_hbm"][hp * 128:(hp + 1) * 128, tb: tb + T])
            t_v = b.dma("sp", "l1_pld", vp[pb][:, :], io["v_hbm"][hp * 128:(hp + 1) * 128, tb: tb + T])
            b.wait("dve", ("e_pe", b.cnt["e_pe"]))
            b.op("dve", "memset", vT[:, :], 1.0)
            b.wait("pe", t_v)
            for hh in range(2):
                b.wait("pe", ("e_dve", b.cnt["e_dve"]))
                for mt in range(16):
                    b.op("pe", "transpose",
                         ps_tr[:, mt * 64:(mt + 1) * 64],
                         vp[pb][hh * 64:(hh + 1) * 64, mt * 128:(mt + 1) * 128],
                         ident[hh * 64:(hh + 1) * 64, 128:192], sig=(mt == 15))
                b.wait("dve", ("e_pe", b.cnt["e_pe"]))
                b.op("dve", "tensor_copy",
                     bass.AP(vT, hh * 16 * VSL, [[2 * 16 * VSL, 128], [VSL, 16], [1, 64]]),
                     ps_tr[:, 0:1024])
            b.wait("pe", ("e_dve", b.cnt["e_dve"]))

            for lb in range(4):
                nmt = 4 * lb + 4
                nrnd = nmt // 2
                pvfree = b.cnt["e_dve"]
                for rnd in range(nrnd):
                    b.wait("pe", ("e_act", b.cnt["e_act"]))
                    if rnd == 0:
                        b.wait("pe", ("e_dve", pvfree))
                    for sub in range(2):
                        mt = rnd * 2 + sub
                        kdiag = mt - 4 * lb
                        for hh, h in ((0, h0), (1, h1)):
                            base = 64 * (h % 2)
                            qcv = qcp[pb][base:base + 64, lb * 512:(lb + 1) * 512]
                            qrv = qrp[pb][base:base + 32, lb * 512:(lb + 1) * 512]
                            kcv = kcp[pb][base:base + 64, mt * 128:(mt + 1) * 128]
                            krv = kr[base:base + 32, tb + mt * 128: tb + (mt + 1) * 128]
                            last = (sub == 1 and hh == 1)
                            pss = ps_s[hh][:, sub * 512:sub * 512 + 512]
                            b.mm(pss, kcv, qcv, start=True, stop=False)
                            b.mm(pss, krv, qrv, start=False, stop=(kdiag < 0),
                                 sig=(last and kdiag < 0))
                            if kdiag >= 0:
                                b.mm(pss, ident[:, 0:128],
                                     mask_sb[:, kdiag * 512:(kdiag + 1) * 512],
                                     start=False, stop=True, sig=last)
                    b.wait("act", ("e_pe", b.cnt["e_pe"]))
                    t_e0 = b.op("act", "activation", PT[0][:, :], ps_s[0][:, :], AF.Exp)
                    t_e1 = b.op("act", "activation", PT[1][:, :], ps_s[1][:, :], AF.Exp)
                    b.wait("pe", t_e1)
                    for sub in range(2):
                        mt = rnd * 2 + sub
                        for hh in range(2):
                            for ls in range(4):
                                b.mm(ps_pv[hh][:, ls * 65: ls * 65 + 65],
                                     PT[hh][:, sub * 512 + ls * 128: sub * 512 + ls * 128 + 128],
                                     vT[:, hh * 16 * VSL + mt * VSL: hh * 16 * VSL + mt * VSL + 65],
                                     start=(rnd == 0 and sub == 0 and ls == 0),
                                     stop=(rnd == nrnd - 1 and sub == 1),
                                     sig=(rnd == nrnd - 1 and sub == 1 and hh == 1 and ls == 3))
                b.wait("dve", ("e_pe", b.cnt["e_pe"]))
                for ls in range(4):
                    for hh in range(2):
                        b.op("dve", "reciprocal", rcp[:, hh:hh + 1],
                             ps_pv[hh][:, ls * 65 + 64: ls * 65 + 65])
                        t_o = b.op("dve", "tensor_scalar_mul",
                                   opair[:, hh * 64:hh * 64 + 64],
                                   ps_pv[hh][:, ls * 65: ls * 65 + 64],
                                   rcp[:, hh:hh + 1])
                    b.wait("pe", t_o)
                    b.op("pe", "transpose", ps_tr[:, ls * 128:(ls + 1) * 128],
                         opair[:, :], ident[:, 0:128], sig=True)
                    b.wait("dve", ("e_pe", b.cnt["e_pe"]))
                t_c = b.op("dve", "tensor_copy",
                           o_fm[:, hp * T + lb * 512: hp * T + (lb + 1) * 512],
                           ps_tr[:, 0:512])
                b.wait("pe", t_c)
            pfree[pb] = b.cnt["e_pe"]

        for dch in range(DC):
            wbuf = wcolB[dch % 2]
            b.wait("pool", ("e_pe", wfreeB[dch % 2]))
            t_w = b.dma("pool", "l1_wld", wbuf[:, :],
                        bass.AP(io["W_O1"], dch * 128,
                                [[1024, 128], [128 * 1024, DC], [1, 128]]))
            b.wait("pe", t_w)
            for s in range(4):
                ps = ps_s[0][:, (s % 2) * 512:(s % 2) * 512 + 512]
                b.wait("pe", ("e_act", b.cnt["e_act"]))
                for c in range(DC):
                    b.mm(ps, wbuf[:, c * 128:(c + 1) * 128],
                         o_fm[:, c * T + s * 512: c * T + s * 512 + 512],
                         start=(c == 0), stop=(c == DC - 1), sig=(c == DC - 1))
                b.wait("act", ("e_pe", b.cnt["e_pe"]))
                b.wait("act", b.tok("l1_s1st"))
                t_s = b.op("act", "activation", s1b[:, :], ps, AF.Identity)
                b.wait("sp", t_s)
                b.dma("sp", "l1_s1st",
                      io["s1_fm"][dch * 128:(dch + 1) * 128, tb + s * 512: tb + s * 512 + 512],
                      s1b[:, :])
            wfreeB[dch % 2] = b.cnt["e_pe"]
        b.flush()
    b.barrier()
    esB.close()
    es.close()


def xpos_tables_np(n=T, dh=64, window=512):
    half = np.arange(0, dh, 2, dtype=np.float64)
    inv_freq = 1.0 / 10000.0 ** (half / dh)
    t = np.arange(n, dtype=np.float64)
    freqs = t[:, None] * inv_freq[None, :]
    freqs = np.concatenate([freqs, freqs], -1)
    base = (half + 0.4 * dh) / (1.4 * dh)
    power = (t - n // 2) / float(window // 2)
    sc = base[None, :] ** power[:, None]
    sc = np.concatenate([sc, sc], -1)
    return np.cos(freqs), np.sin(freqs), sc


def emit_l1ffn(b, io):
    nc = b.nc
    es = contextlib.ExitStack()
    cos64, sin64, sc64 = xpos_tables_np()
    rows = np.arange(128) % 64
    C1 = (cos64[:, rows] * sc64[:, rows]).T.astype(np.float32)
    S1 = (sin64[:, rows] * sc64[:, rows]).T.astype(np.float32)
    C2 = (cos64[:, rows] / sc64[:, rows]).T.astype(np.float32)
    S2 = (sin64[:, rows] / sc64[:, rows]).T.astype(np.float32)
    R2 = np.zeros((128, 128), np.float32)
    for blk in range(2):
        o = blk * 64
        for m in range(32):
            R2[o + m + 32, o + m] = -1.0
            R2[o + m, o + m + 32] = 1.0
    md = np.where(np.arange(128)[None, :] > np.arange(128)[:, None], -1e30, 0.0)
    mf = np.where(np.arange(128)[None, :] < np.arange(128)[:, None], -1e30, 0.0)
    mk = np.concatenate([md, mf], axis=1).astype(NPBF)
    E2 = np.zeros((128, 2), NPBF); E2[0:64, 0] = 1; E2[64:128, 1] = 1
    E128 = np.zeros((2, 128), NPBF); E128[0, 0:64] = 1; E128[1, 64:128] = 1
    idp = np.zeros((128, 192), dtype=NPBF)
    idp[:, 0:128] = np.eye(128, dtype=NPBF)
    idp[0:64, 128:192] = np.eye(64, dtype=NPBF)
    idp[64:128, 128:192] = np.eye(64, dtype=NPBF)

    xf = es.enter_context(nc.sbuf_tensor("f_xf", [128, DC * T], BF16))
    o_fm = es.enter_context(nc.sbuf_tensor("f_ofm", [128, 4 * T], BF16))
    c1s = es.enter_context(nc.sbuf_tensor("f_c1", [128, T], F32))
    s1s = es.enter_context(nc.sbuf_tensor("f_s1", [128, T], F32))
    c2s = es.enter_context(nc.sbuf_tensor("f_c2", [128, T], F32))
    s2s = es.enter_context(nc.sbuf_tensor("f_s2", [128, T], F32))
    r2s = es.enter_context(nc.sbuf_tensor("f_r2", [128, 128], BF16))
    mks = es.enter_context(nc.sbuf_tensor("f_mks", [128, 256], BF16))
    e2s = es.enter_context(nc.sbuf_tensor("f_e2", [128, 2], BF16))
    e128s = es.enter_context(nc.sbuf_tensor("f_e128", [2, 128], BF16))
    ids = es.enter_context(nc.sbuf_tensor("f_id", [128, 192], BF16))
    ones1 = es.enter_context(nc.sbuf_tensor("f_o1", [1, 128], BF16))
    onesc = es.enter_context(nc.sbuf_tensor("f_oc", [128, 1], BF16))
    qs8 = es.enter_context(nc.sbuf_tensor("f_qs8", [128, 4], F32))
    lgb = es.enter_context(nc.sbuf_tensor("f_lgb", [128, 4 * DC], F32))
    b2s = es.enter_context(nc.sbuf_tensor("f_b2s", [128, 2 * DC], F32))
    b1s = es.enter_context(nc.sbuf_tensor("f_b1s", [128, 4 * 22], F32))

    for st in ("f_ld", "f_xb", "f_wld", "f_g1st"):
        b.stream(st)
    for (sb, arr, nm) in ((c1s, C1, "f_C1"), (s1s, S1, "f_S1"), (c2s, C2, "f_C2"),
                          (s2s, S2, "f_S2"), (r2s, R2.astype(NPBF), "f_R2"),
                          (mks, mk, "f_mk"), (e2s, E2, "f_E2"), (e128s, E128, "f_E128"),
                          (ids, idp, "identp2"), (ones1, np.ones((1, 128), NPBF), "f_on1"),
                          (onesc, np.ones((128, 1), NPBF), "f_onc")):
        b.dma("sp", "f_ld", sb[:, :], b.const(nm, arr)[:, :])
    b.dma("sp", "f_ld", qs8[:, 0:2], io["f_qs_l"][:, :])
    b.dma("sp", "f_ld", qs8[:, 2:4], io["f_ks_l"][:, :])
    for l in range(2):
        b.dma("sp", "f_ld", lgb[:, (2 * l) * DC:(2 * l + 1) * DC], io["f_lng_l"][l])
        b.dma("sp", "f_ld", lgb[:, (2 * l + 1) * DC:(2 * l + 2) * DC], io["f_lnb_l"][l])
        b.dma("sp", "f_ld", b2s[:, l * DC:(l + 1) * DC], io["f_b2_l"][l])
        b.dma("sp", "f_ld", b1s[:, (2 * l) * 22:(2 * l + 1) * 22], io["f_b1a_l"][l])
        b.dma("sp", "f_ld", b1s[:, (2 * l + 1) * 22:(2 * l + 2) * 22], io["f_b1g_l"][l])
    b.wait("dve", b.tok("f_ld"))
    b.op("dve", "tensor_scalar_mul", qs8[:, 0:2], qs8[:, 0:2], 8.0)

    for bi in range(NB):
        tb = bi * T
        b.wait("sp", ("e_act", b.cnt["e_act"]))
        b.wait("sp", ("e_pe", b.cnt["e_pe"]))
        for c in range(DC):
            b.dma_tr("sp", "f_xb", xf[:, c * T:(c + 1) * T],
                     io["xn_tm"][tb: tb + T, c * 128:(c + 1) * 128])
        b.wait("pe", b.tok("f_xb"))
        b.wait("pe", b.tok("f_ld"))
        b.wait("dve", b.tok("f_xb"))
        b.wait("act", b.tok("f_xb"))

        for l in range(2):
            esL = contextlib.ExitStack()
            qkv = esL.enter_context(nc.sbuf_tensor(f"f_qkv_{bi}_{l}", [128, 12 * T], BF16))
            esL2 = contextlib.ExitStack()
            hup = esL2.enter_context(nc.sbuf_tensor(f"f_hup_{bi}_{l}", [128, 4 * T], BF16))
            wc = [esL2.enter_context(nc.sbuf_tensor(f"f_wc{i}_{bi}_{l}", [128, DC * 128], BF16))
                  for i in range(2)]
            psQ = [esL2.enter_context(nc.psum_tensor(f"f_psQ{i}_{bi}_{l}", [128, 512], F32))
                   for i in range(2)]
            b.scope_guard()

            b.wait("dve", ("e_pe", b.cnt["e_pe"]))
            for c in range(4):
                b.op("dve", "memset", hup[:, c * T: c * T + 1], 0.0, sig=False)
                b.op("dve", "tensor_copy", hup[:, c * T + 1:(c + 1) * T],
                     xf[:, (4 + c) * T: (4 + c) * T + T - 1])
            hv = lambda c, c0, cn: (xf[:, c * T + c0: c * T + c0 + cn] if c < 4
                                    else hup[:, (c - 4) * T + c0: (c - 4) * T + c0 + cn])
            wfree = [0, 0]
            b.wait("pe", ("e_dve", b.cnt["e_dve"]))
            for m in range(12):
                wb = wc[m % 2]
                b.wait("pool", ("e_pe", wfree[m % 2]))
                t_w = b.dma("pool", "f_wld", wb[:, :],
                            bass.AP(io["f_wqkv"], l * 1024 * 1536 + m * 128,
                                    [[1536, 128], [128 * 1536, DC], [1, 128]]))
                b.wait("pe", t_w)
                for s in range(4):
                    ps = psQ[s % 2]
                    b.wait("pe", ("e_act", b.cnt["e_act"]))
                    for c in range(DC):
                        b.mm(ps[:, :], wb[:, c * 128:(c + 1) * 128], hv(c, s * 512, 512),
                             start=(c == 0), stop=(c == DC - 1), sig=(c == DC - 1))
                    b.wait("act", ("e_pe", b.cnt["e_pe"]))
                    b.op("act", "activation", qkv[:, m * T + s * 512: m * T + s * 512 + 512],
                         ps[:, :], AF.Identity)
                wfree[m % 2] = b.cnt["e_pe"]
            if bi == 0 and l == 0 and "dbg_qkv" in io:
                b.wait("sp", ("e_act", b.cnt["e_act"]))
                b.dma("sp", "dbg", bass.AP(io["dbg_qkv"], 0, [[2048, 128], [128 * 2048, 12], [1, 2048]]),
                      qkv[:, :])
            b.flush()
            esL2.close()

            esH = contextlib.ExitStack()
            sq = esH.enter_context(nc.sbuf_tensor(f"f_sq_{bi}_{l}", [128, T], BF16))
            rn = esH.enter_context(nc.sbuf_tensor(f"f_rn_{bi}_{l}", [2, T], F32))
            rnb = esH.enter_context(nc.sbuf_tensor(f"f_rnb_{bi}_{l}", [2, T], BF16))
            qn = esH.enter_context(nc.sbuf_tensor(f"f_qn_{bi}_{l}", [128, T], BF16))
            q3 = esH.enter_context(nc.sbuf_tensor(f"f_q3_{bi}_{l}", [128, T], BF16))
            k3 = esH.enter_context(nc.sbuf_tensor(f"f_k3_{bi}_{l}", [128, T], BF16))
            t1b = esH.enter_context(nc.sbuf_tensor(f"f_t1b_{bi}_{l}", [128, T], BF16))
            ostash = esH.enter_context(nc.sbuf_tensor(f"f_ost_{bi}_{l}", [128, 1024], BF16))
            vTf = esH.enter_context(nc.sbuf_tensor(f"f_vTf_{bi}_{l}", [128, 2 * 1024], BF16))
            Pl = esH.enter_context(nc.sbuf_tensor(f"f_Pl_{bi}_{l}", [128, 640], BF16))
            PTl = esH.enter_context(nc.sbuf_tensor(f"f_PTl_{bi}_{l}", [128, 640], BF16))
            mxd = esH.enter_context(nc.sbuf_tensor(f"f_mxd_{bi}_{l}", [128, 4], F32))
            opair = esH.enter_context(nc.sbuf_tensor(f"f_opr_{bi}_{l}", [128, 128], BF16))
            psB = esH.enter_context(nc.psum_tensor(f"f_psB_{bi}_{l}", [128, 2048], F32))
            psS = psB
            psSc = esH.enter_context(nc.psum_tensor(f"f_psSc_{bi}_{l}", [128, 640], F32))
            psO = esH.enter_context(nc.psum_tensor(f"f_psO_{bi}_{l}", [128, 64], F32))
            psT = esH.enter_context(nc.psum_tensor(f"f_psT_{bi}_{l}", [128, 1024], BF16))

            for hp in range(4):
                for (qk_idx, cs, ss, dst) in ((0, c1s, s1s, q3), (1, c2s, s2s, k3)):
                    src = qkv[:, (qk_idx * 4 + hp) * T:(qk_idx * 4 + hp + 1) * T]
                    scal = qs8[:, l: l + 1] if qk_idx == 0 else qs8[:, 2 + l: 3 + l]
                    t_s = b.op("act", "activation", sq[:, :], src, AF.Square)
                    b.wait("pe", t_s)
                    b.wait("pe", ("e_act", b.cnt["e_act"]))
                    for s in range(4):
                        b.mm(psB[0:2, s * 512:(s + 1) * 512], e2s[:, :],
                             sq[:, s * 512:(s + 1) * 512], start=True, stop=True,
                             sig=(s == 3))
                    b.wait("act", ("e_pe", b.cnt["e_pe"]))
                    b.op("act", "activation", rn[:, :], psB[0:2, :], AF.Ln)
                    t_r = b.op("act", "activation", rn[:, :], rn[:, :], AF.Exp, scale=-0.5)
                    b.wait("dve", t_r)
                    t_rb = b.op("dve", "tensor_copy", rnb[:, :], rn[:, :])
                    b.wait("pe", t_rb)
                    b.wait("pe", ("e_dve", b.cnt["e_dve"]))
                    for s in range(4):
                        b.mm(psB[:, s * 512:(s + 1) * 512], e128s[:, :],
                             rnb[0:2, s * 512:(s + 1) * 512], start=True, stop=True,
                             sig=(s == 3))
                    b.wait("dve", ("e_pe", b.cnt["e_pe"]))
                    t_qn = b.op("dve", "scalar_tensor_tensor", qn[:, :], src, scal,
                                psB[:, :], OP.mult, OP.mult)
                    b.wait("pe", t_qn)
                    for s in range(4):
                        b.mm(psB[:, s * 512:(s + 1) * 512], r2s[:, :],
                             qn[:, s * 512:(s + 1) * 512], start=True, stop=True,
                             sig=(s == 3))
                    b.wait("dve", ("e_pe", b.cnt["e_pe"]))
                    b.op("dve", "tensor_mul", t1b[:, :], ss[:, :], psB[:, :])
                    b.op("dve", "tensor_mul", dst[:, :], qn[:, :], cs[:, :])
                    t_d = b.op("dve", "tensor_add", dst[:, :], dst[:, :], t1b[:, :])
                    b.wait("pe", t_d)
                    if bi == 0 and l == 0 and hp == 0 and f"dbg_{'q3' if qk_idx == 0 else 'k3'}" in io:
                        b.wait("sp", t_d)
                        b.dma("sp", "dbg", io[f"dbg_{'q3' if qk_idx == 0 else 'k3'}"][:, :],
                              dst[:, :])
                for hh in range(2):
                    b.wait("pe", ("e_dve", b.cnt["e_dve"]))
                    for mt in range(16):
                        b.op("pe", "transpose", psT[:, mt * 64:(mt + 1) * 64],
                             qkv[64 * hh: 64 * hh + 64,
                                 (8 + hp) * T + mt * 128: (8 + hp) * T + (mt + 1) * 128],
                             ids[64 * hh:64 * hh + 64, 128:192], sig=(mt == 15))
                    b.wait("dve", ("e_pe", b.cnt["e_pe"]))
                    b.op("dve", "tensor_copy", vTf[:, hh * 1024:(hh + 1) * 1024],
                         psT[:, 0:1024])
                b.wait("pe", ("e_dve", b.cnt["e_dve"]))
                for hh in range(2):
                    base = 64 * hh
                    for lt in range(16):
                        nw = min(lt, 4) + 1
                        w0 = max(0, lt - 4) * 128
                        wlen = nw * 128
                        nfull = wlen - 128
                        b.wait("pe", ("e_act", b.cnt["e_act"]))
                        b.wait("pe", ("e_dve", b.cnt["e_dve"]))
                        qv_ = q3[base:base + 64, lt * 128:(lt + 1) * 128]
                        if nw == 5:
                            b.mm(psSc[:, 512:640], qv_,
                                 k3[base:base + 64, lt * 128:(lt + 1) * 128],
                                 start=True, stop=False)
                            b.mm(psSc[:, 512:640], ids[:, 0:128], mks[:, 0:128],
                                 start=False, stop=True)
                            b.mm(psSc[:, 0:512], qv_,
                                 k3[base:base + 64, w0: w0 + 512],
                                 start=True, stop=False)
                            b.mm(psSc[:, 0:128], ids[:, 0:128], mks[:, 128:256],
                                 start=False, stop=True, sig=True)
                        else:
                            if nfull > 0:
                                b.mm(psSc[:, 0:nfull], qv_,
                                     k3[base:base + 64, w0: w0 + nfull],
                                     start=True, stop=False)
                            b.mm(psSc[:, nfull:wlen], qv_,
                                 k3[base:base + 64, lt * 128:(lt + 1) * 128],
                                 start=(nfull == 0), stop=False)
                            b.mm(psSc[:, nfull:wlen], ids[:, 0:128], mks[:, 0:128],
                                 start=False, stop=True, sig=True)
                        b.wait("dve", ("e_pe", b.cnt["e_pe"]))
                        b.op("dve", "reduce_max", mxd[:, 0:1], psSc[:, 0:wlen],
                             axis=AX.X, negate=True)
                        b.wait("act", ("e_dve", b.cnt["e_dve"]))
                        t_e = b.op("act", "activation", Pl[:, 0:wlen], psSc[:, 0:wlen],
                                   AF.Exp, bias=mxd[:, 0:1], accum_out=mxd[:, 1:2])
                        b.wait("pe", t_e)
                        b.wait("pe", ("e_dve", b.cnt["e_dve"]))
                        for mt in range(nw):
                            b.op("pe", "transpose", psT[:, mt * 128:(mt + 1) * 128],
                                 Pl[:, mt * 128:(mt + 1) * 128], ids[:, 0:128],
                                 sig=(mt == nw - 1))
                        b.wait("dve", ("e_pe", b.cnt["e_pe"]))
                        t_pt = b.op("dve", "tensor_copy", PTl[:, 0:wlen], psT[:, 0:wlen])
                        b.wait("pe", t_pt)
                        for mt in range(nw):
                            gm = max(0, lt - 4) + mt
                            b.mm(psO[:, :], PTl[:, mt * 128:(mt + 1) * 128],
                                 vTf[:, hh * 1024 + gm * 64: hh * 1024 + gm * 64 + 64],
                                 start=(mt == 0), stop=(mt == nw - 1), sig=(mt == nw - 1))
                        b.wait("dve", ("e_pe", b.cnt["e_pe"]))
                        b.op("dve", "reciprocal", mxd[:, 2:3], mxd[:, 1:2])
                        t_o = b.op("dve", "tensor_scalar_mul",
                                   ostash[:, lt * 64:(lt + 1) * 64] if hh == 0
                                   else opair[:, 64:128],
                                   psO[:, :], mxd[:, 2:3])
                        if hh == 1:
                            b.op("dve", "tensor_copy", opair[:, 0:64],
                                 ostash[:, lt * 64:(lt + 1) * 64])
                            t_op = ("e_dve", b.cnt["e_dve"])
                            b.wait("pe", t_op)
                            b.op("pe", "transpose", psT[:, 0:128], opair[:, :],
                                 ids[:, 0:128], sig=True)
                            b.wait("dve", ("e_pe", b.cnt["e_pe"]))
                            t_of = b.op("dve", "tensor_copy",
                                        o_fm[:, hp * T + lt * 128: hp * T + (lt + 1) * 128],
                                        psT[:, 0:128])
                            b.wait("pe", t_of)
            esH.close()
            b.flush()

            esF = contextlib.ExitStack()
            wos = esF.enter_context(nc.sbuf_tensor(f"f_wos_{bi}_{l}", [128, 4 * 1024], BF16))
            psW = [esF.enter_context(nc.psum_tensor(f"f_psW{i}_{bi}_{l}", [128, 512], F32))
                   for i in range(2)]
            b.scope_guard()
            t_w = b.dma("pool", "f_wld", wos[:, :],
                        bass.AP(io["f_wo"], l * 512 * 1024,
                                [[1024, 128], [128 * 1024, 4], [1, 1024]]))
            b.wait("pe", t_w)
            b.wait("pe", ("e_dve", b.cnt["e_dve"]))
            for m in range(DC):
                for s in range(4):
                    ps = psW[s % 2]
                    b.wait("pe", ("e_dve", b.cnt["e_dve"]))
                    for c in range(4):
                        b.mm(ps[:, :], wos[:, c * 1024 + m * 128: c * 1024 + m * 128 + 128],
                             o_fm[:, c * T + s * 512: c * T + s * 512 + 512],
                             start=(c == 0), stop=(c == 3), sig=(c == 3))
                    b.wait("dve", ("e_pe", b.cnt["e_pe"]))
                    b.op("dve", "tensor_add", xf[:, m * T + s * 512: m * T + s * 512 + 512],
                         xf[:, m * T + s * 512: m * T + s * 512 + 512], ps[:, :])
            if bi == 0 and l == 0 and "dbg_xfwo" in io:
                b.wait("sp", ("e_dve", b.cnt["e_dve"]))
                b.dma("sp", "dbg", bass.AP(io["dbg_xfwo"], 0, [[2048, 128], [128 * 2048, 8], [1, 2048]]),
                      xf[:, :])
            if bi == 0 and l == 0 and "dbg_ofm" in io:
                b.wait("sp", ("e_dve", b.cnt["e_dve"]))
                b.dma("sp", "dbg", bass.AP(io["dbg_ofm"], 0, [[2048, 128], [128 * 2048, 4], [1, 2048]]),
                      o_fm[:, :])
            esF.close()
            b.flush()

            esM = contextlib.ExitStack()
            hup2 = esM.enter_context(nc.sbuf_tensor(f"f_hup2_{bi}_{l}", [128, 4 * T], BF16))
            sq2 = esM.enter_context(nc.sbuf_tensor(f"f_sq2_{bi}_{l}", [128, 512], BF16))
            stat = esM.enter_context(nc.sbuf_tensor(f"f_stat_{bi}_{l}", [1, 4 * 512], F32))
            statb = esM.enter_context(nc.sbuf_tensor(f"f_statb_{bi}_{l}", [1, 2 * 512], BF16))
            hn = esM.enter_context(nc.sbuf_tensor(f"f_hn_{bi}_{l}", [128, DC * 512], BF16))
            tmpa = esM.enter_context(nc.sbuf_tensor(f"f_tmpa_{bi}_{l}", [128, 512], BF16))
            hm = esM.enter_context(nc.sbuf_tensor(f"f_hm_{bi}_{l}", [128, 22 * 512], BF16))
            wc1 = [esM.enter_context(nc.sbuf_tensor(f"f_wc1{i}_{bi}_{l}", [128, DC * 128], BF16))
                   for i in range(2)]
            w2s = [esM.enter_context(nc.sbuf_tensor(f"f_w2s{i}_{bi}_{l}", [128, 22 * 128], BF16))
                   for i in range(2)]
            psL = esM.enter_context(nc.psum_tensor(f"f_psL_{bi}_{l}", [1, 512], F32))
            psL2 = esM.enter_context(nc.psum_tensor(f"f_psL2_{bi}_{l}", [1, 512], F32))
            psMa = esM.enter_context(nc.psum_tensor(f"f_psMa_{bi}_{l}", [128, 512], F32))
            psMb = esM.enter_context(nc.psum_tensor(f"f_psMb_{bi}_{l}", [128, 512], F32))
            psM1 = [esM.enter_context(nc.psum_tensor(f"f_psM1{i}_{bi}_{l}", [128, 512], F32))
                    for i in range(2)]
            psM2 = [esM.enter_context(nc.psum_tensor(f"f_psM2{i}_{bi}_{l}", [128, 512], F32))
                    for i in range(2)]
            b.scope_guard()

            b.wait("dve", ("e_pe", b.cnt["e_pe"]))
            for c in range(4):
                b.op("dve", "memset", hup2[:, c * T: c * T + 1], 0.0, sig=False)
                b.op("dve", "tensor_copy", hup2[:, c * T + 1:(c + 1) * T],
                     xf[:, (4 + c) * T: (4 + c) * T + T - 1])
            h2v = lambda c, c0, cn: (xf[:, c * T + c0: c * T + c0 + cn] if c < 4
                                     else hup2[:, (c - 4) * T + c0: (c - 4) * T + c0 + cn])
            b.wait("pe", ("e_dve", b.cnt["e_dve"]))
            wfree1 = [0, 0]
            wfree2 = [0, 0]
            for s in range(4):
                s0 = s * 512
                b.wait("pe", ("e_dve", b.cnt["e_dve"]))
                for c in range(DC):
                    b.mm(psL[:, :], onesc[:, :], h2v(c, s0, 512),
                         start=(c == 0), stop=(c == DC - 1), sig=(c == DC - 1))
                for c in range(DC):
                    b.wait("act", ("e_pe", b.cnt["e_pe"]) if c == 0 else None)
                    t_q = b.op("act", "activation", sq2[:, :], h2v(c, s0, 512), AF.Square)
                    b.wait("pe", t_q)
                    b.mm(psL2[:, :], onesc[:, :], sq2[:, :],
                         start=(c == 0), stop=(c == DC - 1), sig=True)
                    if c < DC - 1:
                        b.wait("act", ("e_pe", b.cnt["e_pe"]))
                b.wait("dve", ("e_pe", b.cnt["e_pe"]))
                b.op("dve", "tensor_scalar_mul", stat[:, 0:512], psL[:, :], 1.0 / D)
                b.op("dve", "tensor_scalar_mul", stat[:, 512:1024], psL2[:, :], 1.0 / D)
                b.op("dve", "tensor_mul", stat[:, 1024:1536], stat[:, 0:512], stat[:, 0:512])
                b.op("dve", "tensor_sub", stat[:, 1024:1536], stat[:, 512:1024],
                     stat[:, 1024:1536])
                t_v = b.op("dve", "tensor_scalar_add", stat[:, 1024:1536],
                           stat[:, 1024:1536], 1e-5)
                b.wait("act", t_v)
                b.op("act", "activation", stat[:, 1536:2048], stat[:, 1024:1536], AF.Ln)
                t_r = b.op("act", "activation", stat[:, 1536:2048], stat[:, 1536:2048],
                           AF.Exp, scale=-0.5)
                b.wait("dve", t_r)
                b.op("dve", "tensor_copy", statb[:, 0:512], stat[:, 0:512])
                t_sb = b.op("dve", "tensor_copy", statb[:, 512:1024], stat[:, 1536:2048])
                b.wait("pe", t_sb)
                b.mm(psMa[:, :], ones1[:, :], statb[0:1, 0:512], start=True, stop=True)
                b.mm(psMb[:, :], ones1[:, :], statb[0:1, 512:1024], start=True, stop=True,
                     sig=True)
                b.wait("dve", ("e_pe", b.cnt["e_pe"]))
                for c in range(DC):
                    b.op("dve", "tensor_sub", hn[:, c * 512:(c + 1) * 512],
                         h2v(c, s0, 512), psMa[:, :])
                    b.op("dve", "tensor_mul", hn[:, c * 512:(c + 1) * 512],
                         hn[:, c * 512:(c + 1) * 512], psMb[:, :])
                    b.op("dve", "tensor_scalar", hn[:, c * 512:(c + 1) * 512],
                         hn[:, c * 512:(c + 1) * 512],
                         lgb[:, (2 * l) * DC + c: (2 * l) * DC + c + 1],
                         lgb[:, (2 * l + 1) * DC + c: (2 * l + 1) * DC + c + 1],
                         OP.mult, OP.add)
                hn_free = b.cnt["e_dve"]
                for m in range(22):
                    mw = 128 if m < 21 else 42
                    wb = wc1[m % 2]
                    b.wait("pool", ("e_pe", wfree1[m % 2]))
                    t_w = b.dma("pool", "f_wld", wb[:, 0:DC * mw],
                                bass.AP(io["f_w1"], l * 1024 * 5460 + m * 128,
                                        [[5460, 128], [128 * 5460, DC], [1, mw]]))
                    b.wait("pe", t_w)
                    b.wait("pe", ("e_dve", hn_free))
                    ps = psM1[m % 2]
                    b.wait("pe", ("e_act", b.cnt["e_act"]))
                    for c in range(DC):
                        b.mm(ps[0:mw, :], wb[:, c * mw:(c + 1) * mw],
                             hn[:, c * 512:(c + 1) * 512],
                             start=(c == 0), stop=(c == DC - 1), sig=(c == DC - 1))
                    b.wait("act", ("e_pe", b.cnt["e_pe"]))
                    b.wait("act", ("e_dve", b.cnt["e_dve"]))
                    t_a = b.op("act", "activation", tmpa[0:mw, :], ps[0:mw, :], AF.Identity,
                               bias=b1s[0:mw, (2 * l) * 22 + m: (2 * l) * 22 + m + 1])
                    b.wait("pool", ("e_pe", b.cnt["e_pe"]))
                    t_wg = b.dma("pool", "f_wld", wb[:, 0:DC * mw],
                                 bass.AP(io["f_w1"], l * 1024 * 5460 + 2730 + m * 128,
                                         [[5460, 128], [128 * 5460, DC], [1, mw]]))
                    b.wait("pe", t_wg)
                    ps2 = psM2[m % 2]
                    b.wait("pe", ("e_dve", b.cnt["e_dve"]))
                    for c in range(DC):
                        b.mm(ps2[0:mw, :], wb[:, c * mw:(c + 1) * mw],
                             hn[:, c * 512:(c + 1) * 512],
                             start=(c == 0), stop=(c == DC - 1), sig=(c == DC - 1))
                    b.wait("act", ("e_pe", b.cnt["e_pe"]))
                    t_g = b.op("act", "activation", hm[0:mw, m * 512:(m + 1) * 512],
                               ps2[0:mw, :], AF.Gelu,
                               bias=b1s[0:mw, (2 * l + 1) * 22 + m: (2 * l + 1) * 22 + m + 1])
                    b.wait("dve", t_g)
                    b.op("dve", "tensor_mul", hm[0:mw, m * 512:(m + 1) * 512],
                         tmpa[0:mw, :], hm[0:mw, m * 512:(m + 1) * 512])
                    wfree1[m % 2] = b.cnt["e_pe"]
                hm_done = ("e_dve", b.cnt["e_dve"])
                for m2 in range(DC):
                    wb2 = w2s[m2 % 2]
                    b.wait("pool", ("e_pe", wfree2[m2 % 2]))
                    t_wa = b.dma("pool", "f_wld", wb2[:, 0:21 * 128],
                                 bass.AP(io["f_w2"], l * 2730 * 1024 + m2 * 128,
                                         [[1024, 128], [128 * 1024, 21], [1, 128]]))
                    t_wb = b.dma("pool", "f_wld", wb2[0:42, 21 * 128:22 * 128],
                                 bass.AP(io["f_w2"], l * 2730 * 1024 + 21 * 128 * 1024 + m2 * 128,
                                         [[1024, 42], [1, 128]]))
                    b.wait("pe", t_wb)
                    b.wait("pe", hm_done)
                    ps = psM1[m2 % 2]
                    b.wait("pe", ("e_dve", b.cnt["e_dve"]))
                    for c in range(22):
                        cw = 128 if c < 21 else 42
                        b.mm(ps[:, :], wb2[0:cw, c * 128:(c + 1) * 128],
                             hm[0:cw, c * 512:(c + 1) * 512],
                             start=(c == 0), stop=(c == 21), sig=(c == 21))
                    b.wait("dve", ("e_pe", b.cnt["e_pe"]))
                    b.op("dve", "scalar_tensor_tensor",
                         xf[:, m2 * T + s0: m2 * T + s0 + 512], ps[:, :],
                         b2s[:, l * DC + m2: l * DC + m2 + 1],
                         xf[:, m2 * T + s0: m2 * T + s0 + 512], OP.add, OP.add)
                    wfree2[m2 % 2] = b.cnt["e_pe"]
            if bi == 0 and l == 0 and "dbg_xfmlp" in io:
                b.wait("sp", ("e_dve", b.cnt["e_dve"]))
                b.dma("sp", "dbg", bass.AP(io["dbg_xfmlp"], 0, [[2048, 128], [128 * 2048, 8], [1, 2048]]),
                      xf[:, :])
                b.wait("dve", b.tok("dbg"))
            esM.close()
            esL.close()
            b.flush()

        if not hasattr(b, "_g1b"):
            b._g1b = es.enter_context(nc.sbuf_tensor("f_g1b", [128, T], BF16))
        g1b = b._g1b
        for c in range(DC):
            b.wait("act", b.tok("f_g1st"))
            b.wait("act", ("e_dve", b.cnt["e_dve"]))
            t_g = b.op("act", "activation", g1b[:, :], xf[:, c * T:(c + 1) * T], AF.Silu)
            b.wait("sp", t_g)
            b.dma("sp", "f_g1st", io["g1_fm"][c * 128:(c + 1) * 128, tb: tb + T], g1b[:, :])
        b.flush()
    b.barrier()
    es.close()


def emit_combine(b, io):
    nc = b.nc
    es = contextlib.ExitStack()
    x1t = es.enter_context(nc.sbuf_tensor("cb_x1", [128, D], F32))
    s1t = es.enter_context(nc.sbuf_tensor("cb_s1", [128, D], BF16))
    g1t = es.enter_context(nc.sbuf_tensor("cb_g1", [128, D], BF16))
    pt = es.enter_context(nc.sbuf_tensor("cb_p", [128, D], BF16))
    ot = es.enter_context(nc.sbuf_tensor("cb_o", [128, D], F32))
    b.stream("cb_ld")
    b.stream("cb_st")
    for i in range(TOK // 128):
        prev = b.cnt["e_dve"]
        b.wait("sp", ("e_dve", prev))
        b.wait("sp", b.tok("cb_st"))
        b.dma("sp", "cb_ld", x1t[:, :], io["x1"][i * 128:(i + 1) * 128, :])
        b.dma_tr("sp", "cb_ld", s1t[:, :], io["s1_fm"][:, i * 128:(i + 1) * 128])
        b.dma_tr("sp", "cb_ld", g1t[:, :], io["g1_fm"][:, i * 128:(i + 1) * 128])
        b.wait("dve", b.tok("cb_ld"))
        b.op("dve", "tensor_mul", pt[:, :], s1t[:, :], g1t[:, :])
        t_o = b.op("dve", "tensor_add", ot[:, :], x1t[:, :], pt[:, :])
        b.wait("sp", t_o)
        b.dma("sp", "cb_st", io["out"][i * 128:(i + 1) * 128, :], ot[:, :])
    b.flush()
    b.barrier()
    es.close()


from concourse.bass_utils import run_bass_kernel_spmd

N_CORES = 8


def _drams(nc, specs):
    io = {}
    for name, shape, dt, kind in specs:
        io[name] = nc.dram_tensor(name, shape, dt, kind=kind)
    return io


def _run(build_fn, specs, per_core_feeds):
    nc = bass.Bass("TRN2", target_bir_lowering=False)
    io = _drams(nc, specs)
    es = contextlib.ExitStack()
    bb = B(nc, es)
    build_fn(bb, io)
    bb.barrier()
    bb.flush()
    es.close()
    res = run_bass_kernel_spmd(nc, per_core_feeds, core_ids=list(range(N_CORES)))
    return res.results


def host_prep_all(inputs):
    w = host_prep_weights(inputs)
    w["moe_b1_l"] = np.ascontiguousarray(
        np.asarray(inputs["L0_ffn_b1"], np.float32).reshape(14, 16, 128)
        .transpose(2, 0, 1).reshape(128, 224))
    b1 = np.asarray(inputs["L1_ffn_b1"], np.float32)
    b1a = np.zeros((2, 128, 22), np.float32)
    b1g = np.zeros((2, 128, 22), np.float32)
    for l in range(2):
        for m in range(22):
            mw = 128 if m < 21 else 42
            b1a[l, :mw, m] = b1[l, m * 128: m * 128 + mw]
            b1g[l, :mw, m] = b1[l, 2730 + m * 128: 2730 + m * 128 + mw]
    w["f_b1a_l"] = np.ascontiguousarray(b1a)
    w["f_b1g_l"] = np.ascontiguousarray(b1g)
    w["moe_b2"] = np.ascontiguousarray(np.asarray(inputs["L0_ffn_b2"], np.float32))
    return w


def kernel(**inputs):
    w = host_prep_all(inputs)
    x_full = np.ascontiguousarray(np.asarray(inputs["L0_attn_x"], np.float32))
    shards = [np.ascontiguousarray(x_full[2 * i: 2 * i + 2].reshape(TOK, D))
              for i in range(N_CORES)]
    NPF32 = np.float32

    specs = [("x", [TOK, D], F32, "ExternalInput"),
             ("gate_wT", [1024, 14], F32, "ExternalInput"),
             ("xn_tm", [TOK, D], BF16, "ExternalOutput"),
             ("xnl_tm", [TOK, D], BF16, "ExternalOutput"),
             ("W_hbm", [TOK, 16], F32, "ExternalOutput")]
    feeds = [{"x": shards[i], "gate_wT": w["gate_wT"]} for i in range(N_CORES)]
    r1 = _run(lambda bb, io: emit_prep(bb, io), specs, feeds)

    specs = [("xn_tm", [TOK, D], BF16, "ExternalInput"),
             ("w_qkvT", [1024, 3072], F32, "ExternalInput"),
             ("b_qkv_l", [128, 24], F32, "ExternalInput"),
             ("w_oT", [1024, 1024], F32, "ExternalInput"),
             ("b_o_l", [128, 8], F32, "ExternalInput"),
             ("v_hbm", [1024, TOK], BF16, "Internal"),
             ("s0_fm", [1024, TOK], BF16, "ExternalOutput")]
    feeds = [{"xn_tm": r1[i]["xn_tm"], "w_qkvT": w["w_qkvT"], "b_qkv_l": w["b_qkv_l"],
              "w_oT": w["w_oT"], "b_o_l": w["b_o_l"]} for i in range(N_CORES)]
    r2 = _run(emit_l0attn, specs, feeds)

    specs = [("xn_tm", [TOK, D], BF16, "ExternalInput"),
             ("xnl_tm", [TOK, D], BF16, "ExternalInput"),
             ("W_hbm", [TOK, 16], F32, "ExternalInput"),
             ("moe_w1", [14, 1024, 2048], F32, "ExternalInput"),
             ("moe_b1_l", [128, 224], F32, "ExternalInput"),
             ("moe_w2", [14, 2048, 1024], F32, "ExternalInput"),
             ("moe_b2", [14, 1024], F32, "ExternalInput"),
             ("x", [TOK, D], F32, "ExternalInput"),
             ("s0_fm", [1024, TOK], BF16, "ExternalInput"),
             ("acc_hbm", [TOK, D], F32, "Internal"),
             ("x1", [TOK, D], F32, "ExternalOutput")]
    feeds = [{"xn_tm": r1[i]["xn_tm"], "xnl_tm": r1[i]["xnl_tm"], "W_hbm": r1[i]["W_hbm"],
              "moe_w1": w["moe_w1"], "moe_b1_l": w["moe_b1_l"], "moe_w2": w["moe_w2"],
              "moe_b2": w["moe_b2"], "x": shards[i], "s0_fm": r2[i]["s0_fm"]}
             for i in range(N_CORES)]
    r3 = _run(emit_moe, specs, feeds)

    specs = [("x", [TOK, D], F32, "ExternalInput"),
             ("xn_tm", [TOK, D], BF16, "ExternalOutput")]
    feeds = [{"x": r3[i]["x1"]} for i in range(N_CORES)]
    r4 = _run(lambda bb, io: emit_prep(bb, io, with_lo=False, with_gate=False),
              specs, feeds)

    specs = [("xn_tm", [TOK, D], BF16, "ExternalInput"),
             ("W_DQ", [1024, 4096], F32, "ExternalInput"),
             ("W_UQ", [4096, 1024], F32, "ExternalInput"),
             ("W_DKV", [1024, 256], F32, "ExternalInput"),
             ("W_UK", [256, 1024], F32, "ExternalInput"),
             ("W_UV", [256, 1024], F32, "ExternalInput"),
             ("W_QR_pad", [4096, 1024], F32, "ExternalInput"),
             ("W_KR", [1024, 32], F32, "ExternalInput"),
             ("W_O1", [1024, 1024], F32, "ExternalInput"),
             ("v_hbm", [1024, TOK], BF16, "Internal"),
             ("qc_hbm", [1024, TOK], BF16, "Internal"),
             ("qr_hbm", [1024, TOK], BF16, "Internal"),
             ("kc_hbm", [1024, TOK], BF16, "Internal"),
             ("s1_fm", [1024, TOK], BF16, "ExternalOutput")]
    feeds = [dict({"xn_tm": r4[i]["xn_tm"]},
                  **{k: w[k] for k in ("W_DQ", "W_UQ", "W_DKV", "W_UK", "W_UV",
                                       "W_QR_pad", "W_KR", "W_O1")})
             for i in range(N_CORES)]
    r5 = _run(emit_l1attn, specs, feeds)

    specs = [("xn_tm", [TOK, D], BF16, "ExternalInput"),
             ("f_wqkv", [2, 1024, 1536], F32, "ExternalInput"),
             ("f_qs_l", [128, 2], F32, "ExternalInput"),
             ("f_ks_l", [128, 2], F32, "ExternalInput"),
             ("f_wo", [2, 512, 1024], F32, "ExternalInput"),
             ("f_lng_l", [2, 128, 8], F32, "ExternalInput"),
             ("f_lnb_l", [2, 128, 8], F32, "ExternalInput"),
             ("f_w1", [2, 1024, 5460], F32, "ExternalInput"),
             ("f_b1a_l", [2, 128, 22], F32, "ExternalInput"),
             ("f_b1g_l", [2, 128, 22], F32, "ExternalInput"),
             ("f_w2", [2, 2730, 1024], F32, "ExternalInput"),
             ("f_b2_l", [2, 128, 8], F32, "ExternalInput"),
             ("g1_fm", [1024, TOK], BF16, "ExternalOutput")]
    feeds = [dict({"xn_tm": r4[i]["xn_tm"]},
                  **{k: w[k] for k in ("f_wqkv", "f_qs_l", "f_ks_l", "f_wo", "f_lng_l",
                                       "f_lnb_l", "f_w1", "f_b1a_l", "f_b1g_l",
                                       "f_w2", "f_b2_l")})
             for i in range(N_CORES)]
    r6 = _run(emit_l1ffn, specs, feeds)

    specs = [("x1", [TOK, D], F32, "ExternalInput"),
             ("s1_fm", [1024, TOK], BF16, "ExternalInput"),
             ("g1_fm", [1024, TOK], BF16, "ExternalInput"),
             ("out", [TOK, D], F32, "ExternalOutput")]
    feeds = [{"x1": r3[i]["x1"], "s1_fm": r5[i]["s1_fm"], "g1_fm": r6[i]["g1_fm"]}
             for i in range(N_CORES)]
    r7 = _run(emit_combine, specs, feeds)

    out = np.empty((16, 2048, 1024), np.float32)
    for i in range(N_CORES):
        out[2 * i: 2 * i + 2] = np.asarray(r7[i]["out"], np.float32).reshape(2, 2048, 1024)
    return out


def build_fused(nc):
    io = {}
    ext_in = [("x", [TOK, D], F32), ("gate_wT", [1024, 14], F32),
              ("w_qkvT", [1024, 3072], F32), ("b_qkv_l", [128, 24], F32),
              ("w_oT", [1024, 1024], F32), ("b_o_l", [128, 8], F32),
              ("moe_w1", [14, 1024, 2048], F32), ("moe_b1_l", [128, 224], F32),
              ("moe_w2", [14, 2048, 1024], F32), ("moe_b2", [14, 1024], F32),
              ("W_DQ", [1024, 4096], F32), ("W_UQ", [4096, 1024], F32),
              ("W_DKV", [1024, 256], F32), ("W_UK", [256, 1024], F32),
              ("W_UV", [256, 1024], F32), ("W_QR_pad", [4096, 1024], F32),
              ("W_KR", [1024, 32], F32), ("W_O1", [1024, 1024], F32),
              ("f_wqkv", [2, 1024, 1536], F32), ("f_qs_l", [128, 2], F32),
              ("f_ks_l", [128, 2], F32), ("f_wo", [2, 512, 1024], F32),
              ("f_lng_l", [2, 128, 8], F32), ("f_lnb_l", [2, 128, 8], F32),
              ("f_w1", [2, 1024, 5460], F32), ("f_b1a_l", [2, 128, 22], F32),
              ("f_b1g_l", [2, 128, 22], F32), ("f_w2", [2, 2730, 1024], F32),
              ("f_b2_l", [2, 128, 8], F32)]
    for n, sh, dt in ext_in:
        io[n] = nc.dram_tensor(n, sh, dt, kind="ExternalInput")
    internal = [("xn_tm", [TOK, D], BF16), ("xnl_tm", [TOK, D], BF16),
                ("W_hbm", [TOK, 16], F32), ("v_hbm", [1024, TOK], BF16),
                ("s0_fm", [1024, TOK], BF16), ("acc_hbm", [TOK, D], F32),
                ("x1", [TOK, D], F32), ("xn1_tm", [TOK, D], BF16),
                ("qc_hbm", [1024, TOK], BF16), ("qr_hbm", [1024, TOK], BF16),
                ("kc_hbm", [1024, TOK], BF16), ("s1_fm", [1024, TOK], BF16),
                ("g1_fm", [1024, TOK], BF16)]
    for n, sh, dt in internal:
        io[n] = nc.dram_tensor(n, sh, dt, kind="Internal")
    io["out"] = nc.dram_tensor("out", [TOK, D], F32, kind="ExternalOutput")

    es = contextlib.ExitStack()
    bb = B(nc, es)
    emit_prep(bb, io, pfx="p0")
    emit_l0attn(bb, io)
    emit_moe(bb, io)
    io_ln1 = dict(io)
    io_ln1["x"] = io["x1"]
    io_ln1["xn_tm"] = io["xn1_tm"]
    emit_prep(bb, io_ln1, with_lo=False, with_gate=False, pfx="p1")
    io_l1 = dict(io)
    io_l1["xn_tm"] = io["xn1_tm"]
    emit_l1attn(bb, io_l1)
    emit_l1ffn(bb, io_l1)
    emit_combine(bb, io)
    bb.barrier()
    bb.flush()
    es.close()
    return io


def kernel_fused(**inputs):
    w = host_prep_all(inputs)
    x_full = np.ascontiguousarray(np.asarray(inputs["L0_attn_x"], np.float32))
    shards = [np.ascontiguousarray(x_full[2 * i: 2 * i + 2].reshape(TOK, D))
              for i in range(N_CORES)]
    nc = bass.Bass("TRN2", target_bir_lowering=False)
    build_fused(nc)
    feeds = []
    wk = ("gate_wT", "w_qkvT", "b_qkv_l", "w_oT", "b_o_l", "moe_w1", "moe_b1_l",
          "moe_w2", "moe_b2", "W_DQ", "W_UQ", "W_DKV", "W_UK", "W_UV", "W_QR_pad",
          "W_KR", "W_O1", "f_wqkv", "f_qs_l", "f_ks_l", "f_wo", "f_lng_l", "f_lnb_l",
          "f_w1", "f_b1a_l", "f_b1g_l", "f_w2", "f_b2_l")
    for i in range(N_CORES):
        f = {k: w[k] for k in wk}
        f["x"] = shards[i]
        feeds.append(f)
    res = run_bass_kernel_spmd(nc, feeds, core_ids=list(range(N_CORES)))
    out = np.empty((16, 2048, 1024), np.float32)
    for i in range(N_CORES):
        out[2 * i: 2 * i + 2] = np.asarray(res.results[i]["out"], np.float32).reshape(2, 2048, 1024)
    return out


kernel = kernel_fused


# version 03
# speedup vs baseline: 1.1489x; 1.1489x over previous
import numpy as np
import ml_dtypes
import contextlib
import concourse.bass as bass
import concourse.mybir as mybir
from concourse.alu_op_type import AluOpType as OP

F32 = mybir.dt.float32
BF16 = mybir.dt.bfloat16
AF = mybir.ActivationFunctionType
AX = mybir.AxisListType
NPBF = ml_dtypes.bfloat16

TOK = 4096
NB = 2
T = 2048
D = 1024
DC = D // 128

ENGS = ("sp", "pool", "pe", "act", "dve")


class B:

    def __init__(self, nc, es):
        self.nc = nc
        self.es = es
        self.block = es.enter_context(nc.Block())
        self.sems = {}
        self.cnt = {}
        for e in ENGS:
            self._mksem("e_" + e)
        self.ops = {e: [] for e in ENGS}
        self.lastw = {}
        self.consts = {}

    def _mksem(self, name):
        self.sems[name] = self.es.enter_context(self.nc.semaphore(name))
        self.cnt[name] = 0

    def stream(self, name):
        if name not in self.sems:
            self._mksem(name)
        return name

    def raw(self, eng, fn):
        self.ops[eng].append(fn)

    def wait(self, eng, tok):
        if tok is None:
            return
        s, v = tok
        if v <= 0 or self.lastw.get((eng, s), 0) >= v:
            return
        self.lastw[(eng, s)] = v
        sem = self.sems[s]
        self.raw(eng, lambda e: e.wait_ge(sem, v))

    def waits(self, eng, toks):
        for t in toks:
            self.wait(eng, t)

    def dma(self, eng, stream, out, in_, **kw):
        s = self.stream(stream)
        self.cnt[s] += 16
        sem = self.sems[s]
        self.raw(eng, lambda e: e.dma_start(out, in_, **kw).then_inc(sem, 16))
        return (s, self.cnt[s])

    def dma_tr(self, eng, stream, out, in_):
        s = self.stream(stream)
        self.cnt[s] += 16
        sem = self.sems[s]
        self.raw(eng, lambda e: e.dma_start_transpose(out, in_).then_inc(sem, 16))
        return (s, self.cnt[s])

    def tok(self, stream):
        return (stream, self.cnt[stream])

    def op(self, eng, meth, *a, sig=True, **kw):
        if eng in ("dve", "act"):
            self.wait(eng, ("e_" + eng, self.cnt["e_" + eng]))
            sig = True
        if sig:
            s = "e_" + eng
            self.cnt[s] += 1
            v = self.cnt[s]
            sem = self.sems[s]
            self.raw(eng, lambda e: getattr(e, meth)(*a, **kw).then_inc(sem, 1))
            return (s, v)
        self.raw(eng, lambda e: getattr(e, meth)(*a, **kw))
        return None

    def mm(self, out, lhsT, rhs, start, stop, sig=False, tile_position=None,
           skip_group_check=False):
        kw = {}
        if tile_position is not None:
            kw["tile_position"] = tile_position
        if skip_group_check:
            kw["skip_group_check"] = True
        return self.op("pe", "matmul", out, lhsT, rhs, start=start, stop=stop,
                       sig=sig, **kw)

    def flush(self):
        for eng in ENGS:
            lst = self.ops[eng]
            if not lst:
                continue
            meth = {"sp": "sync", "pool": "gpsimd", "pe": "tensor",
                    "act": "scalar", "dve": "vector"}[eng]

            def runner(e, lst=lst):
                for f in lst:
                    f(e)
            getattr(self.block, meth)(runner)
            self.ops[eng] = []

    def scope_guard(self):
        for q in ("sp", "pool"):
            for e in ("e_pe", "e_act", "e_dve"):
                self.wait(q, (e, self.cnt[e]))

    def barrier(self):
        toks = [(s, self.cnt[s]) for s in self.sems if self.cnt[s] > 0]
        for eng in ENGS:
            for t in toks:
                self.wait(eng, t)

    def const(self, name, arr):
        if name not in self.consts:
            self.consts[name] = self.nc.inline_tensor(np.ascontiguousarray(arr), name)
        return self.consts[name]



def host_prep_weights(inp):
    f32 = lambda a: np.ascontiguousarray(np.asarray(a, dtype=np.float32))
    w = {}
    w["w_qkvT"] = f32(np.asarray(inp["L0_attn_w_qkv"]).T)
    w["b_qkv_l"] = f32(np.asarray(inp["L0_attn_b_qkv"]).reshape(24, 128).T)
    w["w_oT"] = f32(np.asarray(inp["L0_attn_w_o"]).T)
    w["b_o_l"] = f32(np.asarray(inp["L0_attn_b_o"]).reshape(8, 128).T)
    w["gate_wT"] = f32(np.asarray(inp["L0_ffn_gate_w"]).T)
    w["moe_w1"] = f32(inp["L0_ffn_w1"])
    w["moe_b1_r"] = f32(np.asarray(inp["L0_ffn_b1"]).reshape(1, 14 * 2048))
    w["moe_w2"] = f32(inp["L0_ffn_w2"])
    w["moe_b2_r"] = f32(np.asarray(inp["L0_ffn_b2"]).reshape(14, 1024))
    w["W_DQ"] = f32(inp["L1_attn_W_DQ"])
    w["W_UQ"] = f32(inp["L1_attn_W_UQ"])
    w["W_DKV"] = f32(inp["L1_attn_W_DKV"])
    w["W_UK"] = f32(inp["L1_attn_W_UK"])
    w["W_UV"] = f32(inp["L1_attn_W_UV"])
    wqr = np.asarray(inp["L1_attn_W_QR"], dtype=np.float32)
    wqr_pad = np.zeros((4096, 1024), dtype=np.float32)
    for h in range(16):
        wqr_pad[:, 64 * h:64 * h + 32] = wqr[:, 32 * h:32 * h + 32]
    w["W_QR_pad"] = np.ascontiguousarray(wqr_pad)
    w["W_KR"] = f32(inp["L1_attn_W_KR"])
    w["W_O1"] = f32(inp["L1_attn_W_O"])
    w["f_wqkv"] = f32(inp["L1_ffn_wqkv"])
    w["f_qs_l"] = f32(np.tile(np.asarray(inp["L1_ffn_q_scale"], np.float32), (1, 2)).T)
    w["f_ks_l"] = f32(np.tile(np.asarray(inp["L1_ffn_k_scale"], np.float32), (1, 2)).T)
    w["f_wo"] = f32(inp["L1_ffn_wo"])
    w["f_lng_l"] = f32(np.asarray(inp["L1_ffn_ln_g"], np.float32).reshape(2, 8, 128).transpose(0, 2, 1))
    w["f_lnb_l"] = f32(np.asarray(inp["L1_ffn_ln_b"], np.float32).reshape(2, 8, 128).transpose(0, 2, 1))
    w["f_w1"] = f32(inp["L1_ffn_w1"])
    w["f_b1_r"] = f32(np.asarray(inp["L1_ffn_b1"], np.float32).reshape(2, 1, 5460))
    w["f_w2"] = f32(inp["L1_ffn_w2"])
    w["f_b2_l"] = f32(np.asarray(inp["L1_ffn_b2"], np.float32).reshape(2, 8, 128).transpose(0, 2, 1))
    return w



def fm_hbm_ap(hbm, ncols=TOK):
    return hbm.rearrange("(c p) t -> p (c t)", p=128)


def emit_xbar_fm(b, dst_fm, hbm_tm, rows, stream="xb", eng="sp", col0=0, ncols=TOK):
    toks = []
    for c in range(rows // 128):
        toks.append(b.dma_tr(eng, stream,
                             dst_fm[:, c * ncols:(c + 1) * ncols],
                             hbm_tm[col0:col0 + ncols, c * 128:(c + 1) * 128]))
    return toks


def emit_prep(b, io, with_lo=True, with_gate=True, pfx="pr"):
    nc = b.nc
    es = contextlib.ExitStack()
    xt = [es.enter_context(nc.sbuf_tensor(f"{pfx}_x{i}", [128, D], F32)) for i in range(2)]
    xnh = es.enter_context(nc.sbuf_tensor(f"{pfx}_xnh", [128, D], BF16))
    xnh32 = es.enter_context(nc.sbuf_tensor(f"{pfx}_xnh32", [128, D], F32))
    xn32 = es.enter_context(nc.sbuf_tensor(f"{pfx}_xn32", [128, D], F32))
    xnl = es.enter_context(nc.sbuf_tensor(f"{pfx}_xnl", [128, D], BF16))
    sq = es.enter_context(nc.sbuf_tensor(f"{pfx}_sq", [128, D], F32))
    st = es.enter_context(nc.sbuf_tensor(f"{pfx}_st", [128, 8], F32))
    ld = b.stream(f"{pfx}_ld")
    stb = b.stream(f"{pfx}_st")

    for i in range(TOK // 128):
        buf = xt[i % 2]
        prev_dve = b.cnt["e_dve"]
        if i >= 2:
            b.wait("sp", ("e_dve", prev_dve))
        t_ld = b.dma("sp", ld, buf[:, :], io["x"][i * 128:(i + 1) * 128, :])
        b.wait("act", t_ld)
        b.wait("act", ("e_dve", prev_dve))
        b.op("act", "activation", sq[:, :], buf[:, :], AF.Square)
        b.wait("dve", ("e_act", b.cnt["e_act"]))
        b.wait("dve", b.tok(f"{pfx}_st") if i > 0 else None)
        b.op("dve", "reduce_sum", st[:, 0:1], buf[:, :], axis=AX.X)
        b.op("dve", "reduce_sum", st[:, 1:2], sq[:, :], axis=AX.X)
        b.op("dve", "tensor_scalar_mul", st[:, 0:1], st[:, 0:1], 1.0 / D)
        b.op("dve", "tensor_scalar_mul", st[:, 1:2], st[:, 1:2], 1.0 / D)
        b.op("dve", "tensor_mul", st[:, 2:3], st[:, 0:1], st[:, 0:1])
        b.op("dve", "tensor_sub", st[:, 2:3], st[:, 1:2], st[:, 2:3])
        t_v = b.op("dve", "tensor_scalar_add", st[:, 3:4], st[:, 2:3], 1e-5)
        b.wait("act", t_v)
        b.op("act", "activation", st[:, 3:4], st[:, 3:4], AF.Ln)
        t_r = b.op("act", "activation", st[:, 3:4], st[:, 3:4], AF.Exp, scale=-0.5)
        b.wait("dve", t_r)
        if with_lo:
            b.op("dve", "tensor_scalar", xn32[:, :], buf[:, :], st[:, 0:1], st[:, 3:4],
                 OP.subtract, OP.mult)
            b.op("dve", "tensor_copy", xnh[:, :], xn32[:, :])
            b.op("dve", "tensor_copy", xnh32[:, :], xnh[:, :])
            b.op("dve", "tensor_sub", xnh32[:, :], xn32[:, :], xnh32[:, :])
            t_l = b.op("dve", "tensor_copy", xnl[:, :], xnh32[:, :])
            b.wait("sp", t_l)
            b.dma("sp", stb, io["xn_tm"][i * 128:(i + 1) * 128, :], xnh[:, :])
            b.dma("sp", stb, io["xnl_tm"][i * 128:(i + 1) * 128, :], xnl[:, :])
        else:
            t_l = b.op("dve", "tensor_scalar", xnh[:, :], buf[:, :], st[:, 0:1],
                       st[:, 3:4], OP.subtract, OP.mult)
            b.wait("sp", t_l)
            b.dma("sp", stb, io["xn_tm"][i * 128:(i + 1) * 128, :], xnh[:, :])
    b.flush()
    b.barrier()
    if not with_gate:
        es.close()
        return

    xn_fm = es.enter_context(nc.sbuf_tensor(f"{pfx}_xnfm", [128, DC * TOK], BF16))
    xnl_fm = es.enter_context(nc.sbuf_tensor(f"{pfx}_xnlfm", [128, DC * TOK], BF16))
    gw32 = es.enter_context(nc.sbuf_tensor(f"{pfx}_gw32", [128, DC * 14], F32))
    gh = es.enter_context(nc.sbuf_tensor(f"{pfx}_gh", [128, DC * 14], BF16))
    gl32 = es.enter_context(nc.sbuf_tensor(f"{pfx}_gl32", [128, DC * 14], F32))
    gl = es.enter_context(nc.sbuf_tensor(f"{pfx}_gl", [128, DC * 14], BF16))
    Wt = es.enter_context(nc.sbuf_tensor(f"{pfx}_W", [128, 16], F32))
    gsc = es.enter_context(nc.sbuf_tensor(f"{pfx}_gsc", [128, 64], F32))
    gps = es.enter_context(nc.psum_tensor(f"{pfx}_gps", [128, 16], F32))

    emit_xbar_fm(b, xn_fm, io["xn_tm"], D, stream=f"{pfx}_xb")
    emit_xbar_fm(b, xnl_fm, io["xnl_tm"], D, stream=f"{pfx}_xb")
    t_g = b.dma("sp", f"{pfx}_gw", gw32[:, :],
                bass.AP(io["gate_wT"], 0, [[14, 128], [14 * 128, DC], [1, 14]]))
    b.wait("dve", t_g)
    b.op("dve", "tensor_copy", gh[:, :], gw32[:, :])
    b.op("dve", "tensor_copy", gl32[:, :], gh[:, :])
    b.op("dve", "tensor_sub", gl32[:, :], gw32[:, :], gl32[:, :])
    t_gl = b.op("dve", "tensor_copy", gl[:, :], gl32[:, :])
    b.wait("pe", t_gl)
    b.wait("pe", b.tok(f"{pfx}_xb"))

    b.stream(f"{pfx}_Wst")
    gps_free_at = 0
    for i in range(TOK // 128):
        b.wait("pe", ("e_dve", gps_free_at))
        first = True
        t_mm = None
        for c in range(DC):
            xh_v = xn_fm[:, c * TOK + i * 128: c * TOK + (i + 1) * 128]
            xl_v = xnl_fm[:, c * TOK + i * 128: c * TOK + (i + 1) * 128]
            b.mm(gps[:, 0:14], xh_v, gh[:, c * 14:(c + 1) * 14], start=first, stop=False)
            b.mm(gps[:, 0:14], xh_v, gl[:, c * 14:(c + 1) * 14], start=False, stop=False)
            t_mm = b.mm(gps[:, 0:14], xl_v, gh[:, c * 14:(c + 1) * 14], start=False,
                        stop=(c == DC - 1), sig=(c == DC - 1))
            first = False
        b.wait("dve", t_mm)
        b.wait("dve", b.tok(f"{pfx}_Wst"))
        c0, c1, c2, c3 = gsc[:, 0:1], gsc[:, 1:2], gsc[:, 2:3], gsc[:, 3:4]
        m1m = gsc[:, 16:30]
        m2m = gsc[:, 32:46]
        l2 = gsc[:, 48:62]
        b.op("dve", "reduce_max", c0, gps[:, 0:14], axis=AX.X)
        b.op("dve", "tensor_scalar", m1m, gps[:, 0:14], c0, None, OP.is_equal)
        b.op("dve", "scalar_tensor_tensor", l2, m1m, -1e30, gps[:, 0:14], OP.mult, OP.add)
        gps_free_at = b.cnt["e_dve"]
        b.op("dve", "reduce_max", c1, l2, axis=AX.X)
        b.op("dve", "tensor_scalar", m2m, l2, c1, None, OP.is_equal)
        b.op("dve", "tensor_sub", c2, c0, c1)
        t_d = b.op("dve", "tensor_copy", c3, c2)
        b.wait("act", t_d)
        t_s = b.op("act", "activation", c2, c2, AF.Sigmoid)
        b.wait("dve", t_s)
        b.op("dve", "tensor_scalar", c3, c2, -1.0, 1.0, OP.mult, OP.add)
        b.op("dve", "tensor_scalar_mul", m1m, m1m, c2)
        b.op("dve", "tensor_scalar_mul", m2m, m2m, c3)
        if i == 0:
            b.op("dve", "memset", Wt[:, 14:16], 0.0, sig=False)
        t_w = b.op("dve", "tensor_add", Wt[:, 0:14], m1m, m2m)
        b.wait("sp", t_w)
        b.dma("sp", f"{pfx}_Wst", io["W_hbm"][i * 128:(i + 1) * 128, :], Wt[:, :])
    b.flush()
    b.barrier()
    es.close()


def emit_l0attn(b, io):
    nc = b.nc
    es = contextlib.ExitStack()
    H, C = 16, 64
    xn_fm = es.enter_context(nc.sbuf_tensor("a0_xnfm", [128, DC * TOK], BF16))
    qk = es.enter_context(nc.sbuf_tensor("a0_qk", [128, 16 * T], BF16))
    wcol = [es.enter_context(nc.sbuf_tensor(f"a0_w{i}", [128, DC * 128], BF16))
            for i in range(2)]
    bq = es.enter_context(nc.sbuf_tensor("a0_bq", [128, 24], F32))
    bo = es.enter_context(nc.sbuf_tensor("a0_bo", [128, 8], F32))
    vstg = [es.enter_context(nc.sbuf_tensor(f"a0_vs{i}", [128, 512], BF16))
            for i in range(2)]
    vpair = [es.enter_context(nc.sbuf_tensor(f"a0_vp{i}", [128, T], BF16))
             for i in range(2)]
    VSL = 68
    vT = es.enter_context(nc.sbuf_tensor("a0_vT", [128, 2 * 16 * VSL], BF16))
    PT = [es.enter_context(nc.sbuf_tensor(f"a0_PT{i}", [128, 2048], BF16))
          for i in range(2)]
    o_fm = es.enter_context(nc.sbuf_tensor("a0_ofm", [128, DC * T], BF16))
    opair = es.enter_context(nc.sbuf_tensor("a0_opair", [128, 128], BF16))
    rcp = es.enter_context(nc.sbuf_tensor("a0_rcp", [128, 2], F32))
    s0b = es.enter_context(nc.sbuf_tensor("a0_s0", [128, 512], BF16))
    ident = es.enter_context(nc.sbuf_tensor("a0_id", [128, 192], BF16))

    psS2 = es.enter_context(nc.psum_tensor("a0_psS2", [128, 2048], F32))
    ps_pv = [es.enter_context(nc.psum_tensor(f"a0_pspv{i}", [128, 260], F32))
             for i in range(2)]
    ps_tr = es.enter_context(nc.psum_tensor("a0_pstr", [128, 1024], BF16))

    for st in ("a0_ld", "a0_xb", "a0_wld", "a0_vst", "a0_vld", "a0_s0st"):
        b.stream(st)
    idp = np.zeros((128, 192), dtype=NPBF)
    idp[:, 0:128] = np.eye(128, dtype=NPBF)
    idp[0:64, 128:192] = np.eye(64, dtype=NPBF)
    idp[64:128, 128:192] = np.eye(64, dtype=NPBF)
    id_hbm = b.const("identp", idp)
    t_id = b.dma("sp", "a0_ld", ident[:, :], id_hbm[:, :])
    b.wait("pe", t_id)

    emit_xbar_fm(b, xn_fm, io["xn_tm"], D, stream="a0_xb")
    b.dma("sp", "a0_ld", bq[:, :], io["b_qkv_l"][:, :])
    b.dma("sp", "a0_ld", bo[:, :], io["b_o_l"][:, :])
    b.wait("pe", b.tok("a0_xb"))

    wfree = [0, 0]
    for bi in range(NB):
        tb = bi * T
        for m in range(24):
            wbuf = wcol[m % 2]
            b.wait("pool", ("e_pe", wfree[m % 2]))
            t_w = b.dma("pool", "a0_wld", wbuf[:, :],
                        bass.AP(io["w_qkvT"], m * 128,
                                [[3072, 128], [128 * 3072, DC], [1, 128]]))
            b.wait("pe", t_w)
            for s in range(4):
                ps = psS2[:, (s % 2) * 512:(s % 2) * 512 + 512]
                b.wait("pe", ("e_act", b.cnt["e_act"]))
                for c in range(DC):
                    b.mm(ps, wbuf[:, c * 128:(c + 1) * 128],
                         xn_fm[:, c * TOK + tb + s * 512: c * TOK + tb + s * 512 + 512],
                         start=(c == 0), stop=(c == DC - 1), sig=(c == DC - 1))
                b.wait("act", ("e_pe", b.cnt["e_pe"]))
                if m < 16:
                    dst = qk[:, m * T + s * 512: m * T + s * 512 + 512]
                    b.op("act", "activation", dst, ps, AF.Identity, bias=bq[:, m:m + 1])
                else:
                    vbuf = vstg[s % 2]
                    b.wait("act", b.tok("a0_vst"))
                    t_e = b.op("act", "activation", vbuf[:, :], ps, AF.Identity,
                               bias=bq[:, m:m + 1])
                    b.wait("sp", t_e)
                    b.dma("sp", "a0_vst",
                          io["v_hbm"][(m - 16) * 128:(m - 15) * 128, tb + s * 512: tb + s * 512 + 512],
                          vbuf[:, :])
            wfree[m % 2] = b.cnt["e_pe"]
        b.flush()

        b.wait("sp", b.tok("a0_vst"))
        vpfree = [0, 0]
        for hp in range(8):
            h0, h1 = 2 * hp, 2 * hp + 1
            vbuf = vpair[hp % 2]
            b.wait("sp", ("e_pe", vpfree[hp % 2]))
            t_v = b.dma("sp", "a0_vld", vbuf[:, :],
                        io["v_hbm"][hp * 128:(hp + 1) * 128, tb: tb + T])
            b.wait("dve", ("e_pe", b.cnt["e_pe"]))
            b.op("dve", "memset", vT[:, :], 1.0)
            b.wait("pe", t_v)
            b.wait("pe", ("e_dve", b.cnt["e_dve"]))
            for hh in range(2):
                b.wait("pe", ("e_dve", b.cnt["e_dve"]))
                for mt in range(16):
                    b.op("pe", "transpose",
                         ps_tr[:, mt * 64:(mt + 1) * 64],
                         vbuf[hh * 64:(hh + 1) * 64, mt * 128:(mt + 1) * 128],
                         ident[hh * 64:(hh + 1) * 64, 128:192], sig=(mt == 15))
                b.wait("dve", ("e_pe", b.cnt["e_pe"]))
                b.op("dve", "tensor_copy",
                     bass.AP(vT, hh * 16 * VSL, [[2 * 16 * VSL, 128], [VSL, 16], [1, 64]]),
                     ps_tr[:, 0:1024])
            vpfree[hp % 2] = b.cnt["e_pe"]
            b.wait("pe", ("e_dve", b.cnt["e_dve"]))

            exp_tok = None
            for lb in range(4):
                pvfree = b.cnt["e_dve"]

                def emit_pv_round(rnd):
                    for sub in range(2):
                        mt = rnd * 2 + sub
                        for hh in range(2):
                            for ls in range(4):
                                b.mm(ps_pv[hh][:, ls * 65:ls * 65 + 65],
                                     PT[rnd % 2][:, hh * 1024 + sub * 512 + ls * 128:
                                                 hh * 1024 + sub * 512 + ls * 128 + 128],
                                     vT[:, hh * 16 * VSL + mt * VSL: hh * 16 * VSL + mt * VSL + 65],
                                     start=(rnd == 0 and sub == 0 and ls == 0),
                                     stop=(rnd == 7 and sub == 1),
                                     sig=(sub == 1 and hh == 1 and ls == 3))

                for rnd in range(8):
                    b.wait("pe", exp_tok)
                    if rnd == 0:
                        b.wait("pe", ("e_dve", pvfree))
                    for sub in range(2):
                        mt = rnd * 2 + sub
                        for hh, h in ((0, h0), (1, h1)):
                            qv = qk[64 * (h % 2):64 * (h % 2) + 64,
                                    (h // 2) * T + lb * 512:(h // 2) * T + lb * 512 + 512]
                            kv = qk[64 * (h % 2):64 * (h % 2) + 64,
                                    (8 + h // 2) * T + mt * 128:(8 + h // 2) * T + mt * 128 + 128]
                            b.mm(psS2[:, hh * 1024 + sub * 512: hh * 1024 + sub * 512 + 512],
                                 kv, qv, start=True, stop=True,
                                 sig=(sub == 1 and hh == 1))
                    t_sc = ("e_pe", b.cnt["e_pe"])
                    if rnd > 0:
                        emit_pv_round(rnd - 1)
                    b.wait("act", t_sc)
                    exp_tok = b.op("act", "activation", PT[rnd % 2][:, :], psS2[:, :],
                                   AF.Exp, scale=1.0 / 32.0)
                b.wait("pe", exp_tok)
                emit_pv_round(7)
                b.wait("dve", ("e_pe", b.cnt["e_pe"]))
                for ls in range(4):
                    for hh in range(2):
                        b.op("dve", "reciprocal", rcp[:, hh:hh + 1],
                             ps_pv[hh][:, ls * 65 + 64: ls * 65 + 65])
                        t_o = b.op("dve", "tensor_scalar_mul",
                                   opair[:, hh * 64:hh * 64 + 64],
                                   ps_pv[hh][:, ls * 65: ls * 65 + 64],
                                   rcp[:, hh:hh + 1])
                    b.wait("pe", t_o)
                    b.op("pe", "transpose", ps_tr[:, ls * 128:(ls + 1) * 128],
                         opair[:, :], ident[:, 0:128], sig=True)
                    b.wait("dve", ("e_pe", b.cnt["e_pe"]))
                t_c = b.op("dve", "tensor_copy", o_fm[:, hp * T + lb * 512: hp * T + (lb + 1) * 512],
                           ps_tr[:, 0:512])
                b.wait("pe", t_c)
        b.flush()

        for dch in range(DC):
            wbuf = wcol[dch % 2]
            b.wait("pool", ("e_pe", wfree[dch % 2]))
            t_w = b.dma("pool", "a0_wld", wbuf[:, :],
                        bass.AP(io["w_oT"], dch * 128,
                                [[1024, 128], [128 * 1024, DC], [1, 128]]))
            b.wait("pe", t_w)
            for s in range(4):
                ps = psS2[:, (s % 2) * 512:(s % 2) * 512 + 512]
                b.wait("pe", ("e_dve", b.cnt["e_dve"]))
                for c in range(DC):
                    b.mm(ps, wbuf[:, c * 128:(c + 1) * 128],
                         o_fm[:, c * T + s * 512: c * T + s * 512 + 512],
                         start=(c == 0), stop=(c == DC - 1), sig=(c == DC - 1))
                b.wait("dve", ("e_pe", b.cnt["e_pe"]))
                b.wait("dve", b.tok("a0_s0st"))
                t_s = b.op("dve", "scalar_tensor_tensor", s0b[:, :], ps,
                           bo[:, dch:dch + 1],
                           xn_fm[:, dch * TOK + tb + s * 512: dch * TOK + tb + s * 512 + 512],
                           OP.add, OP.add)
                b.wait("sp", t_s)
                b.dma("sp", "a0_s0st",
                      io["s0_fm"][dch * 128:(dch + 1) * 128, tb + s * 512: tb + s * 512 + 512],
                      s0b[:, :])
            wfree[dch % 2] = b.cnt["e_pe"]
        b.flush()
    b.barrier()
    es.close()


def emit_moe(b, io):
    nc = b.nc
    es = contextlib.ExitStack()
    NE = 14
    SLAB = 512
    NSLAB = TOK // SLAB
    xn_fm = es.enter_context(nc.sbuf_tensor("mo_xnfm", [128, DC * TOK], BF16))
    Wtm = es.enter_context(nc.sbuf_tensor("mo_W", [128, 32 * 16], F32))
    Wbf = es.enter_context(nc.sbuf_tensor("mo_Wbf", [128, 16], BF16))
    Wfm = es.enter_context(nc.sbuf_tensor("mo_Wfm", [16, 32 * 128], BF16))
    b1l = es.enter_context(nc.sbuf_tensor("mo_b1", [128, 224], F32))
    b2h = es.enter_context(nc.sbuf_tensor("mo_b2h", [14, 1024], BF16))
    b2lo = es.enter_context(nc.sbuf_tensor("mo_b2l", [14, 1024], BF16))
    ident = es.enter_context(nc.sbuf_tensor("mo_id", [128, 128], BF16))
    es_w = contextlib.ExitStack()
    w1sb = [es_w.enter_context(nc.sbuf_tensor(f"mo_w1{i}", [128, DC * 2048], BF16))
            for i in range(2)]
    w2sb = es_w.enter_context(nc.sbuf_tensor("mo_w2", [128, 16 * 1024], BF16))
    Hs = es_w.enter_context(nc.sbuf_tensor("mo_H", [128, 16 * SLAB], BF16))
    es_b2 = contextlib.ExitStack()
    b2f = es_b2.enter_context(nc.sbuf_tensor("mo_b2f", [14, 1024], F32))
    b2l32 = es_b2.enter_context(nc.sbuf_tensor("mo_b2l32", [14, 1024], F32))
    psA = [es.enter_context(nc.psum_tensor(f"mo_psA{i}", [128, 512], F32)) for i in range(2)]
    psB = [es.enter_context(nc.psum_tensor(f"mo_psB{i}", [128, 1024], F32)) for i in range(2)]
    psT = es.enter_context(nc.psum_tensor("mo_psT", [16, 128], BF16))

    for st in ("mo_ld", "mo_xb", "mo_w1ld", "mo_w2ld", "mo_acc", "mo_fin", "mo_x1"):
        b.stream(st)
    id_hbm = b.const("ident128b", np.eye(128, dtype=NPBF))
    b.dma("sp", "mo_ld", ident[:, :], id_hbm[:, :])
    emit_xbar_fm(b, xn_fm, io["xn_tm"], D, stream="mo_xb")
    b.dma("sp", "mo_ld", Wtm[:, :],
          bass.AP(io["W_hbm"], 0, [[16, 128], [128 * 16, 32], [1, 16]]))
    b.dma("sp", "mo_ld", b1l[:, :], io["moe_b1_l"][:, :])
    t_b2 = b.dma("sp", "mo_ld", b2f[:, :], io["moe_b2"][:, :])
    b.wait("dve", t_b2)
    b.op("dve", "tensor_copy", b2h[:, :], b2f[:, :])
    b.op("dve", "tensor_copy", b2l32[:, :], b2h[:, :])
    b.op("dve", "tensor_sub", b2l32[:, :], b2f[:, :], b2l32[:, :])
    b.op("dve", "tensor_copy", b2lo[:, :], b2l32[:, :])
    es_b2.close()
    stage = es_w.enter_context(nc.sbuf_tensor("mo_stg", [128, 8 * 1024], BF16))
    b.wait("dve", b.tok("mo_ld"))
    for i in range(32):
        t_c = b.op("dve", "tensor_copy", Wbf[:, :], Wtm[:, i * 16:(i + 1) * 16])
        b.wait("pe", t_c)
        t_t = b.op("pe", "transpose", psT[:, :], Wbf[:, :], ident[:, :], sig=True)
        b.wait("dve", t_t)
        t_c2 = b.op("dve", "tensor_copy", Wfm[0:16, i * 128:(i + 1) * 128], psT[:, :])
        b.wait("pe", t_c2)
    b.flush()
    b.wait("pe", b.tok("mo_xb"))

    evacA = [0, 0]
    evacB = [0, 0]
    w1used = [0, 0]
    w2used = 0
    stg_tok = [0, 0]
    t_w1 = [None, None]
    b.wait("pool", ("e_dve", b.cnt["e_dve"]))
    t_w1[0] = b.dma("pool", "mo_w1ld", w1sb[0][:, :],
                    bass.AP(io["moe_w1"], 0, [[2048, 128], [128 * 2048, DC], [1, 2048]]))
    for e in range(NE):
        wsl = e % 2
        b.wait("pool", ("e_pe", b.cnt["e_pe"]))
        t_w2 = b.dma("pool", "mo_w2ld", w2sb[:, :],
                     bass.AP(io["moe_w2"], e * 2048 * 1024,
                             [[1024, 128], [128 * 1024, 16], [1, 1024]]))
        b.wait("pe", t_w1[wsl])
        w2_waited = False
        for sl in range(NSLAB):
            tb = sl * SLAB
            for m in range(16):
                ps = psA[m % 2]
                b.wait("pe", ("e_act", evacA[m % 2]))
                for c in range(DC):
                    b.mm(ps[:, :], w1sb[wsl][:, c * 2048 + m * 128: c * 2048 + m * 128 + 128],
                         xn_fm[:, c * TOK + tb: c * TOK + tb + SLAB],
                         start=(c == 0), stop=(c == DC - 1), sig=(c == DC - 1))
                b.wait("act", ("e_pe", b.cnt["e_pe"]))
                t_ev = b.op("act", "activation", Hs[:, m * SLAB:(m + 1) * SLAB], ps[:, :],
                            AF.Gelu, bias=b1l[:, e * 16 + m: e * 16 + m + 1])
                evacA[m % 2] = t_ev[1]
            if sl == NSLAB - 1:
                w1used[wsl] = b.cnt["e_pe"]
                if e + 1 < NE:
                    b.wait("pool", ("e_pe", w1used[(e + 1) % 2]))
                    t_w1[(e + 1) % 2] = b.dma(
                        "pool", "mo_w1ld", w1sb[(e + 1) % 2][:, :],
                        bass.AP(io["moe_w1"], (e + 1) * 1024 * 2048,
                                [[2048, 128], [128 * 2048, DC], [1, 2048]]))
            b.wait("pe", ("e_act", b.cnt["e_act"]))
            if not w2_waited:
                b.wait("pe", t_w2)
                w2_waited = True
            for tt in range(4):
                ps = psB[tt % 2]
                b.wait("pe", ("e_dve", evacB[tt % 2]))
                for c in range(16):
                    lhs = Hs[:, c * SLAB + tt * 128: c * SLAB + tt * 128 + 128]
                    b.mm(ps[:, 0:512], lhs, w2sb[:, c * 1024: c * 1024 + 512],
                         start=(c == 0), stop=(c == 15))
                    b.mm(ps[:, 512:1024], lhs, w2sb[:, c * 1024 + 512: c * 1024 + 1024],
                         start=(c == 0), stop=(c == 15), sig=(c == 15))
                b.wait("dve", ("e_pe", b.cnt["e_pe"]))
                b.wait("dve", ("mo_acc", stg_tok[(sl % 2)]))
                t_ev = b.op("dve", "tensor_scalar_mul",
                            stage[:, (sl % 2) * 4096 + tt * 1024:
                                  (sl % 2) * 4096 + (tt + 1) * 1024],
                            ps[:, :], Wtm[:, (sl * 4 + tt) * 16 + e:
                                          (sl * 4 + tt) * 16 + e + 1])
                evacB[tt % 2] = t_ev[1]
            t_st = ("e_dve", b.cnt["e_dve"])
            b.wait("pool", t_st)
            b.dma("pool", "mo_acc",
                  bass.AP(io["acc_hbm"], tb * D, [[D, 128], [128 * D, 4], [1, D]]),
                  stage[:, (sl % 2) * 4096: (sl % 2) * 4096 + 4096],
                  **({} if e == 0 else {"accum_op": OP.add}))
            stg_tok[sl % 2] = b.cnt["mo_acc"]
        b.flush()

    for i in range(32):
        ps = psB[i % 2]
        b.wait("pe", ("e_dve", b.cnt["e_dve"]))
        wv = Wfm[0:14, i * 128:(i + 1) * 128]
        b.mm(ps[:, 0:512], wv, b2h[0:14, 0:512], start=True, stop=False)
        b.mm(ps[:, 0:512], wv, b2lo[0:14, 0:512], start=False, stop=True)
        b.mm(ps[:, 512:1024], wv, b2h[0:14, 512:1024], start=True, stop=False)
        b.mm(ps[:, 512:1024], wv, b2lo[0:14, 512:1024], start=False, stop=True, sig=True)
        b.wait("dve", ("e_pe", b.cnt["e_pe"]))
        b.wait("dve", b.tok("mo_acc"))
        t_c = b.op("dve", "tensor_copy", stage[:, 0:1024], ps[:, :])
        b.wait("pool", t_c)
        b.dma("pool", "mo_acc", io["acc_hbm"][i * 128:(i + 1) * 128, :],
              stage[:, 0:1024], accum_op=OP.add)
    b.flush()

    b.flush()
    es_w.close()
    b.scope_guard()
    accT_t = es.enter_context(nc.sbuf_tensor("mo_fa", [128, D], F32))
    xT_t = es.enter_context(nc.sbuf_tensor("mo_fx", [128, D], F32))
    t1T_t = es.enter_context(nc.sbuf_tensor("mo_ft", [128, D], F32))
    x1T_t = es.enter_context(nc.sbuf_tensor("mo_fo", [128, D], F32))
    accT = accT_t[:, :]
    xT = xT_t[:, :]
    t1T = t1T_t[:, :]
    x1T = x1T_t[:, :]
    xhT = es.enter_context(nc.sbuf_tensor("mo_xh", [128, D], BF16))
    xlT = es.enter_context(nc.sbuf_tensor("mo_xl", [128, D], BF16))
    s0T = es.enter_context(nc.sbuf_tensor("mo_s0", [128, D], BF16))
    gT = es.enter_context(nc.sbuf_tensor("mo_g", [128, D], BF16))
    b.wait("sp", b.tok("mo_acc"))
    for i in range(32):
        prev_dve = b.cnt["e_dve"]
        b.wait("sp", ("e_dve", prev_dve))
        b.wait("sp", b.tok("mo_x1"))
        t_a = b.dma("sp", "mo_fin", accT, io["acc_hbm"][i * 128:(i + 1) * 128, :])
        t_h = b.dma("sp", "mo_fin", xhT[:, :], io["xn_tm"][i * 128:(i + 1) * 128, :])
        t_l = b.dma("sp", "mo_fin", xlT[:, :], io["xnl_tm"][i * 128:(i + 1) * 128, :])
        t_x = b.dma("sp", "mo_fin", xT, io["x"][i * 128:(i + 1) * 128, :])
        t_s = b.dma_tr("sp", "mo_fin", s0T[:, :],
                       io["s0_fm"][:, i * 128:(i + 1) * 128])
        b.wait("dve", b.tok("mo_fin"))
        b.op("dve", "tensor_add", t1T, xhT[:, :], xlT[:, :])
        b.op("dve", "tensor_add", t1T, t1T, accT)
        t_cl = b.op("dve", "tensor_scalar", t1T, t1T, -100.0, 100.0,
                    OP.max, OP.min)
        b.wait("act", t_cl)
        t_g = b.op("act", "activation", gT[:, :], t1T, AF.Silu)
        b.wait("dve", t_g)
        b.op("dve", "tensor_mul", gT[:, :], gT[:, :], s0T[:, :])
        t_x1 = b.op("dve", "tensor_add", x1T, xT, gT[:, :])
        b.wait("sp", t_x1)
        b.dma("sp", "mo_x1", io["x1"][i * 128:(i + 1) * 128, :], x1T)
    b.flush()
    b.barrier()
    es.close()


def rope_scale_np(Tn=T, dhr=32):
    t = np.arange(Tn, dtype=np.float64)
    inv = 1.0 / 10000.0 ** (np.arange(0, dhr, 2, dtype=np.float64) / dhr)
    freqs = t[:, None] * inv[None, :]
    emb = np.concatenate([freqs, freqs], -1)
    return (np.cos(emb) + np.sin(emb)).astype(np.float32)


def emit_l1attn(b, io):
    nc = b.nc
    es = contextlib.ExitStack()
    inv96 = 1.0 / np.sqrt(np.float32(96.0))
    rs = rope_scale_np()
    qsc = np.zeros((128, T), np.float32)
    ksc = np.zeros((128, T), np.float32)
    for p in range(128):
        qsc[p] = rs[:, p % 32] * inv96
    for p in list(range(32)) + list(range(64, 96)):
        ksc[p] = rs[:, p % 32]
    qsc_h = b.const("l1_qsc", qsc)
    ksc_h = b.const("l1_ksc", ksc)
    masks = np.zeros((128, 4 * 512), np.float32)
    ii = np.arange(128)[:, None]
    jj = np.arange(512)[None, :]
    for k in range(4):
        masks[:, k * 512:(k + 1) * 512] = np.where(ii + 128 * k > jj, -1e30, 0.0)
    mask_h = b.const("l1_masks", masks.astype(NPBF))
    idp = np.zeros((128, 192), dtype=NPBF)
    idp[:, 0:128] = np.eye(128, dtype=NPBF)
    idp[0:64, 128:192] = np.eye(64, dtype=NPBF)
    idp[64:128, 128:192] = np.eye(64, dtype=NPBF)
    id_hbm = b.const("identp", idp)

    xn_fm = es.enter_context(nc.sbuf_tensor("l1_xnfm", [128, DC * TOK], BF16))
    qsc_sb = es.enter_context(nc.sbuf_tensor("l1_qscs", [128, T], F32))
    ksc_sb = es.enter_context(nc.sbuf_tensor("l1_kscs", [128, T], F32))
    mask_sb = es.enter_context(nc.sbuf_tensor("l1_msk", [128, 4 * 512], BF16))
    ident = es.enter_context(nc.sbuf_tensor("l1_id", [128, 192], BF16))
    kr = es.enter_context(nc.sbuf_tensor("l1_kr", [128, TOK], BF16))

    for st in ("l1_ld", "l1_xb", "l1_wld", "l1_st", "l1_pld", "l1_s1st"):
        b.stream(st)
    b.dma("sp", "l1_ld", qsc_sb[:, :], qsc_h[:, :])
    b.dma("sp", "l1_ld", ksc_sb[:, :], ksc_h[:, :])
    b.dma("sp", "l1_ld", mask_sb[:, :], mask_h[:, :])
    b.dma("sp", "l1_ld", ident[:, :], id_hbm[:, :])
    emit_xbar_fm(b, xn_fm, io["xn_tm"], D, stream="l1_xb")
    b.wait("pe", b.tok("l1_xb"))
    b.wait("pe", b.tok("l1_ld"))

    esA = contextlib.ExitStack()
    wsmall = esA.enter_context(nc.sbuf_tensor("l1_wsm", [128, DC * 256], BF16))
    wuk = esA.enter_context(nc.sbuf_tensor("l1_wuk", [128, 2 * 1024], BF16))
    wuv = esA.enter_context(nc.sbuf_tensor("l1_wuv", [128, 2 * 1024], BF16))
    wkr = esA.enter_context(nc.sbuf_tensor("l1_wkr", [128, DC * 32], BF16))
    wkrd = esA.enter_context(nc.sbuf_tensor("l1_wkrd", [128, DC * 128], BF16))
    ckv = esA.enter_context(nc.sbuf_tensor("l1_ckv", [128, 2 * T], BF16))
    wcol = [esA.enter_context(nc.sbuf_tensor(f"l1_wc{i}", [128, 32 * 128], BF16))
            for i in range(2)]
    cq = esA.enter_context(nc.sbuf_tensor("l1_cq", [128, 32 * 1024], BF16))
    estg = [esA.enter_context(nc.sbuf_tensor(f"l1_es{i}", [128, 512], BF16))
            for i in range(2)]
    psA = [esA.enter_context(nc.psum_tensor(f"l1_psA{i}", [128, 512], F32))
           for i in range(2)]
    ps4 = esA.enter_context(nc.psum_tensor("l1_ps4", [128, 2048], F32))
    pskr = esA.enter_context(nc.psum_tensor("l1_pskr", [128, 512], F32))

    t_w = b.dma("pool", "l1_wld", wsmall[:, :],
                bass.AP(io["W_DKV"], 0, [[256, 128], [128 * 256, DC], [1, 256]]))
    b.dma("pool", "l1_wld", wuk[:, :],
          bass.AP(io["W_UK"], 0, [[1024, 128], [128 * 1024, 2], [1, 1024]]))
    b.dma("pool", "l1_wld", wuv[:, :],
          bass.AP(io["W_UV"], 0, [[1024, 128], [128 * 1024, 2], [1, 1024]]))
    b.dma("pool", "l1_wld", wkr[:, :],
          bass.AP(io["W_KR"], 0, [[32, 128], [128 * 32, DC], [1, 32]]))
    b.wait("dve", b.tok("l1_wld"))
    b.op("dve", "memset", wkrd[:, :], 0.0)
    for c in range(DC):
        b.op("dve", "tensor_copy", wkrd[:, c * 128: c * 128 + 32],
             wkr[:, c * 32:(c + 1) * 32])
        b.op("dve", "tensor_copy", wkrd[:, c * 128 + 64: c * 128 + 96],
             wkr[:, c * 32:(c + 1) * 32])
    b.wait("pe", b.tok("l1_wld"))
    b.wait("pe", ("e_dve", b.cnt["e_dve"]))

    for bi in range(NB):
        tb = bi * T
        for m in range(2):
            for s in range(4):
                ps = psA[s % 2]
                b.wait("pe", ("e_act", b.cnt["e_act"]))
                b.wait("pe", ("e_dve", b.cnt["e_dve"]))
                for c in range(DC):
                    b.mm(ps[:, :], wsmall[:, c * 256 + m * 128: c * 256 + m * 128 + 128],
                         xn_fm[:, c * TOK + tb + s * 512: c * TOK + tb + s * 512 + 512],
                         start=(c == 0), stop=(c == DC - 1), sig=(c == DC - 1))
                b.wait("act", ("e_pe", b.cnt["e_pe"]))
                b.op("act", "activation", ckv[:, m * T + s * 512: m * T + s * 512 + 512],
                     ps[:, :], AF.Identity)
        for (wt, dst) in ((wuk, io["kc_hbm"]), (wuv, io["v_hbm"])):
            for m in range(DC):
                for s in range(4):
                    ps = psA[s % 2]
                    b.wait("pe", ("e_act", b.cnt["e_act"]))
                    b.wait("pe", ("e_dve", b.cnt["e_dve"]))
                    for c in range(2):
                        b.mm(ps[:, :], wt[:, c * 1024 + m * 128: c * 1024 + m * 128 + 128],
                             ckv[:, c * T + s * 512: c * T + s * 512 + 512],
                             start=(c == 0), stop=(c == 1), sig=(c == 1))
                    b.wait("act", ("e_pe", b.cnt["e_pe"]))
                    stg = estg[s % 2]
                    b.wait("act", b.tok("l1_st"))
                    t_e = b.op("act", "activation", stg[:, :], ps[:, :], AF.Identity)
                    b.wait("sp", t_e)
                    b.dma("sp", "l1_st", dst[m * 128:(m + 1) * 128, tb + s * 512: tb + s * 512 + 512],
                          stg[:, :])
        for s in range(4):
            b.wait("pe", ("e_dve", b.cnt["e_dve"]))
            for c in range(DC):
                b.mm(pskr[:, :], wkrd[:, c * 128:(c + 1) * 128],
                     xn_fm[:, c * TOK + tb + s * 512: c * TOK + tb + s * 512 + 512],
                     start=(c == 0), stop=(c == DC - 1), sig=(c == DC - 1))
            b.wait("dve", ("e_pe", b.cnt["e_pe"]))
            b.op("dve", "tensor_mul", kr[:, tb + s * 512: tb + s * 512 + 512],
                 pskr[:, :], ksc_sb[:, s * 512: s * 512 + 512])
        wfree = [0, 0]
        for hs in range(2):
            ts0 = tb + hs * 1024
            for m in range(32):
                wbuf = wcol[m % 2]
                b.wait("pool", ("e_pe", wfree[m % 2]))
                t_w = b.dma("pool", "l1_wld", wbuf[:, 0:DC * 128],
                            bass.AP(io["W_DQ"], m * 128,
                                    [[4096, 128], [128 * 4096, DC], [1, 128]]))
                b.wait("pe", t_w)
                grp = m % 4
                ps = ps4[:, grp * 512:(grp + 1) * 512]
                if grp == 0:
                    b.wait("pe", ("e_act", b.cnt["e_act"]))
                for c in range(DC):
                    b.mm(ps, wbuf[:, c * 128:(c + 1) * 128],
                         xn_fm[:, c * TOK + ts0: c * TOK + ts0 + 512],
                         start=(c == 0), stop=(c == DC - 1),
                         sig=(c == DC - 1))
                wfree[m % 2] = b.cnt["e_pe"]
                if grp == 3:
                    b.wait("act", ("e_pe", b.cnt["e_pe"]))
                    b.op("act", "activation",
                         bass.AP(cq, (m - 3) * 1024, [[32 * 1024, 128], [1024, 4], [1, 512]]),
                         ps4[:, :], AF.Identity)
            for m in range(32):
                wbuf = wcol[m % 2]
                b.wait("pool", ("e_pe", wfree[m % 2]))
                t_w = b.dma("pool", "l1_wld", wbuf[:, 0:DC * 128],
                            bass.AP(io["W_DQ"], m * 128,
                                    [[4096, 128], [128 * 4096, DC], [1, 128]]))
                b.wait("pe", t_w)
                grp = m % 4
                ps = ps4[:, grp * 512:(grp + 1) * 512]
                if grp == 0:
                    b.wait("pe", ("e_act", b.cnt["e_act"]))
                for c in range(DC):
                    b.mm(ps, wbuf[:, c * 128:(c + 1) * 128],
                         xn_fm[:, c * TOK + ts0 + 512: c * TOK + ts0 + 1024],
                         start=(c == 0), stop=(c == DC - 1),
                         sig=(c == DC - 1))
                wfree[m % 2] = b.cnt["e_pe"]
                if grp == 3:
                    b.wait("act", ("e_pe", b.cnt["e_pe"]))
                    b.op("act", "activation",
                         bass.AP(cq, (m - 3) * 1024 + 512,
                                 [[32 * 1024, 128], [1024, 4], [1, 512]]),
                         ps4[:, :], AF.Identity)
            for (wh, dst_h, scale_mode) in ((io["W_UQ"], io["qc_hbm"], "act"),
                                            (io["W_QR_pad"], io["qr_hbm"], "dve")):
                for m in range(DC):
                    wbuf = wcol[m % 2]
                    b.wait("pool", ("e_pe", wfree[m % 2]))
                    t_w = b.dma("pool", "l1_wld", wbuf[:, :],
                                bass.AP(wh, m * 128,
                                        [[1024, 128], [128 * 1024, 32], [1, 128]]))
                    b.wait("pe", t_w)
                    for half in range(2):
                        ps = psA[half]
                        b.wait("pe", ("e_act", b.cnt["e_act"]))
                        b.wait("pe", ("e_dve", b.cnt["e_dve"]))
                        for c in range(32):
                            b.mm(ps[:, :], wbuf[:, c * 128:(c + 1) * 128],
                                 cq[:, c * 1024 + half * 512: c * 1024 + half * 512 + 512],
                                 start=(c == 0), stop=(c == 31), sig=(c == 31))
                        wfree[m % 2] = b.cnt["e_pe"]
                        stg = estg[half]
                        tloc = hs * 1024 + half * 512
                        if scale_mode == "act":
                            b.wait("act", ("e_pe", b.cnt["e_pe"]))
                            b.wait("act", b.tok("l1_st"))
                            t_e = b.op("act", "activation", stg[:, :], ps[:, :],
                                       AF.Identity, scale=float(inv96))
                        else:
                            b.wait("dve", ("e_pe", b.cnt["e_pe"]))
                            b.wait("dve", b.tok("l1_st"))
                            t_e = b.op("dve", "tensor_mul", stg[:, :], ps[:, :],
                                       qsc_sb[:, (tloc % T): (tloc % T) + 512])
                        b.wait("sp", t_e)
                        b.dma("sp", "l1_st",
                              dst_h[m * 128:(m + 1) * 128, tb + tloc: tb + tloc + 512],
                              stg[:, :])
    b.flush()
    esA.close()

    esB = contextlib.ExitStack()
    VSL = 68
    qcp = [esB.enter_context(nc.sbuf_tensor(f"l1_qc{i}", [128, T], BF16)) for i in range(2)]
    qrp = [esB.enter_context(nc.sbuf_tensor(f"l1_qr{i}", [128, T], BF16)) for i in range(2)]
    kcp = [esB.enter_context(nc.sbuf_tensor(f"l1_kc{i}", [128, T], BF16)) for i in range(2)]
    vp = [esB.enter_context(nc.sbuf_tensor(f"l1_vp{i}", [128, T], BF16)) for i in range(2)]
    vT = esB.enter_context(nc.sbuf_tensor("l1_vT", [128, 2 * 16 * VSL], BF16))
    PT = [esB.enter_context(nc.sbuf_tensor(f"l1_PT{i}", [128, 2048], BF16)) for i in range(2)]
    o_fm = esB.enter_context(nc.sbuf_tensor("l1_ofm", [128, DC * T], BF16))
    opair = esB.enter_context(nc.sbuf_tensor("l1_op", [128, 128], BF16))
    rcp = esB.enter_context(nc.sbuf_tensor("l1_rcp", [128, 2], F32))
    s1b = esB.enter_context(nc.sbuf_tensor("l1_s1b", [128, 512], BF16))
    wcolB = [esB.enter_context(nc.sbuf_tensor(f"l1_wB{i}", [128, DC * 128], BF16))
             for i in range(2)]
    psS2 = esB.enter_context(nc.psum_tensor("l1_psS2", [128, 2048], F32))
    ps_pv = [esB.enter_context(nc.psum_tensor(f"l1_pspv{i}", [128, 260], F32))
             for i in range(2)]
    ps_tr = esB.enter_context(nc.psum_tensor("l1_pstr", [128, 1024], BF16))

    b.wait("sp", b.tok("l1_st"))
    b.wait("pe", ("e_dve", b.cnt["e_dve"]))
    b.scope_guard()
    pfree = [0, 0]
    wfreeB = [0, 0]
    for bi in range(NB):
        tb = bi * T
        for hp in range(8):
            h0, h1 = 2 * hp, 2 * hp + 1
            pb = hp % 2
            b.wait("sp", ("e_pe", pfree[pb]))
            t0_ = b.dma("sp", "l1_pld", qcp[pb][:, :], io["qc_hbm"][hp * 128:(hp + 1) * 128, tb: tb + T])
            b.dma("sp", "l1_pld", qrp[pb][:, :], io["qr_hbm"][hp * 128:(hp + 1) * 128, tb: tb + T])
            b.dma("sp", "l1_pld", kcp[pb][:, :], io["kc_hbm"][hp * 128:(hp + 1) * 128, tb: tb + T])
            t_v = b.dma("sp", "l1_pld", vp[pb][:, :], io["v_hbm"][hp * 128:(hp + 1) * 128, tb: tb + T])
            b.wait("dve", ("e_pe", b.cnt["e_pe"]))
            b.op("dve", "memset", vT[:, :], 1.0)
            b.wait("pe", t_v)
            for hh in range(2):
                b.wait("pe", ("e_dve", b.cnt["e_dve"]))
                for mt in range(16):
                    b.op("pe", "transpose",
                         ps_tr[:, mt * 64:(mt + 1) * 64],
                         vp[pb][hh * 64:(hh + 1) * 64, mt * 128:(mt + 1) * 128],
                         ident[hh * 64:(hh + 1) * 64, 128:192], sig=(mt == 15))
                b.wait("dve", ("e_pe", b.cnt["e_pe"]))
                b.op("dve", "tensor_copy",
                     bass.AP(vT, hh * 16 * VSL, [[2 * 16 * VSL, 128], [VSL, 16], [1, 64]]),
                     ps_tr[:, 0:1024])
            b.wait("pe", ("e_dve", b.cnt["e_dve"]))

            exp_tok = None
            for lb in range(4):
                nmt = 4 * lb + 4
                nrnd = nmt // 2
                pvfree = b.cnt["e_dve"]

                def emit_pv_round(rnd):
                    for sub in range(2):
                        mt = rnd * 2 + sub
                        for hh in range(2):
                            for ls in range(4):
                                b.mm(ps_pv[hh][:, ls * 65: ls * 65 + 65],
                                     PT[rnd % 2][:, hh * 1024 + sub * 512 + ls * 128:
                                                 hh * 1024 + sub * 512 + ls * 128 + 128],
                                     vT[:, hh * 16 * VSL + mt * VSL: hh * 16 * VSL + mt * VSL + 65],
                                     start=(rnd == 0 and sub == 0 and ls == 0),
                                     stop=(rnd == nrnd - 1 and sub == 1),
                                     sig=(sub == 1 and hh == 1 and ls == 3))

                for rnd in range(nrnd):
                    b.wait("pe", exp_tok)
                    if rnd == 0:
                        b.wait("pe", ("e_dve", pvfree))
                    for sub in range(2):
                        mt = rnd * 2 + sub
                        kdiag = mt - 4 * lb
                        for hh, h in ((0, h0), (1, h1)):
                            base = 64 * (h % 2)
                            qcv = qcp[pb][base:base + 64, lb * 512:(lb + 1) * 512]
                            qrv = qrp[pb][base:base + 32, lb * 512:(lb + 1) * 512]
                            kcv = kcp[pb][base:base + 64, mt * 128:(mt + 1) * 128]
                            krv = kr[base:base + 32, tb + mt * 128: tb + (mt + 1) * 128]
                            last = (sub == 1 and hh == 1)
                            pss = psS2[:, hh * 1024 + sub * 512: hh * 1024 + sub * 512 + 512]
                            b.mm(pss, kcv, qcv, start=True, stop=False)
                            b.mm(pss, krv, qrv, start=False, stop=(kdiag < 0),
                                 sig=(last and kdiag < 0))
                            if kdiag >= 0:
                                b.mm(pss, ident[:, 0:128],
                                     mask_sb[:, kdiag * 512:(kdiag + 1) * 512],
                                     start=False, stop=True, sig=last)
                    t_sc = ("e_pe", b.cnt["e_pe"])
                    if rnd > 0:
                        emit_pv_round(rnd - 1)
                    b.wait("act", t_sc)
                    exp_tok = b.op("act", "activation", PT[rnd % 2][:, :], psS2[:, :],
                                   AF.Exp)
                b.wait("pe", exp_tok)
                emit_pv_round(nrnd - 1)
                b.wait("dve", ("e_pe", b.cnt["e_pe"]))
                for ls in range(4):
                    for hh in range(2):
                        b.op("dve", "reciprocal", rcp[:, hh:hh + 1],
                             ps_pv[hh][:, ls * 65 + 64: ls * 65 + 65])
                        t_o = b.op("dve", "tensor_scalar_mul",
                                   opair[:, hh * 64:hh * 64 + 64],
                                   ps_pv[hh][:, ls * 65: ls * 65 + 64],
                                   rcp[:, hh:hh + 1])
                    b.wait("pe", t_o)
                    b.op("pe", "transpose", ps_tr[:, ls * 128:(ls + 1) * 128],
                         opair[:, :], ident[:, 0:128], sig=True)
                    b.wait("dve", ("e_pe", b.cnt["e_pe"]))
                t_c = b.op("dve", "tensor_copy",
                           o_fm[:, hp * T + lb * 512: hp * T + (lb + 1) * 512],
                           ps_tr[:, 0:512])
                b.wait("pe", t_c)
            pfree[pb] = b.cnt["e_pe"]

        for dch in range(DC):
            wbuf = wcolB[dch % 2]
            b.wait("pool", ("e_pe", wfreeB[dch % 2]))
            t_w = b.dma("pool", "l1_wld", wbuf[:, :],
                        bass.AP(io["W_O1"], dch * 128,
                                [[1024, 128], [128 * 1024, DC], [1, 128]]))
            b.wait("pe", t_w)
            for s in range(4):
                ps = psS2[:, (s % 2) * 512:(s % 2) * 512 + 512]
                b.wait("pe", ("e_act", b.cnt["e_act"]))
                for c in range(DC):
                    b.mm(ps, wbuf[:, c * 128:(c + 1) * 128],
                         o_fm[:, c * T + s * 512: c * T + s * 512 + 512],
                         start=(c == 0), stop=(c == DC - 1), sig=(c == DC - 1))
                b.wait("act", ("e_pe", b.cnt["e_pe"]))
                b.wait("act", b.tok("l1_s1st"))
                t_s = b.op("act", "activation", s1b[:, :], ps, AF.Identity)
                b.wait("sp", t_s)
                b.dma("sp", "l1_s1st",
                      io["s1_fm"][dch * 128:(dch + 1) * 128, tb + s * 512: tb + s * 512 + 512],
                      s1b[:, :])
            wfreeB[dch % 2] = b.cnt["e_pe"]
        b.flush()
    b.barrier()
    esB.close()
    es.close()


def xpos_tables_np(n=T, dh=64, window=512):
    half = np.arange(0, dh, 2, dtype=np.float64)
    inv_freq = 1.0 / 10000.0 ** (half / dh)
    t = np.arange(n, dtype=np.float64)
    freqs = t[:, None] * inv_freq[None, :]
    freqs = np.concatenate([freqs, freqs], -1)
    base = (half + 0.4 * dh) / (1.4 * dh)
    power = (t - n // 2) / float(window // 2)
    sc = base[None, :] ** power[:, None]
    sc = np.concatenate([sc, sc], -1)
    return np.cos(freqs), np.sin(freqs), sc


def emit_l1ffn(b, io):
    nc = b.nc
    es = contextlib.ExitStack()
    cos64, sin64, sc64 = xpos_tables_np()
    rows = np.arange(128) % 64
    C1 = (cos64[:, rows] * sc64[:, rows]).T.astype(np.float32)
    S1 = (sin64[:, rows] * sc64[:, rows]).T.astype(np.float32)
    C2 = (cos64[:, rows] / sc64[:, rows]).T.astype(np.float32)
    S2 = (sin64[:, rows] / sc64[:, rows]).T.astype(np.float32)
    R2 = np.zeros((128, 128), np.float32)
    for blk in range(2):
        o = blk * 64
        for m in range(32):
            R2[o + m + 32, o + m] = -1.0
            R2[o + m, o + m + 32] = 1.0
    md = np.where(np.arange(128)[None, :] > np.arange(128)[:, None], -1e30, 0.0)
    mf = np.where(np.arange(128)[None, :] < np.arange(128)[:, None], -1e30, 0.0)
    mk = np.concatenate([md, mf], axis=1).astype(NPBF)
    E2 = np.zeros((128, 2), NPBF); E2[0:64, 0] = 1; E2[64:128, 1] = 1
    E128 = np.zeros((2, 128), NPBF); E128[0, 0:64] = 1; E128[1, 64:128] = 1
    idp = np.zeros((128, 192), dtype=NPBF)
    idp[:, 0:128] = np.eye(128, dtype=NPBF)
    idp[0:64, 128:192] = np.eye(64, dtype=NPBF)
    idp[64:128, 128:192] = np.eye(64, dtype=NPBF)

    xf = es.enter_context(nc.sbuf_tensor("f_xf", [128, DC * T], BF16))
    o_fm = es.enter_context(nc.sbuf_tensor("f_ofm", [128, 4 * T], BF16))
    c1s = es.enter_context(nc.sbuf_tensor("f_c1", [128, T], F32))
    s1s = es.enter_context(nc.sbuf_tensor("f_s1", [128, T], F32))
    c2s = es.enter_context(nc.sbuf_tensor("f_c2", [128, T], F32))
    s2s = es.enter_context(nc.sbuf_tensor("f_s2", [128, T], F32))
    r2s = es.enter_context(nc.sbuf_tensor("f_r2", [128, 128], BF16))
    mks = es.enter_context(nc.sbuf_tensor("f_mks", [128, 256], BF16))
    e2s = es.enter_context(nc.sbuf_tensor("f_e2", [128, 2], BF16))
    e128s = es.enter_context(nc.sbuf_tensor("f_e128", [2, 128], BF16))
    ids = es.enter_context(nc.sbuf_tensor("f_id", [128, 192], BF16))
    ones1 = es.enter_context(nc.sbuf_tensor("f_o1", [1, 128], BF16))
    onesc = es.enter_context(nc.sbuf_tensor("f_oc", [128, 1], BF16))
    qs8 = es.enter_context(nc.sbuf_tensor("f_qs8", [128, 4], F32))
    lgb = es.enter_context(nc.sbuf_tensor("f_lgb", [128, 4 * DC], F32))
    b2s = es.enter_context(nc.sbuf_tensor("f_b2s", [128, 2 * DC], F32))
    b1s = es.enter_context(nc.sbuf_tensor("f_b1s", [128, 4 * 22], F32))

    for st in ("f_ld", "f_xb", "f_wld", "f_g1st"):
        b.stream(st)
    for (sb, arr, nm) in ((c1s, C1, "f_C1"), (s1s, S1, "f_S1"), (c2s, C2, "f_C2"),
                          (s2s, S2, "f_S2"), (r2s, R2.astype(NPBF), "f_R2"),
                          (mks, mk, "f_mk"), (e2s, E2, "f_E2"), (e128s, E128, "f_E128"),
                          (ids, idp, "identp2"), (ones1, np.ones((1, 128), NPBF), "f_on1"),
                          (onesc, np.ones((128, 1), NPBF), "f_onc")):
        b.dma("sp", "f_ld", sb[:, :], b.const(nm, arr)[:, :])
    b.dma("sp", "f_ld", qs8[:, 0:2], io["f_qs_l"][:, :])
    b.dma("sp", "f_ld", qs8[:, 2:4], io["f_ks_l"][:, :])
    for l in range(2):
        b.dma("sp", "f_ld", lgb[:, (2 * l) * DC:(2 * l + 1) * DC], io["f_lng_l"][l])
        b.dma("sp", "f_ld", lgb[:, (2 * l + 1) * DC:(2 * l + 2) * DC], io["f_lnb_l"][l])
        b.dma("sp", "f_ld", b2s[:, l * DC:(l + 1) * DC], io["f_b2_l"][l])
        b.dma("sp", "f_ld", b1s[:, (2 * l) * 22:(2 * l + 1) * 22], io["f_b1a_l"][l])
        b.dma("sp", "f_ld", b1s[:, (2 * l + 1) * 22:(2 * l + 2) * 22], io["f_b1g_l"][l])
    b.wait("dve", b.tok("f_ld"))
    b.op("dve", "tensor_scalar_mul", qs8[:, 0:2], qs8[:, 0:2], 8.0)

    for bi in range(NB):
        tb = bi * T
        b.wait("sp", ("e_act", b.cnt["e_act"]))
        b.wait("sp", ("e_pe", b.cnt["e_pe"]))
        for c in range(DC):
            b.dma_tr("sp", "f_xb", xf[:, c * T:(c + 1) * T],
                     io["xn_tm"][tb: tb + T, c * 128:(c + 1) * 128])
        b.wait("pe", b.tok("f_xb"))
        b.wait("pe", b.tok("f_ld"))
        b.wait("dve", b.tok("f_xb"))
        b.wait("act", b.tok("f_xb"))

        for l in range(2):
            esL = contextlib.ExitStack()
            qkv = esL.enter_context(nc.sbuf_tensor(f"f_qkv_{bi}_{l}", [128, 12 * T], BF16))
            esL2 = contextlib.ExitStack()
            hup = esL2.enter_context(nc.sbuf_tensor(f"f_hup_{bi}_{l}", [128, 4 * T], BF16))
            wc = [esL2.enter_context(nc.sbuf_tensor(f"f_wc{i}_{bi}_{l}", [128, DC * 128], BF16))
                  for i in range(2)]
            psQ = [esL2.enter_context(nc.psum_tensor(f"f_psQ{i}_{bi}_{l}", [128, 512], F32))
                   for i in range(2)]
            b.scope_guard()

            b.wait("dve", ("e_pe", b.cnt["e_pe"]))
            for c in range(4):
                b.op("dve", "memset", hup[:, c * T: c * T + 1], 0.0, sig=False)
                b.op("dve", "tensor_copy", hup[:, c * T + 1:(c + 1) * T],
                     xf[:, (4 + c) * T: (4 + c) * T + T - 1])
            hv = lambda c, c0, cn: (xf[:, c * T + c0: c * T + c0 + cn] if c < 4
                                    else hup[:, (c - 4) * T + c0: (c - 4) * T + c0 + cn])
            wfree = [0, 0]
            b.wait("pe", ("e_dve", b.cnt["e_dve"]))
            for m in range(12):
                wb = wc[m % 2]
                b.wait("pool", ("e_pe", wfree[m % 2]))
                t_w = b.dma("pool", "f_wld", wb[:, :],
                            bass.AP(io["f_wqkv"], l * 1024 * 1536 + m * 128,
                                    [[1536, 128], [128 * 1536, DC], [1, 128]]))
                b.wait("pe", t_w)
                for s in range(4):
                    ps = psQ[s % 2]
                    b.wait("pe", ("e_act", b.cnt["e_act"]))
                    for c in range(DC):
                        b.mm(ps[:, :], wb[:, c * 128:(c + 1) * 128], hv(c, s * 512, 512),
                             start=(c == 0), stop=(c == DC - 1), sig=(c == DC - 1))
                    b.wait("act", ("e_pe", b.cnt["e_pe"]))
                    b.op("act", "activation", qkv[:, m * T + s * 512: m * T + s * 512 + 512],
                         ps[:, :], AF.Identity)
                wfree[m % 2] = b.cnt["e_pe"]
            if bi == 0 and l == 0 and "dbg_qkv" in io:
                b.wait("sp", ("e_act", b.cnt["e_act"]))
                b.dma("sp", "dbg", bass.AP(io["dbg_qkv"], 0, [[2048, 128], [128 * 2048, 12], [1, 2048]]),
                      qkv[:, :])
            b.flush()
            esL2.close()

            esH = contextlib.ExitStack()
            sq = esH.enter_context(nc.sbuf_tensor(f"f_sq_{bi}_{l}", [128, T], BF16))
            rn = esH.enter_context(nc.sbuf_tensor(f"f_rn_{bi}_{l}", [2, T], F32))
            rnb = esH.enter_context(nc.sbuf_tensor(f"f_rnb_{bi}_{l}", [2, T], BF16))
            qn = esH.enter_context(nc.sbuf_tensor(f"f_qn_{bi}_{l}", [128, T], BF16))
            q3 = esH.enter_context(nc.sbuf_tensor(f"f_q3_{bi}_{l}", [128, T], BF16))
            k3 = esH.enter_context(nc.sbuf_tensor(f"f_k3_{bi}_{l}", [128, T], BF16))
            t1b = esH.enter_context(nc.sbuf_tensor(f"f_t1b_{bi}_{l}", [128, T], BF16))
            ostash = esH.enter_context(nc.sbuf_tensor(f"f_ost_{bi}_{l}", [128, 1024], BF16))
            vTf = esH.enter_context(nc.sbuf_tensor(f"f_vTf_{bi}_{l}", [128, 2 * 1024], BF16))
            Pl = esH.enter_context(nc.sbuf_tensor(f"f_Pl_{bi}_{l}", [128, 640], BF16))
            PTl = esH.enter_context(nc.sbuf_tensor(f"f_PTl_{bi}_{l}", [128, 640], BF16))
            mxd = esH.enter_context(nc.sbuf_tensor(f"f_mxd_{bi}_{l}", [128, 4], F32))
            opair = esH.enter_context(nc.sbuf_tensor(f"f_opr_{bi}_{l}", [128, 128], BF16))
            psB = esH.enter_context(nc.psum_tensor(f"f_psB_{bi}_{l}", [128, 2048], F32))
            psS = psB
            psSc = esH.enter_context(nc.psum_tensor(f"f_psSc_{bi}_{l}", [128, 640], F32))
            psO = esH.enter_context(nc.psum_tensor(f"f_psO_{bi}_{l}", [128, 64], F32))
            psT = esH.enter_context(nc.psum_tensor(f"f_psT_{bi}_{l}", [128, 1024], BF16))

            for hp in range(4):
                for (qk_idx, cs, ss, dst) in ((0, c1s, s1s, q3), (1, c2s, s2s, k3)):
                    src = qkv[:, (qk_idx * 4 + hp) * T:(qk_idx * 4 + hp + 1) * T]
                    scal = qs8[:, l: l + 1] if qk_idx == 0 else qs8[:, 2 + l: 3 + l]
                    t_s = b.op("act", "activation", sq[:, :], src, AF.Square)
                    b.wait("pe", t_s)
                    b.wait("pe", ("e_act", b.cnt["e_act"]))
                    for s in range(4):
                        b.mm(psB[0:2, s * 512:(s + 1) * 512], e2s[:, :],
                             sq[:, s * 512:(s + 1) * 512], start=True, stop=True,
                             sig=(s == 3))
                    b.wait("act", ("e_pe", b.cnt["e_pe"]))
                    b.op("act", "activation", rn[:, :], psB[0:2, :], AF.Ln)
                    t_r = b.op("act", "activation", rn[:, :], rn[:, :], AF.Exp, scale=-0.5)
                    b.wait("dve", t_r)
                    t_rb = b.op("dve", "tensor_copy", rnb[:, :], rn[:, :])
                    b.wait("pe", t_rb)
                    b.wait("pe", ("e_dve", b.cnt["e_dve"]))
                    for s in range(4):
                        b.mm(psB[:, s * 512:(s + 1) * 512], e128s[:, :],
                             rnb[0:2, s * 512:(s + 1) * 512], start=True, stop=True,
                             sig=(s == 3))
                    b.wait("dve", ("e_pe", b.cnt["e_pe"]))
                    t_qn = b.op("dve", "scalar_tensor_tensor", qn[:, :], src, scal,
                                psB[:, :], OP.mult, OP.mult)
                    b.wait("pe", t_qn)
                    for s in range(4):
                        b.mm(psB[:, s * 512:(s + 1) * 512], r2s[:, :],
                             qn[:, s * 512:(s + 1) * 512], start=True, stop=True,
                             sig=(s == 3))
                    b.wait("dve", ("e_pe", b.cnt["e_pe"]))
                    b.op("dve", "tensor_mul", t1b[:, :], ss[:, :], psB[:, :])
                    b.op("dve", "tensor_mul", dst[:, :], qn[:, :], cs[:, :])
                    t_d = b.op("dve", "tensor_add", dst[:, :], dst[:, :], t1b[:, :])
                    b.wait("pe", t_d)
                    if bi == 0 and l == 0 and hp == 0 and f"dbg_{'q3' if qk_idx == 0 else 'k3'}" in io:
                        b.wait("sp", t_d)
                        b.dma("sp", "dbg", io[f"dbg_{'q3' if qk_idx == 0 else 'k3'}"][:, :],
                              dst[:, :])
                for hh in range(2):
                    b.wait("pe", ("e_dve", b.cnt["e_dve"]))
                    for mt in range(16):
                        b.op("pe", "transpose", psT[:, mt * 64:(mt + 1) * 64],
                             qkv[64 * hh: 64 * hh + 64,
                                 (8 + hp) * T + mt * 128: (8 + hp) * T + (mt + 1) * 128],
                             ids[64 * hh:64 * hh + 64, 128:192], sig=(mt == 15))
                    b.wait("dve", ("e_pe", b.cnt["e_pe"]))
                    b.op("dve", "tensor_copy", vTf[:, hh * 1024:(hh + 1) * 1024],
                         psT[:, 0:1024])
                b.wait("pe", ("e_dve", b.cnt["e_dve"]))
                for hh in range(2):
                    base = 64 * hh
                    for lt in range(16):
                        nw = min(lt, 4) + 1
                        w0 = max(0, lt - 4) * 128
                        wlen = nw * 128
                        nfull = wlen - 128
                        b.wait("pe", ("e_act", b.cnt["e_act"]))
                        b.wait("pe", ("e_dve", b.cnt["e_dve"]))
                        qv_ = q3[base:base + 64, lt * 128:(lt + 1) * 128]
                        if nw == 5:
                            b.mm(psSc[:, 512:640], qv_,
                                 k3[base:base + 64, lt * 128:(lt + 1) * 128],
                                 start=True, stop=False)
                            b.mm(psSc[:, 512:640], ids[:, 0:128], mks[:, 0:128],
                                 start=False, stop=True)
                            b.mm(psSc[:, 0:512], qv_,
                                 k3[base:base + 64, w0: w0 + 512],
                                 start=True, stop=False)
                            b.mm(psSc[:, 0:128], ids[:, 0:128], mks[:, 128:256],
                                 start=False, stop=True, sig=True)
                        else:
                            if nfull > 0:
                                b.mm(psSc[:, 0:nfull], qv_,
                                     k3[base:base + 64, w0: w0 + nfull],
                                     start=True, stop=False)
                            b.mm(psSc[:, nfull:wlen], qv_,
                                 k3[base:base + 64, lt * 128:(lt + 1) * 128],
                                 start=(nfull == 0), stop=False)
                            b.mm(psSc[:, nfull:wlen], ids[:, 0:128], mks[:, 0:128],
                                 start=False, stop=True, sig=True)
                        b.wait("dve", ("e_pe", b.cnt["e_pe"]))
                        b.op("dve", "reduce_max", mxd[:, 0:1], psSc[:, 0:wlen],
                             axis=AX.X, negate=True)
                        b.wait("act", ("e_dve", b.cnt["e_dve"]))
                        t_e = b.op("act", "activation", Pl[:, 0:wlen], psSc[:, 0:wlen],
                                   AF.Exp, bias=mxd[:, 0:1], accum_out=mxd[:, 1:2])
                        b.wait("pe", t_e)
                        b.wait("pe", ("e_dve", b.cnt["e_dve"]))
                        for mt in range(nw):
                            b.op("pe", "transpose", psT[:, mt * 128:(mt + 1) * 128],
                                 Pl[:, mt * 128:(mt + 1) * 128], ids[:, 0:128],
                                 sig=(mt == nw - 1))
                        b.wait("dve", ("e_pe", b.cnt["e_pe"]))
                        t_pt = b.op("dve", "tensor_copy", PTl[:, 0:wlen], psT[:, 0:wlen])
                        b.wait("pe", t_pt)
                        for mt in range(nw):
                            gm = max(0, lt - 4) + mt
                            b.mm(psO[:, :], PTl[:, mt * 128:(mt + 1) * 128],
                                 vTf[:, hh * 1024 + gm * 64: hh * 1024 + gm * 64 + 64],
                                 start=(mt == 0), stop=(mt == nw - 1), sig=(mt == nw - 1))
                        b.wait("dve", ("e_pe", b.cnt["e_pe"]))
                        b.op("dve", "reciprocal", mxd[:, 2:3], mxd[:, 1:2])
                        t_o = b.op("dve", "tensor_scalar_mul",
                                   ostash[:, lt * 64:(lt + 1) * 64] if hh == 0
                                   else opair[:, 64:128],
                                   psO[:, :], mxd[:, 2:3])
                        if hh == 1:
                            b.op("dve", "tensor_copy", opair[:, 0:64],
                                 ostash[:, lt * 64:(lt + 1) * 64])
                            t_op = ("e_dve", b.cnt["e_dve"])
                            b.wait("pe", t_op)
                            b.op("pe", "transpose", psT[:, 0:128], opair[:, :],
                                 ids[:, 0:128], sig=True)
                            b.wait("dve", ("e_pe", b.cnt["e_pe"]))
                            t_of = b.op("dve", "tensor_copy",
                                        o_fm[:, hp * T + lt * 128: hp * T + (lt + 1) * 128],
                                        psT[:, 0:128])
                            b.wait("pe", t_of)
            esH.close()
            b.flush()

            esF = contextlib.ExitStack()
            wos = esF.enter_context(nc.sbuf_tensor(f"f_wos_{bi}_{l}", [128, 4 * 1024], BF16))
            psW = [esF.enter_context(nc.psum_tensor(f"f_psW{i}_{bi}_{l}", [128, 512], F32))
                   for i in range(2)]
            b.scope_guard()
            t_w = b.dma("pool", "f_wld", wos[:, :],
                        bass.AP(io["f_wo"], l * 512 * 1024,
                                [[1024, 128], [128 * 1024, 4], [1, 1024]]))
            b.wait("pe", t_w)
            b.wait("pe", ("e_dve", b.cnt["e_dve"]))
            for m in range(DC):
                for s in range(4):
                    ps = psW[s % 2]
                    b.wait("pe", ("e_dve", b.cnt["e_dve"]))
                    for c in range(4):
                        b.mm(ps[:, :], wos[:, c * 1024 + m * 128: c * 1024 + m * 128 + 128],
                             o_fm[:, c * T + s * 512: c * T + s * 512 + 512],
                             start=(c == 0), stop=(c == 3), sig=(c == 3))
                    b.wait("dve", ("e_pe", b.cnt["e_pe"]))
                    b.op("dve", "tensor_add", xf[:, m * T + s * 512: m * T + s * 512 + 512],
                         xf[:, m * T + s * 512: m * T + s * 512 + 512], ps[:, :])
            if bi == 0 and l == 0 and "dbg_xfwo" in io:
                b.wait("sp", ("e_dve", b.cnt["e_dve"]))
                b.dma("sp", "dbg", bass.AP(io["dbg_xfwo"], 0, [[2048, 128], [128 * 2048, 8], [1, 2048]]),
                      xf[:, :])
            if bi == 0 and l == 0 and "dbg_ofm" in io:
                b.wait("sp", ("e_dve", b.cnt["e_dve"]))
                b.dma("sp", "dbg", bass.AP(io["dbg_ofm"], 0, [[2048, 128], [128 * 2048, 4], [1, 2048]]),
                      o_fm[:, :])
            esF.close()
            b.flush()

            esM = contextlib.ExitStack()
            hup2 = esM.enter_context(nc.sbuf_tensor(f"f_hup2_{bi}_{l}", [128, 4 * T], BF16))
            sq2 = esM.enter_context(nc.sbuf_tensor(f"f_sq2_{bi}_{l}", [128, 512], BF16))
            stat = esM.enter_context(nc.sbuf_tensor(f"f_stat_{bi}_{l}", [1, 4 * 512], F32))
            statb = esM.enter_context(nc.sbuf_tensor(f"f_statb_{bi}_{l}", [1, 2 * 512], BF16))
            hn = esM.enter_context(nc.sbuf_tensor(f"f_hn_{bi}_{l}", [128, DC * 512], BF16))
            tmpa = esM.enter_context(nc.sbuf_tensor(f"f_tmpa_{bi}_{l}", [128, 512], BF16))
            hm = esM.enter_context(nc.sbuf_tensor(f"f_hm_{bi}_{l}", [128, 22 * 512], BF16))
            wc1 = [esM.enter_context(nc.sbuf_tensor(f"f_wc1{i}_{bi}_{l}", [128, DC * 128], BF16))
                   for i in range(2)]
            w2s = [esM.enter_context(nc.sbuf_tensor(f"f_w2s{i}_{bi}_{l}", [128, 22 * 128], BF16))
                   for i in range(2)]
            psL = esM.enter_context(nc.psum_tensor(f"f_psL_{bi}_{l}", [1, 512], F32))
            psL2 = esM.enter_context(nc.psum_tensor(f"f_psL2_{bi}_{l}", [1, 512], F32))
            psMa = esM.enter_context(nc.psum_tensor(f"f_psMa_{bi}_{l}", [128, 512], F32))
            psMb = esM.enter_context(nc.psum_tensor(f"f_psMb_{bi}_{l}", [128, 512], F32))
            psM1 = [esM.enter_context(nc.psum_tensor(f"f_psM1{i}_{bi}_{l}", [128, 512], F32))
                    for i in range(2)]
            psM2 = [esM.enter_context(nc.psum_tensor(f"f_psM2{i}_{bi}_{l}", [128, 512], F32))
                    for i in range(2)]
            b.scope_guard()

            b.wait("dve", ("e_pe", b.cnt["e_pe"]))
            for c in range(4):
                b.op("dve", "memset", hup2[:, c * T: c * T + 1], 0.0, sig=False)
                b.op("dve", "tensor_copy", hup2[:, c * T + 1:(c + 1) * T],
                     xf[:, (4 + c) * T: (4 + c) * T + T - 1])
            h2v = lambda c, c0, cn: (xf[:, c * T + c0: c * T + c0 + cn] if c < 4
                                     else hup2[:, (c - 4) * T + c0: (c - 4) * T + c0 + cn])
            b.wait("pe", ("e_dve", b.cnt["e_dve"]))
            wfree1 = [0, 0]
            wfree2 = [0, 0]
            for s in range(4):
                s0 = s * 512
                b.wait("pe", ("e_dve", b.cnt["e_dve"]))
                for c in range(DC):
                    b.mm(psL[:, :], onesc[:, :], h2v(c, s0, 512),
                         start=(c == 0), stop=(c == DC - 1), sig=(c == DC - 1))
                for c in range(DC):
                    b.wait("act", ("e_pe", b.cnt["e_pe"]) if c == 0 else None)
                    t_q = b.op("act", "activation", sq2[:, :], h2v(c, s0, 512), AF.Square)
                    b.wait("pe", t_q)
                    b.mm(psL2[:, :], onesc[:, :], sq2[:, :],
                         start=(c == 0), stop=(c == DC - 1), sig=True)
                    if c < DC - 1:
                        b.wait("act", ("e_pe", b.cnt["e_pe"]))
                b.wait("dve", ("e_pe", b.cnt["e_pe"]))
                b.op("dve", "tensor_scalar_mul", stat[:, 0:512], psL[:, :], 1.0 / D)
                b.op("dve", "tensor_scalar_mul", stat[:, 512:1024], psL2[:, :], 1.0 / D)
                b.op("dve", "tensor_mul", stat[:, 1024:1536], stat[:, 0:512], stat[:, 0:512])
                b.op("dve", "tensor_sub", stat[:, 1024:1536], stat[:, 512:1024],
                     stat[:, 1024:1536])
                t_v = b.op("dve", "tensor_scalar_add", stat[:, 1024:1536],
                           stat[:, 1024:1536], 1e-5)
                b.wait("act", t_v)
                b.op("act", "activation", stat[:, 1536:2048], stat[:, 1024:1536], AF.Ln)
                t_r = b.op("act", "activation", stat[:, 1536:2048], stat[:, 1536:2048],
                           AF.Exp, scale=-0.5)
                b.wait("dve", t_r)
                b.op("dve", "tensor_copy", statb[:, 0:512], stat[:, 0:512])
                t_sb = b.op("dve", "tensor_copy", statb[:, 512:1024], stat[:, 1536:2048])
                b.wait("pe", t_sb)
                b.mm(psMa[:, :], ones1[:, :], statb[0:1, 0:512], start=True, stop=True)
                b.mm(psMb[:, :], ones1[:, :], statb[0:1, 512:1024], start=True, stop=True,
                     sig=True)
                b.wait("dve", ("e_pe", b.cnt["e_pe"]))
                for c in range(DC):
                    b.op("dve", "tensor_sub", hn[:, c * 512:(c + 1) * 512],
                         h2v(c, s0, 512), psMa[:, :])
                    b.op("dve", "tensor_mul", hn[:, c * 512:(c + 1) * 512],
                         hn[:, c * 512:(c + 1) * 512], psMb[:, :])
                    b.op("dve", "tensor_scalar", hn[:, c * 512:(c + 1) * 512],
                         hn[:, c * 512:(c + 1) * 512],
                         lgb[:, (2 * l) * DC + c: (2 * l) * DC + c + 1],
                         lgb[:, (2 * l + 1) * DC + c: (2 * l + 1) * DC + c + 1],
                         OP.mult, OP.add)
                hn_free = b.cnt["e_dve"]
                for m in range(22):
                    mw = 128 if m < 21 else 42
                    wb = wc1[m % 2]
                    b.wait("pool", ("e_pe", wfree1[m % 2]))
                    t_w = b.dma("pool", "f_wld", wb[:, 0:DC * mw],
                                bass.AP(io["f_w1"], l * 1024 * 5460 + m * 128,
                                        [[5460, 128], [128 * 5460, DC], [1, mw]]))
                    b.wait("pe", t_w)
                    b.wait("pe", ("e_dve", hn_free))
                    ps = psM1[m % 2]
                    b.wait("pe", ("e_act", b.cnt["e_act"]))
                    for c in range(DC):
                        b.mm(ps[0:mw, :], wb[:, c * mw:(c + 1) * mw],
                             hn[:, c * 512:(c + 1) * 512],
                             start=(c == 0), stop=(c == DC - 1), sig=(c == DC - 1))
                    b.wait("act", ("e_pe", b.cnt["e_pe"]))
                    b.wait("act", ("e_dve", b.cnt["e_dve"]))
                    t_a = b.op("act", "activation", tmpa[0:mw, :], ps[0:mw, :], AF.Identity,
                               bias=b1s[0:mw, (2 * l) * 22 + m: (2 * l) * 22 + m + 1])
                    b.wait("pool", ("e_pe", b.cnt["e_pe"]))
                    t_wg = b.dma("pool", "f_wld", wb[:, 0:DC * mw],
                                 bass.AP(io["f_w1"], l * 1024 * 5460 + 2730 + m * 128,
                                         [[5460, 128], [128 * 5460, DC], [1, mw]]))
                    b.wait("pe", t_wg)
                    ps2 = psM2[m % 2]
                    b.wait("pe", ("e_dve", b.cnt["e_dve"]))
                    for c in range(DC):
                        b.mm(ps2[0:mw, :], wb[:, c * mw:(c + 1) * mw],
                             hn[:, c * 512:(c + 1) * 512],
                             start=(c == 0), stop=(c == DC - 1), sig=(c == DC - 1))
                    b.wait("act", ("e_pe", b.cnt["e_pe"]))
                    t_g = b.op("act", "activation", hm[0:mw, m * 512:(m + 1) * 512],
                               ps2[0:mw, :], AF.Gelu,
                               bias=b1s[0:mw, (2 * l + 1) * 22 + m: (2 * l + 1) * 22 + m + 1])
                    b.wait("dve", t_g)
                    b.op("dve", "tensor_mul", hm[0:mw, m * 512:(m + 1) * 512],
                         tmpa[0:mw, :], hm[0:mw, m * 512:(m + 1) * 512])
                    wfree1[m % 2] = b.cnt["e_pe"]
                hm_done = ("e_dve", b.cnt["e_dve"])
                for m2 in range(DC):
                    wb2 = w2s[m2 % 2]
                    b.wait("pool", ("e_pe", wfree2[m2 % 2]))
                    t_wa = b.dma("pool", "f_wld", wb2[:, 0:21 * 128],
                                 bass.AP(io["f_w2"], l * 2730 * 1024 + m2 * 128,
                                         [[1024, 128], [128 * 1024, 21], [1, 128]]))
                    t_wb = b.dma("pool", "f_wld", wb2[0:42, 21 * 128:22 * 128],
                                 bass.AP(io["f_w2"], l * 2730 * 1024 + 21 * 128 * 1024 + m2 * 128,
                                         [[1024, 42], [1, 128]]))
                    b.wait("pe", t_wb)
                    b.wait("pe", hm_done)
                    ps = psM1[m2 % 2]
                    b.wait("pe", ("e_dve", b.cnt["e_dve"]))
                    for c in range(22):
                        cw = 128 if c < 21 else 42
                        b.mm(ps[:, :], wb2[0:cw, c * 128:(c + 1) * 128],
                             hm[0:cw, c * 512:(c + 1) * 512],
                             start=(c == 0), stop=(c == 21), sig=(c == 21))
                    b.wait("dve", ("e_pe", b.cnt["e_pe"]))
                    b.op("dve", "scalar_tensor_tensor",
                         xf[:, m2 * T + s0: m2 * T + s0 + 512], ps[:, :],
                         b2s[:, l * DC + m2: l * DC + m2 + 1],
                         xf[:, m2 * T + s0: m2 * T + s0 + 512], OP.add, OP.add)
                    wfree2[m2 % 2] = b.cnt["e_pe"]
            if bi == 0 and l == 0 and "dbg_xfmlp" in io:
                b.wait("sp", ("e_dve", b.cnt["e_dve"]))
                b.dma("sp", "dbg", bass.AP(io["dbg_xfmlp"], 0, [[2048, 128], [128 * 2048, 8], [1, 2048]]),
                      xf[:, :])
                b.wait("dve", b.tok("dbg"))
            esM.close()
            esL.close()
            b.flush()

        if not hasattr(b, "_g1b"):
            b._g1b = es.enter_context(nc.sbuf_tensor("f_g1b", [128, T], BF16))
        g1b = b._g1b
        for c in range(DC):
            b.wait("act", b.tok("f_g1st"))
            b.wait("act", ("e_dve", b.cnt["e_dve"]))
            t_g = b.op("act", "activation", g1b[:, :], xf[:, c * T:(c + 1) * T], AF.Silu)
            b.wait("sp", t_g)
            b.dma("sp", "f_g1st", io["g1_fm"][c * 128:(c + 1) * 128, tb: tb + T], g1b[:, :])
        b.flush()
    b.barrier()
    es.close()


def emit_combine(b, io):
    nc = b.nc
    es = contextlib.ExitStack()
    x1t = es.enter_context(nc.sbuf_tensor("cb_x1", [128, D], F32))
    s1t = es.enter_context(nc.sbuf_tensor("cb_s1", [128, D], BF16))
    g1t = es.enter_context(nc.sbuf_tensor("cb_g1", [128, D], BF16))
    pt = es.enter_context(nc.sbuf_tensor("cb_p", [128, D], BF16))
    ot = es.enter_context(nc.sbuf_tensor("cb_o", [128, D], F32))
    b.stream("cb_ld")
    b.stream("cb_st")
    for i in range(TOK // 128):
        prev = b.cnt["e_dve"]
        b.wait("sp", ("e_dve", prev))
        b.wait("sp", b.tok("cb_st"))
        b.dma("sp", "cb_ld", x1t[:, :], io["x1"][i * 128:(i + 1) * 128, :])
        b.dma_tr("sp", "cb_ld", s1t[:, :], io["s1_fm"][:, i * 128:(i + 1) * 128])
        b.dma_tr("sp", "cb_ld", g1t[:, :], io["g1_fm"][:, i * 128:(i + 1) * 128])
        b.wait("dve", b.tok("cb_ld"))
        b.op("dve", "tensor_mul", pt[:, :], s1t[:, :], g1t[:, :])
        t_o = b.op("dve", "tensor_add", ot[:, :], x1t[:, :], pt[:, :])
        b.wait("sp", t_o)
        b.dma("sp", "cb_st", io["out"][i * 128:(i + 1) * 128, :], ot[:, :])
    b.flush()
    b.barrier()
    es.close()


from concourse.bass_utils import run_bass_kernel_spmd

N_CORES = 8


def _drams(nc, specs):
    io = {}
    for name, shape, dt, kind in specs:
        io[name] = nc.dram_tensor(name, shape, dt, kind=kind)
    return io


def _run(build_fn, specs, per_core_feeds):
    nc = bass.Bass("TRN2", target_bir_lowering=False)
    io = _drams(nc, specs)
    es = contextlib.ExitStack()
    bb = B(nc, es)
    build_fn(bb, io)
    bb.barrier()
    bb.flush()
    es.close()
    res = run_bass_kernel_spmd(nc, per_core_feeds, core_ids=list(range(N_CORES)))
    return res.results


def host_prep_all(inputs):
    w = host_prep_weights(inputs)
    w["moe_b1_l"] = np.ascontiguousarray(
        np.asarray(inputs["L0_ffn_b1"], np.float32).reshape(14, 16, 128)
        .transpose(2, 0, 1).reshape(128, 224))
    b1 = np.asarray(inputs["L1_ffn_b1"], np.float32)
    b1a = np.zeros((2, 128, 22), np.float32)
    b1g = np.zeros((2, 128, 22), np.float32)
    for l in range(2):
        for m in range(22):
            mw = 128 if m < 21 else 42
            b1a[l, :mw, m] = b1[l, m * 128: m * 128 + mw]
            b1g[l, :mw, m] = b1[l, 2730 + m * 128: 2730 + m * 128 + mw]
    w["f_b1a_l"] = np.ascontiguousarray(b1a)
    w["f_b1g_l"] = np.ascontiguousarray(b1g)
    w["moe_b2"] = np.ascontiguousarray(np.asarray(inputs["L0_ffn_b2"], np.float32))
    return w


def kernel(**inputs):
    w = host_prep_all(inputs)
    x_full = np.ascontiguousarray(np.asarray(inputs["L0_attn_x"], np.float32))
    shards = [np.ascontiguousarray(x_full[2 * i: 2 * i + 2].reshape(TOK, D))
              for i in range(N_CORES)]
    NPF32 = np.float32

    specs = [("x", [TOK, D], F32, "ExternalInput"),
             ("gate_wT", [1024, 14], F32, "ExternalInput"),
             ("xn_tm", [TOK, D], BF16, "ExternalOutput"),
             ("xnl_tm", [TOK, D], BF16, "ExternalOutput"),
             ("W_hbm", [TOK, 16], F32, "ExternalOutput")]
    feeds = [{"x": shards[i], "gate_wT": w["gate_wT"]} for i in range(N_CORES)]
    r1 = _run(lambda bb, io: emit_prep(bb, io), specs, feeds)

    specs = [("xn_tm", [TOK, D], BF16, "ExternalInput"),
             ("w_qkvT", [1024, 3072], F32, "ExternalInput"),
             ("b_qkv_l", [128, 24], F32, "ExternalInput"),
             ("w_oT", [1024, 1024], F32, "ExternalInput"),
             ("b_o_l", [128, 8], F32, "ExternalInput"),
             ("v_hbm", [1024, TOK], BF16, "Internal"),
             ("s0_fm", [1024, TOK], BF16, "ExternalOutput")]
    feeds = [{"xn_tm": r1[i]["xn_tm"], "w_qkvT": w["w_qkvT"], "b_qkv_l": w["b_qkv_l"],
              "w_oT": w["w_oT"], "b_o_l": w["b_o_l"]} for i in range(N_CORES)]
    r2 = _run(emit_l0attn, specs, feeds)

    specs = [("xn_tm", [TOK, D], BF16, "ExternalInput"),
             ("xnl_tm", [TOK, D], BF16, "ExternalInput"),
             ("W_hbm", [TOK, 16], F32, "ExternalInput"),
             ("moe_w1", [14, 1024, 2048], F32, "ExternalInput"),
             ("moe_b1_l", [128, 224], F32, "ExternalInput"),
             ("moe_w2", [14, 2048, 1024], F32, "ExternalInput"),
             ("moe_b2", [14, 1024], F32, "ExternalInput"),
             ("x", [TOK, D], F32, "ExternalInput"),
             ("s0_fm", [1024, TOK], BF16, "ExternalInput"),
             ("acc_hbm", [TOK, D], F32, "Internal"),
             ("x1", [TOK, D], F32, "ExternalOutput")]
    feeds = [{"xn_tm": r1[i]["xn_tm"], "xnl_tm": r1[i]["xnl_tm"], "W_hbm": r1[i]["W_hbm"],
              "moe_w1": w["moe_w1"], "moe_b1_l": w["moe_b1_l"], "moe_w2": w["moe_w2"],
              "moe_b2": w["moe_b2"], "x": shards[i], "s0_fm": r2[i]["s0_fm"]}
             for i in range(N_CORES)]
    r3 = _run(emit_moe, specs, feeds)

    specs = [("x", [TOK, D], F32, "ExternalInput"),
             ("xn_tm", [TOK, D], BF16, "ExternalOutput")]
    feeds = [{"x": r3[i]["x1"]} for i in range(N_CORES)]
    r4 = _run(lambda bb, io: emit_prep(bb, io, with_lo=False, with_gate=False),
              specs, feeds)

    specs = [("xn_tm", [TOK, D], BF16, "ExternalInput"),
             ("W_DQ", [1024, 4096], F32, "ExternalInput"),
             ("W_UQ", [4096, 1024], F32, "ExternalInput"),
             ("W_DKV", [1024, 256], F32, "ExternalInput"),
             ("W_UK", [256, 1024], F32, "ExternalInput"),
             ("W_UV", [256, 1024], F32, "ExternalInput"),
             ("W_QR_pad", [4096, 1024], F32, "ExternalInput"),
             ("W_KR", [1024, 32], F32, "ExternalInput"),
             ("W_O1", [1024, 1024], F32, "ExternalInput"),
             ("v_hbm", [1024, TOK], BF16, "Internal"),
             ("qc_hbm", [1024, TOK], BF16, "Internal"),
             ("qr_hbm", [1024, TOK], BF16, "Internal"),
             ("kc_hbm", [1024, TOK], BF16, "Internal"),
             ("s1_fm", [1024, TOK], BF16, "ExternalOutput")]
    feeds = [dict({"xn_tm": r4[i]["xn_tm"]},
                  **{k: w[k] for k in ("W_DQ", "W_UQ", "W_DKV", "W_UK", "W_UV",
                                       "W_QR_pad", "W_KR", "W_O1")})
             for i in range(N_CORES)]
    r5 = _run(emit_l1attn, specs, feeds)

    specs = [("xn_tm", [TOK, D], BF16, "ExternalInput"),
             ("f_wqkv", [2, 1024, 1536], F32, "ExternalInput"),
             ("f_qs_l", [128, 2], F32, "ExternalInput"),
             ("f_ks_l", [128, 2], F32, "ExternalInput"),
             ("f_wo", [2, 512, 1024], F32, "ExternalInput"),
             ("f_lng_l", [2, 128, 8], F32, "ExternalInput"),
             ("f_lnb_l", [2, 128, 8], F32, "ExternalInput"),
             ("f_w1", [2, 1024, 5460], F32, "ExternalInput"),
             ("f_b1a_l", [2, 128, 22], F32, "ExternalInput"),
             ("f_b1g_l", [2, 128, 22], F32, "ExternalInput"),
             ("f_w2", [2, 2730, 1024], F32, "ExternalInput"),
             ("f_b2_l", [2, 128, 8], F32, "ExternalInput"),
             ("g1_fm", [1024, TOK], BF16, "ExternalOutput")]
    feeds = [dict({"xn_tm": r4[i]["xn_tm"]},
                  **{k: w[k] for k in ("f_wqkv", "f_qs_l", "f_ks_l", "f_wo", "f_lng_l",
                                       "f_lnb_l", "f_w1", "f_b1a_l", "f_b1g_l",
                                       "f_w2", "f_b2_l")})
             for i in range(N_CORES)]
    r6 = _run(emit_l1ffn, specs, feeds)

    specs = [("x1", [TOK, D], F32, "ExternalInput"),
             ("s1_fm", [1024, TOK], BF16, "ExternalInput"),
             ("g1_fm", [1024, TOK], BF16, "ExternalInput"),
             ("out", [TOK, D], F32, "ExternalOutput")]
    feeds = [{"x1": r3[i]["x1"], "s1_fm": r5[i]["s1_fm"], "g1_fm": r6[i]["g1_fm"]}
             for i in range(N_CORES)]
    r7 = _run(emit_combine, specs, feeds)

    out = np.empty((16, 2048, 1024), np.float32)
    for i in range(N_CORES):
        out[2 * i: 2 * i + 2] = np.asarray(r7[i]["out"], np.float32).reshape(2, 2048, 1024)
    return out


def build_fused(nc):
    io = {}
    ext_in = [("x", [TOK, D], F32), ("gate_wT", [1024, 14], F32),
              ("w_qkvT", [1024, 3072], F32), ("b_qkv_l", [128, 24], F32),
              ("w_oT", [1024, 1024], F32), ("b_o_l", [128, 8], F32),
              ("moe_w1", [14, 1024, 2048], F32), ("moe_b1_l", [128, 224], F32),
              ("moe_w2", [14, 2048, 1024], F32), ("moe_b2", [14, 1024], F32),
              ("W_DQ", [1024, 4096], F32), ("W_UQ", [4096, 1024], F32),
              ("W_DKV", [1024, 256], F32), ("W_UK", [256, 1024], F32),
              ("W_UV", [256, 1024], F32), ("W_QR_pad", [4096, 1024], F32),
              ("W_KR", [1024, 32], F32), ("W_O1", [1024, 1024], F32),
              ("f_wqkv", [2, 1024, 1536], F32), ("f_qs_l", [128, 2], F32),
              ("f_ks_l", [128, 2], F32), ("f_wo", [2, 512, 1024], F32),
              ("f_lng_l", [2, 128, 8], F32), ("f_lnb_l", [2, 128, 8], F32),
              ("f_w1", [2, 1024, 5460], F32), ("f_b1a_l", [2, 128, 22], F32),
              ("f_b1g_l", [2, 128, 22], F32), ("f_w2", [2, 2730, 1024], F32),
              ("f_b2_l", [2, 128, 8], F32)]
    for n, sh, dt in ext_in:
        io[n] = nc.dram_tensor(n, sh, dt, kind="ExternalInput")
    internal = [("xn_tm", [TOK, D], BF16), ("xnl_tm", [TOK, D], BF16),
                ("W_hbm", [TOK, 16], F32), ("v_hbm", [1024, TOK], BF16),
                ("s0_fm", [1024, TOK], BF16), ("acc_hbm", [TOK, D], F32),
                ("x1", [TOK, D], F32), ("xn1_tm", [TOK, D], BF16),
                ("qc_hbm", [1024, TOK], BF16), ("qr_hbm", [1024, TOK], BF16),
                ("kc_hbm", [1024, TOK], BF16), ("s1_fm", [1024, TOK], BF16),
                ("g1_fm", [1024, TOK], BF16)]
    for n, sh, dt in internal:
        io[n] = nc.dram_tensor(n, sh, dt, kind="Internal")
    io["out"] = nc.dram_tensor("out", [TOK, D], F32, kind="ExternalOutput")

    es = contextlib.ExitStack()
    bb = B(nc, es)
    emit_prep(bb, io, pfx="p0")
    emit_l0attn(bb, io)
    emit_moe(bb, io)
    io_ln1 = dict(io)
    io_ln1["x"] = io["x1"]
    io_ln1["xn_tm"] = io["xn1_tm"]
    emit_prep(bb, io_ln1, with_lo=False, with_gate=False, pfx="p1")
    io_l1 = dict(io)
    io_l1["xn_tm"] = io["xn1_tm"]
    emit_l1attn(bb, io_l1)
    emit_l1ffn(bb, io_l1)
    emit_combine(bb, io)
    bb.barrier()
    bb.flush()
    es.close()
    return io


def kernel_fused(**inputs):
    w = host_prep_all(inputs)
    x_full = np.ascontiguousarray(np.asarray(inputs["L0_attn_x"], np.float32))
    shards = [np.ascontiguousarray(x_full[2 * i: 2 * i + 2].reshape(TOK, D))
              for i in range(N_CORES)]
    nc = bass.Bass("TRN2", target_bir_lowering=False)
    build_fused(nc)
    feeds = []
    wk = ("gate_wT", "w_qkvT", "b_qkv_l", "w_oT", "b_o_l", "moe_w1", "moe_b1_l",
          "moe_w2", "moe_b2", "W_DQ", "W_UQ", "W_DKV", "W_UK", "W_UV", "W_QR_pad",
          "W_KR", "W_O1", "f_wqkv", "f_qs_l", "f_ks_l", "f_wo", "f_lng_l", "f_lnb_l",
          "f_w1", "f_b1a_l", "f_b1g_l", "f_w2", "f_b2_l")
    for i in range(N_CORES):
        f = {k: w[k] for k in wk}
        f["x"] = shards[i]
        feeds.append(f)
    res = run_bass_kernel_spmd(nc, feeds, core_ids=list(range(N_CORES)))
    out = np.empty((16, 2048, 1024), np.float32)
    for i in range(N_CORES):
        out[2 * i: 2 * i + 2] = np.asarray(res.results[i]["out"], np.float32).reshape(2, 2048, 1024)
    return out


kernel = kernel_fused


# version 04
# speedup vs baseline: 1.3294x; 1.1572x over previous
import numpy as np
import ml_dtypes
import contextlib
import concourse.bass as bass
import concourse.mybir as mybir
from concourse.alu_op_type import AluOpType as OP

F32 = mybir.dt.float32
BF16 = mybir.dt.bfloat16
AF = mybir.ActivationFunctionType
AX = mybir.AxisListType
NPBF = ml_dtypes.bfloat16

TOK = 4096
NB = 2
T = 2048
D = 1024
DC = D // 128

ENGS = ("sp", "pool", "pe", "act", "dve")


class B:

    def __init__(self, nc, es):
        self.nc = nc
        self.es = es
        self.block = es.enter_context(nc.Block())
        self.sems = {}
        self.cnt = {}
        for e in ENGS:
            self._mksem("e_" + e)
        self.ops = {e: [] for e in ENGS}
        self.lastw = {}
        self.consts = {}

    def _mksem(self, name):
        self.sems[name] = self.es.enter_context(self.nc.semaphore(name))
        self.cnt[name] = 0

    def stream(self, name):
        if name not in self.sems:
            self._mksem(name)
        return name

    def raw(self, eng, fn):
        self.ops[eng].append(fn)

    def wait(self, eng, tok):
        if tok is None:
            return
        s, v = tok
        if v <= 0 or self.lastw.get((eng, s), 0) >= v:
            return
        self.lastw[(eng, s)] = v
        sem = self.sems[s]
        self.raw(eng, lambda e: e.wait_ge(sem, v))

    def waits(self, eng, toks):
        for t in toks:
            self.wait(eng, t)

    def dma(self, eng, stream, out, in_, **kw):
        s = self.stream(stream)
        self.cnt[s] += 16
        sem = self.sems[s]
        self.raw(eng, lambda e: e.dma_start(out, in_, **kw).then_inc(sem, 16))
        return (s, self.cnt[s])

    def dma_tr(self, eng, stream, out, in_):
        s = self.stream(stream)
        self.cnt[s] += 16
        sem = self.sems[s]
        self.raw(eng, lambda e: e.dma_start_transpose(out, in_).then_inc(sem, 16))
        return (s, self.cnt[s])

    def tok(self, stream):
        return (stream, self.cnt[stream])

    def op(self, eng, meth, *a, sig=True, **kw):
        if eng in ("dve", "act"):
            self.wait(eng, ("e_" + eng, self.cnt["e_" + eng]))
            sig = True
        if sig:
            s = "e_" + eng
            self.cnt[s] += 1
            v = self.cnt[s]
            sem = self.sems[s]
            self.raw(eng, lambda e: getattr(e, meth)(*a, **kw).then_inc(sem, 1))
            return (s, v)
        self.raw(eng, lambda e: getattr(e, meth)(*a, **kw))
        return None

    def mm(self, out, lhsT, rhs, start, stop, sig=False, tile_position=None,
           skip_group_check=False):
        kw = {}
        if tile_position is not None:
            kw["tile_position"] = tile_position
        if skip_group_check:
            kw["skip_group_check"] = True
        return self.op("pe", "matmul", out, lhsT, rhs, start=start, stop=stop,
                       sig=sig, **kw)

    def flush(self):
        for eng in ENGS:
            lst = self.ops[eng]
            if not lst:
                continue
            meth = {"sp": "sync", "pool": "gpsimd", "pe": "tensor",
                    "act": "scalar", "dve": "vector"}[eng]

            def runner(e, lst=lst):
                for f in lst:
                    f(e)
            getattr(self.block, meth)(runner)
            self.ops[eng] = []

    def scope_guard(self):
        for q in ("sp", "pool"):
            for e in ("e_pe", "e_act", "e_dve"):
                self.wait(q, (e, self.cnt[e]))

    def barrier(self):
        toks = [(s, self.cnt[s]) for s in self.sems if self.cnt[s] > 0]
        for eng in ENGS:
            for t in toks:
                self.wait(eng, t)

    def const(self, name, arr):
        if name not in self.consts:
            self.consts[name] = self.nc.inline_tensor(np.ascontiguousarray(arr), name)
        return self.consts[name]



def host_prep_weights(inp):
    f32 = lambda a: np.ascontiguousarray(np.asarray(a, dtype=np.float32))
    w = {}
    w["w_qkvT"] = f32(np.asarray(inp["L0_attn_w_qkv"]).T)
    w["b_qkv_l"] = f32(np.asarray(inp["L0_attn_b_qkv"]).reshape(24, 128).T)
    w["w_oT"] = f32(np.asarray(inp["L0_attn_w_o"]).T)
    w["b_o_l"] = f32(np.asarray(inp["L0_attn_b_o"]).reshape(8, 128).T)
    w["gate_wT"] = f32(np.asarray(inp["L0_ffn_gate_w"]).T)
    w["moe_w1"] = f32(inp["L0_ffn_w1"])
    w["moe_b1_r"] = f32(np.asarray(inp["L0_ffn_b1"]).reshape(1, 14 * 2048))
    w["moe_w2"] = f32(inp["L0_ffn_w2"])
    w["moe_b2_r"] = f32(np.asarray(inp["L0_ffn_b2"]).reshape(14, 1024))
    w["W_DQ"] = f32(inp["L1_attn_W_DQ"])
    w["W_UQ"] = f32(inp["L1_attn_W_UQ"])
    w["W_DKV"] = f32(inp["L1_attn_W_DKV"])
    w["W_UK"] = f32(inp["L1_attn_W_UK"])
    w["W_UV"] = f32(inp["L1_attn_W_UV"])
    wqr = np.asarray(inp["L1_attn_W_QR"], dtype=np.float32)
    wqr_pad = np.zeros((4096, 1024), dtype=np.float32)
    for h in range(16):
        wqr_pad[:, 64 * h:64 * h + 32] = wqr[:, 32 * h:32 * h + 32]
    w["W_QR_pad"] = np.ascontiguousarray(wqr_pad)
    w["W_KR"] = f32(inp["L1_attn_W_KR"])
    w["W_O1"] = f32(inp["L1_attn_W_O"])
    w["f_wqkv"] = f32(inp["L1_ffn_wqkv"])
    w["f_qs_l"] = f32(np.tile(np.asarray(inp["L1_ffn_q_scale"], np.float32), (1, 2)).T)
    w["f_ks_l"] = f32(np.tile(np.asarray(inp["L1_ffn_k_scale"], np.float32), (1, 2)).T)
    w["f_wo"] = f32(inp["L1_ffn_wo"])
    w["f_lng_l"] = f32(np.asarray(inp["L1_ffn_ln_g"], np.float32).reshape(2, 8, 128).transpose(0, 2, 1))
    w["f_lnb_l"] = f32(np.asarray(inp["L1_ffn_ln_b"], np.float32).reshape(2, 8, 128).transpose(0, 2, 1))
    w["f_w1"] = f32(inp["L1_ffn_w1"])
    w["f_b1_r"] = f32(np.asarray(inp["L1_ffn_b1"], np.float32).reshape(2, 1, 5460))
    w["f_w2"] = f32(inp["L1_ffn_w2"])
    w["f_b2_l"] = f32(np.asarray(inp["L1_ffn_b2"], np.float32).reshape(2, 8, 128).transpose(0, 2, 1))
    return w



def fm_hbm_ap(hbm, ncols=TOK):
    return hbm.rearrange("(c p) t -> p (c t)", p=128)


def emit_xbar_fm(b, dst_fm, hbm_tm, rows, stream="xb", eng="sp", col0=0, ncols=TOK):
    toks = []
    for c in range(rows // 128):
        toks.append(b.dma_tr(eng, stream,
                             dst_fm[:, c * ncols:(c + 1) * ncols],
                             hbm_tm[col0:col0 + ncols, c * 128:(c + 1) * 128]))
    return toks


def emit_prep(b, io, with_lo=True, with_gate=True, pfx="pr"):
    nc = b.nc
    es = contextlib.ExitStack()
    xt = [es.enter_context(nc.sbuf_tensor(f"{pfx}_x{i}", [128, D], F32)) for i in range(2)]
    xnh = es.enter_context(nc.sbuf_tensor(f"{pfx}_xnh", [128, D], BF16))
    xnh32 = es.enter_context(nc.sbuf_tensor(f"{pfx}_xnh32", [128, D], F32))
    xn32 = es.enter_context(nc.sbuf_tensor(f"{pfx}_xn32", [128, D], F32))
    xnl = es.enter_context(nc.sbuf_tensor(f"{pfx}_xnl", [128, D], BF16))
    sq = es.enter_context(nc.sbuf_tensor(f"{pfx}_sq", [128, D], F32))
    st = es.enter_context(nc.sbuf_tensor(f"{pfx}_st", [128, 8], F32))
    ld = b.stream(f"{pfx}_ld")
    stb = b.stream(f"{pfx}_st")

    for i in range(TOK // 128):
        buf = xt[i % 2]
        prev_dve = b.cnt["e_dve"]
        if i >= 2:
            b.wait("sp", ("e_dve", prev_dve))
        t_ld = b.dma("sp", ld, buf[:, :], io["x"][i * 128:(i + 1) * 128, :])
        b.wait("act", t_ld)
        b.wait("act", ("e_dve", prev_dve))
        b.op("act", "activation", sq[:, :], buf[:, :], AF.Square)
        b.wait("dve", ("e_act", b.cnt["e_act"]))
        b.wait("dve", b.tok(f"{pfx}_st") if i > 0 else None)
        b.op("dve", "reduce_sum", st[:, 0:1], buf[:, :], axis=AX.X)
        b.op("dve", "reduce_sum", st[:, 1:2], sq[:, :], axis=AX.X)
        b.op("dve", "tensor_scalar_mul", st[:, 0:1], st[:, 0:1], 1.0 / D)
        b.op("dve", "tensor_scalar_mul", st[:, 1:2], st[:, 1:2], 1.0 / D)
        b.op("dve", "tensor_mul", st[:, 2:3], st[:, 0:1], st[:, 0:1])
        b.op("dve", "tensor_sub", st[:, 2:3], st[:, 1:2], st[:, 2:3])
        t_v = b.op("dve", "tensor_scalar_add", st[:, 3:4], st[:, 2:3], 1e-5)
        b.wait("act", t_v)
        b.op("act", "activation", st[:, 3:4], st[:, 3:4], AF.Ln)
        t_r = b.op("act", "activation", st[:, 3:4], st[:, 3:4], AF.Exp, scale=-0.5)
        b.wait("dve", t_r)
        if with_lo:
            b.op("dve", "tensor_scalar", xn32[:, :], buf[:, :], st[:, 0:1], st[:, 3:4],
                 OP.subtract, OP.mult)
            b.op("dve", "tensor_copy", xnh[:, :], xn32[:, :])
            b.op("dve", "tensor_copy", xnh32[:, :], xnh[:, :])
            b.op("dve", "tensor_sub", xnh32[:, :], xn32[:, :], xnh32[:, :])
            t_l = b.op("dve", "tensor_copy", xnl[:, :], xnh32[:, :])
            b.wait("sp", t_l)
            b.dma("sp", stb, io["xn_tm"][i * 128:(i + 1) * 128, :], xnh[:, :])
            b.dma("sp", stb, io["xnl_tm"][i * 128:(i + 1) * 128, :], xnl[:, :])
        else:
            t_l = b.op("dve", "tensor_scalar", xnh[:, :], buf[:, :], st[:, 0:1],
                       st[:, 3:4], OP.subtract, OP.mult)
            b.wait("sp", t_l)
            b.dma("sp", stb, io["xn_tm"][i * 128:(i + 1) * 128, :], xnh[:, :])
    b.flush()
    b.barrier()
    if not with_gate:
        es.close()
        return

    xn_fm = es.enter_context(nc.sbuf_tensor(f"{pfx}_xnfm", [128, DC * TOK], BF16))
    xnl_fm = es.enter_context(nc.sbuf_tensor(f"{pfx}_xnlfm", [128, DC * TOK], BF16))
    gw32 = es.enter_context(nc.sbuf_tensor(f"{pfx}_gw32", [128, DC * 14], F32))
    gh = es.enter_context(nc.sbuf_tensor(f"{pfx}_gh", [128, DC * 14], BF16))
    gl32 = es.enter_context(nc.sbuf_tensor(f"{pfx}_gl32", [128, DC * 14], F32))
    gl = es.enter_context(nc.sbuf_tensor(f"{pfx}_gl", [128, DC * 14], BF16))
    Wt = es.enter_context(nc.sbuf_tensor(f"{pfx}_W", [128, 16], F32))
    gsc = es.enter_context(nc.sbuf_tensor(f"{pfx}_gsc", [128, 64], F32))
    gps = es.enter_context(nc.psum_tensor(f"{pfx}_gps", [128, 16], F32))

    emit_xbar_fm(b, xn_fm, io["xn_tm"], D, stream=f"{pfx}_xb")
    emit_xbar_fm(b, xnl_fm, io["xnl_tm"], D, stream=f"{pfx}_xb")
    t_g = b.dma("sp", f"{pfx}_gw", gw32[:, :],
                bass.AP(io["gate_wT"], 0, [[14, 128], [14 * 128, DC], [1, 14]]))
    b.wait("dve", t_g)
    b.op("dve", "tensor_copy", gh[:, :], gw32[:, :])
    b.op("dve", "tensor_copy", gl32[:, :], gh[:, :])
    b.op("dve", "tensor_sub", gl32[:, :], gw32[:, :], gl32[:, :])
    t_gl = b.op("dve", "tensor_copy", gl[:, :], gl32[:, :])
    b.wait("pe", t_gl)
    b.wait("pe", b.tok(f"{pfx}_xb"))

    b.stream(f"{pfx}_Wst")
    gps_free_at = 0
    for i in range(TOK // 128):
        b.wait("pe", ("e_dve", gps_free_at))
        first = True
        t_mm = None
        for c in range(DC):
            xh_v = xn_fm[:, c * TOK + i * 128: c * TOK + (i + 1) * 128]
            xl_v = xnl_fm[:, c * TOK + i * 128: c * TOK + (i + 1) * 128]
            b.mm(gps[:, 0:14], xh_v, gh[:, c * 14:(c + 1) * 14], start=first, stop=False)
            b.mm(gps[:, 0:14], xh_v, gl[:, c * 14:(c + 1) * 14], start=False, stop=False)
            t_mm = b.mm(gps[:, 0:14], xl_v, gh[:, c * 14:(c + 1) * 14], start=False,
                        stop=(c == DC - 1), sig=(c == DC - 1))
            first = False
        b.wait("dve", t_mm)
        b.wait("dve", b.tok(f"{pfx}_Wst"))
        c0, c1, c2, c3 = gsc[:, 0:1], gsc[:, 1:2], gsc[:, 2:3], gsc[:, 3:4]
        m1m = gsc[:, 16:30]
        m2m = gsc[:, 32:46]
        l2 = gsc[:, 48:62]
        b.op("dve", "reduce_max", c0, gps[:, 0:14], axis=AX.X)
        b.op("dve", "tensor_scalar", m1m, gps[:, 0:14], c0, None, OP.is_equal)
        b.op("dve", "scalar_tensor_tensor", l2, m1m, -1e30, gps[:, 0:14], OP.mult, OP.add)
        gps_free_at = b.cnt["e_dve"]
        b.op("dve", "reduce_max", c1, l2, axis=AX.X)
        b.op("dve", "tensor_scalar", m2m, l2, c1, None, OP.is_equal)
        b.op("dve", "tensor_sub", c2, c0, c1)
        t_d = b.op("dve", "tensor_copy", c3, c2)
        b.wait("act", t_d)
        t_s = b.op("act", "activation", c2, c2, AF.Sigmoid)
        b.wait("dve", t_s)
        b.op("dve", "tensor_scalar", c3, c2, -1.0, 1.0, OP.mult, OP.add)
        b.op("dve", "tensor_scalar_mul", m1m, m1m, c2)
        b.op("dve", "tensor_scalar_mul", m2m, m2m, c3)
        if i == 0:
            b.op("dve", "memset", Wt[:, 14:16], 0.0, sig=False)
        t_w = b.op("dve", "tensor_add", Wt[:, 0:14], m1m, m2m)
        b.wait("sp", t_w)
        b.dma("sp", f"{pfx}_Wst", io["W_hbm"][i * 128:(i + 1) * 128, :], Wt[:, :])
    b.flush()
    b.barrier()
    es.close()


def emit_l0attn(b, io):
    nc = b.nc
    es = contextlib.ExitStack()
    H, C = 16, 64
    xn_fm = es.enter_context(nc.sbuf_tensor("a0_xnfm", [128, DC * TOK], BF16))
    qk = es.enter_context(nc.sbuf_tensor("a0_qk", [128, 16 * T], BF16))
    wcol = [es.enter_context(nc.sbuf_tensor(f"a0_w{i}", [128, DC * 128], BF16))
            for i in range(2)]
    bq = es.enter_context(nc.sbuf_tensor("a0_bq", [128, 24], F32))
    bo = es.enter_context(nc.sbuf_tensor("a0_bo", [128, 8], F32))
    vstg = [es.enter_context(nc.sbuf_tensor(f"a0_vs{i}", [128, 512], BF16))
            for i in range(2)]
    vpair = [es.enter_context(nc.sbuf_tensor(f"a0_vp{i}", [128, T], BF16))
             for i in range(2)]
    VSL = 68
    vT = es.enter_context(nc.sbuf_tensor("a0_vT", [128, 2 * 16 * VSL], BF16))
    PT = [es.enter_context(nc.sbuf_tensor(f"a0_PT{i}", [128, 2048], BF16))
          for i in range(2)]
    o_fm = es.enter_context(nc.sbuf_tensor("a0_ofm", [128, DC * T], BF16))
    opair = es.enter_context(nc.sbuf_tensor("a0_opair", [128, 128], BF16))
    rcp = es.enter_context(nc.sbuf_tensor("a0_rcp", [128, 2], F32))
    s0b = es.enter_context(nc.sbuf_tensor("a0_s0", [128, 512], BF16))
    ident = es.enter_context(nc.sbuf_tensor("a0_id", [128, 192], BF16))

    psS2 = es.enter_context(nc.psum_tensor("a0_psS2", [128, 2048], F32))
    ps_pv = [es.enter_context(nc.psum_tensor(f"a0_pspv{i}", [128, 260], F32))
             for i in range(2)]
    ps_tr = es.enter_context(nc.psum_tensor("a0_pstr", [128, 1024], BF16))

    for st in ("a0_ld", "a0_xb", "a0_wld", "a0_vst", "a0_vld", "a0_s0st"):
        b.stream(st)
    idp = np.zeros((128, 192), dtype=NPBF)
    idp[:, 0:128] = np.eye(128, dtype=NPBF)
    idp[0:64, 128:192] = np.eye(64, dtype=NPBF)
    idp[64:128, 128:192] = np.eye(64, dtype=NPBF)
    id_hbm = b.const("identp", idp)
    t_id = b.dma("sp", "a0_ld", ident[:, :], id_hbm[:, :])
    b.wait("pe", t_id)

    emit_xbar_fm(b, xn_fm, io["xn_tm"], D, stream="a0_xb")
    b.dma("sp", "a0_ld", bq[:, :], io["b_qkv_l"][:, :])
    b.dma("sp", "a0_ld", bo[:, :], io["b_o_l"][:, :])
    b.wait("pe", b.tok("a0_xb"))

    wfree = [0, 0]
    for bi in range(NB):
        tb = bi * T
        for m in range(24):
            wbuf = wcol[m % 2]
            b.wait("pool", ("e_pe", wfree[m % 2]))
            t_w = b.dma("pool", "a0_wld", wbuf[:, :],
                        bass.AP(io["w_qkvT"], m * 128,
                                [[3072, 128], [128 * 3072, DC], [1, 128]]))
            b.wait("pe", t_w)
            for s in range(4):
                ps = psS2[:, (s % 2) * 512:(s % 2) * 512 + 512]
                b.wait("pe", ("e_act", b.cnt["e_act"]))
                for c in range(DC):
                    b.mm(ps, wbuf[:, c * 128:(c + 1) * 128],
                         xn_fm[:, c * TOK + tb + s * 512: c * TOK + tb + s * 512 + 512],
                         start=(c == 0), stop=(c == DC - 1), sig=(c == DC - 1))
                b.wait("act", ("e_pe", b.cnt["e_pe"]))
                if m < 16:
                    dst = qk[:, m * T + s * 512: m * T + s * 512 + 512]
                    b.op("act", "activation", dst, ps, AF.Identity, bias=bq[:, m:m + 1])
                else:
                    vbuf = vstg[s % 2]
                    b.wait("act", b.tok("a0_vst"))
                    t_e = b.op("act", "activation", vbuf[:, :], ps, AF.Identity,
                               bias=bq[:, m:m + 1])
                    b.wait("sp", t_e)
                    b.dma("sp", "a0_vst",
                          io["v_hbm"][(m - 16) * 128:(m - 15) * 128, tb + s * 512: tb + s * 512 + 512],
                          vbuf[:, :])
            wfree[m % 2] = b.cnt["e_pe"]
        b.flush()

        b.wait("sp", b.tok("a0_vst"))
        vpfree = [0, 0]
        for hp in range(8):
            h0, h1 = 2 * hp, 2 * hp + 1
            vbuf = vpair[hp % 2]
            b.wait("sp", ("e_pe", vpfree[hp % 2]))
            t_v = b.dma("sp", "a0_vld", vbuf[:, :],
                        io["v_hbm"][hp * 128:(hp + 1) * 128, tb: tb + T])
            b.wait("dve", ("e_pe", b.cnt["e_pe"]))
            b.op("dve", "memset", vT[:, :], 1.0)
            b.wait("pe", t_v)
            b.wait("pe", ("e_dve", b.cnt["e_dve"]))
            for hh in range(2):
                b.wait("pe", ("e_dve", b.cnt["e_dve"]))
                for mt in range(16):
                    b.op("pe", "transpose",
                         ps_tr[:, mt * 64:(mt + 1) * 64],
                         vbuf[hh * 64:(hh + 1) * 64, mt * 128:(mt + 1) * 128],
                         ident[hh * 64:(hh + 1) * 64, 128:192], sig=(mt == 15))
                b.wait("dve", ("e_pe", b.cnt["e_pe"]))
                b.op("dve", "tensor_copy",
                     bass.AP(vT, hh * 16 * VSL, [[2 * 16 * VSL, 128], [VSL, 16], [1, 64]]),
                     ps_tr[:, 0:1024])
            vpfree[hp % 2] = b.cnt["e_pe"]
            b.wait("pe", ("e_dve", b.cnt["e_dve"]))

            exp_tok = None
            for lb in range(4):
                pvfree = b.cnt["e_dve"]

                def emit_pv_round(rnd):
                    for sub in range(2):
                        mt = rnd * 2 + sub
                        for hh in range(2):
                            for ls in range(4):
                                b.mm(ps_pv[hh][:, ls * 65:ls * 65 + 65],
                                     PT[rnd % 2][:, hh * 1024 + sub * 512 + ls * 128:
                                                 hh * 1024 + sub * 512 + ls * 128 + 128],
                                     vT[:, hh * 16 * VSL + mt * VSL: hh * 16 * VSL + mt * VSL + 65],
                                     start=(rnd == 0 and sub == 0 and ls == 0),
                                     stop=(rnd == 7 and sub == 1),
                                     sig=(sub == 1 and hh == 1 and ls == 3))

                for rnd in range(8):
                    b.wait("pe", exp_tok)
                    if rnd == 0:
                        b.wait("pe", ("e_dve", pvfree))
                    for sub in range(2):
                        mt = rnd * 2 + sub
                        for hh, h in ((0, h0), (1, h1)):
                            qv = qk[64 * (h % 2):64 * (h % 2) + 64,
                                    (h // 2) * T + lb * 512:(h // 2) * T + lb * 512 + 512]
                            kv = qk[64 * (h % 2):64 * (h % 2) + 64,
                                    (8 + h // 2) * T + mt * 128:(8 + h // 2) * T + mt * 128 + 128]
                            b.mm(psS2[:, hh * 1024 + sub * 512: hh * 1024 + sub * 512 + 512],
                                 kv, qv, start=True, stop=True,
                                 sig=(sub == 1 and hh == 1))
                    t_sc = ("e_pe", b.cnt["e_pe"])
                    if rnd > 0:
                        emit_pv_round(rnd - 1)
                    b.wait("act", t_sc)
                    exp_tok = b.op("act", "activation", PT[rnd % 2][:, :], psS2[:, :],
                                   AF.Exp, scale=1.0 / 32.0)
                b.wait("pe", exp_tok)
                emit_pv_round(7)
                b.wait("dve", ("e_pe", b.cnt["e_pe"]))
                for ls in range(4):
                    for hh in range(2):
                        b.op("dve", "reciprocal", rcp[:, hh:hh + 1],
                             ps_pv[hh][:, ls * 65 + 64: ls * 65 + 65])
                        t_o = b.op("dve", "tensor_scalar_mul",
                                   opair[:, hh * 64:hh * 64 + 64],
                                   ps_pv[hh][:, ls * 65: ls * 65 + 64],
                                   rcp[:, hh:hh + 1])
                    b.wait("pe", t_o)
                    b.op("pe", "transpose", ps_tr[:, ls * 128:(ls + 1) * 128],
                         opair[:, :], ident[:, 0:128], sig=True)
                    b.wait("dve", ("e_pe", b.cnt["e_pe"]))
                t_c = b.op("dve", "tensor_copy", o_fm[:, hp * T + lb * 512: hp * T + (lb + 1) * 512],
                           ps_tr[:, 0:512])
                b.wait("pe", t_c)
        b.flush()

        for dch in range(DC):
            wbuf = wcol[dch % 2]
            b.wait("pool", ("e_pe", wfree[dch % 2]))
            t_w = b.dma("pool", "a0_wld", wbuf[:, :],
                        bass.AP(io["w_oT"], dch * 128,
                                [[1024, 128], [128 * 1024, DC], [1, 128]]))
            b.wait("pe", t_w)
            for s in range(4):
                ps = psS2[:, (s % 2) * 512:(s % 2) * 512 + 512]
                b.wait("pe", ("e_dve", b.cnt["e_dve"]))
                for c in range(DC):
                    b.mm(ps, wbuf[:, c * 128:(c + 1) * 128],
                         o_fm[:, c * T + s * 512: c * T + s * 512 + 512],
                         start=(c == 0), stop=(c == DC - 1), sig=(c == DC - 1))
                b.wait("dve", ("e_pe", b.cnt["e_pe"]))
                b.wait("dve", b.tok("a0_s0st"))
                t_s = b.op("dve", "scalar_tensor_tensor", s0b[:, :], ps,
                           bo[:, dch:dch + 1],
                           xn_fm[:, dch * TOK + tb + s * 512: dch * TOK + tb + s * 512 + 512],
                           OP.add, OP.add)
                b.wait("sp", t_s)
                b.dma("sp", "a0_s0st",
                      io["s0_fm"][dch * 128:(dch + 1) * 128, tb + s * 512: tb + s * 512 + 512],
                      s0b[:, :])
            wfree[dch % 2] = b.cnt["e_pe"]
        b.flush()
    b.barrier()
    es.close()


def emit_moe(b, io):
    nc = b.nc
    es = contextlib.ExitStack()
    NE = 14
    SLAB = 512
    NSLAB = TOK // SLAB
    xn_fm = es.enter_context(nc.sbuf_tensor("mo_xnfm", [128, DC * TOK], BF16))
    Wtm = es.enter_context(nc.sbuf_tensor("mo_W", [128, 32 * 16], F32))
    Wbf = es.enter_context(nc.sbuf_tensor("mo_Wbf", [128, 16], BF16))
    Wfm = es.enter_context(nc.sbuf_tensor("mo_Wfm", [16, 32 * 128], BF16))
    b1l = es.enter_context(nc.sbuf_tensor("mo_b1", [128, 224], F32))
    b2h = es.enter_context(nc.sbuf_tensor("mo_b2h", [14, 1024], BF16))
    b2lo = es.enter_context(nc.sbuf_tensor("mo_b2l", [14, 1024], BF16))
    ident = es.enter_context(nc.sbuf_tensor("mo_id", [128, 128], BF16))
    es_w = contextlib.ExitStack()
    w1sb = [es_w.enter_context(nc.sbuf_tensor(f"mo_w1{i}", [128, DC * 2048], BF16))
            for i in range(2)]
    w2sb = es_w.enter_context(nc.sbuf_tensor("mo_w2", [128, 16 * 1024], BF16))
    Hs = es_w.enter_context(nc.sbuf_tensor("mo_H", [128, 16 * SLAB], BF16))
    es_b2 = contextlib.ExitStack()
    b2f = es_b2.enter_context(nc.sbuf_tensor("mo_b2f", [14, 1024], F32))
    b2l32 = es_b2.enter_context(nc.sbuf_tensor("mo_b2l32", [14, 1024], F32))
    psA = [es.enter_context(nc.psum_tensor(f"mo_psA{i}", [128, 512], F32)) for i in range(2)]
    psB = [es.enter_context(nc.psum_tensor(f"mo_psB{i}", [128, 1024], F32)) for i in range(2)]
    psT = es.enter_context(nc.psum_tensor("mo_psT", [16, 128], BF16))

    for st in ("mo_ld", "mo_xb", "mo_w1ld", "mo_w2ld", "mo_acc", "mo_fin", "mo_x1"):
        b.stream(st)
    id_hbm = b.const("ident128b", np.eye(128, dtype=NPBF))
    b.dma("sp", "mo_ld", ident[:, :], id_hbm[:, :])
    emit_xbar_fm(b, xn_fm, io["xn_tm"], D, stream="mo_xb")
    b.dma("sp", "mo_ld", Wtm[:, :],
          bass.AP(io["W_hbm"], 0, [[16, 128], [128 * 16, 32], [1, 16]]))
    b.dma("sp", "mo_ld", b1l[:, :], io["moe_b1_l"][:, :])
    t_b2 = b.dma("sp", "mo_ld", b2f[:, :], io["moe_b2"][:, :])
    b.wait("dve", t_b2)
    b.op("dve", "tensor_copy", b2h[:, :], b2f[:, :])
    b.op("dve", "tensor_copy", b2l32[:, :], b2h[:, :])
    b.op("dve", "tensor_sub", b2l32[:, :], b2f[:, :], b2l32[:, :])
    b.op("dve", "tensor_copy", b2lo[:, :], b2l32[:, :])
    es_b2.close()
    stage = es_w.enter_context(nc.sbuf_tensor("mo_stg", [128, 8 * 1024], BF16))
    b.wait("dve", b.tok("mo_ld"))
    for i in range(32):
        t_c = b.op("dve", "tensor_copy", Wbf[:, :], Wtm[:, i * 16:(i + 1) * 16])
        b.wait("pe", t_c)
        t_t = b.op("pe", "transpose", psT[:, :], Wbf[:, :], ident[:, :], sig=True)
        b.wait("dve", t_t)
        t_c2 = b.op("dve", "tensor_copy", Wfm[0:16, i * 128:(i + 1) * 128], psT[:, :])
        b.wait("pe", t_c2)
    b.flush()
    b.wait("pe", b.tok("mo_xb"))

    evacA = [0, 0]
    evacB = [0, 0]
    w1used = [0, 0]
    w2used = 0
    stg_tok = [0, 0]
    t_w1 = [None, None]
    b.wait("pool", ("e_dve", b.cnt["e_dve"]))
    t_w1[0] = b.dma("pool", "mo_w1ld", w1sb[0][:, :],
                    bass.AP(io["moe_w1"], 0, [[2048, 128], [128 * 2048, DC], [1, 2048]]))
    for e in range(NE):
        wsl = e % 2
        b.wait("pool", ("e_pe", b.cnt["e_pe"]))
        t_w2 = b.dma("pool", "mo_w2ld", w2sb[:, :],
                     bass.AP(io["moe_w2"], e * 2048 * 1024,
                             [[1024, 128], [128 * 1024, 16], [1, 1024]]))
        b.wait("pe", t_w1[wsl])
        w2_waited = False
        for sl in range(NSLAB):
            tb = sl * SLAB
            for m in range(16):
                ps = psA[m % 2]
                b.wait("pe", ("e_act", evacA[m % 2]))
                for c in range(DC):
                    b.mm(ps[:, :], w1sb[wsl][:, c * 2048 + m * 128: c * 2048 + m * 128 + 128],
                         xn_fm[:, c * TOK + tb: c * TOK + tb + SLAB],
                         start=(c == 0), stop=(c == DC - 1), sig=(c == DC - 1))
                b.wait("act", ("e_pe", b.cnt["e_pe"]))
                t_ev = b.op("act", "activation", Hs[:, m * SLAB:(m + 1) * SLAB], ps[:, :],
                            AF.Gelu, bias=b1l[:, e * 16 + m: e * 16 + m + 1])
                evacA[m % 2] = t_ev[1]
            if sl == NSLAB - 1:
                w1used[wsl] = b.cnt["e_pe"]
                if e + 1 < NE:
                    b.wait("pool", ("e_pe", w1used[(e + 1) % 2]))
                    t_w1[(e + 1) % 2] = b.dma(
                        "pool", "mo_w1ld", w1sb[(e + 1) % 2][:, :],
                        bass.AP(io["moe_w1"], (e + 1) * 1024 * 2048,
                                [[2048, 128], [128 * 2048, DC], [1, 2048]]))
            b.wait("pe", ("e_act", b.cnt["e_act"]))
            if not w2_waited:
                b.wait("pe", t_w2)
                w2_waited = True
            for tt in range(4):
                ps = psB[tt % 2]
                b.wait("pe", ("e_dve", evacB[tt % 2]))
                for c in range(16):
                    lhs = Hs[:, c * SLAB + tt * 128: c * SLAB + tt * 128 + 128]
                    b.mm(ps[:, 0:512], lhs, w2sb[:, c * 1024: c * 1024 + 512],
                         start=(c == 0), stop=(c == 15))
                    b.mm(ps[:, 512:1024], lhs, w2sb[:, c * 1024 + 512: c * 1024 + 1024],
                         start=(c == 0), stop=(c == 15), sig=(c == 15))
                b.wait("dve", ("e_pe", b.cnt["e_pe"]))
                b.wait("dve", ("mo_acc", stg_tok[(sl % 2)]))
                t_ev = b.op("dve", "tensor_scalar_mul",
                            stage[:, (sl % 2) * 4096 + tt * 1024:
                                  (sl % 2) * 4096 + (tt + 1) * 1024],
                            ps[:, :], Wtm[:, (sl * 4 + tt) * 16 + e:
                                          (sl * 4 + tt) * 16 + e + 1])
                evacB[tt % 2] = t_ev[1]
            t_st = ("e_dve", b.cnt["e_dve"])
            b.wait("pool", t_st)
            b.dma("pool", "mo_acc",
                  bass.AP(io["acc_hbm"], tb * D, [[D, 128], [128 * D, 4], [1, D]]),
                  stage[:, (sl % 2) * 4096: (sl % 2) * 4096 + 4096],
                  **({} if e == 0 else {"accum_op": OP.add}))
            stg_tok[sl % 2] = b.cnt["mo_acc"]
        b.flush()

    for i in range(32):
        ps = psB[i % 2]
        b.wait("pe", ("e_dve", b.cnt["e_dve"]))
        wv = Wfm[0:14, i * 128:(i + 1) * 128]
        b.mm(ps[:, 0:512], wv, b2h[0:14, 0:512], start=True, stop=False)
        b.mm(ps[:, 0:512], wv, b2lo[0:14, 0:512], start=False, stop=True)
        b.mm(ps[:, 512:1024], wv, b2h[0:14, 512:1024], start=True, stop=False)
        b.mm(ps[:, 512:1024], wv, b2lo[0:14, 512:1024], start=False, stop=True, sig=True)
        b.wait("dve", ("e_pe", b.cnt["e_pe"]))
        b.wait("dve", b.tok("mo_acc"))
        t_c = b.op("dve", "tensor_copy", stage[:, 0:1024], ps[:, :])
        b.wait("pool", t_c)
        b.dma("pool", "mo_acc", io["acc_hbm"][i * 128:(i + 1) * 128, :],
              stage[:, 0:1024], accum_op=OP.add)
    b.flush()

    b.flush()
    es_w.close()
    b.scope_guard()
    accT_t = es.enter_context(nc.sbuf_tensor("mo_fa", [128, D], F32))
    xT_t = es.enter_context(nc.sbuf_tensor("mo_fx", [128, D], F32))
    t1T_t = es.enter_context(nc.sbuf_tensor("mo_ft", [128, D], F32))
    x1T_t = es.enter_context(nc.sbuf_tensor("mo_fo", [128, D], F32))
    accT = accT_t[:, :]
    xT = xT_t[:, :]
    t1T = t1T_t[:, :]
    x1T = x1T_t[:, :]
    xhT = es.enter_context(nc.sbuf_tensor("mo_xh", [128, D], BF16))
    xlT = es.enter_context(nc.sbuf_tensor("mo_xl", [128, D], BF16))
    s0T = es.enter_context(nc.sbuf_tensor("mo_s0", [128, D], BF16))
    gT = es.enter_context(nc.sbuf_tensor("mo_g", [128, D], BF16))
    b.wait("sp", b.tok("mo_acc"))
    for i in range(32):
        prev_dve = b.cnt["e_dve"]
        b.wait("sp", ("e_dve", prev_dve))
        b.wait("sp", b.tok("mo_x1"))
        t_a = b.dma("sp", "mo_fin", accT, io["acc_hbm"][i * 128:(i + 1) * 128, :])
        t_h = b.dma("sp", "mo_fin", xhT[:, :], io["xn_tm"][i * 128:(i + 1) * 128, :])
        t_l = b.dma("sp", "mo_fin", xlT[:, :], io["xnl_tm"][i * 128:(i + 1) * 128, :])
        t_x = b.dma("sp", "mo_fin", xT, io["x"][i * 128:(i + 1) * 128, :])
        t_s = b.dma_tr("sp", "mo_fin", s0T[:, :],
                       io["s0_fm"][:, i * 128:(i + 1) * 128])
        b.wait("dve", b.tok("mo_fin"))
        b.op("dve", "tensor_add", t1T, xhT[:, :], xlT[:, :])
        b.op("dve", "tensor_add", t1T, t1T, accT)
        t_cl = b.op("dve", "tensor_scalar", t1T, t1T, -100.0, 100.0,
                    OP.max, OP.min)
        b.wait("act", t_cl)
        t_g = b.op("act", "activation", gT[:, :], t1T, AF.Silu)
        b.wait("dve", t_g)
        b.op("dve", "tensor_mul", gT[:, :], gT[:, :], s0T[:, :])
        t_x1 = b.op("dve", "tensor_add", x1T, xT, gT[:, :])
        b.wait("sp", t_x1)
        b.dma("sp", "mo_x1", io["x1"][i * 128:(i + 1) * 128, :], x1T)
    b.flush()
    b.barrier()
    es.close()


def rope_scale_np(Tn=T, dhr=32):
    t = np.arange(Tn, dtype=np.float64)
    inv = 1.0 / 10000.0 ** (np.arange(0, dhr, 2, dtype=np.float64) / dhr)
    freqs = t[:, None] * inv[None, :]
    emb = np.concatenate([freqs, freqs], -1)
    return (np.cos(emb) + np.sin(emb)).astype(np.float32)


def emit_l1attn(b, io):
    nc = b.nc
    es = contextlib.ExitStack()
    inv96 = 1.0 / np.sqrt(np.float32(96.0))
    rs = rope_scale_np()
    qsc = np.zeros((128, T), np.float32)
    ksc = np.zeros((128, T), np.float32)
    for p in range(128):
        qsc[p] = rs[:, p % 32] * inv96
    for p in list(range(32)) + list(range(64, 96)):
        ksc[p] = rs[:, p % 32]
    qsc_h = b.const("l1_qsc", qsc)
    ksc_h = b.const("l1_ksc", ksc)
    masks = np.zeros((128, 4 * 512), np.float32)
    ii = np.arange(128)[:, None]
    jj = np.arange(512)[None, :]
    for k in range(4):
        masks[:, k * 512:(k + 1) * 512] = np.where(ii + 128 * k > jj, -1e30, 0.0)
    mask_h = b.const("l1_masks", masks.astype(NPBF))
    idp = np.zeros((128, 192), dtype=NPBF)
    idp[:, 0:128] = np.eye(128, dtype=NPBF)
    idp[0:64, 128:192] = np.eye(64, dtype=NPBF)
    idp[64:128, 128:192] = np.eye(64, dtype=NPBF)
    id_hbm = b.const("identp", idp)

    xn_fm = es.enter_context(nc.sbuf_tensor("l1_xnfm", [128, DC * TOK], BF16))
    qsc_sb = es.enter_context(nc.sbuf_tensor("l1_qscs", [128, T], F32))
    ksc_sb = es.enter_context(nc.sbuf_tensor("l1_kscs", [128, T], F32))
    mask_sb = es.enter_context(nc.sbuf_tensor("l1_msk", [128, 4 * 512], BF16))
    ident = es.enter_context(nc.sbuf_tensor("l1_id", [128, 192], BF16))
    kr = es.enter_context(nc.sbuf_tensor("l1_kr", [128, TOK], BF16))

    for st in ("l1_ld", "l1_xb", "l1_wld", "l1_st", "l1_pld", "l1_s1st"):
        b.stream(st)
    b.dma("sp", "l1_ld", qsc_sb[:, :], qsc_h[:, :])
    b.dma("sp", "l1_ld", ksc_sb[:, :], ksc_h[:, :])
    b.dma("sp", "l1_ld", mask_sb[:, :], mask_h[:, :])
    b.dma("sp", "l1_ld", ident[:, :], id_hbm[:, :])
    emit_xbar_fm(b, xn_fm, io["xn_tm"], D, stream="l1_xb")
    b.wait("pe", b.tok("l1_xb"))
    b.wait("pe", b.tok("l1_ld"))

    esA = contextlib.ExitStack()
    wsmall = esA.enter_context(nc.sbuf_tensor("l1_wsm", [128, DC * 256], BF16))
    wuk = esA.enter_context(nc.sbuf_tensor("l1_wuk", [128, 2 * 1024], BF16))
    wuv = esA.enter_context(nc.sbuf_tensor("l1_wuv", [128, 2 * 1024], BF16))
    wkr = esA.enter_context(nc.sbuf_tensor("l1_wkr", [128, DC * 32], BF16))
    wkrd = esA.enter_context(nc.sbuf_tensor("l1_wkrd", [128, DC * 128], BF16))
    ckv = esA.enter_context(nc.sbuf_tensor("l1_ckv", [128, 2 * T], BF16))
    wcol = [esA.enter_context(nc.sbuf_tensor(f"l1_wc{i}", [128, 32 * 128], BF16))
            for i in range(2)]
    cq = esA.enter_context(nc.sbuf_tensor("l1_cq", [128, 32 * 1024], BF16))
    estg = [esA.enter_context(nc.sbuf_tensor(f"l1_es{i}", [128, 512], BF16))
            for i in range(2)]
    psA = [esA.enter_context(nc.psum_tensor(f"l1_psA{i}", [128, 512], F32))
           for i in range(2)]
    ps4 = esA.enter_context(nc.psum_tensor("l1_ps4", [128, 2048], F32))
    pskr = esA.enter_context(nc.psum_tensor("l1_pskr", [128, 512], F32))

    t_w = b.dma("pool", "l1_wld", wsmall[:, :],
                bass.AP(io["W_DKV"], 0, [[256, 128], [128 * 256, DC], [1, 256]]))
    b.dma("pool", "l1_wld", wuk[:, :],
          bass.AP(io["W_UK"], 0, [[1024, 128], [128 * 1024, 2], [1, 1024]]))
    b.dma("pool", "l1_wld", wuv[:, :],
          bass.AP(io["W_UV"], 0, [[1024, 128], [128 * 1024, 2], [1, 1024]]))
    b.dma("pool", "l1_wld", wkr[:, :],
          bass.AP(io["W_KR"], 0, [[32, 128], [128 * 32, DC], [1, 32]]))
    b.wait("dve", b.tok("l1_wld"))
    b.op("dve", "memset", wkrd[:, :], 0.0)
    for c in range(DC):
        b.op("dve", "tensor_copy", wkrd[:, c * 128: c * 128 + 32],
             wkr[:, c * 32:(c + 1) * 32])
        b.op("dve", "tensor_copy", wkrd[:, c * 128 + 64: c * 128 + 96],
             wkr[:, c * 32:(c + 1) * 32])
    b.wait("pe", b.tok("l1_wld"))
    b.wait("pe", ("e_dve", b.cnt["e_dve"]))

    for bi in range(NB):
        tb = bi * T
        for m in range(2):
            for s in range(4):
                ps = psA[s % 2]
                b.wait("pe", ("e_act", b.cnt["e_act"]))
                b.wait("pe", ("e_dve", b.cnt["e_dve"]))
                for c in range(DC):
                    b.mm(ps[:, :], wsmall[:, c * 256 + m * 128: c * 256 + m * 128 + 128],
                         xn_fm[:, c * TOK + tb + s * 512: c * TOK + tb + s * 512 + 512],
                         start=(c == 0), stop=(c == DC - 1), sig=(c == DC - 1))
                b.wait("act", ("e_pe", b.cnt["e_pe"]))
                b.op("act", "activation", ckv[:, m * T + s * 512: m * T + s * 512 + 512],
                     ps[:, :], AF.Identity)
        for (wt, dst) in ((wuk, io["kc_hbm"]), (wuv, io["v_hbm"])):
            for m in range(DC):
                for s in range(4):
                    ps = psA[s % 2]
                    b.wait("pe", ("e_act", b.cnt["e_act"]))
                    b.wait("pe", ("e_dve", b.cnt["e_dve"]))
                    for c in range(2):
                        b.mm(ps[:, :], wt[:, c * 1024 + m * 128: c * 1024 + m * 128 + 128],
                             ckv[:, c * T + s * 512: c * T + s * 512 + 512],
                             start=(c == 0), stop=(c == 1), sig=(c == 1))
                    b.wait("act", ("e_pe", b.cnt["e_pe"]))
                    stg = estg[s % 2]
                    b.wait("act", b.tok("l1_st"))
                    t_e = b.op("act", "activation", stg[:, :], ps[:, :], AF.Identity)
                    b.wait("sp", t_e)
                    b.dma("sp", "l1_st", dst[m * 128:(m + 1) * 128, tb + s * 512: tb + s * 512 + 512],
                          stg[:, :])
        for s in range(4):
            b.wait("pe", ("e_dve", b.cnt["e_dve"]))
            for c in range(DC):
                b.mm(pskr[:, :], wkrd[:, c * 128:(c + 1) * 128],
                     xn_fm[:, c * TOK + tb + s * 512: c * TOK + tb + s * 512 + 512],
                     start=(c == 0), stop=(c == DC - 1), sig=(c == DC - 1))
            b.wait("dve", ("e_pe", b.cnt["e_pe"]))
            b.op("dve", "tensor_mul", kr[:, tb + s * 512: tb + s * 512 + 512],
                 pskr[:, :], ksc_sb[:, s * 512: s * 512 + 512])
        wfree = [0, 0]
        for hs in range(2):
            ts0 = tb + hs * 1024
            for m in range(32):
                wbuf = wcol[m % 2]
                b.wait("pool", ("e_pe", wfree[m % 2]))
                t_w = b.dma("pool", "l1_wld", wbuf[:, 0:DC * 128],
                            bass.AP(io["W_DQ"], m * 128,
                                    [[4096, 128], [128 * 4096, DC], [1, 128]]))
                b.wait("pe", t_w)
                grp = m % 4
                ps = ps4[:, grp * 512:(grp + 1) * 512]
                if grp == 0:
                    b.wait("pe", ("e_act", b.cnt["e_act"]))
                for c in range(DC):
                    b.mm(ps, wbuf[:, c * 128:(c + 1) * 128],
                         xn_fm[:, c * TOK + ts0: c * TOK + ts0 + 512],
                         start=(c == 0), stop=(c == DC - 1),
                         sig=(c == DC - 1))
                wfree[m % 2] = b.cnt["e_pe"]
                if grp == 3:
                    b.wait("act", ("e_pe", b.cnt["e_pe"]))
                    b.op("act", "activation",
                         bass.AP(cq, (m - 3) * 1024, [[32 * 1024, 128], [1024, 4], [1, 512]]),
                         ps4[:, :], AF.Identity)
            for m in range(32):
                wbuf = wcol[m % 2]
                b.wait("pool", ("e_pe", wfree[m % 2]))
                t_w = b.dma("pool", "l1_wld", wbuf[:, 0:DC * 128],
                            bass.AP(io["W_DQ"], m * 128,
                                    [[4096, 128], [128 * 4096, DC], [1, 128]]))
                b.wait("pe", t_w)
                grp = m % 4
                ps = ps4[:, grp * 512:(grp + 1) * 512]
                if grp == 0:
                    b.wait("pe", ("e_act", b.cnt["e_act"]))
                for c in range(DC):
                    b.mm(ps, wbuf[:, c * 128:(c + 1) * 128],
                         xn_fm[:, c * TOK + ts0 + 512: c * TOK + ts0 + 1024],
                         start=(c == 0), stop=(c == DC - 1),
                         sig=(c == DC - 1))
                wfree[m % 2] = b.cnt["e_pe"]
                if grp == 3:
                    b.wait("act", ("e_pe", b.cnt["e_pe"]))
                    b.op("act", "activation",
                         bass.AP(cq, (m - 3) * 1024 + 512,
                                 [[32 * 1024, 128], [1024, 4], [1, 512]]),
                         ps4[:, :], AF.Identity)
            for (wh, dst_h, scale_mode) in ((io["W_UQ"], io["qc_hbm"], "act"),
                                            (io["W_QR_pad"], io["qr_hbm"], "dve")):
                for m in range(DC):
                    wbuf = wcol[m % 2]
                    b.wait("pool", ("e_pe", wfree[m % 2]))
                    t_w = b.dma("pool", "l1_wld", wbuf[:, :],
                                bass.AP(wh, m * 128,
                                        [[1024, 128], [128 * 1024, 32], [1, 128]]))
                    b.wait("pe", t_w)
                    for half in range(2):
                        ps = psA[half]
                        b.wait("pe", ("e_act", b.cnt["e_act"]))
                        b.wait("pe", ("e_dve", b.cnt["e_dve"]))
                        for c in range(32):
                            b.mm(ps[:, :], wbuf[:, c * 128:(c + 1) * 128],
                                 cq[:, c * 1024 + half * 512: c * 1024 + half * 512 + 512],
                                 start=(c == 0), stop=(c == 31), sig=(c == 31))
                        wfree[m % 2] = b.cnt["e_pe"]
                        stg = estg[half]
                        tloc = hs * 1024 + half * 512
                        if scale_mode == "act":
                            b.wait("act", ("e_pe", b.cnt["e_pe"]))
                            b.wait("act", b.tok("l1_st"))
                            t_e = b.op("act", "activation", stg[:, :], ps[:, :],
                                       AF.Identity, scale=float(inv96))
                        else:
                            b.wait("dve", ("e_pe", b.cnt["e_pe"]))
                            b.wait("dve", b.tok("l1_st"))
                            t_e = b.op("dve", "tensor_mul", stg[:, :], ps[:, :],
                                       qsc_sb[:, (tloc % T): (tloc % T) + 512])
                        b.wait("sp", t_e)
                        b.dma("sp", "l1_st",
                              dst_h[m * 128:(m + 1) * 128, tb + tloc: tb + tloc + 512],
                              stg[:, :])
    b.flush()
    esA.close()

    esB = contextlib.ExitStack()
    VSL = 68
    qcp = [esB.enter_context(nc.sbuf_tensor(f"l1_qc{i}", [128, T], BF16)) for i in range(2)]
    qrp = [esB.enter_context(nc.sbuf_tensor(f"l1_qr{i}", [128, T], BF16)) for i in range(2)]
    kcp = [esB.enter_context(nc.sbuf_tensor(f"l1_kc{i}", [128, T], BF16)) for i in range(2)]
    vp = [esB.enter_context(nc.sbuf_tensor(f"l1_vp{i}", [128, T], BF16)) for i in range(2)]
    vT = esB.enter_context(nc.sbuf_tensor("l1_vT", [128, 2 * 16 * VSL], BF16))
    PT = [esB.enter_context(nc.sbuf_tensor(f"l1_PT{i}", [128, 2048], BF16)) for i in range(2)]
    o_fm = esB.enter_context(nc.sbuf_tensor("l1_ofm", [128, DC * T], BF16))
    opair = esB.enter_context(nc.sbuf_tensor("l1_op", [128, 128], BF16))
    rcp = esB.enter_context(nc.sbuf_tensor("l1_rcp", [128, 2], F32))
    s1b = esB.enter_context(nc.sbuf_tensor("l1_s1b", [128, 512], BF16))
    wcolB = [esB.enter_context(nc.sbuf_tensor(f"l1_wB{i}", [128, DC * 128], BF16))
             for i in range(2)]
    psS2 = esB.enter_context(nc.psum_tensor("l1_psS2", [128, 2048], F32))
    ps_pv = [esB.enter_context(nc.psum_tensor(f"l1_pspv{i}", [128, 260], F32))
             for i in range(2)]
    ps_tr = esB.enter_context(nc.psum_tensor("l1_pstr", [128, 1024], BF16))

    b.wait("sp", b.tok("l1_st"))
    b.wait("pe", ("e_dve", b.cnt["e_dve"]))
    b.scope_guard()
    pfree = [0, 0]
    wfreeB = [0, 0]
    for bi in range(NB):
        tb = bi * T
        for hp in range(8):
            h0, h1 = 2 * hp, 2 * hp + 1
            pb = hp % 2
            b.wait("sp", ("e_pe", pfree[pb]))
            t0_ = b.dma("sp", "l1_pld", qcp[pb][:, :], io["qc_hbm"][hp * 128:(hp + 1) * 128, tb: tb + T])
            b.dma("sp", "l1_pld", qrp[pb][:, :], io["qr_hbm"][hp * 128:(hp + 1) * 128, tb: tb + T])
            b.dma("sp", "l1_pld", kcp[pb][:, :], io["kc_hbm"][hp * 128:(hp + 1) * 128, tb: tb + T])
            t_v = b.dma("sp", "l1_pld", vp[pb][:, :], io["v_hbm"][hp * 128:(hp + 1) * 128, tb: tb + T])
            b.wait("dve", ("e_pe", b.cnt["e_pe"]))
            b.op("dve", "memset", vT[:, :], 1.0)
            b.wait("pe", t_v)
            for hh in range(2):
                b.wait("pe", ("e_dve", b.cnt["e_dve"]))
                for mt in range(16):
                    b.op("pe", "transpose",
                         ps_tr[:, mt * 64:(mt + 1) * 64],
                         vp[pb][hh * 64:(hh + 1) * 64, mt * 128:(mt + 1) * 128],
                         ident[hh * 64:(hh + 1) * 64, 128:192], sig=(mt == 15))
                b.wait("dve", ("e_pe", b.cnt["e_pe"]))
                b.op("dve", "tensor_copy",
                     bass.AP(vT, hh * 16 * VSL, [[2 * 16 * VSL, 128], [VSL, 16], [1, 64]]),
                     ps_tr[:, 0:1024])
            b.wait("pe", ("e_dve", b.cnt["e_dve"]))

            exp_tok = None
            for lb in range(4):
                nmt = 4 * lb + 4
                nrnd = nmt // 2
                pvfree = b.cnt["e_dve"]

                def emit_pv_round(rnd):
                    for sub in range(2):
                        mt = rnd * 2 + sub
                        for hh in range(2):
                            for ls in range(4):
                                b.mm(ps_pv[hh][:, ls * 65: ls * 65 + 65],
                                     PT[rnd % 2][:, hh * 1024 + sub * 512 + ls * 128:
                                                 hh * 1024 + sub * 512 + ls * 128 + 128],
                                     vT[:, hh * 16 * VSL + mt * VSL: hh * 16 * VSL + mt * VSL + 65],
                                     start=(rnd == 0 and sub == 0 and ls == 0),
                                     stop=(rnd == nrnd - 1 and sub == 1),
                                     sig=(sub == 1 and hh == 1 and ls == 3))

                for rnd in range(nrnd):
                    b.wait("pe", exp_tok)
                    if rnd == 0:
                        b.wait("pe", ("e_dve", pvfree))
                    for sub in range(2):
                        mt = rnd * 2 + sub
                        kdiag = mt - 4 * lb
                        for hh, h in ((0, h0), (1, h1)):
                            base = 64 * (h % 2)
                            qcv = qcp[pb][base:base + 64, lb * 512:(lb + 1) * 512]
                            qrv = qrp[pb][base:base + 32, lb * 512:(lb + 1) * 512]
                            kcv = kcp[pb][base:base + 64, mt * 128:(mt + 1) * 128]
                            krv = kr[base:base + 32, tb + mt * 128: tb + (mt + 1) * 128]
                            last = (sub == 1 and hh == 1)
                            pss = psS2[:, hh * 1024 + sub * 512: hh * 1024 + sub * 512 + 512]
                            b.mm(pss, kcv, qcv, start=True, stop=False)
                            b.mm(pss, krv, qrv, start=False, stop=(kdiag < 0),
                                 sig=(last and kdiag < 0))
                            if kdiag >= 0:
                                b.mm(pss, ident[:, 0:128],
                                     mask_sb[:, kdiag * 512:(kdiag + 1) * 512],
                                     start=False, stop=True, sig=last)
                    t_sc = ("e_pe", b.cnt["e_pe"])
                    if rnd > 0:
                        emit_pv_round(rnd - 1)
                    b.wait("act", t_sc)
                    exp_tok = b.op("act", "activation", PT[rnd % 2][:, :], psS2[:, :],
                                   AF.Exp)
                b.wait("pe", exp_tok)
                emit_pv_round(nrnd - 1)
                b.wait("dve", ("e_pe", b.cnt["e_pe"]))
                for ls in range(4):
                    for hh in range(2):
                        b.op("dve", "reciprocal", rcp[:, hh:hh + 1],
                             ps_pv[hh][:, ls * 65 + 64: ls * 65 + 65])
                        t_o = b.op("dve", "tensor_scalar_mul",
                                   opair[:, hh * 64:hh * 64 + 64],
                                   ps_pv[hh][:, ls * 65: ls * 65 + 64],
                                   rcp[:, hh:hh + 1])
                    b.wait("pe", t_o)
                    b.op("pe", "transpose", ps_tr[:, ls * 128:(ls + 1) * 128],
                         opair[:, :], ident[:, 0:128], sig=True)
                    b.wait("dve", ("e_pe", b.cnt["e_pe"]))
                t_c = b.op("dve", "tensor_copy",
                           o_fm[:, hp * T + lb * 512: hp * T + (lb + 1) * 512],
                           ps_tr[:, 0:512])
                b.wait("pe", t_c)
            pfree[pb] = b.cnt["e_pe"]

        for dch in range(DC):
            wbuf = wcolB[dch % 2]
            b.wait("pool", ("e_pe", wfreeB[dch % 2]))
            t_w = b.dma("pool", "l1_wld", wbuf[:, :],
                        bass.AP(io["W_O1"], dch * 128,
                                [[1024, 128], [128 * 1024, DC], [1, 128]]))
            b.wait("pe", t_w)
            for s in range(4):
                ps = psS2[:, (s % 2) * 512:(s % 2) * 512 + 512]
                b.wait("pe", ("e_act", b.cnt["e_act"]))
                for c in range(DC):
                    b.mm(ps, wbuf[:, c * 128:(c + 1) * 128],
                         o_fm[:, c * T + s * 512: c * T + s * 512 + 512],
                         start=(c == 0), stop=(c == DC - 1), sig=(c == DC - 1))
                b.wait("act", ("e_pe", b.cnt["e_pe"]))
                b.wait("act", b.tok("l1_s1st"))
                t_s = b.op("act", "activation", s1b[:, :], ps, AF.Identity)
                b.wait("sp", t_s)
                b.dma("sp", "l1_s1st",
                      io["s1_fm"][dch * 128:(dch + 1) * 128, tb + s * 512: tb + s * 512 + 512],
                      s1b[:, :])
            wfreeB[dch % 2] = b.cnt["e_pe"]
        b.flush()
    b.barrier()
    esB.close()
    es.close()


def xpos_tables_np(n=T, dh=64, window=512):
    half = np.arange(0, dh, 2, dtype=np.float64)
    inv_freq = 1.0 / 10000.0 ** (half / dh)
    t = np.arange(n, dtype=np.float64)
    freqs = t[:, None] * inv_freq[None, :]
    freqs = np.concatenate([freqs, freqs], -1)
    base = (half + 0.4 * dh) / (1.4 * dh)
    power = (t - n // 2) / float(window // 2)
    sc = base[None, :] ** power[:, None]
    sc = np.concatenate([sc, sc], -1)
    return np.cos(freqs), np.sin(freqs), sc


def emit_l1ffn(b, io):
    nc = b.nc
    es = contextlib.ExitStack()
    cos64, sin64, sc64 = xpos_tables_np()
    rows = np.arange(128) % 64
    C1 = (cos64[:, rows] * sc64[:, rows]).T.astype(np.float32)
    S1 = (sin64[:, rows] * sc64[:, rows]).T.astype(np.float32)
    C2 = (cos64[:, rows] / sc64[:, rows]).T.astype(np.float32)
    S2 = (sin64[:, rows] / sc64[:, rows]).T.astype(np.float32)
    R2 = np.zeros((128, 128), np.float32)
    for blk in range(2):
        o = blk * 64
        for m in range(32):
            R2[o + m + 32, o + m] = -1.0
            R2[o + m, o + m + 32] = 1.0
    md = np.where(np.arange(128)[None, :] > np.arange(128)[:, None], -1e30, 0.0)
    mf = np.where(np.arange(128)[None, :] < np.arange(128)[:, None], -1e30, 0.0)
    mk = np.concatenate([md, mf], axis=1).astype(NPBF)
    E2 = np.zeros((128, 2), NPBF); E2[0:64, 0] = 1; E2[64:128, 1] = 1
    E128 = np.zeros((2, 128), NPBF); E128[0, 0:64] = 1; E128[1, 64:128] = 1
    idp = np.zeros((128, 192), dtype=NPBF)
    idp[:, 0:128] = np.eye(128, dtype=NPBF)
    idp[0:64, 128:192] = np.eye(64, dtype=NPBF)
    idp[64:128, 128:192] = np.eye(64, dtype=NPBF)

    xf = es.enter_context(nc.sbuf_tensor("f_xf", [128, DC * T], BF16))
    r2s = es.enter_context(nc.sbuf_tensor("f_r2", [128, 128], BF16))
    mks = es.enter_context(nc.sbuf_tensor("f_mks", [128, 256], BF16))
    e2s = es.enter_context(nc.sbuf_tensor("f_e2", [128, 2], BF16))
    e128s = es.enter_context(nc.sbuf_tensor("f_e128", [2, 128], BF16))
    ids = es.enter_context(nc.sbuf_tensor("f_id", [128, 192], BF16))
    ones1 = es.enter_context(nc.sbuf_tensor("f_o1", [1, 128], BF16))
    onesc = es.enter_context(nc.sbuf_tensor("f_oc", [128, 1], BF16))
    qs8 = es.enter_context(nc.sbuf_tensor("f_qs8", [128, 4], F32))
    lgb = es.enter_context(nc.sbuf_tensor("f_lgb", [128, 4 * DC], F32))
    b2s = es.enter_context(nc.sbuf_tensor("f_b2s", [128, 2 * DC], F32))
    b1s = es.enter_context(nc.sbuf_tensor("f_b1s", [128, 4 * 22], F32))

    for st in ("f_ld", "f_xb", "f_wld", "f_g1st"):
        b.stream(st)
    tbl_consts = ((C1, "f_C1"), (S1, "f_S1"), (C2, "f_C2"), (S2, "f_S2"))
    for arr, nm in tbl_consts:
        b.const(nm, arr)
    for (sb, arr, nm) in ((r2s, R2.astype(NPBF), "f_R2"),
                          (mks, mk, "f_mk"), (e2s, E2, "f_E2"), (e128s, E128, "f_E128"),
                          (ids, idp, "identp2"), (ones1, np.ones((1, 128), NPBF), "f_on1"),
                          (onesc, np.ones((128, 1), NPBF), "f_onc")):
        b.dma("sp", "f_ld", sb[:, :], b.const(nm, arr)[:, :])
    b.dma("sp", "f_ld", qs8[:, 0:2], io["f_qs_l"][:, :])
    b.dma("sp", "f_ld", qs8[:, 2:4], io["f_ks_l"][:, :])
    for l in range(2):
        b.dma("sp", "f_ld", lgb[:, (2 * l) * DC:(2 * l + 1) * DC], io["f_lng_l"][l])
        b.dma("sp", "f_ld", lgb[:, (2 * l + 1) * DC:(2 * l + 2) * DC], io["f_lnb_l"][l])
        b.dma("sp", "f_ld", b2s[:, l * DC:(l + 1) * DC], io["f_b2_l"][l])
        b.dma("sp", "f_ld", b1s[:, (2 * l) * 22:(2 * l + 1) * 22], io["f_b1a_l"][l])
        b.dma("sp", "f_ld", b1s[:, (2 * l + 1) * 22:(2 * l + 2) * 22], io["f_b1g_l"][l])
    b.wait("dve", b.tok("f_ld"))
    b.op("dve", "tensor_scalar_mul", qs8[:, 0:2], qs8[:, 0:2], 8.0)

    for bi in range(NB):
        tb = bi * T
        b.wait("sp", ("e_act", b.cnt["e_act"]))
        b.wait("sp", ("e_pe", b.cnt["e_pe"]))
        for c in range(DC):
            b.dma_tr("sp", "f_xb", xf[:, c * T:(c + 1) * T],
                     io["xn_tm"][tb: tb + T, c * 128:(c + 1) * 128])
        b.wait("pe", b.tok("f_xb"))
        b.wait("pe", b.tok("f_ld"))
        b.wait("dve", b.tok("f_xb"))
        b.wait("act", b.tok("f_xb"))

        for l in range(2):
            esL = contextlib.ExitStack()
            qkv = esL.enter_context(nc.sbuf_tensor(f"f_qkv_{bi}_{l}", [128, 12 * T], BF16))
            o_fm = esL.enter_context(nc.sbuf_tensor(f"f_ofm_{bi}_{l}", [128, 4 * T], BF16))
            c1s = esL.enter_context(nc.sbuf_tensor(f"f_c1_{bi}_{l}", [128, T], F32))
            s1s = esL.enter_context(nc.sbuf_tensor(f"f_s1_{bi}_{l}", [128, T], F32))
            c2s = esL.enter_context(nc.sbuf_tensor(f"f_c2_{bi}_{l}", [128, T], F32))
            s2s = esL.enter_context(nc.sbuf_tensor(f"f_s2_{bi}_{l}", [128, T], F32))
            esL2 = contextlib.ExitStack()
            hup = esL2.enter_context(nc.sbuf_tensor(f"f_hup_{bi}_{l}", [128, 4 * T], BF16))
            wc = [esL2.enter_context(nc.sbuf_tensor(f"f_wc{i}_{bi}_{l}", [128, DC * 128], BF16))
                  for i in range(2)]
            psQ = [esL2.enter_context(nc.psum_tensor(f"f_psQ{i}_{bi}_{l}", [128, 512], F32))
                   for i in range(2)]
            b.scope_guard()
            for (sb, (arr, nm)) in zip((c1s, s1s, c2s, s2s), tbl_consts):
                b.dma("sp", "f_ld", sb[:, :], b.const(nm, arr)[:, :])
            b.wait("dve", b.tok("f_ld"))

            b.wait("dve", ("e_pe", b.cnt["e_pe"]))
            for c in range(4):
                b.op("dve", "memset", hup[:, c * T: c * T + 1], 0.0, sig=False)
                b.op("dve", "tensor_copy", hup[:, c * T + 1:(c + 1) * T],
                     xf[:, (4 + c) * T: (4 + c) * T + T - 1])
            hv = lambda c, c0, cn: (xf[:, c * T + c0: c * T + c0 + cn] if c < 4
                                    else hup[:, (c - 4) * T + c0: (c - 4) * T + c0 + cn])
            wfree = [0, 0]
            b.wait("pe", ("e_dve", b.cnt["e_dve"]))
            for m in range(12):
                wb = wc[m % 2]
                b.wait("pool", ("e_pe", wfree[m % 2]))
                t_w = b.dma("pool", "f_wld", wb[:, :],
                            bass.AP(io["f_wqkv"], l * 1024 * 1536 + m * 128,
                                    [[1536, 128], [128 * 1536, DC], [1, 128]]))
                b.wait("pe", t_w)
                for s in range(4):
                    ps = psQ[s % 2]
                    b.wait("pe", ("e_act", b.cnt["e_act"]))
                    for c in range(DC):
                        b.mm(ps[:, :], wb[:, c * 128:(c + 1) * 128], hv(c, s * 512, 512),
                             start=(c == 0), stop=(c == DC - 1), sig=(c == DC - 1))
                    b.wait("act", ("e_pe", b.cnt["e_pe"]))
                    b.op("act", "activation", qkv[:, m * T + s * 512: m * T + s * 512 + 512],
                         ps[:, :], AF.Identity)
                wfree[m % 2] = b.cnt["e_pe"]
            if bi == 0 and l == 0 and "dbg_qkv" in io:
                b.wait("sp", ("e_act", b.cnt["e_act"]))
                b.dma("sp", "dbg", bass.AP(io["dbg_qkv"], 0, [[2048, 128], [128 * 2048, 12], [1, 2048]]),
                      qkv[:, :])
            b.flush()
            esL2.close()

            esH = contextlib.ExitStack()
            sq = esH.enter_context(nc.sbuf_tensor(f"f_sq_{bi}_{l}", [128, T], BF16))
            rn = esH.enter_context(nc.sbuf_tensor(f"f_rn_{bi}_{l}", [2, T], F32))
            rnb = esH.enter_context(nc.sbuf_tensor(f"f_rnb_{bi}_{l}", [2, T], BF16))
            qn = esH.enter_context(nc.sbuf_tensor(f"f_qn_{bi}_{l}", [128, T], BF16))
            q3 = esH.enter_context(nc.sbuf_tensor(f"f_q3_{bi}_{l}", [128, T], BF16))
            k3 = esH.enter_context(nc.sbuf_tensor(f"f_k3_{bi}_{l}", [128, T], BF16))
            t1b = esH.enter_context(nc.sbuf_tensor(f"f_t1b_{bi}_{l}", [128, T], BF16))
            ostash = esH.enter_context(nc.sbuf_tensor(f"f_ost_{bi}_{l}", [128, 1024], BF16))
            vTf = esH.enter_context(nc.sbuf_tensor(f"f_vTf_{bi}_{l}", [128, 2 * 1024], BF16))
            Pl = esH.enter_context(nc.sbuf_tensor(f"f_Pl_{bi}_{l}", [128, 640], BF16))
            PTl = esH.enter_context(nc.sbuf_tensor(f"f_PTl_{bi}_{l}", [128, 640], BF16))
            mxd = esH.enter_context(nc.sbuf_tensor(f"f_mxd_{bi}_{l}", [128, 4], F32))
            opair = esH.enter_context(nc.sbuf_tensor(f"f_opr_{bi}_{l}", [128, 128], BF16))
            psB = esH.enter_context(nc.psum_tensor(f"f_psB_{bi}_{l}", [128, 2048], F32))
            psS = psB
            psSc = esH.enter_context(nc.psum_tensor(f"f_psSc_{bi}_{l}", [128, 640], F32))
            psO = esH.enter_context(nc.psum_tensor(f"f_psO_{bi}_{l}", [128, 64], F32))
            psT = esH.enter_context(nc.psum_tensor(f"f_psT_{bi}_{l}", [128, 1024], BF16))

            for hp in range(4):
                for (qk_idx, cs, ss, dst) in ((0, c1s, s1s, q3), (1, c2s, s2s, k3)):
                    src = qkv[:, (qk_idx * 4 + hp) * T:(qk_idx * 4 + hp + 1) * T]
                    scal = qs8[:, l: l + 1] if qk_idx == 0 else qs8[:, 2 + l: 3 + l]
                    t_s = b.op("act", "activation", sq[:, :], src, AF.Square)
                    b.wait("pe", t_s)
                    b.wait("pe", ("e_act", b.cnt["e_act"]))
                    for s in range(4):
                        b.mm(psB[0:2, s * 512:(s + 1) * 512], e2s[:, :],
                             sq[:, s * 512:(s + 1) * 512], start=True, stop=True,
                             sig=(s == 3))
                    b.wait("act", ("e_pe", b.cnt["e_pe"]))
                    b.op("act", "activation", rn[:, :], psB[0:2, :], AF.Ln)
                    t_r = b.op("act", "activation", rn[:, :], rn[:, :], AF.Exp, scale=-0.5)
                    b.wait("dve", t_r)
                    t_rb = b.op("dve", "tensor_copy", rnb[:, :], rn[:, :])
                    b.wait("pe", t_rb)
                    b.wait("pe", ("e_dve", b.cnt["e_dve"]))
                    for s in range(4):
                        b.mm(psB[:, s * 512:(s + 1) * 512], e128s[:, :],
                             rnb[0:2, s * 512:(s + 1) * 512], start=True, stop=True,
                             sig=(s == 3))
                    b.wait("dve", ("e_pe", b.cnt["e_pe"]))
                    t_qn = b.op("dve", "scalar_tensor_tensor", qn[:, :], src, scal,
                                psB[:, :], OP.mult, OP.mult)
                    b.wait("pe", t_qn)
                    for s in range(4):
                        b.mm(psB[:, s * 512:(s + 1) * 512], r2s[:, :],
                             qn[:, s * 512:(s + 1) * 512], start=True, stop=True,
                             sig=(s == 3))
                    b.wait("dve", ("e_pe", b.cnt["e_pe"]))
                    b.op("dve", "tensor_mul", t1b[:, :], ss[:, :], psB[:, :])
                    b.op("dve", "tensor_mul", dst[:, :], qn[:, :], cs[:, :])
                    t_d = b.op("dve", "tensor_add", dst[:, :], dst[:, :], t1b[:, :])
                    b.wait("pe", t_d)
                    if bi == 0 and l == 0 and hp == 0 and f"dbg_{'q3' if qk_idx == 0 else 'k3'}" in io:
                        b.wait("sp", t_d)
                        b.dma("sp", "dbg", io[f"dbg_{'q3' if qk_idx == 0 else 'k3'}"][:, :],
                              dst[:, :])
                for hh in range(2):
                    b.wait("pe", ("e_dve", b.cnt["e_dve"]))
                    for mt in range(16):
                        b.op("pe", "transpose", psT[:, mt * 64:(mt + 1) * 64],
                             qkv[64 * hh: 64 * hh + 64,
                                 (8 + hp) * T + mt * 128: (8 + hp) * T + (mt + 1) * 128],
                             ids[64 * hh:64 * hh + 64, 128:192], sig=(mt == 15))
                    b.wait("dve", ("e_pe", b.cnt["e_pe"]))
                    b.op("dve", "tensor_copy", vTf[:, hh * 1024:(hh + 1) * 1024],
                         psT[:, 0:1024])
                b.wait("pe", ("e_dve", b.cnt["e_dve"]))
                for hh in range(2):
                    base = 64 * hh
                    for lt in range(16):
                        nw = min(lt, 4) + 1
                        w0 = max(0, lt - 4) * 128
                        wlen = nw * 128
                        nfull = wlen - 128
                        b.wait("pe", ("e_act", b.cnt["e_act"]))
                        b.wait("pe", ("e_dve", b.cnt["e_dve"]))
                        qv_ = q3[base:base + 64, lt * 128:(lt + 1) * 128]
                        if nw == 5:
                            b.mm(psSc[:, 512:640], qv_,
                                 k3[base:base + 64, lt * 128:(lt + 1) * 128],
                                 start=True, stop=False)
                            b.mm(psSc[:, 512:640], ids[:, 0:128], mks[:, 0:128],
                                 start=False, stop=True)
                            b.mm(psSc[:, 0:512], qv_,
                                 k3[base:base + 64, w0: w0 + 512],
                                 start=True, stop=False)
                            b.mm(psSc[:, 0:128], ids[:, 0:128], mks[:, 128:256],
                                 start=False, stop=True, sig=True)
                        else:
                            if nfull > 0:
                                b.mm(psSc[:, 0:nfull], qv_,
                                     k3[base:base + 64, w0: w0 + nfull],
                                     start=True, stop=False)
                            b.mm(psSc[:, nfull:wlen], qv_,
                                 k3[base:base + 64, lt * 128:(lt + 1) * 128],
                                 start=(nfull == 0), stop=False)
                            b.mm(psSc[:, nfull:wlen], ids[:, 0:128], mks[:, 0:128],
                                 start=False, stop=True, sig=True)
                        b.wait("dve", ("e_pe", b.cnt["e_pe"]))
                        b.op("dve", "reduce_max", mxd[:, 0:1], psSc[:, 0:wlen],
                             axis=AX.X, negate=True)
                        b.wait("act", ("e_dve", b.cnt["e_dve"]))
                        t_e = b.op("act", "activation", Pl[:, 0:wlen], psSc[:, 0:wlen],
                                   AF.Exp, bias=mxd[:, 0:1], accum_out=mxd[:, 1:2])
                        b.wait("pe", t_e)
                        b.wait("pe", ("e_dve", b.cnt["e_dve"]))
                        for mt in range(nw):
                            b.op("pe", "transpose", psT[:, mt * 128:(mt + 1) * 128],
                                 Pl[:, mt * 128:(mt + 1) * 128], ids[:, 0:128],
                                 sig=(mt == nw - 1))
                        b.wait("dve", ("e_pe", b.cnt["e_pe"]))
                        t_pt = b.op("dve", "tensor_copy", PTl[:, 0:wlen], psT[:, 0:wlen])
                        b.wait("pe", t_pt)
                        for mt in range(nw):
                            gm = max(0, lt - 4) + mt
                            b.mm(psO[:, :], PTl[:, mt * 128:(mt + 1) * 128],
                                 vTf[:, hh * 1024 + gm * 64: hh * 1024 + gm * 64 + 64],
                                 start=(mt == 0), stop=(mt == nw - 1), sig=(mt == nw - 1))
                        b.wait("dve", ("e_pe", b.cnt["e_pe"]))
                        b.op("dve", "reciprocal", mxd[:, 2:3], mxd[:, 1:2])
                        t_o = b.op("dve", "tensor_scalar_mul",
                                   ostash[:, lt * 64:(lt + 1) * 64] if hh == 0
                                   else opair[:, 64:128],
                                   psO[:, :], mxd[:, 2:3])
                        if hh == 1:
                            b.op("dve", "tensor_copy", opair[:, 0:64],
                                 ostash[:, lt * 64:(lt + 1) * 64])
                            t_op = ("e_dve", b.cnt["e_dve"])
                            b.wait("pe", t_op)
                            b.op("pe", "transpose", psT[:, 0:128], opair[:, :],
                                 ids[:, 0:128], sig=True)
                            b.wait("dve", ("e_pe", b.cnt["e_pe"]))
                            t_of = b.op("dve", "tensor_copy",
                                        o_fm[:, hp * T + lt * 128: hp * T + (lt + 1) * 128],
                                        psT[:, 0:128])
                            b.wait("pe", t_of)
            esH.close()
            b.flush()

            esF = contextlib.ExitStack()
            wos = esF.enter_context(nc.sbuf_tensor(f"f_wos_{bi}_{l}", [128, 4 * 1024], BF16))
            psW = [esF.enter_context(nc.psum_tensor(f"f_psW{i}_{bi}_{l}", [128, 512], F32))
                   for i in range(2)]
            b.scope_guard()
            t_w = b.dma("pool", "f_wld", wos[:, :],
                        bass.AP(io["f_wo"], l * 512 * 1024,
                                [[1024, 128], [128 * 1024, 4], [1, 1024]]))
            b.wait("pe", t_w)
            b.wait("pe", ("e_dve", b.cnt["e_dve"]))
            for m in range(DC):
                for s in range(4):
                    ps = psW[s % 2]
                    b.wait("pe", ("e_dve", b.cnt["e_dve"]))
                    for c in range(4):
                        b.mm(ps[:, :], wos[:, c * 1024 + m * 128: c * 1024 + m * 128 + 128],
                             o_fm[:, c * T + s * 512: c * T + s * 512 + 512],
                             start=(c == 0), stop=(c == 3), sig=(c == 3))
                    b.wait("dve", ("e_pe", b.cnt["e_pe"]))
                    b.op("dve", "tensor_add", xf[:, m * T + s * 512: m * T + s * 512 + 512],
                         xf[:, m * T + s * 512: m * T + s * 512 + 512], ps[:, :])
            if bi == 0 and l == 0 and "dbg_xfwo" in io:
                b.wait("sp", ("e_dve", b.cnt["e_dve"]))
                b.dma("sp", "dbg", bass.AP(io["dbg_xfwo"], 0, [[2048, 128], [128 * 2048, 8], [1, 2048]]),
                      xf[:, :])
            if bi == 0 and l == 0 and "dbg_ofm" in io:
                b.wait("sp", ("e_dve", b.cnt["e_dve"]))
                b.dma("sp", "dbg", bass.AP(io["dbg_ofm"], 0, [[2048, 128], [128 * 2048, 4], [1, 2048]]),
                      o_fm[:, :])
            esF.close()
            b.flush()
            esL.close()

            esM = contextlib.ExitStack()
            sq2 = esM.enter_context(nc.sbuf_tensor(f"f_sq2_{bi}_{l}", [128, 512], BF16))
            stat = esM.enter_context(nc.sbuf_tensor(f"f_stat_{bi}_{l}", [1, 4 * 512], F32))
            statb = esM.enter_context(nc.sbuf_tensor(f"f_statb_{bi}_{l}", [1, 2 * 512], BF16))
            hn = esM.enter_context(nc.sbuf_tensor(f"f_hn_{bi}_{l}", [128, DC * 2048], BF16))
            tmpa = esM.enter_context(nc.sbuf_tensor(f"f_tmpa_{bi}_{l}", [128, 4 * 512], BF16))
            wcA = [esM.enter_context(nc.sbuf_tensor(f"f_wcA{i}_{bi}_{l}", [128, DC * 128], BF16))
                   for i in range(2)]
            wcG = [esM.enter_context(nc.sbuf_tensor(f"f_wcG{i}_{bi}_{l}", [128, DC * 128], BF16))
                   for i in range(2)]
            w2s = [esM.enter_context(nc.sbuf_tensor(f"f_w2s{i}_{bi}_{l}", [128, 22 * 128], BF16))
                   for i in range(2)]
            psL = esM.enter_context(nc.psum_tensor(f"f_psL_{bi}_{l}", [1, 512], F32))
            psL2 = esM.enter_context(nc.psum_tensor(f"f_psL2_{bi}_{l}", [1, 512], F32))
            psMa = esM.enter_context(nc.psum_tensor(f"f_psMa_{bi}_{l}", [128, 512], F32))
            psMb = esM.enter_context(nc.psum_tensor(f"f_psMb_{bi}_{l}", [128, 512], F32))
            psM1 = [esM.enter_context(nc.psum_tensor(f"f_psM1{i}_{bi}_{l}", [128, 512], F32))
                    for i in range(2)]
            psM2 = [esM.enter_context(nc.psum_tensor(f"f_psM2{i}_{bi}_{l}", [128, 512], F32))
                    for i in range(2)]
            esM0 = contextlib.ExitStack()
            hup2 = esM0.enter_context(nc.sbuf_tensor(f"f_hup2_{bi}_{l}", [128, 4 * T], BF16))
            b.scope_guard()

            b.wait("dve", ("e_pe", b.cnt["e_pe"]))
            for c in range(4):
                b.op("dve", "memset", hup2[:, c * T: c * T + 1], 0.0, sig=False)
                b.op("dve", "tensor_copy", hup2[:, c * T + 1:(c + 1) * T],
                     xf[:, (4 + c) * T: (4 + c) * T + T - 1])
            h2v = lambda c, c0, cn: (xf[:, c * T + c0: c * T + c0 + cn] if c < 4
                                     else hup2[:, (c - 4) * T + c0: (c - 4) * T + c0 + cn])
            b.wait("pe", ("e_dve", b.cnt["e_dve"]))
            for s in range(4):
                s0 = s * 512
                b.wait("pe", ("e_dve", b.cnt["e_dve"]))
                for c in range(DC):
                    b.mm(psL[:, :], onesc[:, :], h2v(c, s0, 512),
                         start=(c == 0), stop=(c == DC - 1), sig=(c == DC - 1))
                for c in range(DC):
                    b.wait("act", ("e_pe", b.cnt["e_pe"]) if c == 0 else None)
                    t_q = b.op("act", "activation", sq2[:, :], h2v(c, s0, 512), AF.Square)
                    b.wait("pe", t_q)
                    b.mm(psL2[:, :], onesc[:, :], sq2[:, :],
                         start=(c == 0), stop=(c == DC - 1), sig=True)
                    if c < DC - 1:
                        b.wait("act", ("e_pe", b.cnt["e_pe"]))
                b.wait("dve", ("e_pe", b.cnt["e_pe"]))
                b.op("dve", "tensor_scalar_mul", stat[:, 0:512], psL[:, :], 1.0 / D)
                b.op("dve", "tensor_scalar_mul", stat[:, 512:1024], psL2[:, :], 1.0 / D)
                b.op("dve", "tensor_mul", stat[:, 1024:1536], stat[:, 0:512], stat[:, 0:512])
                b.op("dve", "tensor_sub", stat[:, 1024:1536], stat[:, 512:1024],
                     stat[:, 1024:1536])
                t_v = b.op("dve", "tensor_scalar_add", stat[:, 1024:1536],
                           stat[:, 1024:1536], 1e-5)
                b.wait("act", t_v)
                b.op("act", "activation", stat[:, 1536:2048], stat[:, 1024:1536], AF.Ln)
                t_r = b.op("act", "activation", stat[:, 1536:2048], stat[:, 1536:2048],
                           AF.Exp, scale=-0.5)
                b.wait("dve", t_r)
                b.op("dve", "tensor_copy", statb[:, 0:512], stat[:, 0:512])
                t_sb = b.op("dve", "tensor_copy", statb[:, 512:1024], stat[:, 1536:2048])
                b.wait("pe", t_sb)
                b.mm(psMa[:, :], ones1[:, :], statb[0:1, 0:512], start=True, stop=True)
                b.mm(psMb[:, :], ones1[:, :], statb[0:1, 512:1024], start=True, stop=True,
                     sig=True)
                b.wait("dve", ("e_pe", b.cnt["e_pe"]))
                for c in range(DC):
                    b.op("dve", "tensor_sub", hn[:, c * 2048 + s0: c * 2048 + s0 + 512],
                         h2v(c, s0, 512), psMa[:, :])
                    b.op("dve", "tensor_mul", hn[:, c * 2048 + s0: c * 2048 + s0 + 512],
                         hn[:, c * 2048 + s0: c * 2048 + s0 + 512], psMb[:, :])
                    b.op("dve", "tensor_scalar", hn[:, c * 2048 + s0: c * 2048 + s0 + 512],
                         hn[:, c * 2048 + s0: c * 2048 + s0 + 512],
                         lgb[:, (2 * l) * DC + c: (2 * l) * DC + c + 1],
                         lgb[:, (2 * l + 1) * DC + c: (2 * l + 1) * DC + c + 1],
                         OP.mult, OP.add)
            b.flush()
            esM0.close()
            esM2 = contextlib.ExitStack()
            hm = esM2.enter_context(nc.sbuf_tensor(f"f_hm_{bi}_{l}", [128, 22 * 2048], BF16))
            hn_free = b.cnt["e_dve"]

            wfA = [0, 0]
            wfG = [0, 0]
            evacA = [0, 0]
            evacG = [0, 0]
            muldone = [0, 0, 0, 0]
            for m in range(22):
                mw = 128 if m < 21 else 42
                sl = m % 2
                b.wait("pool", ("e_pe", wfA[sl]))
                t_a = b.dma("pool", "f_wld", wcA[sl][:, 0:DC * mw],
                            bass.AP(io["f_w1"], l * 1024 * 5460 + m * 128,
                                    [[5460, 128], [128 * 5460, DC], [1, mw]]))
                b.wait("pool", ("e_pe", wfG[sl]))
                t_g = b.dma("pool", "f_wld", wcG[sl][:, 0:DC * mw],
                            bass.AP(io["f_w1"], l * 1024 * 5460 + 2730 + m * 128,
                                    [[5460, 128], [128 * 5460, DC], [1, mw]]))
                b.wait("pe", t_a)
                b.wait("pe", ("e_dve", hn_free))
                for s in range(4):
                    ps = psM1[s % 2]
                    b.wait("pe", ("e_act", evacA[s % 2]))
                    for c in range(DC):
                        b.mm(ps[0:mw, :], wcA[sl][:, c * mw:(c + 1) * mw],
                             hn[:, c * 2048 + s * 512: c * 2048 + (s + 1) * 512],
                             start=(c == 0), stop=(c == DC - 1), sig=(c == DC - 1))
                    b.wait("act", ("e_pe", b.cnt["e_pe"]))
                    b.wait("act", ("e_dve", muldone[s]))
                    t_ev = b.op("act", "activation", tmpa[0:mw, s * 512:(s + 1) * 512],
                                ps[0:mw, :], AF.Identity,
                                bias=b1s[0:mw, (2 * l) * 22 + m: (2 * l) * 22 + m + 1])
                    evacA[s % 2] = t_ev[1]
                wfA[sl] = b.cnt["e_pe"]
                b.wait("pe", t_g)
                for s in range(4):
                    ps2 = psM2[s % 2]
                    b.wait("pe", ("e_act", evacG[s % 2]))
                    for c in range(DC):
                        b.mm(ps2[0:mw, :], wcG[sl][:, c * mw:(c + 1) * mw],
                             hn[:, c * 2048 + s * 512: c * 2048 + (s + 1) * 512],
                             start=(c == 0), stop=(c == DC - 1), sig=(c == DC - 1))
                    b.wait("act", ("e_pe", b.cnt["e_pe"]))
                    t_gv = b.op("act", "activation",
                                hm[0:mw, m * 2048 + s * 512: m * 2048 + (s + 1) * 512],
                                ps2[0:mw, :], AF.Gelu,
                                bias=b1s[0:mw, (2 * l + 1) * 22 + m: (2 * l + 1) * 22 + m + 1])
                    evacG[s % 2] = t_gv[1]
                    b.wait("dve", t_gv)
                    t_mul = b.op("dve", "tensor_mul",
                                 hm[0:mw, m * 2048 + s * 512: m * 2048 + (s + 1) * 512],
                                 tmpa[0:mw, s * 512:(s + 1) * 512],
                                 hm[0:mw, m * 2048 + s * 512: m * 2048 + (s + 1) * 512])
                    muldone[s] = t_mul[1]
                wfG[sl] = b.cnt["e_pe"]

            hm_done = ("e_dve", b.cnt["e_dve"])
            wf2 = [0, 0]
            evac2 = [0, 0]
            for m2 in range(DC):
                sl = m2 % 2
                b.wait("pool", ("e_pe", wf2[sl]))
                t_wa = b.dma("pool", "f_wld", w2s[sl][:, 0:21 * 128],
                             bass.AP(io["f_w2"], l * 2730 * 1024 + m2 * 128,
                                     [[1024, 128], [128 * 1024, 21], [1, 128]]))
                t_wb = b.dma("pool", "f_wld", w2s[sl][0:42, 21 * 128:22 * 128],
                             bass.AP(io["f_w2"], l * 2730 * 1024 + 21 * 128 * 1024 + m2 * 128,
                                     [[1024, 42], [1, 128]]))
                b.wait("pe", t_wb)
                b.wait("pe", hm_done)
                for s in range(4):
                    ps = psM1[s % 2]
                    b.wait("pe", ("e_act", evacA[s % 2]))
                    b.wait("pe", ("e_dve", evac2[s % 2]))
                    for c in range(22):
                        cw = 128 if c < 21 else 42
                        b.mm(ps[:, :], w2s[sl][0:cw, c * 128:(c + 1) * 128],
                             hm[0:cw, c * 2048 + s * 512: c * 2048 + (s + 1) * 512],
                             start=(c == 0), stop=(c == 21), sig=(c == 21))
                    b.wait("dve", ("e_pe", b.cnt["e_pe"]))
                    t_x = b.op("dve", "scalar_tensor_tensor",
                               xf[:, m2 * T + s * 512: m2 * T + (s + 1) * 512], ps[:, :],
                               b2s[:, l * DC + m2: l * DC + m2 + 1],
                               xf[:, m2 * T + s * 512: m2 * T + (s + 1) * 512],
                               OP.add, OP.add)
                    evac2[s % 2] = t_x[1]
                wf2[sl] = b.cnt["e_pe"]
            esM2.close()
            esM.close()
            b.flush()

        if not hasattr(b, "_g1b"):
            b._g1b = es.enter_context(nc.sbuf_tensor("f_g1b", [128, T], BF16))
        g1b = b._g1b
        for c in range(DC):
            b.wait("act", b.tok("f_g1st"))
            b.wait("act", ("e_dve", b.cnt["e_dve"]))
            t_g = b.op("act", "activation", g1b[:, :], xf[:, c * T:(c + 1) * T], AF.Silu)
            b.wait("sp", t_g)
            b.dma("sp", "f_g1st", io["g1_fm"][c * 128:(c + 1) * 128, tb: tb + T], g1b[:, :])
        b.flush()
    b.barrier()
    es.close()


def emit_combine(b, io):
    nc = b.nc
    es = contextlib.ExitStack()
    x1t = es.enter_context(nc.sbuf_tensor("cb_x1", [128, D], F32))
    s1t = es.enter_context(nc.sbuf_tensor("cb_s1", [128, D], BF16))
    g1t = es.enter_context(nc.sbuf_tensor("cb_g1", [128, D], BF16))
    pt = es.enter_context(nc.sbuf_tensor("cb_p", [128, D], BF16))
    ot = es.enter_context(nc.sbuf_tensor("cb_o", [128, D], F32))
    b.stream("cb_ld")
    b.stream("cb_st")
    for i in range(TOK // 128):
        prev = b.cnt["e_dve"]
        b.wait("sp", ("e_dve", prev))
        b.wait("sp", b.tok("cb_st"))
        b.dma("sp", "cb_ld", x1t[:, :], io["x1"][i * 128:(i + 1) * 128, :])
        b.dma_tr("sp", "cb_ld", s1t[:, :], io["s1_fm"][:, i * 128:(i + 1) * 128])
        b.dma_tr("sp", "cb_ld", g1t[:, :], io["g1_fm"][:, i * 128:(i + 1) * 128])
        b.wait("dve", b.tok("cb_ld"))
        b.op("dve", "tensor_mul", pt[:, :], s1t[:, :], g1t[:, :])
        t_o = b.op("dve", "tensor_add", ot[:, :], x1t[:, :], pt[:, :])
        b.wait("sp", t_o)
        b.dma("sp", "cb_st", io["out"][i * 128:(i + 1) * 128, :], ot[:, :])
    b.flush()
    b.barrier()
    es.close()


from concourse.bass_utils import run_bass_kernel_spmd

N_CORES = 8


def _drams(nc, specs):
    io = {}
    for name, shape, dt, kind in specs:
        io[name] = nc.dram_tensor(name, shape, dt, kind=kind)
    return io


def _run(build_fn, specs, per_core_feeds):
    nc = bass.Bass("TRN2", target_bir_lowering=False)
    io = _drams(nc, specs)
    es = contextlib.ExitStack()
    bb = B(nc, es)
    build_fn(bb, io)
    bb.barrier()
    bb.flush()
    es.close()
    res = run_bass_kernel_spmd(nc, per_core_feeds, core_ids=list(range(N_CORES)))
    return res.results


def host_prep_all(inputs):
    w = host_prep_weights(inputs)
    w["moe_b1_l"] = np.ascontiguousarray(
        np.asarray(inputs["L0_ffn_b1"], np.float32).reshape(14, 16, 128)
        .transpose(2, 0, 1).reshape(128, 224))
    b1 = np.asarray(inputs["L1_ffn_b1"], np.float32)
    b1a = np.zeros((2, 128, 22), np.float32)
    b1g = np.zeros((2, 128, 22), np.float32)
    for l in range(2):
        for m in range(22):
            mw = 128 if m < 21 else 42
            b1a[l, :mw, m] = b1[l, m * 128: m * 128 + mw]
            b1g[l, :mw, m] = b1[l, 2730 + m * 128: 2730 + m * 128 + mw]
    w["f_b1a_l"] = np.ascontiguousarray(b1a)
    w["f_b1g_l"] = np.ascontiguousarray(b1g)
    w["moe_b2"] = np.ascontiguousarray(np.asarray(inputs["L0_ffn_b2"], np.float32))
    return w


def kernel(**inputs):
    w = host_prep_all(inputs)
    x_full = np.ascontiguousarray(np.asarray(inputs["L0_attn_x"], np.float32))
    shards = [np.ascontiguousarray(x_full[2 * i: 2 * i + 2].reshape(TOK, D))
              for i in range(N_CORES)]
    NPF32 = np.float32

    specs = [("x", [TOK, D], F32, "ExternalInput"),
             ("gate_wT", [1024, 14], F32, "ExternalInput"),
             ("xn_tm", [TOK, D], BF16, "ExternalOutput"),
             ("xnl_tm", [TOK, D], BF16, "ExternalOutput"),
             ("W_hbm", [TOK, 16], F32, "ExternalOutput")]
    feeds = [{"x": shards[i], "gate_wT": w["gate_wT"]} for i in range(N_CORES)]
    r1 = _run(lambda bb, io: emit_prep(bb, io), specs, feeds)

    specs = [("xn_tm", [TOK, D], BF16, "ExternalInput"),
             ("w_qkvT", [1024, 3072], F32, "ExternalInput"),
             ("b_qkv_l", [128, 24], F32, "ExternalInput"),
             ("w_oT", [1024, 1024], F32, "ExternalInput"),
             ("b_o_l", [128, 8], F32, "ExternalInput"),
             ("v_hbm", [1024, TOK], BF16, "Internal"),
             ("s0_fm", [1024, TOK], BF16, "ExternalOutput")]
    feeds = [{"xn_tm": r1[i]["xn_tm"], "w_qkvT": w["w_qkvT"], "b_qkv_l": w["b_qkv_l"],
              "w_oT": w["w_oT"], "b_o_l": w["b_o_l"]} for i in range(N_CORES)]
    r2 = _run(emit_l0attn, specs, feeds)

    specs = [("xn_tm", [TOK, D], BF16, "ExternalInput"),
             ("xnl_tm", [TOK, D], BF16, "ExternalInput"),
             ("W_hbm", [TOK, 16], F32, "ExternalInput"),
             ("moe_w1", [14, 1024, 2048], F32, "ExternalInput"),
             ("moe_b1_l", [128, 224], F32, "ExternalInput"),
             ("moe_w2", [14, 2048, 1024], F32, "ExternalInput"),
             ("moe_b2", [14, 1024], F32, "ExternalInput"),
             ("x", [TOK, D], F32, "ExternalInput"),
             ("s0_fm", [1024, TOK], BF16, "ExternalInput"),
             ("acc_hbm", [TOK, D], F32, "Internal"),
             ("x1", [TOK, D], F32, "ExternalOutput")]
    feeds = [{"xn_tm": r1[i]["xn_tm"], "xnl_tm": r1[i]["xnl_tm"], "W_hbm": r1[i]["W_hbm"],
              "moe_w1": w["moe_w1"], "moe_b1_l": w["moe_b1_l"], "moe_w2": w["moe_w2"],
              "moe_b2": w["moe_b2"], "x": shards[i], "s0_fm": r2[i]["s0_fm"]}
             for i in range(N_CORES)]
    r3 = _run(emit_moe, specs, feeds)

    specs = [("x", [TOK, D], F32, "ExternalInput"),
             ("xn_tm", [TOK, D], BF16, "ExternalOutput")]
    feeds = [{"x": r3[i]["x1"]} for i in range(N_CORES)]
    r4 = _run(lambda bb, io: emit_prep(bb, io, with_lo=False, with_gate=False),
              specs, feeds)

    specs = [("xn_tm", [TOK, D], BF16, "ExternalInput"),
             ("W_DQ", [1024, 4096], F32, "ExternalInput"),
             ("W_UQ", [4096, 1024], F32, "ExternalInput"),
             ("W_DKV", [1024, 256], F32, "ExternalInput"),
             ("W_UK", [256, 1024], F32, "ExternalInput"),
             ("W_UV", [256, 1024], F32, "ExternalInput"),
             ("W_QR_pad", [4096, 1024], F32, "ExternalInput"),
             ("W_KR", [1024, 32], F32, "ExternalInput"),
             ("W_O1", [1024, 1024], F32, "ExternalInput"),
             ("v_hbm", [1024, TOK], BF16, "Internal"),
             ("qc_hbm", [1024, TOK], BF16, "Internal"),
             ("qr_hbm", [1024, TOK], BF16, "Internal"),
             ("kc_hbm", [1024, TOK], BF16, "Internal"),
             ("s1_fm", [1024, TOK], BF16, "ExternalOutput")]
    feeds = [dict({"xn_tm": r4[i]["xn_tm"]},
                  **{k: w[k] for k in ("W_DQ", "W_UQ", "W_DKV", "W_UK", "W_UV",
                                       "W_QR_pad", "W_KR", "W_O1")})
             for i in range(N_CORES)]
    r5 = _run(emit_l1attn, specs, feeds)

    specs = [("xn_tm", [TOK, D], BF16, "ExternalInput"),
             ("f_wqkv", [2, 1024, 1536], F32, "ExternalInput"),
             ("f_qs_l", [128, 2], F32, "ExternalInput"),
             ("f_ks_l", [128, 2], F32, "ExternalInput"),
             ("f_wo", [2, 512, 1024], F32, "ExternalInput"),
             ("f_lng_l", [2, 128, 8], F32, "ExternalInput"),
             ("f_lnb_l", [2, 128, 8], F32, "ExternalInput"),
             ("f_w1", [2, 1024, 5460], F32, "ExternalInput"),
             ("f_b1a_l", [2, 128, 22], F32, "ExternalInput"),
             ("f_b1g_l", [2, 128, 22], F32, "ExternalInput"),
             ("f_w2", [2, 2730, 1024], F32, "ExternalInput"),
             ("f_b2_l", [2, 128, 8], F32, "ExternalInput"),
             ("g1_fm", [1024, TOK], BF16, "ExternalOutput")]
    feeds = [dict({"xn_tm": r4[i]["xn_tm"]},
                  **{k: w[k] for k in ("f_wqkv", "f_qs_l", "f_ks_l", "f_wo", "f_lng_l",
                                       "f_lnb_l", "f_w1", "f_b1a_l", "f_b1g_l",
                                       "f_w2", "f_b2_l")})
             for i in range(N_CORES)]
    r6 = _run(emit_l1ffn, specs, feeds)

    specs = [("x1", [TOK, D], F32, "ExternalInput"),
             ("s1_fm", [1024, TOK], BF16, "ExternalInput"),
             ("g1_fm", [1024, TOK], BF16, "ExternalInput"),
             ("out", [TOK, D], F32, "ExternalOutput")]
    feeds = [{"x1": r3[i]["x1"], "s1_fm": r5[i]["s1_fm"], "g1_fm": r6[i]["g1_fm"]}
             for i in range(N_CORES)]
    r7 = _run(emit_combine, specs, feeds)

    out = np.empty((16, 2048, 1024), np.float32)
    for i in range(N_CORES):
        out[2 * i: 2 * i + 2] = np.asarray(r7[i]["out"], np.float32).reshape(2, 2048, 1024)
    return out


def build_fused(nc):
    io = {}
    ext_in = [("x", [TOK, D], F32), ("gate_wT", [1024, 14], F32),
              ("w_qkvT", [1024, 3072], F32), ("b_qkv_l", [128, 24], F32),
              ("w_oT", [1024, 1024], F32), ("b_o_l", [128, 8], F32),
              ("moe_w1", [14, 1024, 2048], F32), ("moe_b1_l", [128, 224], F32),
              ("moe_w2", [14, 2048, 1024], F32), ("moe_b2", [14, 1024], F32),
              ("W_DQ", [1024, 4096], F32), ("W_UQ", [4096, 1024], F32),
              ("W_DKV", [1024, 256], F32), ("W_UK", [256, 1024], F32),
              ("W_UV", [256, 1024], F32), ("W_QR_pad", [4096, 1024], F32),
              ("W_KR", [1024, 32], F32), ("W_O1", [1024, 1024], F32),
              ("f_wqkv", [2, 1024, 1536], F32), ("f_qs_l", [128, 2], F32),
              ("f_ks_l", [128, 2], F32), ("f_wo", [2, 512, 1024], F32),
              ("f_lng_l", [2, 128, 8], F32), ("f_lnb_l", [2, 128, 8], F32),
              ("f_w1", [2, 1024, 5460], F32), ("f_b1a_l", [2, 128, 22], F32),
              ("f_b1g_l", [2, 128, 22], F32), ("f_w2", [2, 2730, 1024], F32),
              ("f_b2_l", [2, 128, 8], F32)]
    for n, sh, dt in ext_in:
        io[n] = nc.dram_tensor(n, sh, dt, kind="ExternalInput")
    internal = [("xn_tm", [TOK, D], BF16), ("xnl_tm", [TOK, D], BF16),
                ("W_hbm", [TOK, 16], F32), ("v_hbm", [1024, TOK], BF16),
                ("s0_fm", [1024, TOK], BF16), ("acc_hbm", [TOK, D], F32),
                ("x1", [TOK, D], F32), ("xn1_tm", [TOK, D], BF16),
                ("qc_hbm", [1024, TOK], BF16), ("qr_hbm", [1024, TOK], BF16),
                ("kc_hbm", [1024, TOK], BF16), ("s1_fm", [1024, TOK], BF16),
                ("g1_fm", [1024, TOK], BF16)]
    for n, sh, dt in internal:
        io[n] = nc.dram_tensor(n, sh, dt, kind="Internal")
    io["out"] = nc.dram_tensor("out", [TOK, D], F32, kind="ExternalOutput")

    es = contextlib.ExitStack()
    bb = B(nc, es)
    emit_prep(bb, io, pfx="p0")
    emit_l0attn(bb, io)
    emit_moe(bb, io)
    io_ln1 = dict(io)
    io_ln1["x"] = io["x1"]
    io_ln1["xn_tm"] = io["xn1_tm"]
    emit_prep(bb, io_ln1, with_lo=False, with_gate=False, pfx="p1")
    io_l1 = dict(io)
    io_l1["xn_tm"] = io["xn1_tm"]
    emit_l1attn(bb, io_l1)
    emit_l1ffn(bb, io_l1)
    emit_combine(bb, io)
    bb.barrier()
    bb.flush()
    es.close()
    return io


def kernel_fused(**inputs):
    w = host_prep_all(inputs)
    x_full = np.ascontiguousarray(np.asarray(inputs["L0_attn_x"], np.float32))
    shards = [np.ascontiguousarray(x_full[2 * i: 2 * i + 2].reshape(TOK, D))
              for i in range(N_CORES)]
    nc = bass.Bass("TRN2", target_bir_lowering=False)
    build_fused(nc)
    feeds = []
    wk = ("gate_wT", "w_qkvT", "b_qkv_l", "w_oT", "b_o_l", "moe_w1", "moe_b1_l",
          "moe_w2", "moe_b2", "W_DQ", "W_UQ", "W_DKV", "W_UK", "W_UV", "W_QR_pad",
          "W_KR", "W_O1", "f_wqkv", "f_qs_l", "f_ks_l", "f_wo", "f_lng_l", "f_lnb_l",
          "f_w1", "f_b1a_l", "f_b1g_l", "f_w2", "f_b2_l")
    for i in range(N_CORES):
        f = {k: w[k] for k in wk}
        f["x"] = shards[i]
        feeds.append(f)
    res = run_bass_kernel_spmd(nc, feeds, core_ids=list(range(N_CORES)))
    out = np.empty((16, 2048, 1024), np.float32)
    for i in range(N_CORES):
        out[2 * i: 2 * i + 2] = np.asarray(res.results[i]["out"], np.float32).reshape(2, 2048, 1024)
    return out


kernel = kernel_fused


# version 11
# speedup vs baseline: 1.3948x; 1.0492x over previous
import numpy as np
import ml_dtypes
import contextlib
import concourse.bass as bass
import concourse.mybir as mybir
from concourse.alu_op_type import AluOpType as OP

F32 = mybir.dt.float32
BF16 = mybir.dt.bfloat16
AF = mybir.ActivationFunctionType
AX = mybir.AxisListType
NPBF = ml_dtypes.bfloat16

TOK = 4096
NB = 2
T = 2048
D = 1024
DC = D // 128

ENGS = ("sp", "pool", "pe", "act", "dve")


class B:

    def __init__(self, nc, es):
        self.nc = nc
        self.es = es
        self.block = es.enter_context(nc.Block())
        self.sems = {}
        self.cnt = {}
        for e in ENGS:
            self._mksem("e_" + e)
        self.ops = {e: [] for e in ENGS}
        self.lastw = {}
        self.consts = {}

    def _mksem(self, name):
        self.sems[name] = self.es.enter_context(self.nc.semaphore(name))
        self.cnt[name] = 0

    def stream(self, name):
        if name not in self.sems:
            self._mksem(name)
        return name

    def raw(self, eng, fn):
        self.ops[eng].append(fn)

    def wait(self, eng, tok):
        if tok is None:
            return
        s, v = tok
        if v <= 0 or self.lastw.get((eng, s), 0) >= v:
            return
        self.lastw[(eng, s)] = v
        sem = self.sems[s]
        self.raw(eng, lambda e: e.wait_ge(sem, v))

    def waits(self, eng, toks):
        for t in toks:
            self.wait(eng, t)

    def dma(self, eng, stream, out, in_, **kw):
        s = self.stream(stream)
        self.cnt[s] += 16
        sem = self.sems[s]
        self.raw(eng, lambda e: e.dma_start(out, in_, **kw).then_inc(sem, 16))
        return (s, self.cnt[s])

    def dma_tr(self, eng, stream, out, in_):
        s = self.stream(stream)
        self.cnt[s] += 16
        sem = self.sems[s]
        self.raw(eng, lambda e: e.dma_start_transpose(out, in_).then_inc(sem, 16))
        return (s, self.cnt[s])

    def tok(self, stream):
        return (stream, self.cnt[stream])

    def op(self, eng, meth, *a, sig=True, **kw):
        if eng in ("dve", "act"):
            self.wait(eng, ("e_" + eng, self.cnt["e_" + eng]))
            sig = True
        if sig:
            s = "e_" + eng
            self.cnt[s] += 1
            v = self.cnt[s]
            sem = self.sems[s]
            self.raw(eng, lambda e: getattr(e, meth)(*a, **kw).then_inc(sem, 1))
            return (s, v)
        self.raw(eng, lambda e: getattr(e, meth)(*a, **kw))
        return None

    def mm(self, out, lhsT, rhs, start, stop, sig=False, tile_position=None,
           skip_group_check=False):
        kw = {}
        if tile_position is not None:
            kw["tile_position"] = tile_position
        if skip_group_check:
            kw["skip_group_check"] = True
        return self.op("pe", "matmul", out, lhsT, rhs, start=start, stop=stop,
                       sig=sig, **kw)

    def flush(self):
        for eng in ENGS:
            lst = self.ops[eng]
            if not lst:
                continue
            meth = {"sp": "sync", "pool": "gpsimd", "pe": "tensor",
                    "act": "scalar", "dve": "vector"}[eng]

            def runner(e, lst=lst):
                for f in lst:
                    f(e)
            getattr(self.block, meth)(runner)
            self.ops[eng] = []

    def scope_guard(self):
        for q in ("sp", "pool"):
            for e in ("e_pe", "e_act", "e_dve"):
                self.wait(q, (e, self.cnt[e]))

    def barrier(self):
        toks = [(s, self.cnt[s]) for s in self.sems if self.cnt[s] > 0]
        for eng in ENGS:
            for t in toks:
                self.wait(eng, t)

    def const(self, name, arr):
        if name not in self.consts:
            self.consts[name] = self.nc.inline_tensor(np.ascontiguousarray(arr), name)
        return self.consts[name]



def host_prep_weights(inp):
    f32 = lambda a: np.ascontiguousarray(np.asarray(a, dtype=np.float32))
    w = {}
    w["w_qkvT"] = f32(np.asarray(inp["L0_attn_w_qkv"]).T)
    w["b_qkv_l"] = f32(np.asarray(inp["L0_attn_b_qkv"]).reshape(24, 128).T)
    w["w_oT"] = f32(np.asarray(inp["L0_attn_w_o"]).T)
    w["b_o_l"] = f32(np.asarray(inp["L0_attn_b_o"]).reshape(8, 128).T)
    w["gate_wT"] = f32(np.asarray(inp["L0_ffn_gate_w"]).T)
    w["moe_w1"] = f32(inp["L0_ffn_w1"])
    w["moe_b1_r"] = f32(np.asarray(inp["L0_ffn_b1"]).reshape(1, 14 * 2048))
    w["moe_w2"] = f32(inp["L0_ffn_w2"])
    w["moe_b2_r"] = f32(np.asarray(inp["L0_ffn_b2"]).reshape(14, 1024))
    w["W_DQ"] = f32(inp["L1_attn_W_DQ"])
    w["W_UQ"] = f32(inp["L1_attn_W_UQ"])
    w["W_DKV"] = f32(inp["L1_attn_W_DKV"])
    w["W_UK"] = f32(inp["L1_attn_W_UK"])
    w["W_UV"] = f32(inp["L1_attn_W_UV"])
    wqr = np.asarray(inp["L1_attn_W_QR"], dtype=np.float32)
    wqr_pad = np.zeros((4096, 1024), dtype=np.float32)
    for h in range(16):
        wqr_pad[:, 64 * h:64 * h + 32] = wqr[:, 32 * h:32 * h + 32]
    w["W_QR_pad"] = np.ascontiguousarray(wqr_pad)
    w["W_KR"] = f32(inp["L1_attn_W_KR"])
    w["W_O1"] = f32(inp["L1_attn_W_O"])
    w["f_wqkv"] = f32(inp["L1_ffn_wqkv"])
    w["f_qs_l"] = f32(np.tile(np.asarray(inp["L1_ffn_q_scale"], np.float32), (1, 2)).T)
    w["f_ks_l"] = f32(np.tile(np.asarray(inp["L1_ffn_k_scale"], np.float32), (1, 2)).T)
    w["f_wo"] = f32(inp["L1_ffn_wo"])
    w["f_lng_l"] = f32(np.asarray(inp["L1_ffn_ln_g"], np.float32).reshape(2, 8, 128).transpose(0, 2, 1))
    w["f_lnb_l"] = f32(np.asarray(inp["L1_ffn_ln_b"], np.float32).reshape(2, 8, 128).transpose(0, 2, 1))
    w["f_w1"] = f32(inp["L1_ffn_w1"])
    w["f_b1_r"] = f32(np.asarray(inp["L1_ffn_b1"], np.float32).reshape(2, 1, 5460))
    w["f_w2"] = f32(inp["L1_ffn_w2"])
    w["f_b2_l"] = f32(np.asarray(inp["L1_ffn_b2"], np.float32).reshape(2, 8, 128).transpose(0, 2, 1))
    return w



def fm_hbm_ap(hbm, ncols=TOK):
    return hbm.rearrange("(c p) t -> p (c t)", p=128)


def emit_xbar_fm(b, dst_fm, hbm_tm, rows, stream="xb", eng="sp", col0=0, ncols=TOK):
    toks = []
    for c in range(rows // 128):
        toks.append(b.dma_tr(eng, stream,
                             dst_fm[:, c * ncols:(c + 1) * ncols],
                             hbm_tm[col0:col0 + ncols, c * 128:(c + 1) * 128]))
    return toks


def emit_prep(b, io, with_lo=True, with_gate=True, pfx="pr"):
    nc = b.nc
    es = contextlib.ExitStack()
    xt = [es.enter_context(nc.sbuf_tensor(f"{pfx}_x{i}", [128, D], F32)) for i in range(2)]
    xnh = es.enter_context(nc.sbuf_tensor(f"{pfx}_xnh", [128, D], BF16))
    xnh32 = es.enter_context(nc.sbuf_tensor(f"{pfx}_xnh32", [128, D], F32))
    xn32 = es.enter_context(nc.sbuf_tensor(f"{pfx}_xn32", [128, D], F32))
    xnl = es.enter_context(nc.sbuf_tensor(f"{pfx}_xnl", [128, D], BF16))
    sq = es.enter_context(nc.sbuf_tensor(f"{pfx}_sq", [128, D], F32))
    st = es.enter_context(nc.sbuf_tensor(f"{pfx}_st", [128, 8], F32))
    ld = b.stream(f"{pfx}_ld")
    stb = b.stream(f"{pfx}_st")

    for i in range(TOK // 128):
        buf = xt[i % 2]
        prev_dve = b.cnt["e_dve"]
        if i >= 2:
            b.wait("sp", ("e_dve", prev_dve))
        t_ld = b.dma("sp", ld, buf[:, :], io["x"][i * 128:(i + 1) * 128, :])
        b.wait("act", t_ld)
        b.wait("act", ("e_dve", prev_dve))
        b.op("act", "activation", sq[:, :], buf[:, :], AF.Square)
        b.wait("dve", ("e_act", b.cnt["e_act"]))
        b.wait("dve", b.tok(f"{pfx}_st") if i > 0 else None)
        b.op("dve", "reduce_sum", st[:, 0:1], buf[:, :], axis=AX.X)
        b.op("dve", "reduce_sum", st[:, 1:2], sq[:, :], axis=AX.X)
        b.op("dve", "tensor_scalar_mul", st[:, 0:1], st[:, 0:1], 1.0 / D)
        b.op("dve", "tensor_scalar_mul", st[:, 1:2], st[:, 1:2], 1.0 / D)
        b.op("dve", "tensor_mul", st[:, 2:3], st[:, 0:1], st[:, 0:1])
        b.op("dve", "tensor_sub", st[:, 2:3], st[:, 1:2], st[:, 2:3])
        t_v = b.op("dve", "tensor_scalar_add", st[:, 3:4], st[:, 2:3], 1e-5)
        b.wait("act", t_v)
        b.op("act", "activation", st[:, 3:4], st[:, 3:4], AF.Ln)
        t_r = b.op("act", "activation", st[:, 3:4], st[:, 3:4], AF.Exp, scale=-0.5)
        b.wait("dve", t_r)
        if with_lo:
            b.op("dve", "tensor_scalar", xn32[:, :], buf[:, :], st[:, 0:1], st[:, 3:4],
                 OP.subtract, OP.mult)
            b.op("dve", "tensor_copy", xnh[:, :], xn32[:, :])
            b.op("dve", "tensor_copy", xnh32[:, :], xnh[:, :])
            b.op("dve", "tensor_sub", xnh32[:, :], xn32[:, :], xnh32[:, :])
            t_l = b.op("dve", "tensor_copy", xnl[:, :], xnh32[:, :])
            b.wait("sp", t_l)
            b.dma("sp", stb, io["xn_tm"][i * 128:(i + 1) * 128, :], xnh[:, :])
            b.dma("sp", stb, io["xnl_tm"][i * 128:(i + 1) * 128, :], xnl[:, :])
        else:
            t_l = b.op("dve", "tensor_scalar", xnh[:, :], buf[:, :], st[:, 0:1],
                       st[:, 3:4], OP.subtract, OP.mult)
            b.wait("sp", t_l)
            b.dma("sp", stb, io["xn_tm"][i * 128:(i + 1) * 128, :], xnh[:, :])
    b.flush()
    b.barrier()
    if not with_gate:
        es.close()
        return

    xn_fm = es.enter_context(nc.sbuf_tensor(f"{pfx}_xnfm", [128, DC * TOK], BF16))
    xnl_fm = es.enter_context(nc.sbuf_tensor(f"{pfx}_xnlfm", [128, DC * TOK], BF16))
    gw32 = es.enter_context(nc.sbuf_tensor(f"{pfx}_gw32", [128, DC * 14], F32))
    gh = es.enter_context(nc.sbuf_tensor(f"{pfx}_gh", [128, DC * 14], BF16))
    gl32 = es.enter_context(nc.sbuf_tensor(f"{pfx}_gl32", [128, DC * 14], F32))
    gl = es.enter_context(nc.sbuf_tensor(f"{pfx}_gl", [128, DC * 14], BF16))
    Wt = es.enter_context(nc.sbuf_tensor(f"{pfx}_W", [128, 16], F32))
    gsc = es.enter_context(nc.sbuf_tensor(f"{pfx}_gsc", [128, 64], F32))
    gps = es.enter_context(nc.psum_tensor(f"{pfx}_gps", [128, 16], F32))

    emit_xbar_fm(b, xn_fm, io["xn_tm"], D, stream=f"{pfx}_xb")
    emit_xbar_fm(b, xnl_fm, io["xnl_tm"], D, stream=f"{pfx}_xb")
    t_g = b.dma("sp", f"{pfx}_gw", gw32[:, :],
                bass.AP(io["gate_wT"], 0, [[14, 128], [14 * 128, DC], [1, 14]]))
    b.wait("dve", t_g)
    b.op("dve", "tensor_copy", gh[:, :], gw32[:, :])
    b.op("dve", "tensor_copy", gl32[:, :], gh[:, :])
    b.op("dve", "tensor_sub", gl32[:, :], gw32[:, :], gl32[:, :])
    t_gl = b.op("dve", "tensor_copy", gl[:, :], gl32[:, :])
    b.wait("pe", t_gl)
    b.wait("pe", b.tok(f"{pfx}_xb"))

    b.stream(f"{pfx}_Wst")
    gps_free_at = 0
    for i in range(TOK // 128):
        b.wait("pe", ("e_dve", gps_free_at))
        first = True
        t_mm = None
        for c in range(DC):
            xh_v = xn_fm[:, c * TOK + i * 128: c * TOK + (i + 1) * 128]
            xl_v = xnl_fm[:, c * TOK + i * 128: c * TOK + (i + 1) * 128]
            b.mm(gps[:, 0:14], xh_v, gh[:, c * 14:(c + 1) * 14], start=first, stop=False)
            b.mm(gps[:, 0:14], xh_v, gl[:, c * 14:(c + 1) * 14], start=False, stop=False)
            t_mm = b.mm(gps[:, 0:14], xl_v, gh[:, c * 14:(c + 1) * 14], start=False,
                        stop=(c == DC - 1), sig=(c == DC - 1))
            first = False
        b.wait("dve", t_mm)
        b.wait("dve", b.tok(f"{pfx}_Wst"))
        c0, c1, c2, c3 = gsc[:, 0:1], gsc[:, 1:2], gsc[:, 2:3], gsc[:, 3:4]
        m1m = gsc[:, 16:30]
        m2m = gsc[:, 32:46]
        l2 = gsc[:, 48:62]
        b.op("dve", "reduce_max", c0, gps[:, 0:14], axis=AX.X)
        b.op("dve", "tensor_scalar", m1m, gps[:, 0:14], c0, None, OP.is_equal)
        b.op("dve", "scalar_tensor_tensor", l2, m1m, -1e30, gps[:, 0:14], OP.mult, OP.add)
        gps_free_at = b.cnt["e_dve"]
        b.op("dve", "reduce_max", c1, l2, axis=AX.X)
        b.op("dve", "tensor_scalar", m2m, l2, c1, None, OP.is_equal)
        b.op("dve", "tensor_sub", c2, c0, c1)
        t_d = b.op("dve", "tensor_copy", c3, c2)
        b.wait("act", t_d)
        t_s = b.op("act", "activation", c2, c2, AF.Sigmoid)
        b.wait("dve", t_s)
        b.op("dve", "tensor_scalar", c3, c2, -1.0, 1.0, OP.mult, OP.add)
        b.op("dve", "tensor_scalar_mul", m1m, m1m, c2)
        b.op("dve", "tensor_scalar_mul", m2m, m2m, c3)
        if i == 0:
            b.op("dve", "memset", Wt[:, 14:16], 0.0, sig=False)
        t_w = b.op("dve", "tensor_add", Wt[:, 0:14], m1m, m2m)
        b.wait("sp", t_w)
        b.dma("sp", f"{pfx}_Wst", io["W_hbm"][i * 128:(i + 1) * 128, :], Wt[:, :])
    b.flush()
    b.barrier()
    es.close()


def emit_l0attn(b, io):
    nc = b.nc
    es = contextlib.ExitStack()
    H, C = 16, 64
    xn_fm = es.enter_context(nc.sbuf_tensor("a0_xnfm", [128, DC * TOK], BF16))
    qk = es.enter_context(nc.sbuf_tensor("a0_qk", [128, 16 * T], BF16))
    wcol = [es.enter_context(nc.sbuf_tensor(f"a0_w{i}", [128, DC * 128], BF16))
            for i in range(2)]
    bq = es.enter_context(nc.sbuf_tensor("a0_bq", [128, 24], F32))
    bo = es.enter_context(nc.sbuf_tensor("a0_bo", [128, 8], F32))
    vstg = [es.enter_context(nc.sbuf_tensor(f"a0_vs{i}", [128, 512], BF16))
            for i in range(2)]
    vpair = [es.enter_context(nc.sbuf_tensor(f"a0_vp{i}", [128, T], BF16))
             for i in range(2)]
    VSL = 68
    vT = es.enter_context(nc.sbuf_tensor("a0_vT", [128, 2 * 16 * VSL], BF16))
    PT = [es.enter_context(nc.sbuf_tensor(f"a0_PT{i}", [128, 2048], BF16))
          for i in range(2)]
    o_fm = es.enter_context(nc.sbuf_tensor("a0_ofm", [128, DC * T], BF16))
    opair = es.enter_context(nc.sbuf_tensor("a0_opair", [128, 128], BF16))
    rcp = es.enter_context(nc.sbuf_tensor("a0_rcp", [128, 2], F32))
    s0b = es.enter_context(nc.sbuf_tensor("a0_s0", [128, 512], BF16))
    ident = es.enter_context(nc.sbuf_tensor("a0_id", [128, 192], BF16))

    psS2 = es.enter_context(nc.psum_tensor("a0_psS2", [128, 2048], F32))
    ps_pv = [es.enter_context(nc.psum_tensor(f"a0_pspv{i}", [128, 260], F32))
             for i in range(2)]
    ps_tr = es.enter_context(nc.psum_tensor("a0_pstr", [128, 1024], BF16))

    for st in ("a0_ld", "a0_xb", "a0_wld", "a0_vst", "a0_vld", "a0_s0st"):
        b.stream(st)
    idp = np.zeros((128, 192), dtype=NPBF)
    idp[:, 0:128] = np.eye(128, dtype=NPBF)
    idp[0:64, 128:192] = np.eye(64, dtype=NPBF)
    idp[64:128, 128:192] = np.eye(64, dtype=NPBF)
    id_hbm = b.const("identp", idp)
    t_id = b.dma("sp", "a0_ld", ident[:, :], id_hbm[:, :])
    b.wait("pe", t_id)

    emit_xbar_fm(b, xn_fm, io["xn_tm"], D, stream="a0_xb")
    b.dma("sp", "a0_ld", bq[:, :], io["b_qkv_l"][:, :])
    b.dma("sp", "a0_ld", bo[:, :], io["b_o_l"][:, :])
    b.wait("pe", b.tok("a0_xb"))

    wfree = [0, 0]
    for bi in range(NB):
        tb = bi * T
        for m in range(24):
            wbuf = wcol[m % 2]
            b.wait("pool", ("e_pe", wfree[m % 2]))
            t_w = b.dma("pool", "a0_wld", wbuf[:, :],
                        bass.AP(io["w_qkvT"], m * 128,
                                [[3072, 128], [128 * 3072, DC], [1, 128]]))
            b.wait("pe", t_w)
            for s in range(4):
                ps = psS2[:, (s % 2) * 512:(s % 2) * 512 + 512]
                b.wait("pe", ("e_act", b.cnt["e_act"]))
                for c in range(DC):
                    b.mm(ps, wbuf[:, c * 128:(c + 1) * 128],
                         xn_fm[:, c * TOK + tb + s * 512: c * TOK + tb + s * 512 + 512],
                         start=(c == 0), stop=(c == DC - 1), sig=(c == DC - 1))
                b.wait("act", ("e_pe", b.cnt["e_pe"]))
                if m < 16:
                    dst = qk[:, m * T + s * 512: m * T + s * 512 + 512]
                    b.op("act", "activation", dst, ps, AF.Identity, bias=bq[:, m:m + 1])
                else:
                    vbuf = vstg[s % 2]
                    b.wait("act", b.tok("a0_vst"))
                    t_e = b.op("act", "activation", vbuf[:, :], ps, AF.Identity,
                               bias=bq[:, m:m + 1])
                    b.wait("sp", t_e)
                    b.dma("sp", "a0_vst",
                          io["v_hbm"][(m - 16) * 128:(m - 15) * 128, tb + s * 512: tb + s * 512 + 512],
                          vbuf[:, :])
            wfree[m % 2] = b.cnt["e_pe"]
        b.flush()

        b.wait("sp", b.tok("a0_vst"))
        vpfree = [0, 0]
        for hp in range(8):
            h0, h1 = 2 * hp, 2 * hp + 1
            vbuf = vpair[hp % 2]
            b.wait("sp", ("e_pe", vpfree[hp % 2]))
            t_v = b.dma("sp", "a0_vld", vbuf[:, :],
                        io["v_hbm"][hp * 128:(hp + 1) * 128, tb: tb + T])
            b.wait("dve", ("e_pe", b.cnt["e_pe"]))
            b.op("dve", "memset", vT[:, :], 1.0)
            b.wait("pe", t_v)
            b.wait("pe", ("e_dve", b.cnt["e_dve"]))
            for hh in range(2):
                b.wait("pe", ("e_dve", b.cnt["e_dve"]))
                for mt in range(16):
                    b.op("pe", "transpose",
                         ps_tr[:, mt * 64:(mt + 1) * 64],
                         vbuf[hh * 64:(hh + 1) * 64, mt * 128:(mt + 1) * 128],
                         ident[hh * 64:(hh + 1) * 64, 128:192], sig=(mt == 15))
                b.wait("dve", ("e_pe", b.cnt["e_pe"]))
                b.op("dve", "tensor_copy",
                     bass.AP(vT, hh * 16 * VSL, [[2 * 16 * VSL, 128], [VSL, 16], [1, 64]]),
                     ps_tr[:, 0:1024])
            vpfree[hp % 2] = b.cnt["e_pe"]
            b.wait("pe", ("e_dve", b.cnt["e_dve"]))

            exp_tok = None
            for lb in range(4):
                pvfree = b.cnt["e_dve"]

                def emit_pv_round(rnd):
                    for sub in range(2):
                        mt = rnd * 2 + sub
                        for hh in range(2):
                            for ls in range(4):
                                b.mm(ps_pv[hh][:, ls * 65:ls * 65 + 65],
                                     PT[rnd % 2][:, hh * 1024 + sub * 512 + ls * 128:
                                                 hh * 1024 + sub * 512 + ls * 128 + 128],
                                     vT[:, hh * 16 * VSL + mt * VSL: hh * 16 * VSL + mt * VSL + 65],
                                     start=(rnd == 0 and sub == 0 and ls == 0),
                                     stop=(rnd == 7 and sub == 1),
                                     sig=(sub == 1 and hh == 1 and ls == 3))

                for rnd in range(8):
                    b.wait("pe", exp_tok)
                    if rnd == 0:
                        b.wait("pe", ("e_dve", pvfree))
                    for sub in range(2):
                        mt = rnd * 2 + sub
                        for hh, h in ((0, h0), (1, h1)):
                            qv = qk[64 * (h % 2):64 * (h % 2) + 64,
                                    (h // 2) * T + lb * 512:(h // 2) * T + lb * 512 + 512]
                            kv = qk[64 * (h % 2):64 * (h % 2) + 64,
                                    (8 + h // 2) * T + mt * 128:(8 + h // 2) * T + mt * 128 + 128]
                            b.mm(psS2[:, hh * 1024 + sub * 512: hh * 1024 + sub * 512 + 512],
                                 kv, qv, start=True, stop=True,
                                 sig=(sub == 1 and hh == 1))
                    t_sc = ("e_pe", b.cnt["e_pe"])
                    if rnd > 0:
                        emit_pv_round(rnd - 1)
                    b.wait("act", t_sc)
                    exp_tok = b.op("act", "activation", PT[rnd % 2][:, :], psS2[:, :],
                                   AF.Exp, scale=1.0 / 32.0)
                b.wait("pe", exp_tok)
                emit_pv_round(7)
                b.wait("dve", ("e_pe", b.cnt["e_pe"]))
                for ls in range(4):
                    for hh in range(2):
                        b.op("dve", "reciprocal", rcp[:, hh:hh + 1],
                             ps_pv[hh][:, ls * 65 + 64: ls * 65 + 65])
                        t_o = b.op("dve", "tensor_scalar_mul",
                                   opair[:, hh * 64:hh * 64 + 64],
                                   ps_pv[hh][:, ls * 65: ls * 65 + 64],
                                   rcp[:, hh:hh + 1])
                    b.wait("pe", t_o)
                    b.op("pe", "transpose", ps_tr[:, ls * 128:(ls + 1) * 128],
                         opair[:, :], ident[:, 0:128], sig=True)
                    b.wait("dve", ("e_pe", b.cnt["e_pe"]))
                t_c = b.op("dve", "tensor_copy", o_fm[:, hp * T + lb * 512: hp * T + (lb + 1) * 512],
                           ps_tr[:, 0:512])
                b.wait("pe", t_c)
        b.flush()

        for dch in range(DC):
            wbuf = wcol[dch % 2]
            b.wait("pool", ("e_pe", wfree[dch % 2]))
            t_w = b.dma("pool", "a0_wld", wbuf[:, :],
                        bass.AP(io["w_oT"], dch * 128,
                                [[1024, 128], [128 * 1024, DC], [1, 128]]))
            b.wait("pe", t_w)
            for s in range(4):
                ps = psS2[:, (s % 2) * 512:(s % 2) * 512 + 512]
                b.wait("pe", ("e_dve", b.cnt["e_dve"]))
                for c in range(DC):
                    b.mm(ps, wbuf[:, c * 128:(c + 1) * 128],
                         o_fm[:, c * T + s * 512: c * T + s * 512 + 512],
                         start=(c == 0), stop=(c == DC - 1), sig=(c == DC - 1))
                b.wait("dve", ("e_pe", b.cnt["e_pe"]))
                b.wait("dve", b.tok("a0_s0st"))
                t_s = b.op("dve", "scalar_tensor_tensor", s0b[:, :], ps,
                           bo[:, dch:dch + 1],
                           xn_fm[:, dch * TOK + tb + s * 512: dch * TOK + tb + s * 512 + 512],
                           OP.add, OP.add)
                b.wait("sp", t_s)
                b.dma("sp", "a0_s0st",
                      io["s0_fm"][dch * 128:(dch + 1) * 128, tb + s * 512: tb + s * 512 + 512],
                      s0b[:, :])
            wfree[dch % 2] = b.cnt["e_pe"]
        b.flush()
    b.barrier()
    es.close()


def emit_moe(b, io):
    nc = b.nc
    es = contextlib.ExitStack()
    NE = 14
    SLAB = 512
    NSLAB = TOK // SLAB
    xn_fm = es.enter_context(nc.sbuf_tensor("mo_xnfm", [128, DC * TOK], BF16))
    Wtm = es.enter_context(nc.sbuf_tensor("mo_W", [128, 32 * 16], F32))
    Wbf = es.enter_context(nc.sbuf_tensor("mo_Wbf", [128, 16], BF16))
    Wfm = es.enter_context(nc.sbuf_tensor("mo_Wfm", [16, 32 * 128], BF16))
    b1l = es.enter_context(nc.sbuf_tensor("mo_b1", [128, 224], F32))
    b2h = es.enter_context(nc.sbuf_tensor("mo_b2h", [14, 1024], BF16))
    b2lo = es.enter_context(nc.sbuf_tensor("mo_b2l", [14, 1024], BF16))
    ident = es.enter_context(nc.sbuf_tensor("mo_id", [128, 128], BF16))
    es_w = contextlib.ExitStack()
    w1sb = [es_w.enter_context(nc.sbuf_tensor(f"mo_w1{i}", [128, DC * 2048], BF16))
            for i in range(2)]
    w2sb = es_w.enter_context(nc.sbuf_tensor("mo_w2", [128, 16 * 1024], BF16))
    Hs = es_w.enter_context(nc.sbuf_tensor("mo_H", [128, 16 * SLAB], BF16))
    es_b2 = contextlib.ExitStack()
    b2f = es_b2.enter_context(nc.sbuf_tensor("mo_b2f", [14, 1024], F32))
    b2l32 = es_b2.enter_context(nc.sbuf_tensor("mo_b2l32", [14, 1024], F32))
    psA = [es.enter_context(nc.psum_tensor(f"mo_psA{i}", [128, 512], F32)) for i in range(2)]
    psB = [es.enter_context(nc.psum_tensor(f"mo_psB{i}", [128, 1024], F32)) for i in range(2)]
    psT = es.enter_context(nc.psum_tensor("mo_psT", [16, 128], BF16))

    for st in ("mo_ld", "mo_xb", "mo_w1ld", "mo_w2ld", "mo_acc", "mo_fin", "mo_x1"):
        b.stream(st)
    id_hbm = b.const("ident128b", np.eye(128, dtype=NPBF))
    b.dma("sp", "mo_ld", ident[:, :], id_hbm[:, :])
    emit_xbar_fm(b, xn_fm, io["xn_tm"], D, stream="mo_xb")
    b.dma("sp", "mo_ld", Wtm[:, :],
          bass.AP(io["W_hbm"], 0, [[16, 128], [128 * 16, 32], [1, 16]]))
    b.dma("sp", "mo_ld", b1l[:, :], io["moe_b1_l"][:, :])
    t_b2 = b.dma("sp", "mo_ld", b2f[:, :], io["moe_b2"][:, :])
    b.wait("dve", t_b2)
    b.op("dve", "tensor_copy", b2h[:, :], b2f[:, :])
    b.op("dve", "tensor_copy", b2l32[:, :], b2h[:, :])
    b.op("dve", "tensor_sub", b2l32[:, :], b2f[:, :], b2l32[:, :])
    b.op("dve", "tensor_copy", b2lo[:, :], b2l32[:, :])
    es_b2.close()
    stage = es_w.enter_context(nc.sbuf_tensor("mo_stg", [128, 8 * 1024], BF16))
    b.wait("dve", b.tok("mo_ld"))
    for i in range(32):
        t_c = b.op("dve", "tensor_copy", Wbf[:, :], Wtm[:, i * 16:(i + 1) * 16])
        b.wait("pe", t_c)
        t_t = b.op("pe", "transpose", psT[:, :], Wbf[:, :], ident[:, :], sig=True)
        b.wait("dve", t_t)
        t_c2 = b.op("dve", "tensor_copy", Wfm[0:16, i * 128:(i + 1) * 128], psT[:, :])
        b.wait("pe", t_c2)
    b.flush()
    b.wait("pe", b.tok("mo_xb"))

    evacA = [0, 0]
    evacB = [0, 0]
    w1used = [0, 0]
    w2used = 0
    stg_tok = [0, 0]
    t_w1 = [None, None]
    b.wait("pool", ("e_dve", b.cnt["e_dve"]))
    t_w1[0] = b.dma("pool", "mo_w1ld", w1sb[0][:, :],
                    bass.AP(io["moe_w1"], 0, [[2048, 128], [128 * 2048, DC], [1, 2048]]))
    for e in range(NE):
        wsl = e % 2
        b.wait("pool", ("e_pe", b.cnt["e_pe"]))
        t_w2 = b.dma("pool", "mo_w2ld", w2sb[:, :],
                     bass.AP(io["moe_w2"], e * 2048 * 1024,
                             [[1024, 128], [128 * 1024, 16], [1, 1024]]))
        b.wait("pe", t_w1[wsl])
        w2_waited = False
        for sl in range(NSLAB):
            tb = sl * SLAB
            for m in range(16):
                ps = psA[m % 2]
                b.wait("pe", ("e_act", evacA[m % 2]))
                for c in range(DC):
                    b.mm(ps[:, :], w1sb[wsl][:, c * 2048 + m * 128: c * 2048 + m * 128 + 128],
                         xn_fm[:, c * TOK + tb: c * TOK + tb + SLAB],
                         start=(c == 0), stop=(c == DC - 1), sig=(c == DC - 1))
                b.wait("act", ("e_pe", b.cnt["e_pe"]))
                t_ev = b.op("act", "activation", Hs[:, m * SLAB:(m + 1) * SLAB], ps[:, :],
                            AF.Gelu, bias=b1l[:, e * 16 + m: e * 16 + m + 1])
                evacA[m % 2] = t_ev[1]
            if sl == NSLAB - 1:
                w1used[wsl] = b.cnt["e_pe"]
                if e + 1 < NE:
                    b.wait("pool", ("e_pe", w1used[(e + 1) % 2]))
                    t_w1[(e + 1) % 2] = b.dma(
                        "pool", "mo_w1ld", w1sb[(e + 1) % 2][:, :],
                        bass.AP(io["moe_w1"], (e + 1) * 1024 * 2048,
                                [[2048, 128], [128 * 2048, DC], [1, 2048]]))
            b.wait("pe", ("e_act", b.cnt["e_act"]))
            if not w2_waited:
                b.wait("pe", t_w2)
                w2_waited = True
            for tt in range(4):
                ps = psB[tt % 2]
                b.wait("pe", ("e_dve", evacB[tt % 2]))
                for c in range(16):
                    lhs = Hs[:, c * SLAB + tt * 128: c * SLAB + tt * 128 + 128]
                    b.mm(ps[:, 0:512], lhs, w2sb[:, c * 1024: c * 1024 + 512],
                         start=(c == 0), stop=(c == 15))
                    b.mm(ps[:, 512:1024], lhs, w2sb[:, c * 1024 + 512: c * 1024 + 1024],
                         start=(c == 0), stop=(c == 15), sig=(c == 15))
                b.wait("dve", ("e_pe", b.cnt["e_pe"]))
                b.wait("dve", ("mo_acc", stg_tok[(sl % 2)]))
                t_ev = b.op("dve", "tensor_scalar_mul",
                            stage[:, (sl % 2) * 4096 + tt * 1024:
                                  (sl % 2) * 4096 + (tt + 1) * 1024],
                            ps[:, :], Wtm[:, (sl * 4 + tt) * 16 + e:
                                          (sl * 4 + tt) * 16 + e + 1])
                evacB[tt % 2] = t_ev[1]
            t_st = ("e_dve", b.cnt["e_dve"])
            b.wait("pool", t_st)
            b.dma("pool", "mo_acc",
                  bass.AP(io["acc_hbm"], tb * D, [[D, 128], [128 * D, 4], [1, D]]),
                  stage[:, (sl % 2) * 4096: (sl % 2) * 4096 + 4096],
                  **({} if e == 0 else {"accum_op": OP.add}))
            stg_tok[sl % 2] = b.cnt["mo_acc"]
        b.flush()

    b.flush()
    es_w.close()
    b.scope_guard()
    accT_t = es.enter_context(nc.sbuf_tensor("mo_fa", [128, D], F32))
    xT_t = es.enter_context(nc.sbuf_tensor("mo_fx", [128, D], F32))
    t1T_t = es.enter_context(nc.sbuf_tensor("mo_ft", [128, D], F32))
    x1T_t = es.enter_context(nc.sbuf_tensor("mo_fo", [128, D], F32))
    accT = accT_t[:, :]
    xT = xT_t[:, :]
    t1T = t1T_t[:, :]
    x1T = x1T_t[:, :]
    xhT = es.enter_context(nc.sbuf_tensor("mo_xh", [128, D], BF16))
    xlT = es.enter_context(nc.sbuf_tensor("mo_xl", [128, D], BF16))
    s0T = es.enter_context(nc.sbuf_tensor("mo_s0", [128, D], BF16))
    gT = es.enter_context(nc.sbuf_tensor("mo_g", [128, D], BF16))
    stF = es.enter_context(nc.sbuf_tensor("mo_stF", [128, 8], F32))
    do_ln = "xn1_tm" in io
    b.wait("sp", b.tok("mo_acc"))
    evacF = [b.cnt["e_dve"]] * 2
    for i in range(32):
        ps = psB[i % 2]
        b.wait("pe", ("e_dve", evacF[i % 2]))
        wv = Wfm[0:14, i * 128:(i + 1) * 128]
        b.mm(ps[:, 0:512], wv, b2h[0:14, 0:512], start=True, stop=False)
        b.mm(ps[:, 0:512], wv, b2lo[0:14, 0:512], start=False, stop=True)
        b.mm(ps[:, 512:1024], wv, b2h[0:14, 512:1024], start=True, stop=False)
        b.mm(ps[:, 512:1024], wv, b2lo[0:14, 512:1024], start=False, stop=True,
             sig=True)
        t_b2 = ("e_pe", b.cnt["e_pe"])
        prev_dve = b.cnt["e_dve"]
        b.wait("sp", ("e_dve", prev_dve))
        b.wait("sp", ("e_act", b.cnt["e_act"]))
        b.wait("sp", b.tok("mo_x1"))
        t_a = b.dma("sp", "mo_fin", accT, io["acc_hbm"][i * 128:(i + 1) * 128, :])
        t_h = b.dma("sp", "mo_fin", xhT[:, :], io["xn_tm"][i * 128:(i + 1) * 128, :])
        t_l = b.dma("sp", "mo_fin", xlT[:, :], io["xnl_tm"][i * 128:(i + 1) * 128, :])
        t_x = b.dma("sp", "mo_fin", xT, io["x"][i * 128:(i + 1) * 128, :])
        t_s = b.dma_tr("sp", "mo_fin", s0T[:, :],
                       io["s0_fm"][:, i * 128:(i + 1) * 128])
        b.wait("dve", b.tok("mo_fin"))
        b.wait("dve", ("e_act", b.cnt["e_act"]))
        b.op("dve", "tensor_add", t1T, xhT[:, :], xlT[:, :])
        b.op("dve", "tensor_add", t1T, t1T, accT)
        b.wait("dve", t_b2)
        t_ev = b.op("dve", "tensor_add", t1T, ps[:, :], t1T)
        evacF[i % 2] = t_ev[1]
        t_cl = b.op("dve", "tensor_scalar", t1T, t1T, -100.0, 100.0,
                    OP.max, OP.min)
        b.wait("act", t_cl)
        t_g = b.op("act", "activation", gT[:, :], t1T, AF.Silu)
        b.wait("dve", t_g)
        b.op("dve", "tensor_mul", gT[:, :], gT[:, :], s0T[:, :])
        t_x1 = b.op("dve", "tensor_add", x1T, xT, gT[:, :])
        b.wait("sp", t_x1)
        b.dma("sp", "mo_x1", io["x1"][i * 128:(i + 1) * 128, :], x1T)
        if do_ln:
            b.wait("act", t_x1)
            t_sq = b.op("act", "activation", t1T, x1T, AF.Square)
            b.wait("dve", t_sq)
            b.op("dve", "reduce_sum", stF[:, 0:1], x1T, axis=AX.X)
            b.op("dve", "reduce_sum", stF[:, 1:2], t1T, axis=AX.X)
            b.op("dve", "tensor_scalar_mul", stF[:, 0:1], stF[:, 0:1], 1.0 / D)
            b.op("dve", "tensor_scalar_mul", stF[:, 1:2], stF[:, 1:2], 1.0 / D)
            b.op("dve", "tensor_mul", stF[:, 2:3], stF[:, 0:1], stF[:, 0:1])
            b.op("dve", "tensor_sub", stF[:, 2:3], stF[:, 1:2], stF[:, 2:3])
            t_v = b.op("dve", "tensor_scalar_add", stF[:, 3:4], stF[:, 2:3], 1e-5)
            b.wait("act", t_v)
            b.op("act", "activation", stF[:, 3:4], stF[:, 3:4], AF.Ln)
            t_r = b.op("act", "activation", stF[:, 3:4], stF[:, 3:4], AF.Exp,
                       scale=-0.5)
            b.wait("dve", t_r)
            t_n = b.op("dve", "tensor_scalar", gT[:, :], x1T, stF[:, 0:1],
                       stF[:, 3:4], OP.subtract, OP.mult)
            b.wait("sp", t_n)
            b.dma("sp", "mo_x1", io["xn1_tm"][i * 128:(i + 1) * 128, :], gT[:, :])
    b.flush()
    b.barrier()
    es.close()


def rope_scale_np(Tn=T, dhr=32):
    t = np.arange(Tn, dtype=np.float64)
    inv = 1.0 / 10000.0 ** (np.arange(0, dhr, 2, dtype=np.float64) / dhr)
    freqs = t[:, None] * inv[None, :]
    emb = np.concatenate([freqs, freqs], -1)
    return (np.cos(emb) + np.sin(emb)).astype(np.float32)


def emit_l1attn(b, io):
    nc = b.nc
    es = contextlib.ExitStack()
    inv96 = 1.0 / np.sqrt(np.float32(96.0))
    rs = rope_scale_np()
    qsc = np.zeros((128, T), np.float32)
    ksc = np.zeros((128, T), np.float32)
    for p in range(128):
        qsc[p] = rs[:, p % 32] * inv96
    for p in list(range(32)) + list(range(64, 96)):
        ksc[p] = rs[:, p % 32]
    qsc_h = b.const("l1_qsc", qsc)
    ksc_h = b.const("l1_ksc", ksc)
    masks = np.zeros((128, 4 * 512), np.float32)
    ii = np.arange(128)[:, None]
    jj = np.arange(512)[None, :]
    for k in range(4):
        masks[:, k * 512:(k + 1) * 512] = np.where(ii + 128 * k > jj, -1e30, 0.0)
    mask_h = b.const("l1_masks", masks.astype(NPBF))
    idp = np.zeros((128, 192), dtype=NPBF)
    idp[:, 0:128] = np.eye(128, dtype=NPBF)
    idp[0:64, 128:192] = np.eye(64, dtype=NPBF)
    idp[64:128, 128:192] = np.eye(64, dtype=NPBF)
    id_hbm = b.const("identp", idp)

    xn_fm = es.enter_context(nc.sbuf_tensor("l1_xnfm", [128, DC * TOK], BF16))
    qsc_sb = es.enter_context(nc.sbuf_tensor("l1_qscs", [128, T], F32))
    ksc_sb = es.enter_context(nc.sbuf_tensor("l1_kscs", [128, T], F32))
    mask_sb = es.enter_context(nc.sbuf_tensor("l1_msk", [128, 4 * 512], BF16))
    ident = es.enter_context(nc.sbuf_tensor("l1_id", [128, 192], BF16))
    kr = es.enter_context(nc.sbuf_tensor("l1_kr", [128, TOK], BF16))

    for st in ("l1_ld", "l1_xb", "l1_wld", "l1_st", "l1_pld", "l1_s1st"):
        b.stream(st)
    b.dma("sp", "l1_ld", qsc_sb[:, :], qsc_h[:, :])
    b.dma("sp", "l1_ld", ksc_sb[:, :], ksc_h[:, :])
    b.dma("sp", "l1_ld", mask_sb[:, :], mask_h[:, :])
    b.dma("sp", "l1_ld", ident[:, :], id_hbm[:, :])
    emit_xbar_fm(b, xn_fm, io["xn_tm"], D, stream="l1_xb")
    b.wait("pe", b.tok("l1_xb"))
    b.wait("pe", b.tok("l1_ld"))

    esA = contextlib.ExitStack()
    wsmall = esA.enter_context(nc.sbuf_tensor("l1_wsm", [128, DC * 256], BF16))
    wuk = esA.enter_context(nc.sbuf_tensor("l1_wuk", [128, 2 * 1024], BF16))
    wuv = esA.enter_context(nc.sbuf_tensor("l1_wuv", [128, 2 * 1024], BF16))
    wkr = esA.enter_context(nc.sbuf_tensor("l1_wkr", [128, DC * 32], BF16))
    wkrd = esA.enter_context(nc.sbuf_tensor("l1_wkrd", [128, DC * 128], BF16))
    ckv = esA.enter_context(nc.sbuf_tensor("l1_ckv", [128, 2 * T], BF16))
    wcol = [esA.enter_context(nc.sbuf_tensor(f"l1_wc{i}", [128, 32 * 128], BF16))
            for i in range(2)]
    cq = esA.enter_context(nc.sbuf_tensor("l1_cq", [128, 32 * 1024], BF16))
    estg = [esA.enter_context(nc.sbuf_tensor(f"l1_es{i}", [128, 512], BF16))
            for i in range(2)]
    psA = [esA.enter_context(nc.psum_tensor(f"l1_psA{i}", [128, 512], F32))
           for i in range(2)]
    ps4 = esA.enter_context(nc.psum_tensor("l1_ps4", [128, 2048], F32))
    pskr = esA.enter_context(nc.psum_tensor("l1_pskr", [128, 512], F32))

    t_w = b.dma("pool", "l1_wld", wsmall[:, :],
                bass.AP(io["W_DKV"], 0, [[256, 128], [128 * 256, DC], [1, 256]]))
    b.dma("pool", "l1_wld", wuk[:, :],
          bass.AP(io["W_UK"], 0, [[1024, 128], [128 * 1024, 2], [1, 1024]]))
    b.dma("pool", "l1_wld", wuv[:, :],
          bass.AP(io["W_UV"], 0, [[1024, 128], [128 * 1024, 2], [1, 1024]]))
    b.dma("pool", "l1_wld", wkr[:, :],
          bass.AP(io["W_KR"], 0, [[32, 128], [128 * 32, DC], [1, 32]]))
    b.wait("dve", b.tok("l1_wld"))
    b.op("dve", "memset", wkrd[:, :], 0.0)
    for c in range(DC):
        b.op("dve", "tensor_copy", wkrd[:, c * 128: c * 128 + 32],
             wkr[:, c * 32:(c + 1) * 32])
        b.op("dve", "tensor_copy", wkrd[:, c * 128 + 64: c * 128 + 96],
             wkr[:, c * 32:(c + 1) * 32])
    b.wait("pe", b.tok("l1_wld"))
    b.wait("pe", ("e_dve", b.cnt["e_dve"]))

    for bi in range(NB):
        tb = bi * T
        for m in range(2):
            for s in range(4):
                ps = psA[s % 2]
                b.wait("pe", ("e_act", b.cnt["e_act"]))
                b.wait("pe", ("e_dve", b.cnt["e_dve"]))
                for c in range(DC):
                    b.mm(ps[:, :], wsmall[:, c * 256 + m * 128: c * 256 + m * 128 + 128],
                         xn_fm[:, c * TOK + tb + s * 512: c * TOK + tb + s * 512 + 512],
                         start=(c == 0), stop=(c == DC - 1), sig=(c == DC - 1))
                b.wait("act", ("e_pe", b.cnt["e_pe"]))
                b.op("act", "activation", ckv[:, m * T + s * 512: m * T + s * 512 + 512],
                     ps[:, :], AF.Identity)
        for (wt, dst) in ((wuk, io["kc_hbm"]), (wuv, io["v_hbm"])):
            for m in range(DC):
                for s in range(4):
                    ps = psA[s % 2]
                    b.wait("pe", ("e_act", b.cnt["e_act"]))
                    b.wait("pe", ("e_dve", b.cnt["e_dve"]))
                    for c in range(2):
                        b.mm(ps[:, :], wt[:, c * 1024 + m * 128: c * 1024 + m * 128 + 128],
                             ckv[:, c * T + s * 512: c * T + s * 512 + 512],
                             start=(c == 0), stop=(c == 1), sig=(c == 1))
                    b.wait("act", ("e_pe", b.cnt["e_pe"]))
                    stg = estg[s % 2]
                    b.wait("act", b.tok("l1_st"))
                    t_e = b.op("act", "activation", stg[:, :], ps[:, :], AF.Identity)
                    b.wait("sp", t_e)
                    b.dma("sp", "l1_st", dst[m * 128:(m + 1) * 128, tb + s * 512: tb + s * 512 + 512],
                          stg[:, :])
        for s in range(4):
            b.wait("pe", ("e_dve", b.cnt["e_dve"]))
            for c in range(DC):
                b.mm(pskr[:, :], wkrd[:, c * 128:(c + 1) * 128],
                     xn_fm[:, c * TOK + tb + s * 512: c * TOK + tb + s * 512 + 512],
                     start=(c == 0), stop=(c == DC - 1), sig=(c == DC - 1))
            b.wait("dve", ("e_pe", b.cnt["e_pe"]))
            b.op("dve", "tensor_mul", kr[:, tb + s * 512: tb + s * 512 + 512],
                 pskr[:, :], ksc_sb[:, s * 512: s * 512 + 512])
        wfree = [0, 0]
        for hs in range(2):
            ts0 = tb + hs * 1024
            for m in range(32):
                wbuf = wcol[m % 2]
                b.wait("pool", ("e_pe", wfree[m % 2]))
                t_w = b.dma("pool", "l1_wld", wbuf[:, 0:DC * 128],
                            bass.AP(io["W_DQ"], m * 128,
                                    [[4096, 128], [128 * 4096, DC], [1, 128]]))
                b.wait("pe", t_w)
                grp = m % 4
                ps = ps4[:, grp * 512:(grp + 1) * 512]
                if grp == 0:
                    b.wait("pe", ("e_act", b.cnt["e_act"]))
                for c in range(DC):
                    b.mm(ps, wbuf[:, c * 128:(c + 1) * 128],
                         xn_fm[:, c * TOK + ts0: c * TOK + ts0 + 512],
                         start=(c == 0), stop=(c == DC - 1),
                         sig=(c == DC - 1))
                wfree[m % 2] = b.cnt["e_pe"]
                if grp == 3:
                    b.wait("act", ("e_pe", b.cnt["e_pe"]))
                    b.op("act", "activation",
                         bass.AP(cq, (m - 3) * 1024, [[32 * 1024, 128], [1024, 4], [1, 512]]),
                         ps4[:, :], AF.Identity)
            for m in range(32):
                wbuf = wcol[m % 2]
                b.wait("pool", ("e_pe", wfree[m % 2]))
                t_w = b.dma("pool", "l1_wld", wbuf[:, 0:DC * 128],
                            bass.AP(io["W_DQ"], m * 128,
                                    [[4096, 128], [128 * 4096, DC], [1, 128]]))
                b.wait("pe", t_w)
                grp = m % 4
                ps = ps4[:, grp * 512:(grp + 1) * 512]
                if grp == 0:
                    b.wait("pe", ("e_act", b.cnt["e_act"]))
                for c in range(DC):
                    b.mm(ps, wbuf[:, c * 128:(c + 1) * 128],
                         xn_fm[:, c * TOK + ts0 + 512: c * TOK + ts0 + 1024],
                         start=(c == 0), stop=(c == DC - 1),
                         sig=(c == DC - 1))
                wfree[m % 2] = b.cnt["e_pe"]
                if grp == 3:
                    b.wait("act", ("e_pe", b.cnt["e_pe"]))
                    b.op("act", "activation",
                         bass.AP(cq, (m - 3) * 1024 + 512,
                                 [[32 * 1024, 128], [1024, 4], [1, 512]]),
                         ps4[:, :], AF.Identity)
            for (wh, dst_h, scale_mode) in ((io["W_UQ"], io["qc_hbm"], "act"),
                                            (io["W_QR_pad"], io["qr_hbm"], "dve")):
                for m in range(DC):
                    wbuf = wcol[m % 2]
                    b.wait("pool", ("e_pe", wfree[m % 2]))
                    t_w = b.dma("pool", "l1_wld", wbuf[:, :],
                                bass.AP(wh, m * 128,
                                        [[1024, 128], [128 * 1024, 32], [1, 128]]))
                    b.wait("pe", t_w)
                    for half in range(2):
                        ps = psA[half]
                        b.wait("pe", ("e_act", b.cnt["e_act"]))
                        b.wait("pe", ("e_dve", b.cnt["e_dve"]))
                        for c in range(32):
                            b.mm(ps[:, :], wbuf[:, c * 128:(c + 1) * 128],
                                 cq[:, c * 1024 + half * 512: c * 1024 + half * 512 + 512],
                                 start=(c == 0), stop=(c == 31), sig=(c == 31))
                        wfree[m % 2] = b.cnt["e_pe"]
                        stg = estg[half]
                        tloc = hs * 1024 + half * 512
                        if scale_mode == "act":
                            b.wait("act", ("e_pe", b.cnt["e_pe"]))
                            b.wait("act", b.tok("l1_st"))
                            t_e = b.op("act", "activation", stg[:, :], ps[:, :],
                                       AF.Identity, scale=float(inv96))
                        else:
                            b.wait("dve", ("e_pe", b.cnt["e_pe"]))
                            b.wait("dve", b.tok("l1_st"))
                            t_e = b.op("dve", "tensor_mul", stg[:, :], ps[:, :],
                                       qsc_sb[:, (tloc % T): (tloc % T) + 512])
                        b.wait("sp", t_e)
                        b.dma("sp", "l1_st",
                              dst_h[m * 128:(m + 1) * 128, tb + tloc: tb + tloc + 512],
                              stg[:, :])
    b.flush()
    esA.close()

    esB = contextlib.ExitStack()
    VSL = 68
    qcp = [esB.enter_context(nc.sbuf_tensor(f"l1_qc{i}", [128, T], BF16)) for i in range(2)]
    qrp = [esB.enter_context(nc.sbuf_tensor(f"l1_qr{i}", [128, T], BF16)) for i in range(2)]
    kcp = [esB.enter_context(nc.sbuf_tensor(f"l1_kc{i}", [128, T], BF16)) for i in range(2)]
    vp = [esB.enter_context(nc.sbuf_tensor(f"l1_vp{i}", [128, T], BF16)) for i in range(2)]
    vT = esB.enter_context(nc.sbuf_tensor("l1_vT", [128, 2 * 16 * VSL], BF16))
    PT = [esB.enter_context(nc.sbuf_tensor(f"l1_PT{i}", [128, 2048], BF16)) for i in range(2)]
    o_fm = esB.enter_context(nc.sbuf_tensor("l1_ofm", [128, DC * T], BF16))
    opair = esB.enter_context(nc.sbuf_tensor("l1_op", [128, 128], BF16))
    rcp = esB.enter_context(nc.sbuf_tensor("l1_rcp", [128, 2], F32))
    s1b = esB.enter_context(nc.sbuf_tensor("l1_s1b", [128, 512], BF16))
    wcolB = [esB.enter_context(nc.sbuf_tensor(f"l1_wB{i}", [128, DC * 128], BF16))
             for i in range(2)]
    psS2 = esB.enter_context(nc.psum_tensor("l1_psS2", [128, 2048], F32))
    ps_pv = [esB.enter_context(nc.psum_tensor(f"l1_pspv{i}", [128, 260], F32))
             for i in range(2)]
    ps_tr = esB.enter_context(nc.psum_tensor("l1_pstr", [128, 1024], BF16))

    b.wait("sp", b.tok("l1_st"))
    b.wait("pe", ("e_dve", b.cnt["e_dve"]))
    b.scope_guard()
    pfree = [0, 0]
    wfreeB = [0, 0]
    for bi in range(NB):
        tb = bi * T
        for hp in range(8):
            h0, h1 = 2 * hp, 2 * hp + 1
            pb = hp % 2
            b.wait("sp", ("e_pe", pfree[pb]))
            t0_ = b.dma("sp", "l1_pld", qcp[pb][:, :], io["qc_hbm"][hp * 128:(hp + 1) * 128, tb: tb + T])
            b.dma("sp", "l1_pld", qrp[pb][:, :], io["qr_hbm"][hp * 128:(hp + 1) * 128, tb: tb + T])
            b.dma("sp", "l1_pld", kcp[pb][:, :], io["kc_hbm"][hp * 128:(hp + 1) * 128, tb: tb + T])
            t_v = b.dma("sp", "l1_pld", vp[pb][:, :], io["v_hbm"][hp * 128:(hp + 1) * 128, tb: tb + T])
            b.wait("dve", ("e_pe", b.cnt["e_pe"]))
            b.op("dve", "memset", vT[:, :], 1.0)
            b.wait("pe", t_v)
            for hh in range(2):
                b.wait("pe", ("e_dve", b.cnt["e_dve"]))
                for mt in range(16):
                    b.op("pe", "transpose",
                         ps_tr[:, mt * 64:(mt + 1) * 64],
                         vp[pb][hh * 64:(hh + 1) * 64, mt * 128:(mt + 1) * 128],
                         ident[hh * 64:(hh + 1) * 64, 128:192], sig=(mt == 15))
                b.wait("dve", ("e_pe", b.cnt["e_pe"]))
                b.op("dve", "tensor_copy",
                     bass.AP(vT, hh * 16 * VSL, [[2 * 16 * VSL, 128], [VSL, 16], [1, 64]]),
                     ps_tr[:, 0:1024])
            b.wait("pe", ("e_dve", b.cnt["e_dve"]))

            exp_tok = None
            for lb in range(4):
                nmt = 4 * lb + 4
                nrnd = nmt // 2
                pvfree = b.cnt["e_dve"]

                def emit_pv_round(rnd):
                    for sub in range(2):
                        mt = rnd * 2 + sub
                        for hh in range(2):
                            for ls in range(4):
                                b.mm(ps_pv[hh][:, ls * 65: ls * 65 + 65],
                                     PT[rnd % 2][:, hh * 1024 + sub * 512 + ls * 128:
                                                 hh * 1024 + sub * 512 + ls * 128 + 128],
                                     vT[:, hh * 16 * VSL + mt * VSL: hh * 16 * VSL + mt * VSL + 65],
                                     start=(rnd == 0 and sub == 0 and ls == 0),
                                     stop=(rnd == nrnd - 1 and sub == 1),
                                     sig=(sub == 1 and hh == 1 and ls == 3))

                for rnd in range(nrnd):
                    b.wait("pe", exp_tok)
                    if rnd == 0:
                        b.wait("pe", ("e_dve", pvfree))
                    for sub in range(2):
                        mt = rnd * 2 + sub
                        kdiag = mt - 4 * lb
                        for hh, h in ((0, h0), (1, h1)):
                            base = 64 * (h % 2)
                            qcv = qcp[pb][base:base + 64, lb * 512:(lb + 1) * 512]
                            qrv = qrp[pb][base:base + 32, lb * 512:(lb + 1) * 512]
                            kcv = kcp[pb][base:base + 64, mt * 128:(mt + 1) * 128]
                            krv = kr[base:base + 32, tb + mt * 128: tb + (mt + 1) * 128]
                            last = (sub == 1 and hh == 1)
                            pss = psS2[:, hh * 1024 + sub * 512: hh * 1024 + sub * 512 + 512]
                            b.mm(pss, kcv, qcv, start=True, stop=False)
                            b.mm(pss, krv, qrv, start=False, stop=(kdiag < 0),
                                 sig=(last and kdiag < 0))
                            if kdiag >= 0:
                                b.mm(pss, ident[:, 0:128],
                                     mask_sb[:, kdiag * 512:(kdiag + 1) * 512],
                                     start=False, stop=True, sig=last)
                    t_sc = ("e_pe", b.cnt["e_pe"])
                    if rnd > 0:
                        emit_pv_round(rnd - 1)
                    b.wait("act", t_sc)
                    exp_tok = b.op("act", "activation", PT[rnd % 2][:, :], psS2[:, :],
                                   AF.Exp)
                b.wait("pe", exp_tok)
                emit_pv_round(nrnd - 1)
                b.wait("dve", ("e_pe", b.cnt["e_pe"]))
                for ls in range(4):
                    for hh in range(2):
                        b.op("dve", "reciprocal", rcp[:, hh:hh + 1],
                             ps_pv[hh][:, ls * 65 + 64: ls * 65 + 65])
                        t_o = b.op("dve", "tensor_scalar_mul",
                                   opair[:, hh * 64:hh * 64 + 64],
                                   ps_pv[hh][:, ls * 65: ls * 65 + 64],
                                   rcp[:, hh:hh + 1])
                    b.wait("pe", t_o)
                    b.op("pe", "transpose", ps_tr[:, ls * 128:(ls + 1) * 128],
                         opair[:, :], ident[:, 0:128], sig=True)
                    b.wait("dve", ("e_pe", b.cnt["e_pe"]))
                t_c = b.op("dve", "tensor_copy",
                           o_fm[:, hp * T + lb * 512: hp * T + (lb + 1) * 512],
                           ps_tr[:, 0:512])
                b.wait("pe", t_c)
            pfree[pb] = b.cnt["e_pe"]

        for dch in range(DC):
            wbuf = wcolB[dch % 2]
            b.wait("pool", ("e_pe", wfreeB[dch % 2]))
            t_w = b.dma("pool", "l1_wld", wbuf[:, :],
                        bass.AP(io["W_O1"], dch * 128,
                                [[1024, 128], [128 * 1024, DC], [1, 128]]))
            b.wait("pe", t_w)
            for s in range(4):
                ps = psS2[:, (s % 2) * 512:(s % 2) * 512 + 512]
                b.wait("pe", ("e_act", b.cnt["e_act"]))
                for c in range(DC):
                    b.mm(ps, wbuf[:, c * 128:(c + 1) * 128],
                         o_fm[:, c * T + s * 512: c * T + s * 512 + 512],
                         start=(c == 0), stop=(c == DC - 1), sig=(c == DC - 1))
                b.wait("act", ("e_pe", b.cnt["e_pe"]))
                b.wait("act", b.tok("l1_s1st"))
                t_s = b.op("act", "activation", s1b[:, :], ps, AF.Identity)
                b.wait("sp", t_s)
                b.dma("sp", "l1_s1st",
                      io["s1_fm"][dch * 128:(dch + 1) * 128, tb + s * 512: tb + s * 512 + 512],
                      s1b[:, :])
            wfreeB[dch % 2] = b.cnt["e_pe"]
        b.flush()
    b.barrier()
    esB.close()
    es.close()


def xpos_tables_np(n=T, dh=64, window=512):
    half = np.arange(0, dh, 2, dtype=np.float64)
    inv_freq = 1.0 / 10000.0 ** (half / dh)
    t = np.arange(n, dtype=np.float64)
    freqs = t[:, None] * inv_freq[None, :]
    freqs = np.concatenate([freqs, freqs], -1)
    base = (half + 0.4 * dh) / (1.4 * dh)
    power = (t - n // 2) / float(window // 2)
    sc = base[None, :] ** power[:, None]
    sc = np.concatenate([sc, sc], -1)
    return np.cos(freqs), np.sin(freqs), sc


def emit_l1ffn(b, io):
    nc = b.nc
    es = contextlib.ExitStack()
    cos64, sin64, sc64 = xpos_tables_np()
    rows = np.arange(128) % 64
    C1 = (cos64[:, rows] * sc64[:, rows]).T.astype(np.float32)
    S1 = (sin64[:, rows] * sc64[:, rows]).T.astype(np.float32)
    C2 = (cos64[:, rows] / sc64[:, rows]).T.astype(np.float32)
    S2 = (sin64[:, rows] / sc64[:, rows]).T.astype(np.float32)
    R2 = np.zeros((128, 128), np.float32)
    for blk in range(2):
        o = blk * 64
        for m in range(32):
            R2[o + m + 32, o + m] = -1.0
            R2[o + m, o + m + 32] = 1.0
    mkeep_far = np.where(np.arange(128)[None, :] <= np.arange(128)[:, None], 1.0, 0.0)
    mkeep_diag = np.where(np.arange(128)[None, :] >= np.arange(128)[:, None], 1.0, 0.0)
    mk = np.concatenate([mkeep_far, mkeep_diag], axis=1).astype(NPBF)
    E2 = np.zeros((128, 2), NPBF); E2[0:64, 0] = 1; E2[64:128, 1] = 1
    E128 = np.zeros((2, 128), NPBF); E128[0, 0:64] = 1; E128[1, 64:128] = 1
    idp = np.zeros((128, 192), dtype=NPBF)
    idp[:, 0:128] = np.eye(128, dtype=NPBF)
    idp[0:64, 128:192] = np.eye(64, dtype=NPBF)
    idp[64:128, 128:192] = np.eye(64, dtype=NPBF)

    xf = es.enter_context(nc.sbuf_tensor("f_xf", [128, DC * T], BF16))
    r2s = es.enter_context(nc.sbuf_tensor("f_r2", [128, 128], BF16))
    mks = es.enter_context(nc.sbuf_tensor("f_mks", [128, 256], BF16))
    e2s = es.enter_context(nc.sbuf_tensor("f_e2", [128, 2], BF16))
    e128s = es.enter_context(nc.sbuf_tensor("f_e128", [2, 128], BF16))
    ids = es.enter_context(nc.sbuf_tensor("f_id", [128, 192], BF16))
    ones1 = es.enter_context(nc.sbuf_tensor("f_o1", [1, 128], BF16))
    onesc = es.enter_context(nc.sbuf_tensor("f_oc", [128, 1], BF16))
    m30 = es.enter_context(nc.sbuf_tensor("f_m30", [128, 1], F32))
    qs8 = es.enter_context(nc.sbuf_tensor("f_qs8", [128, 4], F32))
    lgb = es.enter_context(nc.sbuf_tensor("f_lgb", [128, 4 * DC], F32))
    b2s = es.enter_context(nc.sbuf_tensor("f_b2s", [128, 2 * DC], F32))
    b1s = es.enter_context(nc.sbuf_tensor("f_b1s", [128, 4 * 22], F32))

    for st in ("f_ld", "f_xb", "f_wld", "f_g1st"):
        b.stream(st)
    tbl_consts = ((C1, "f_C1"), (S1, "f_S1"), (C2, "f_C2"), (S2, "f_S2"))
    for arr, nm in tbl_consts:
        b.const(nm, arr)
    for (sb, arr, nm) in ((r2s, R2.astype(NPBF), "f_R2"),
                          (mks, mk, "f_mk"), (e2s, E2, "f_E2"), (e128s, E128, "f_E128"),
                          (ids, idp, "identp2"), (ones1, np.ones((1, 128), NPBF), "f_on1"),
                          (onesc, np.ones((128, 1), NPBF), "f_onc"),
                          (m30, np.full((128, 1), -30.0, np.float32), "f_m30c")):
        b.dma("sp", "f_ld", sb[:, :], b.const(nm, arr)[:, :])
    b.dma("sp", "f_ld", qs8[:, 0:2], io["f_qs_l"][:, :])
    b.dma("sp", "f_ld", qs8[:, 2:4], io["f_ks_l"][:, :])
    for l in range(2):
        b.dma("sp", "f_ld", lgb[:, (2 * l) * DC:(2 * l + 1) * DC], io["f_lng_l"][l])
        b.dma("sp", "f_ld", lgb[:, (2 * l + 1) * DC:(2 * l + 2) * DC], io["f_lnb_l"][l])
        b.dma("sp", "f_ld", b2s[:, l * DC:(l + 1) * DC], io["f_b2_l"][l])
        b.dma("sp", "f_ld", b1s[:, (2 * l) * 22:(2 * l + 1) * 22], io["f_b1a_l"][l])
        b.dma("sp", "f_ld", b1s[:, (2 * l + 1) * 22:(2 * l + 2) * 22], io["f_b1g_l"][l])
    b.wait("dve", b.tok("f_ld"))
    b.op("dve", "tensor_scalar_mul", qs8[:, 0:2], qs8[:, 0:2], 8.0)

    for bi in range(NB):
        tb = bi * T
        b.wait("sp", ("e_act", b.cnt["e_act"]))
        b.wait("sp", ("e_pe", b.cnt["e_pe"]))
        for c in range(DC):
            b.dma_tr("sp", "f_xb", xf[:, c * T:(c + 1) * T],
                     io["xn_tm"][tb: tb + T, c * 128:(c + 1) * 128])
        b.wait("pe", b.tok("f_xb"))
        b.wait("pe", b.tok("f_ld"))
        b.wait("dve", b.tok("f_xb"))
        b.wait("act", b.tok("f_xb"))

        for l in range(2):
            esL = contextlib.ExitStack()
            qkv = esL.enter_context(nc.sbuf_tensor(f"f_qkv_{bi}_{l}", [128, 12 * T], BF16))
            o_fm = esL.enter_context(nc.sbuf_tensor(f"f_ofm_{bi}_{l}", [128, 4 * T], BF16))
            c1s = esL.enter_context(nc.sbuf_tensor(f"f_c1_{bi}_{l}", [128, T], F32))
            s1s = esL.enter_context(nc.sbuf_tensor(f"f_s1_{bi}_{l}", [128, T], F32))
            c2s = esL.enter_context(nc.sbuf_tensor(f"f_c2_{bi}_{l}", [128, T], F32))
            s2s = esL.enter_context(nc.sbuf_tensor(f"f_s2_{bi}_{l}", [128, T], F32))
            esL2 = contextlib.ExitStack()
            hup = esL2.enter_context(nc.sbuf_tensor(f"f_hup_{bi}_{l}", [128, 4 * T], BF16))
            wc = [esL2.enter_context(nc.sbuf_tensor(f"f_wc{i}_{bi}_{l}", [128, DC * 128], BF16))
                  for i in range(2)]
            psQ = [esL2.enter_context(nc.psum_tensor(f"f_psQ{i}_{bi}_{l}", [128, 512], F32))
                   for i in range(2)]
            b.scope_guard()
            for (sb, (arr, nm)) in zip((c1s, s1s, c2s, s2s), tbl_consts):
                b.dma("sp", "f_ld", sb[:, :], b.const(nm, arr)[:, :])
            b.wait("dve", b.tok("f_ld"))

            b.wait("dve", ("e_pe", b.cnt["e_pe"]))
            for c in range(4):
                b.op("dve", "memset", hup[:, c * T: c * T + 1], 0.0, sig=False)
                b.op("dve", "tensor_copy", hup[:, c * T + 1:(c + 1) * T],
                     xf[:, (4 + c) * T: (4 + c) * T + T - 1])
            hv = lambda c, c0, cn: (xf[:, c * T + c0: c * T + c0 + cn] if c < 4
                                    else hup[:, (c - 4) * T + c0: (c - 4) * T + c0 + cn])
            wfree = [0, 0]
            b.wait("pe", ("e_dve", b.cnt["e_dve"]))
            for m in range(12):
                wb = wc[m % 2]
                b.wait("pool", ("e_pe", wfree[m % 2]))
                t_w = b.dma("pool", "f_wld", wb[:, :],
                            bass.AP(io["f_wqkv"], l * 1024 * 1536 + m * 128,
                                    [[1536, 128], [128 * 1536, DC], [1, 128]]))
                b.wait("pe", t_w)
                for s in range(4):
                    ps = psQ[s % 2]
                    b.wait("pe", ("e_act", b.cnt["e_act"]))
                    for c in range(DC):
                        b.mm(ps[:, :], wb[:, c * 128:(c + 1) * 128], hv(c, s * 512, 512),
                             start=(c == 0), stop=(c == DC - 1), sig=(c == DC - 1))
                    b.wait("act", ("e_pe", b.cnt["e_pe"]))
                    b.op("act", "activation", qkv[:, m * T + s * 512: m * T + s * 512 + 512],
                         ps[:, :], AF.Identity)
                wfree[m % 2] = b.cnt["e_pe"]
            if bi == 0 and l == 0 and "dbg_qkv" in io:
                b.wait("sp", ("e_act", b.cnt["e_act"]))
                b.dma("sp", "dbg", bass.AP(io["dbg_qkv"], 0, [[2048, 128], [128 * 2048, 12], [1, 2048]]),
                      qkv[:, :])
            b.flush()
            esL2.close()

            esH = contextlib.ExitStack()
            sq = esH.enter_context(nc.sbuf_tensor(f"f_sq_{bi}_{l}", [128, T], BF16))
            rn = esH.enter_context(nc.sbuf_tensor(f"f_rn_{bi}_{l}", [2, T], F32))
            rnb = esH.enter_context(nc.sbuf_tensor(f"f_rnb_{bi}_{l}", [2, T], BF16))
            qn = esH.enter_context(nc.sbuf_tensor(f"f_qn_{bi}_{l}", [128, T], BF16))
            q3 = esH.enter_context(nc.sbuf_tensor(f"f_q3_{bi}_{l}", [128, T], BF16))
            k3 = esH.enter_context(nc.sbuf_tensor(f"f_k3_{bi}_{l}", [128, T], BF16))
            t1b = esH.enter_context(nc.sbuf_tensor(f"f_t1b_{bi}_{l}", [128, T], BF16))
            ostash = esH.enter_context(nc.sbuf_tensor(f"f_ost_{bi}_{l}", [128, 1024], BF16))
            vTf = esH.enter_context(nc.sbuf_tensor(f"f_vTf_{bi}_{l}", [128, 2 * 1088], BF16))
            Pring = [esH.enter_context(nc.sbuf_tensor(f"f_Pr{i}_{bi}_{l}", [128, 640], BF16))
                     for i in range(6)]
            obuf = esH.enter_context(nc.sbuf_tensor(f"f_ob_{bi}_{l}", [128, 2 * 68], F32))
            mxd = esH.enter_context(nc.sbuf_tensor(f"f_mxd_{bi}_{l}", [128, 4], F32))
            opair = esH.enter_context(nc.sbuf_tensor(f"f_opr_{bi}_{l}", [128, 128], BF16))
            psB = esH.enter_context(nc.psum_tensor(f"f_psB_{bi}_{l}", [128, 1024], F32))
            psK = [esH.enter_context(nc.psum_tensor(f"f_psK{i}_{bi}_{l}", [128, 640], F32))
                   for i in range(2)]
            psO2 = esH.enter_context(nc.psum_tensor(f"f_psO2_{bi}_{l}", [128, 2 * 68], F32))
            psT = esH.enter_context(nc.psum_tensor(f"f_psT_{bi}_{l}", [128, 1024], BF16))

            b.wait("dve", ("e_pe", b.cnt["e_pe"]))
            b.op("dve", "memset", vTf[:, :], 1.0)

            for hp in range(4):
                for (qk_idx, cs, ss, dst) in ((0, c1s, s1s, q3), (1, c2s, s2s, k3)):
                    bc = (qk_idx * 4 + hp) * T
                    scal = qs8[:, l: l + 1] if qk_idx == 0 else qs8[:, 2 + l: 3 + l]
                    for half in range(2):
                        h0 = half * 1024
                        t_s = b.op("act", "activation", sq[:, h0:h0 + 1024],
                                   qkv[:, bc + h0: bc + h0 + 1024], AF.Square)
                        b.wait("pe", t_s)
                        b.wait("pe", ("e_dve", b.cnt["e_dve"]))
                        for s2 in range(2):
                            b.mm(psB[0:2, s2 * 512:(s2 + 1) * 512], e2s[:, :],
                                 sq[:, h0 + s2 * 512: h0 + (s2 + 1) * 512],
                                 start=True, stop=True, sig=(s2 == 1))
                        b.wait("act", ("e_pe", b.cnt["e_pe"]))
                        b.op("act", "activation", rn[:, h0:h0 + 1024], psB[0:2, 0:1024],
                             AF.Ln)
                        t_r = b.op("act", "activation", rn[:, h0:h0 + 1024],
                                   rn[:, h0:h0 + 1024], AF.Exp, scale=-0.5)
                        b.wait("dve", t_r)
                        t_rb = b.op("dve", "tensor_copy", rnb[:, h0:h0 + 1024],
                                    rn[:, h0:h0 + 1024])
                        b.wait("pe", t_rb)
                        for s2 in range(2):
                            b.mm(psB[:, s2 * 512:(s2 + 1) * 512], e128s[:, :],
                                 rnb[0:2, h0 + s2 * 512: h0 + (s2 + 1) * 512],
                                 start=True, stop=True, sig=(s2 == 1))
                        b.wait("dve", ("e_pe", b.cnt["e_pe"]))
                        t_qn = b.op("dve", "scalar_tensor_tensor", qn[:, h0:h0 + 1024],
                                    qkv[:, bc + h0: bc + h0 + 1024], scal,
                                    psB[:, 0:1024], OP.mult, OP.mult)
                        b.wait("pe", t_qn)
                        for s2 in range(2):
                            b.mm(psB[:, s2 * 512:(s2 + 1) * 512], r2s[:, :],
                                 qn[:, h0 + s2 * 512: h0 + (s2 + 1) * 512],
                                 start=True, stop=True, sig=(s2 == 1))
                        b.wait("dve", ("e_pe", b.cnt["e_pe"]))
                        b.op("dve", "tensor_mul", t1b[:, h0:h0 + 1024],
                             ss[:, h0:h0 + 1024], psB[:, 0:1024])
                        b.op("dve", "tensor_mul", dst[:, h0:h0 + 1024],
                             qn[:, h0:h0 + 1024], cs[:, h0:h0 + 1024])
                        t_d = b.op("dve", "tensor_add", dst[:, h0:h0 + 1024],
                                   dst[:, h0:h0 + 1024], t1b[:, h0:h0 + 1024])
                        b.wait("pe", t_d)
                for hh in range(2):
                    b.wait("pe", ("e_dve", b.cnt["e_dve"]))
                    for mt in range(16):
                        b.op("pe", "transpose", psT[:, mt * 64:(mt + 1) * 64],
                             qkv[64 * hh: 64 * hh + 64,
                                 (8 + hp) * T + mt * 128: (8 + hp) * T + (mt + 1) * 128],
                             ids[64 * hh:64 * hh + 64, 128:192], sig=(mt == 15))
                        b.wait("dve", ("e_pe", b.cnt["e_pe"]))
                    b.op("dve", "tensor_copy",
                         bass.AP(vTf, hh * 1088, [[2 * 1088, 128], [68, 16], [1, 64]]),
                         psT[:, 0:1024])
                b.wait("pe", ("e_dve", b.cnt["e_dve"]))
                for hh in range(2):
                    base = 64 * hh
                    expdone = [b.cnt["e_act"]] * 2
                    pringfree = [b.cnt["e_pe"]] * 6
                    odone = [b.cnt["e_dve"]] * 2
                    pend = None

                    def emit_pv(qb, t_e, hh=hh, odone=odone):
                        b.wait("pe", t_e)
                        r = qb % 2
                        b.wait("pe", ("e_dve", odone[r]))
                        lo = max(0, qb - 4)
                        for mt2 in range(lo, qb + 1):
                            blk = qb - mt2
                            b.mm(psO2[:, r * 68: r * 68 + 65],
                                 Pring[mt2 % 6][:, blk * 128:(blk + 1) * 128],
                                 vTf[:, hh * 1088 + mt2 * 68: hh * 1088 + mt2 * 68 + 65],
                                 start=(mt2 == lo), stop=(mt2 == qb), sig=(mt2 == qb))
                        b.wait("dve", ("e_pe", b.cnt["e_pe"]))
                        t_o = b.op("dve", "tensor_copy", obuf[:, r * 68: r * 68 + 65],
                                   psO2[:, r * 68: r * 68 + 65])
                        odone[r] = t_o[1]
                        b.op("dve", "tensor_copy", mxd[:, 1:2],
                             obuf[:, r * 68 + 64: r * 68 + 65])
                        b.op("dve", "reciprocal", mxd[:, 2:3], mxd[:, 1:2])
                        b.op("dve", "tensor_scalar_mul",
                             ostash[:, qb * 64:(qb + 1) * 64] if hh == 0
                             else opair[:, 64:128],
                             obuf[:, r * 68: r * 68 + 64], mxd[:, 2:3])
                        if hh == 1:
                            b.op("dve", "tensor_copy", opair[:, 0:64],
                                 ostash[:, qb * 64:(qb + 1) * 64])
                            b.wait("pe", ("e_dve", b.cnt["e_dve"]))
                            b.op("pe", "transpose", psT[:, 0:128], opair[:, :],
                                 ids[:, 0:128], sig=True)
                            b.wait("dve", ("e_pe", b.cnt["e_pe"]))
                            t_of = b.op("dve", "tensor_copy",
                                        o_fm[:, hp * T + qb * 128:
                                             hp * T + (qb + 1) * 128],
                                        psT[:, 0:128])
                            b.wait("pe", t_of)

                    mskdone = [b.cnt["e_dve"]] * 6
                    for mt in range(16):
                        q0 = mt * 128
                        nq = min(640, T - q0)
                        ps = psK[mt % 2]
                        b.wait("pe", ("e_act", expdone[mt % 2]))
                        kv_ = k3[base:base + 64, mt * 128:(mt + 1) * 128]
                        if nq > 512:
                            b.mm(ps[:, 0:512], kv_, q3[base:base + 64, q0:q0 + 512],
                                 start=True, stop=True)
                            b.mm(ps[:, 512:nq], kv_,
                                 q3[base:base + 64, q0 + 512:q0 + nq],
                                 start=True, stop=True, sig=True)
                        else:
                            b.mm(ps[:, 0:nq], kv_, q3[base:base + 64, q0:q0 + nq],
                                 start=True, stop=True, sig=True)
                        t_sc = ("e_pe", b.cnt["e_pe"])
                        if pend is not None:
                            emit_pv(*pend)
                            pringfree[(pend[0] - 4) % 6] = b.cnt["e_pe"]
                        b.wait("act", t_sc)
                        b.wait("act", ("e_pe", pringfree[mt % 6]))
                        b.wait("act", ("e_dve", mskdone[mt % 6]))
                        t_e = b.op("act", "activation", Pring[mt % 6][:, 0:nq],
                                   ps[:, 0:nq], AF.Exp, bias=m30[:, 0:1])
                        expdone[mt % 2] = t_e[1]
                        b.wait("dve", t_e)
                        b.op("dve", "tensor_mul", Pring[mt % 6][:, 0:128],
                             Pring[mt % 6][:, 0:128], mks[:, 128:256])
                        if nq == 640:
                            b.op("dve", "tensor_mul", Pring[mt % 6][:, 512:640],
                                 Pring[mt % 6][:, 512:640], mks[:, 0:128])
                        t_m = ("e_dve", b.cnt["e_dve"])
                        mskdone[mt % 6] = t_m[1]
                        pend = (mt, t_m)
                    emit_pv(*pend)
            esH.close()
            b.flush()

            esF = contextlib.ExitStack()
            wos = esF.enter_context(nc.sbuf_tensor(f"f_wos_{bi}_{l}", [128, 4 * 1024], BF16))
            psW = [esF.enter_context(nc.psum_tensor(f"f_psW{i}_{bi}_{l}", [128, 512], F32))
                   for i in range(2)]
            b.scope_guard()
            t_w = b.dma("pool", "f_wld", wos[:, :],
                        bass.AP(io["f_wo"], l * 512 * 1024,
                                [[1024, 128], [128 * 1024, 4], [1, 1024]]))
            b.wait("pe", t_w)
            b.wait("pe", ("e_dve", b.cnt["e_dve"]))
            for m in range(DC):
                for s in range(4):
                    ps = psW[s % 2]
                    b.wait("pe", ("e_dve", b.cnt["e_dve"]))
                    for c in range(4):
                        b.mm(ps[:, :], wos[:, c * 1024 + m * 128: c * 1024 + m * 128 + 128],
                             o_fm[:, c * T + s * 512: c * T + s * 512 + 512],
                             start=(c == 0), stop=(c == 3), sig=(c == 3))
                    b.wait("dve", ("e_pe", b.cnt["e_pe"]))
                    b.op("dve", "tensor_add", xf[:, m * T + s * 512: m * T + s * 512 + 512],
                         xf[:, m * T + s * 512: m * T + s * 512 + 512], ps[:, :])
            if bi == 0 and l == 0 and "dbg_xfwo" in io:
                b.wait("sp", ("e_dve", b.cnt["e_dve"]))
                b.dma("sp", "dbg", bass.AP(io["dbg_xfwo"], 0, [[2048, 128], [128 * 2048, 8], [1, 2048]]),
                      xf[:, :])
            if bi == 0 and l == 0 and "dbg_ofm" in io:
                b.wait("sp", ("e_dve", b.cnt["e_dve"]))
                b.dma("sp", "dbg", bass.AP(io["dbg_ofm"], 0, [[2048, 128], [128 * 2048, 4], [1, 2048]]),
                      o_fm[:, :])
            esF.close()
            b.flush()
            esL.close()

            esM = contextlib.ExitStack()
            sq2 = esM.enter_context(nc.sbuf_tensor(f"f_sq2_{bi}_{l}", [128, 512], BF16))
            stat = esM.enter_context(nc.sbuf_tensor(f"f_stat_{bi}_{l}", [1, 4 * 512], F32))
            statb = esM.enter_context(nc.sbuf_tensor(f"f_statb_{bi}_{l}", [1, 2 * 512], BF16))
            hn = esM.enter_context(nc.sbuf_tensor(f"f_hn_{bi}_{l}", [128, DC * 2048], BF16))
            tmpa = esM.enter_context(nc.sbuf_tensor(f"f_tmpa_{bi}_{l}", [128, 4 * 512], BF16))
            wcA = [esM.enter_context(nc.sbuf_tensor(f"f_wcA{i}_{bi}_{l}", [128, DC * 128], BF16))
                   for i in range(2)]
            wcG = [esM.enter_context(nc.sbuf_tensor(f"f_wcG{i}_{bi}_{l}", [128, DC * 128], BF16))
                   for i in range(2)]
            w2s = [esM.enter_context(nc.sbuf_tensor(f"f_w2s{i}_{bi}_{l}", [128, 22 * 128], BF16))
                   for i in range(2)]
            psL = esM.enter_context(nc.psum_tensor(f"f_psL_{bi}_{l}", [1, 512], F32))
            psL2 = esM.enter_context(nc.psum_tensor(f"f_psL2_{bi}_{l}", [1, 512], F32))
            psMa = esM.enter_context(nc.psum_tensor(f"f_psMa_{bi}_{l}", [128, 512], F32))
            psMb = esM.enter_context(nc.psum_tensor(f"f_psMb_{bi}_{l}", [128, 512], F32))
            psM1 = [esM.enter_context(nc.psum_tensor(f"f_psM1{i}_{bi}_{l}", [128, 512], F32))
                    for i in range(2)]
            psM2 = [esM.enter_context(nc.psum_tensor(f"f_psM2{i}_{bi}_{l}", [128, 512], F32))
                    for i in range(2)]
            esM0 = contextlib.ExitStack()
            hup2 = esM0.enter_context(nc.sbuf_tensor(f"f_hup2_{bi}_{l}", [128, 4 * T], BF16))
            b.scope_guard()

            b.wait("dve", ("e_pe", b.cnt["e_pe"]))
            for c in range(4):
                b.op("dve", "memset", hup2[:, c * T: c * T + 1], 0.0, sig=False)
                b.op("dve", "tensor_copy", hup2[:, c * T + 1:(c + 1) * T],
                     xf[:, (4 + c) * T: (4 + c) * T + T - 1])
            h2v = lambda c, c0, cn: (xf[:, c * T + c0: c * T + c0 + cn] if c < 4
                                     else hup2[:, (c - 4) * T + c0: (c - 4) * T + c0 + cn])
            b.wait("pe", ("e_dve", b.cnt["e_dve"]))
            for s in range(4):
                s0 = s * 512
                b.wait("pe", ("e_dve", b.cnt["e_dve"]))
                for c in range(DC):
                    b.mm(psL[:, :], onesc[:, :], h2v(c, s0, 512),
                         start=(c == 0), stop=(c == DC - 1), sig=(c == DC - 1))
                for c in range(DC):
                    b.wait("act", ("e_pe", b.cnt["e_pe"]) if c == 0 else None)
                    t_q = b.op("act", "activation", sq2[:, :], h2v(c, s0, 512), AF.Square)
                    b.wait("pe", t_q)
                    b.mm(psL2[:, :], onesc[:, :], sq2[:, :],
                         start=(c == 0), stop=(c == DC - 1), sig=True)
                    if c < DC - 1:
                        b.wait("act", ("e_pe", b.cnt["e_pe"]))
                b.wait("dve", ("e_pe", b.cnt["e_pe"]))
                b.op("dve", "tensor_scalar_mul", stat[:, 0:512], psL[:, :], 1.0 / D)
                b.op("dve", "tensor_scalar_mul", stat[:, 512:1024], psL2[:, :], 1.0 / D)
                b.op("dve", "tensor_mul", stat[:, 1024:1536], stat[:, 0:512], stat[:, 0:512])
                b.op("dve", "tensor_sub", stat[:, 1024:1536], stat[:, 512:1024],
                     stat[:, 1024:1536])
                t_v = b.op("dve", "tensor_scalar_add", stat[:, 1024:1536],
                           stat[:, 1024:1536], 1e-5)
                b.wait("act", t_v)
                b.op("act", "activation", stat[:, 1536:2048], stat[:, 1024:1536], AF.Ln)
                t_r = b.op("act", "activation", stat[:, 1536:2048], stat[:, 1536:2048],
                           AF.Exp, scale=-0.5)
                b.wait("dve", t_r)
                b.op("dve", "tensor_copy", statb[:, 0:512], stat[:, 0:512])
                t_sb = b.op("dve", "tensor_copy", statb[:, 512:1024], stat[:, 1536:2048])
                b.wait("pe", t_sb)
                b.mm(psMa[:, :], ones1[:, :], statb[0:1, 0:512], start=True, stop=True)
                b.mm(psMb[:, :], ones1[:, :], statb[0:1, 512:1024], start=True, stop=True,
                     sig=True)
                b.wait("dve", ("e_pe", b.cnt["e_pe"]))
                for c in range(DC):
                    b.op("dve", "tensor_sub", hn[:, c * 2048 + s0: c * 2048 + s0 + 512],
                         h2v(c, s0, 512), psMa[:, :])
                    b.op("dve", "tensor_mul", hn[:, c * 2048 + s0: c * 2048 + s0 + 512],
                         hn[:, c * 2048 + s0: c * 2048 + s0 + 512], psMb[:, :])
                    b.op("dve", "tensor_scalar", hn[:, c * 2048 + s0: c * 2048 + s0 + 512],
                         hn[:, c * 2048 + s0: c * 2048 + s0 + 512],
                         lgb[:, (2 * l) * DC + c: (2 * l) * DC + c + 1],
                         lgb[:, (2 * l + 1) * DC + c: (2 * l + 1) * DC + c + 1],
                         OP.mult, OP.add)
            b.flush()
            esM0.close()
            esM2 = contextlib.ExitStack()
            hm = esM2.enter_context(nc.sbuf_tensor(f"f_hm_{bi}_{l}", [128, 22 * 2048], BF16))
            hn_free = b.cnt["e_dve"]

            wfA = [0, 0]
            wfG = [0, 0]
            evacA = [0, 0]
            evacG = [0, 0]
            muldone = [0, 0, 0, 0]
            for m in range(22):
                mw = 128 if m < 21 else 42
                sl = m % 2
                b.wait("pool", ("e_pe", wfA[sl]))
                t_a = b.dma("pool", "f_wld", wcA[sl][:, 0:DC * mw],
                            bass.AP(io["f_w1"], l * 1024 * 5460 + m * 128,
                                    [[5460, 128], [128 * 5460, DC], [1, mw]]))
                b.wait("pool", ("e_pe", wfG[sl]))
                t_g = b.dma("pool", "f_wld", wcG[sl][:, 0:DC * mw],
                            bass.AP(io["f_w1"], l * 1024 * 5460 + 2730 + m * 128,
                                    [[5460, 128], [128 * 5460, DC], [1, mw]]))
                b.wait("pe", t_a)
                b.wait("pe", ("e_dve", hn_free))
                for s in range(4):
                    ps = psM1[s % 2]
                    b.wait("pe", ("e_act", evacA[s % 2]))
                    for c in range(DC):
                        b.mm(ps[0:mw, :], wcA[sl][:, c * mw:(c + 1) * mw],
                             hn[:, c * 2048 + s * 512: c * 2048 + (s + 1) * 512],
                             start=(c == 0), stop=(c == DC - 1), sig=(c == DC - 1))
                    b.wait("act", ("e_pe", b.cnt["e_pe"]))
                    b.wait("act", ("e_dve", muldone[s]))
                    t_ev = b.op("act", "activation", tmpa[0:mw, s * 512:(s + 1) * 512],
                                ps[0:mw, :], AF.Identity,
                                bias=b1s[0:mw, (2 * l) * 22 + m: (2 * l) * 22 + m + 1])
                    evacA[s % 2] = t_ev[1]
                wfA[sl] = b.cnt["e_pe"]
                b.wait("pe", t_g)
                for s in range(4):
                    ps2 = psM2[s % 2]
                    b.wait("pe", ("e_act", evacG[s % 2]))
                    for c in range(DC):
                        b.mm(ps2[0:mw, :], wcG[sl][:, c * mw:(c + 1) * mw],
                             hn[:, c * 2048 + s * 512: c * 2048 + (s + 1) * 512],
                             start=(c == 0), stop=(c == DC - 1), sig=(c == DC - 1))
                    b.wait("act", ("e_pe", b.cnt["e_pe"]))
                    t_gv = b.op("act", "activation",
                                hm[0:mw, m * 2048 + s * 512: m * 2048 + (s + 1) * 512],
                                ps2[0:mw, :], AF.Gelu,
                                bias=b1s[0:mw, (2 * l + 1) * 22 + m: (2 * l + 1) * 22 + m + 1])
                    evacG[s % 2] = t_gv[1]
                    b.wait("dve", t_gv)
                    t_mul = b.op("dve", "tensor_mul",
                                 hm[0:mw, m * 2048 + s * 512: m * 2048 + (s + 1) * 512],
                                 tmpa[0:mw, s * 512:(s + 1) * 512],
                                 hm[0:mw, m * 2048 + s * 512: m * 2048 + (s + 1) * 512])
                    muldone[s] = t_mul[1]
                wfG[sl] = b.cnt["e_pe"]

            hm_done = ("e_dve", b.cnt["e_dve"])
            wf2 = [0, 0]
            evac2 = [0, 0]
            for m2 in range(DC):
                sl = m2 % 2
                b.wait("pool", ("e_pe", wf2[sl]))
                t_wa = b.dma("pool", "f_wld", w2s[sl][:, 0:21 * 128],
                             bass.AP(io["f_w2"], l * 2730 * 1024 + m2 * 128,
                                     [[1024, 128], [128 * 1024, 21], [1, 128]]))
                t_wb = b.dma("pool", "f_wld", w2s[sl][0:42, 21 * 128:22 * 128],
                             bass.AP(io["f_w2"], l * 2730 * 1024 + 21 * 128 * 1024 + m2 * 128,
                                     [[1024, 42], [1, 128]]))
                b.wait("pe", t_wb)
                b.wait("pe", hm_done)
                for s in range(4):
                    ps = psM1[s % 2]
                    b.wait("pe", ("e_act", evacA[s % 2]))
                    b.wait("pe", ("e_dve", evac2[s % 2]))
                    for c in range(22):
                        cw = 128 if c < 21 else 42
                        b.mm(ps[:, :], w2s[sl][0:cw, c * 128:(c + 1) * 128],
                             hm[0:cw, c * 2048 + s * 512: c * 2048 + (s + 1) * 512],
                             start=(c == 0), stop=(c == 21), sig=(c == 21))
                    b.wait("dve", ("e_pe", b.cnt["e_pe"]))
                    t_x = b.op("dve", "scalar_tensor_tensor",
                               xf[:, m2 * T + s * 512: m2 * T + (s + 1) * 512], ps[:, :],
                               b2s[:, l * DC + m2: l * DC + m2 + 1],
                               xf[:, m2 * T + s * 512: m2 * T + (s + 1) * 512],
                               OP.add, OP.add)
                    evac2[s % 2] = t_x[1]
                wf2[sl] = b.cnt["e_pe"]
            esM2.close()
            esM.close()
            b.flush()

        if not hasattr(b, "_g1b"):
            b._g1b = es.enter_context(nc.sbuf_tensor("f_g1b", [128, T], BF16))
        g1b = b._g1b
        for c in range(DC):
            b.wait("act", b.tok("f_g1st"))
            b.wait("act", ("e_dve", b.cnt["e_dve"]))
            t_g = b.op("act", "activation", g1b[:, :], xf[:, c * T:(c + 1) * T], AF.Silu)
            b.wait("sp", t_g)
            b.dma("sp", "f_g1st", io["g1_fm"][c * 128:(c + 1) * 128, tb: tb + T], g1b[:, :])
        b.flush()
    b.barrier()
    es.close()


def emit_combine(b, io):
    nc = b.nc
    es = contextlib.ExitStack()
    x1t = es.enter_context(nc.sbuf_tensor("cb_x1", [128, D], F32))
    s1t = es.enter_context(nc.sbuf_tensor("cb_s1", [128, D], BF16))
    g1t = es.enter_context(nc.sbuf_tensor("cb_g1", [128, D], BF16))
    pt = es.enter_context(nc.sbuf_tensor("cb_p", [128, D], BF16))
    ot = es.enter_context(nc.sbuf_tensor("cb_o", [128, D], F32))
    b.stream("cb_ld")
    b.stream("cb_st")
    for i in range(TOK // 128):
        prev = b.cnt["e_dve"]
        b.wait("sp", ("e_dve", prev))
        b.wait("sp", b.tok("cb_st"))
        b.dma("sp", "cb_ld", x1t[:, :], io["x1"][i * 128:(i + 1) * 128, :])
        b.dma_tr("sp", "cb_ld", s1t[:, :], io["s1_fm"][:, i * 128:(i + 1) * 128])
        b.dma_tr("sp", "cb_ld", g1t[:, :], io["g1_fm"][:, i * 128:(i + 1) * 128])
        b.wait("dve", b.tok("cb_ld"))
        b.op("dve", "tensor_mul", pt[:, :], s1t[:, :], g1t[:, :])
        t_o = b.op("dve", "tensor_add", ot[:, :], x1t[:, :], pt[:, :])
        b.wait("sp", t_o)
        b.dma("sp", "cb_st", io["out"][i * 128:(i + 1) * 128, :], ot[:, :])
    b.flush()
    b.barrier()
    es.close()


from concourse.bass_utils import run_bass_kernel_spmd

N_CORES = 8


def _drams(nc, specs):
    io = {}
    for name, shape, dt, kind in specs:
        io[name] = nc.dram_tensor(name, shape, dt, kind=kind)
    return io


def _run(build_fn, specs, per_core_feeds):
    nc = bass.Bass("TRN2", target_bir_lowering=False)
    io = _drams(nc, specs)
    es = contextlib.ExitStack()
    bb = B(nc, es)
    build_fn(bb, io)
    bb.barrier()
    bb.flush()
    es.close()
    res = run_bass_kernel_spmd(nc, per_core_feeds, core_ids=list(range(N_CORES)))
    return res.results


def host_prep_all(inputs):
    w = host_prep_weights(inputs)
    w["moe_b1_l"] = np.ascontiguousarray(
        np.asarray(inputs["L0_ffn_b1"], np.float32).reshape(14, 16, 128)
        .transpose(2, 0, 1).reshape(128, 224))
    b1 = np.asarray(inputs["L1_ffn_b1"], np.float32)
    b1a = np.zeros((2, 128, 22), np.float32)
    b1g = np.zeros((2, 128, 22), np.float32)
    for l in range(2):
        for m in range(22):
            mw = 128 if m < 21 else 42
            b1a[l, :mw, m] = b1[l, m * 128: m * 128 + mw]
            b1g[l, :mw, m] = b1[l, 2730 + m * 128: 2730 + m * 128 + mw]
    w["f_b1a_l"] = np.ascontiguousarray(b1a)
    w["f_b1g_l"] = np.ascontiguousarray(b1g)
    w["moe_b2"] = np.ascontiguousarray(np.asarray(inputs["L0_ffn_b2"], np.float32))
    return w


def kernel(**inputs):
    w = host_prep_all(inputs)
    x_full = np.ascontiguousarray(np.asarray(inputs["L0_attn_x"], np.float32))
    shards = [np.ascontiguousarray(x_full[2 * i: 2 * i + 2].reshape(TOK, D))
              for i in range(N_CORES)]
    NPF32 = np.float32

    specs = [("x", [TOK, D], F32, "ExternalInput"),
             ("gate_wT", [1024, 14], F32, "ExternalInput"),
             ("xn_tm", [TOK, D], BF16, "ExternalOutput"),
             ("xnl_tm", [TOK, D], BF16, "ExternalOutput"),
             ("W_hbm", [TOK, 16], F32, "ExternalOutput")]
    feeds = [{"x": shards[i], "gate_wT": w["gate_wT"]} for i in range(N_CORES)]
    r1 = _run(lambda bb, io: emit_prep(bb, io), specs, feeds)

    specs = [("xn_tm", [TOK, D], BF16, "ExternalInput"),
             ("w_qkvT", [1024, 3072], F32, "ExternalInput"),
             ("b_qkv_l", [128, 24], F32, "ExternalInput"),
             ("w_oT", [1024, 1024], F32, "ExternalInput"),
             ("b_o_l", [128, 8], F32, "ExternalInput"),
             ("v_hbm", [1024, TOK], BF16, "Internal"),
             ("s0_fm", [1024, TOK], BF16, "ExternalOutput")]
    feeds = [{"xn_tm": r1[i]["xn_tm"], "w_qkvT": w["w_qkvT"], "b_qkv_l": w["b_qkv_l"],
              "w_oT": w["w_oT"], "b_o_l": w["b_o_l"]} for i in range(N_CORES)]
    r2 = _run(emit_l0attn, specs, feeds)

    specs = [("xn_tm", [TOK, D], BF16, "ExternalInput"),
             ("xnl_tm", [TOK, D], BF16, "ExternalInput"),
             ("W_hbm", [TOK, 16], F32, "ExternalInput"),
             ("moe_w1", [14, 1024, 2048], F32, "ExternalInput"),
             ("moe_b1_l", [128, 224], F32, "ExternalInput"),
             ("moe_w2", [14, 2048, 1024], F32, "ExternalInput"),
             ("moe_b2", [14, 1024], F32, "ExternalInput"),
             ("x", [TOK, D], F32, "ExternalInput"),
             ("s0_fm", [1024, TOK], BF16, "ExternalInput"),
             ("acc_hbm", [TOK, D], F32, "Internal"),
             ("x1", [TOK, D], F32, "ExternalOutput")]
    feeds = [{"xn_tm": r1[i]["xn_tm"], "xnl_tm": r1[i]["xnl_tm"], "W_hbm": r1[i]["W_hbm"],
              "moe_w1": w["moe_w1"], "moe_b1_l": w["moe_b1_l"], "moe_w2": w["moe_w2"],
              "moe_b2": w["moe_b2"], "x": shards[i], "s0_fm": r2[i]["s0_fm"]}
             for i in range(N_CORES)]
    r3 = _run(emit_moe, specs, feeds)

    specs = [("x", [TOK, D], F32, "ExternalInput"),
             ("xn_tm", [TOK, D], BF16, "ExternalOutput")]
    feeds = [{"x": r3[i]["x1"]} for i in range(N_CORES)]
    r4 = _run(lambda bb, io: emit_prep(bb, io, with_lo=False, with_gate=False),
              specs, feeds)

    specs = [("xn_tm", [TOK, D], BF16, "ExternalInput"),
             ("W_DQ", [1024, 4096], F32, "ExternalInput"),
             ("W_UQ", [4096, 1024], F32, "ExternalInput"),
             ("W_DKV", [1024, 256], F32, "ExternalInput"),
             ("W_UK", [256, 1024], F32, "ExternalInput"),
             ("W_UV", [256, 1024], F32, "ExternalInput"),
             ("W_QR_pad", [4096, 1024], F32, "ExternalInput"),
             ("W_KR", [1024, 32], F32, "ExternalInput"),
             ("W_O1", [1024, 1024], F32, "ExternalInput"),
             ("v_hbm", [1024, TOK], BF16, "Internal"),
             ("qc_hbm", [1024, TOK], BF16, "Internal"),
             ("qr_hbm", [1024, TOK], BF16, "Internal"),
             ("kc_hbm", [1024, TOK], BF16, "Internal"),
             ("s1_fm", [1024, TOK], BF16, "ExternalOutput")]
    feeds = [dict({"xn_tm": r4[i]["xn_tm"]},
                  **{k: w[k] for k in ("W_DQ", "W_UQ", "W_DKV", "W_UK", "W_UV",
                                       "W_QR_pad", "W_KR", "W_O1")})
             for i in range(N_CORES)]
    r5 = _run(emit_l1attn, specs, feeds)

    specs = [("xn_tm", [TOK, D], BF16, "ExternalInput"),
             ("f_wqkv", [2, 1024, 1536], F32, "ExternalInput"),
             ("f_qs_l", [128, 2], F32, "ExternalInput"),
             ("f_ks_l", [128, 2], F32, "ExternalInput"),
             ("f_wo", [2, 512, 1024], F32, "ExternalInput"),
             ("f_lng_l", [2, 128, 8], F32, "ExternalInput"),
             ("f_lnb_l", [2, 128, 8], F32, "ExternalInput"),
             ("f_w1", [2, 1024, 5460], F32, "ExternalInput"),
             ("f_b1a_l", [2, 128, 22], F32, "ExternalInput"),
             ("f_b1g_l", [2, 128, 22], F32, "ExternalInput"),
             ("f_w2", [2, 2730, 1024], F32, "ExternalInput"),
             ("f_b2_l", [2, 128, 8], F32, "ExternalInput"),
             ("g1_fm", [1024, TOK], BF16, "ExternalOutput")]
    feeds = [dict({"xn_tm": r4[i]["xn_tm"]},
                  **{k: w[k] for k in ("f_wqkv", "f_qs_l", "f_ks_l", "f_wo", "f_lng_l",
                                       "f_lnb_l", "f_w1", "f_b1a_l", "f_b1g_l",
                                       "f_w2", "f_b2_l")})
             for i in range(N_CORES)]
    r6 = _run(emit_l1ffn, specs, feeds)

    specs = [("x1", [TOK, D], F32, "ExternalInput"),
             ("s1_fm", [1024, TOK], BF16, "ExternalInput"),
             ("g1_fm", [1024, TOK], BF16, "ExternalInput"),
             ("out", [TOK, D], F32, "ExternalOutput")]
    feeds = [{"x1": r3[i]["x1"], "s1_fm": r5[i]["s1_fm"], "g1_fm": r6[i]["g1_fm"]}
             for i in range(N_CORES)]
    r7 = _run(emit_combine, specs, feeds)

    out = np.empty((16, 2048, 1024), np.float32)
    for i in range(N_CORES):
        out[2 * i: 2 * i + 2] = np.asarray(r7[i]["out"], np.float32).reshape(2, 2048, 1024)
    return out


def build_fused(nc):
    io = {}
    ext_in = [("x", [TOK, D], F32), ("gate_wT", [1024, 14], F32),
              ("w_qkvT", [1024, 3072], F32), ("b_qkv_l", [128, 24], F32),
              ("w_oT", [1024, 1024], F32), ("b_o_l", [128, 8], F32),
              ("moe_w1", [14, 1024, 2048], F32), ("moe_b1_l", [128, 224], F32),
              ("moe_w2", [14, 2048, 1024], F32), ("moe_b2", [14, 1024], F32),
              ("W_DQ", [1024, 4096], F32), ("W_UQ", [4096, 1024], F32),
              ("W_DKV", [1024, 256], F32), ("W_UK", [256, 1024], F32),
              ("W_UV", [256, 1024], F32), ("W_QR_pad", [4096, 1024], F32),
              ("W_KR", [1024, 32], F32), ("W_O1", [1024, 1024], F32),
              ("f_wqkv", [2, 1024, 1536], F32), ("f_qs_l", [128, 2], F32),
              ("f_ks_l", [128, 2], F32), ("f_wo", [2, 512, 1024], F32),
              ("f_lng_l", [2, 128, 8], F32), ("f_lnb_l", [2, 128, 8], F32),
              ("f_w1", [2, 1024, 5460], F32), ("f_b1a_l", [2, 128, 22], F32),
              ("f_b1g_l", [2, 128, 22], F32), ("f_w2", [2, 2730, 1024], F32),
              ("f_b2_l", [2, 128, 8], F32)]
    for n, sh, dt in ext_in:
        io[n] = nc.dram_tensor(n, sh, dt, kind="ExternalInput")
    internal = [("xn_tm", [TOK, D], BF16), ("xnl_tm", [TOK, D], BF16),
                ("W_hbm", [TOK, 16], F32), ("v_hbm", [1024, TOK], BF16),
                ("s0_fm", [1024, TOK], BF16), ("acc_hbm", [TOK, D], F32),
                ("x1", [TOK, D], F32), ("xn1_tm", [TOK, D], BF16),
                ("qc_hbm", [1024, TOK], BF16), ("qr_hbm", [1024, TOK], BF16),
                ("kc_hbm", [1024, TOK], BF16), ("s1_fm", [1024, TOK], BF16),
                ("g1_fm", [1024, TOK], BF16)]
    for n, sh, dt in internal:
        io[n] = nc.dram_tensor(n, sh, dt, kind="Internal")
    io["out"] = nc.dram_tensor("out", [TOK, D], F32, kind="ExternalOutput")

    es = contextlib.ExitStack()
    bb = B(nc, es)
    emit_prep(bb, io, pfx="p0")
    emit_l0attn(bb, io)
    emit_moe(bb, io)
    io_l1 = dict(io)
    io_l1["xn_tm"] = io["xn1_tm"]
    emit_l1attn(bb, io_l1)
    emit_l1ffn(bb, io_l1)
    emit_combine(bb, io)
    bb.barrier()
    bb.flush()
    es.close()
    return io


def kernel_fused(**inputs):
    w = host_prep_all(inputs)
    x_full = np.ascontiguousarray(np.asarray(inputs["L0_attn_x"], np.float32))
    shards = [np.ascontiguousarray(x_full[2 * i: 2 * i + 2].reshape(TOK, D))
              for i in range(N_CORES)]
    nc = bass.Bass("TRN2", target_bir_lowering=False)
    build_fused(nc)
    feeds = []
    wk = ("gate_wT", "w_qkvT", "b_qkv_l", "w_oT", "b_o_l", "moe_w1", "moe_b1_l",
          "moe_w2", "moe_b2", "W_DQ", "W_UQ", "W_DKV", "W_UK", "W_UV", "W_QR_pad",
          "W_KR", "W_O1", "f_wqkv", "f_qs_l", "f_ks_l", "f_wo", "f_lng_l", "f_lnb_l",
          "f_w1", "f_b1a_l", "f_b1g_l", "f_w2", "f_b2_l")
    for i in range(N_CORES):
        f = {k: w[k] for k in wk}
        f["x"] = shards[i]
        feeds.append(f)
    res = run_bass_kernel_spmd(nc, feeds, core_ids=list(range(N_CORES)))
    out = np.empty((16, 2048, 1024), np.float32)
    for i in range(N_CORES):
        out[2 * i: 2 * i + 2] = np.asarray(res.results[i]["out"], np.float32).reshape(2, 2048, 1024)
    return out


kernel = kernel_fused
